# version 16
# baseline (speedup 1.0000x reference)
import contextlib
import math
import numpy as np
import concourse.bass as bass
import concourse.mybir as mybir
from concourse.bass_utils import run_bass_kernel_spmd

F32 = mybir.dt.float32
BF16 = mybir.dt.bfloat16
AF = mybir.ActivationFunctionType
ALU = mybir.AluOpType
AX = mybir.AxisListType

T, D = 2048, 1024
TC, DCH = T // 128, D // 128
NCORE = 8
ENGS = ("sync", "gpsimd", "tensor", "vector", "scalar")


class Op:
    __slots__ = ("i", "eng", "fn", "deps", "dma", "special", "inc_amt", "cum", "needed", "sem_name")

    def __init__(self, i, eng, fn, deps, dma, special):
        self.i, self.eng, self.fn = i, eng, fn
        self.deps = [d for d in deps if d is not None]
        self.dma = dma
        self.special = special
        self.inc_amt = 16 if dma else 1
        self.cum = 0
        self.needed = False
        self.sem_name = None


class Prog:

    def __init__(self, bld, name="p"):
        self.bld = bld
        self.name = name
        self.ops = []

    def op(self, eng, fn, deps=(), dma=False, special=False):
        o = Op(len(self.ops), eng, fn, deps, dma, special)
        self.ops.append(o)
        return o.i

    def run(self, barrier=False):
        bld = self.bld
        ops = self.ops
        entry = {sn: v for sn, v in bld.cnt.items() if v > 0} if barrier else {}
        NDMASEM = 6
        for o in ops:
            if o.dma:
                o.needed = True
            for d in o.deps:
                p = ops[d]
                if p.eng != o.eng or p.special or p.dma or p.eng in ("vector", "scalar", "gpsimd"):
                    p.needed = True
        cum = {e: bld.cnt.get("S" + e, 0) for e in ENGS}
        for o in ops:
            if o.dma:
                slot = bld.dslot.get(o.eng, 0)
                bld.dslot[o.eng] = slot + 1
                sn = f"D{o.eng}{slot % NDMASEM}"
                k = bld.cnt.get(sn, 0) // 16 + 1
                bld.cnt[sn] = 16 * k
                o.sem_name = sn
                o.cum = 16 * k
            elif o.needed:
                cum[o.eng] += o.inc_amt
                o.sem_name = "S" + o.eng
                o.cum = cum[o.eng]

        def mk(eng_name):
            mine = [o for o in ops if o.eng == eng_name]
            if not mine:
                return None

            def section(eng):
                waited = {}
                for sn, v in entry.items():
                    eng.wait_ge(bld.sem(sn), v)
                    waited[sn] = v
                for o in mine:
                    for d in o.deps:
                        p = ops[d]
                        if p.eng == eng_name and not (p.special or p.dma) and p.eng not in ("vector", "scalar", "gpsimd"):
                            continue
                        if p.cum == 0:
                            continue
                        sn = p.sem_name
                        if waited.get(sn, -1) < p.cum:
                            eng.wait_ge(bld.sem(sn), p.cum)
                            waited[sn] = p.cum
                    if o.dma and o.cum > 16:
                        sn = o.sem_name
                        if waited.get(sn, -1) < o.cum - 16:
                            eng.wait_ge(bld.sem(sn), o.cum - 16)
                            waited[sn] = o.cum - 16
                    instr = o.fn(eng)
                    if o.needed:
                        assert instr is not None, f"needed op returned None on {eng_name} (op {o.i})"
                        instr.then_inc(bld.sem(o.sem_name), o.inc_amt)
            return section

        sections = {e: mk(e) for e in ENGS}
        with bld.nc.Block() as blk:
            for e in ENGS:
                if sections[e] is not None:
                    getattr(blk, e)(sections[e])
        for e in ENGS:
            bld.cnt["S" + e] = cum[e]


class Bld:
    def __init__(self):
        self.nc = bass.Bass("TRN2", target_bir_lowering=False)
        self.es = contextlib.ExitStack()
        self._sems = {}
        self.cnt = {}
        self._uid = 0
        self.lu = {}
        self.dslot = {}

    def uid(self):
        self._uid += 1
        return self._uid

    def sem(self, name):
        if name not in self._sems:
            self._sems[name] = self.es.enter_context(self.nc.semaphore(name))
            self.cnt.setdefault(name, 0)
        return self._sems[name]

    def _stack(self):
        sc = getattr(self, "_scopes", None)
        return sc[-1] if sc else self.es

    def push_scope(self):
        if not hasattr(self, "_scopes"):
            self._scopes = []
        self._scopes.append(self.es.enter_context(contextlib.ExitStack()))

    def pop_scope(self):
        self._scopes.pop().close()
        self.lu = {}
        for a in ("_ctile", "_cmap"):
            if hasattr(self, a):
                delattr(self, a)

    def sbuf(self, name, shape, dt=F32, **kw):
        return self._stack().enter_context(self.nc.sbuf_tensor(f"{name}_{self.uid()}", shape, dt, **kw))

    def psum(self, name, shape, dt=F32):
        return self._stack().enter_context(self.nc.psum_tensor(f"{name}_{self.uid()}", shape, dt))

    def constcol(self, pr, value):
        if not hasattr(self, "_ctile"):
            self._ctile = self.sbuf("consts", [128, 16])
            self._cmap = {}
        if value not in self._cmap:
            col = len(self._cmap)
            assert col < 16
            ap = self._ctile[:, col:col + 1]
            opid = pr.op("gpsimd", (lambda ap, value: lambda e: e.memset(ap, value))(ap, value))
            self._cmap[value] = (ap, opid)
        return self._cmap[value]

    def war(self, key):
        v = self.lu.get(key)
        if v is None:
            return []
        return list(v) if isinstance(v, (list, tuple)) else [v]

    def setlu(self, key, opid):
        self.lu[key] = opid


def ecopy(e, out, in_):
    if hasattr(e, "tensor_copy"):
        return e.tensor_copy(out, in_)
    return e.copy(out, in_)


def dma_load_tm(pr, dram, sb, nchunks=TC, width=D, eng="sync", deps=()):
    src = dram.rearrange("(c p) d -> p c d", p=128)
    dst = sb[:, 0:nchunks * width].rearrange("p (c d) -> p c d", d=width)
    return pr.op(eng, lambda e: e.dma_start(out=dst, in_=src), deps=deps, dma=True)


def ln_chunk_ops(pr, st, col, src_ap, dep_src, eps):
    c = col
    width = src_ap.shape[-1]
    o1 = pr.op("vector", lambda v: v.tensor_reduce(st["s1"][:, c:c + 1], src_ap, AX.X, ALU.add),
               deps=dep_src)
    o2 = pr.op("scalar", lambda s: s.activation(st["junk"][:, 0:width], src_ap, AF.Square,
                                                accum_out=st["s2"][:, c:c + 1]),
               deps=dep_src)
    o3 = pr.op("vector", lambda v: v.tensor_scalar_mul(st["mu"][:, c:c + 1], st["s1"][:, c:c + 1], 1.0 / width),
               deps=[o1])
    o4 = pr.op("vector", lambda v: v.tensor_mul(st["var"][:, c:c + 1], st["mu"][:, c:c + 1], st["mu"][:, c:c + 1]),
               deps=[o3])
    o5 = pr.op("vector", lambda v: v.scalar_tensor_tensor(st["var"][:, c:c + 1], st["s2"][:, c:c + 1],
                                                          1.0 / width, st["var"][:, c:c + 1],
                                                          ALU.mult, ALU.subtract),
               deps=[o2, o4])
    eps_ap, eps_op = pr.bld.constcol(pr, eps)
    o6 = pr.op("scalar", lambda s: s.activation(st["sd"][:, c:c + 1], st["var"][:, c:c + 1], AF.Sqrt, bias=eps_ap),
               deps=[o5, eps_op])
    o7 = pr.op("vector", lambda v: v.reciprocal(st["rstd"][:, c:c + 1], st["sd"][:, c:c + 1]),
               deps=[o6])
    o8 = pr.op("vector", lambda v: v.scalar_tensor_tensor(st["nmr"][:, c:c + 1], st["mu"][:, c:c + 1],
                                                          -1.0, st["rstd"][:, c:c + 1], ALU.mult, ALU.mult),
               deps=[o7])
    return o7, o8


def ln_apply_chunk(pr, st, col, src_ap, dst_ap, dep_stats, dep_dst=()):
    c = col
    return pr.op("scalar", lambda s: s.activation(dst_ap, src_ap, AF.Identity,
                                                  bias=st["nmr"][:, c:c + 1],
                                                  scale=st["rstd"][:, c:c + 1]),
                 deps=list(dep_stats) + list(dep_dst))


def alloc_stats(bld, tag, nchunks=TC, width=D, junk=None):
    if junk is None:
        junk = bld.sbuf(f"{tag}junk", [128, width], BF16)
    return {
        "junk": junk,
        "s1": bld.sbuf(f"{tag}s1", [128, nchunks]),
        "s2": bld.sbuf(f"{tag}s2", [128, nchunks]),
        "mu": bld.sbuf(f"{tag}mu", [128, nchunks]),
        "var": bld.sbuf(f"{tag}var", [128, nchunks]),
        "sd": bld.sbuf(f"{tag}sd", [128, nchunks]),
        "rstd": bld.sbuf(f"{tag}rstd", [128, nchunks]),
        "nmr": bld.sbuf(f"{tag}nmr", [128, nchunks]),
    }


def emit_ln_full(pr, st, src_sb, dst_sb, dep_src, nchunks=TC, width=D, eps=1e-5):
    outs = []
    for c in range(nchunks):
        src = src_sb[:, c * width:(c + 1) * width]
        dst = dst_sb[:, c * width:(c + 1) * width]
        o7, o8 = ln_chunk_ops(pr, st, c, src, dep_src, eps)
        outs.append(ln_apply_chunk(pr, st, c, src, dst, [o7, o8]))
    return outs


def emit_transpose(pr, src_chunk, dst_chunk, tchunks, fchunks, idn, pT, dep_src,
                   dst_off=0, evac_eng="scalar", dep_idn=()):
    bld = pr.bld
    nb = len(pT)
    all_evacs = []
    gi = 0
    for f in range(fchunks):
        for g in range((tchunks + 7) // 8):
            tcs = list(range(g * 8, min(g * 8 + 8, tchunks)))
            p = pT[gi % nb]
            war = bld.war(id(p))

            def mk_t(p=p, tcs=tcs, f=f):
                def fn(t):
                    for j, tc in enumerate(tcs):
                        instr = t.transpose(p[:, j * 128:(j + 1) * 128],
                                            src_chunk(tc)[:, f * 128:(f + 1) * 128], idn[:])
                    return instr
                return fn
            ot = pr.op("tensor", mk_t(), deps=list(dep_src) + list(dep_idn) + war)

            def mk_v(p=p, tcs=tcs, f=f):
                dst = dst_chunk(f)
                a = dst_off + tcs[0] * 128
                b = dst_off + (tcs[-1] + 1) * 128
                n = len(tcs) * 128
                return lambda v: ecopy(v, dst[:, a:b], p[:, 0:n])
            oe = pr.op(evac_eng, mk_v(), deps=[ot])
            bld.setlu(id(p), oe)
            all_evacs.append(oe)
            gi += 1
    return all_evacs


def emit_proj(pr, lhs, rhs, tiles, kchunks, psums, out_cb, dep_ops):
    bld = pr.bld
    nb = len(psums)
    evacs = []
    for i, tile in enumerate(tiles):
        p = psums[i % nb]
        war = bld.war(id(p))

        def mk_t(p=p, tile=tile):
            def fn(t):
                for k in range(kchunks):
                    instr = t.matmul(p[:], lhs(k, tile), rhs(k, tile),
                                     start=(k == 0), stop=(k == kchunks - 1))
                return instr
            return fn
        ot = pr.op("tensor", mk_t(), deps=list(dep_ops) + war)
        eng, fn, extra = out_cb(tile, p, i)
        oe = pr.op(eng, fn, deps=[ot] + list(extra))
        bld.setlu(id(p), oe)
        evacs.append(oe)
    return evacs


def emit_bcast_rows(pr, specs, ones_row, psums, dep_rows):
    bld = pr.bld
    out = []
    i = 0
    for row_ap, dst, width in specs:
        outs = []
        for b in range((width + 511) // 512):
            w = min(512, width - b * 512)
            p = psums[i % len(psums)]
            war = bld.war(id(p))

            def mk_t(p=p, b=b, w=w, row_ap=row_ap):
                return lambda t: t.matmul(p[:], ones_row[:],
                                          row_ap[:, b * 512:b * 512 + w], start=True, stop=True)
            ot = pr.op("tensor", mk_t(), deps=list(dep_rows) + war)

            def mk_v(p=p, b=b, w=w, dst=dst):
                return lambda v: v.tensor_copy(dst[:, b * 512:b * 512 + w], p[:])
            oe = pr.op("vector", mk_v(), deps=[ot])
            bld.setlu(id(p), oe)
            outs.append(oe)
            i += 1
        out.append(outs)
    return out


def emit_row_to_cols(pr, row_ap, dst_cols, ncols, ones_row, pTs, dep, dep_ones=()):
    bld = pr.bld
    p = pTs[0]
    war = bld.war(id(p))

    def mk_t():
        def fn(t):
            for e in range(ncols):
                instr = t.matmul(p[:, e:e + 1], row_ap[:, e * 128:(e + 1) * 128],
                                 ones_row[:, 0:1], start=True, stop=True)
            return instr
        return fn
    ot = pr.op("tensor", mk_t(), deps=list(dep) + list(dep_ones) + war)
    oe = pr.op("vector", lambda v: v.tensor_copy(dst_cols[:, 0:ncols], p[:, 0:ncols]), deps=[ot])
    bld.setlu(id(p), oe)
    return oe


DEBUG = {"on": False, "bufs": []}


def dbg(name, sb, cols):
    if DEBUG["on"]:
        DEBUG["bufs"].append((name, sb, cols))


def body_block3(bld, pr, dram):
    nc = bld.nc
    H, DX = 8, 64
    x_d, wqkv_d, bqkv_d, wfc_d, bfc_d, gam_d, bet_d, idn_d, xo_d = dram

    xch = bld.sbuf("xch", [128, 3 * D])
    xn_tm = bld.sbuf("xn_tm", [128, TC * D], BF16)
    xn_fm = bld.sbuf("xn_fm", [128, DCH * T], BF16)
    q_tm = bld.sbuf("q_tm", [128, TC * 512], BF16)
    k_tm = bld.sbuf("k_tm", [128, TC * 512], BF16)
    v_fm = bld.sbuf("v_fm", [128, 4 * T], BF16)
    o_fm = xn_fm
    wq_sb = bld.sbuf("wq_sb", [128, DCH * 1536], BF16)
    wfc_sb = bld.sbuf("wfc_sb", [128, 4 * D], BF16)
    idn = bld.sbuf("idn", [128, 128], BF16)
    ones_row = bld.sbuf("ones_row", [1, 128], BF16)
    bqk_b = bld.sbuf("bqk_b", [128, 1536], BF16)
    bfc_b = bld.sbuf("bfc_b", [128, D], BF16)
    gam_b = bld.sbuf("gam_b", [128, D], BF16)
    bet_b = bld.sbuf("bet_b", [128, D], BF16)
    rows = bld.sbuf("rows", [1, 1536 + 3 * D], BF16)
    vcol = bld.sbuf("vcol", [128, 4])
    P_bf = bld.sbuf("P_bf", [128, 4 * 64], BF16)
    nmax = bld.sbuf("nmax", [128, 4])
    ssum = bld.sbuf("ssum", [128, 4])
    F_tm = bld.sbuf("F_tm", [128, 2 * D])
    G_tm = bld.sbuf("G_tm", [128, 2 * D])
    junk = bld.sbuf("junk", [128, D], BF16)
    st1 = alloc_stats(bld, "b3ln1", junk=junk)
    st2 = alloc_stats(bld, "b3ln2", junk=junk)
    st3 = alloc_stats(bld, "b3ln3", junk=junk)

    pA = [bld.psum("pA", [128, 512]) for _ in range(2)]
    pT = [bld.psum("pT", [128, 1024], BF16) for _ in range(2)]
    pS = [bld.psum("pS", [128, 64]) for _ in range(2)]
    pO = [bld.psum("pO", [64, 512]) for _ in range(2)]

    x_src = x_d.rearrange("(c p) d -> p c d", p=128)
    xo_dst = xo_d.rearrange("(c p) d -> p c d", p=128)
    ld_wq = pr.op("gpsimd", lambda e: e.dma_start(
        out=wq_sb[:].rearrange("p (c e) -> p c e", e=1536),
        in_=wqkv_d.rearrange("(c p) e -> p c e", p=128)), dma=True)
    ld_wf = pr.op("gpsimd", lambda e: e.dma_start(
        out=wfc_sb[:].rearrange("p (c e) -> p c e", e=D),
        in_=wfc_d.rearrange("(c p) e -> p c e", p=128)), dma=True)
    ld_r1 = pr.op("gpsimd", lambda e: e.dma_start(out=rows[:, 0:1536], in_=bqkv_d[None, :]), dma=True)
    ld_r2 = pr.op("gpsimd", lambda e: e.dma_start(out=rows[:, 1536:1536 + D], in_=bfc_d[None, :]), dma=True)
    ld_r3 = pr.op("gpsimd", lambda e: e.dma_start(out=rows[:, 1536 + D:1536 + 2 * D], in_=gam_d[None, :]), dma=True)
    ld_r4 = pr.op("gpsimd", lambda e: e.dma_start(out=rows[:, 1536 + 2 * D:1536 + 3 * D], in_=bet_d[None, :]), dma=True)
    ld_id = pr.op("gpsimd", lambda e: e.dma_start(out=idn[:], in_=idn_d), dma=True)
    o_ones = pr.op("gpsimd", lambda e: e.memset(ones_row[:], 1.0))
    ld_rows = [ld_r1, ld_r2, ld_r3, ld_r4, o_ones]

    bcs = emit_bcast_rows(pr, [
        (rows[:, 0:1536], bqk_b, 1536),
        (rows[:, 1536:1536 + D], bfc_b, D),
        (rows[:, 1536 + D:1536 + 2 * D], gam_b, D),
        (rows[:, 1536 + 2 * D:1536 + 3 * D], bet_b, D),
    ], ones_row, pA, ld_rows)
    bc_bqk, bc_bfc, bc_gam, bc_bet = bcs

    o_vcol = emit_row_to_cols(pr, rows[:, 1024:1536], vcol, 4, ones_row, pA, [ld_r1], dep_ones=[o_ones])

    ln1 = []
    for c in range(TC):
        sl = c % 3
        xs = xch[:, sl * D:(sl + 1) * D]
        ld = pr.op("sync", (lambda c, xs: lambda e: e.dma_start(out=xs, in_=x_src[:, c, :]))(c, xs),
                   deps=bld.war(("xch", sl)), dma=True)
        o7, o8 = ln_chunk_ops(pr, st1, c, xs, [ld], 1e-5)
        oap = ln_apply_chunk(pr, st1, c, xs, xn_tm[:, c * D:(c + 1) * D], [o7, o8])
        bld.setlu(("xch", sl), [oap, ln1[-1] if False else o7])
        bld.setlu(("xch", sl), [oap, o7])
        ln1.append(oap)

    tr1 = emit_transpose(pr, lambda c: xn_tm[:, c * D:(c + 1) * D],
                         lambda f: xn_fm[:, f * T:(f + 1) * T], TC, DCH, idn, pT,
                         dep_src=ln1, dep_idn=[ld_id])

    def lhs1(k, m):
        return xn_fm[:, k * T + m * 128: k * T + (m + 1) * 128]

    def qk_cb(dst, bias0, bc):
        def cb(tile, p, i):
            m = tile

            def fn(v):
                return v.tensor_add(dst[:, m * 512:(m + 1) * 512], p[:],
                                    bqk_b[:, bias0:bias0 + 512])
            return ("vector", fn, bc)
        return cb

    pq = emit_proj(pr, lhs1, lambda k, m: wq_sb[:, k * 1536:k * 1536 + 512],
                   list(range(TC)), DCH, pA, qk_cb(q_tm, 0, bc_bqk),
                   dep_ops=tr1 + [ld_wq])
    pk = emit_proj(pr, lhs1, lambda k, m: wq_sb[:, k * 1536 + 512:k * 1536 + 1024],
                   list(range(TC)), DCH, pA, qk_cb(k_tm, 512, bc_bqk),
                   dep_ops=tr1 + [ld_wq])

    def v_cb(tile, p, i):
        e, n = tile

        def fn(v):
            return v.tensor_scalar_add(v_fm[:, e * T + n * 512: e * T + n * 512 + 512],
                                       p[:], vcol[:, e:e + 1])
        return ("vector", fn, [o_vcol])

    pv = emit_proj(pr,
                   lambda k, tile: wq_sb[:, k * 1536 + 1024 + tile[0] * 128: k * 1536 + 1024 + (tile[0] + 1) * 128],
                   lambda k, tile: xn_fm[:, k * T + tile[1] * 512: k * T + (tile[1] + 1) * 512],
                   [(e, n) for e in range(4) for n in range(4)], DCH, pA, v_cb,
                   dep_ops=tr1 + [ld_wq])

    att = []
    for c in range(4):
        p = pS[c % 2]
        war = bld.war(id(p))

        def mk_t(c=c, p=p):
            def fn(t):
                for half in range(2):
                    h = 2 * c + half
                    r0 = half * 64
                    for m in range(TC):
                        instr = t.matmul(p[r0:r0 + 64, :],
                                         q_tm[:, m * 512 + h * 64: m * 512 + (h + 1) * 64],
                                         k_tm[:, m * 512 + h * 64: m * 512 + (h + 1) * 64],
                                         start=(m == 0), stop=(m == TC - 1))
                return instr
            return fn
        ot = pr.op("tensor", mk_t(), deps=pq + pk + war)
        ov1 = pr.op("vector", (lambda c, p: lambda v: v.tensor_reduce(
            nmax[0:64, c:c + 1], p[0:64, :], AX.X, ALU.max, negate=True))(c, p), deps=[ot])
        ov1b = pr.op("vector", (lambda c, p: lambda v: v.tensor_reduce(
            nmax[64:128, c:c + 1], p[64:128, :], AX.X, ALU.max, negate=True))(c, p), deps=[ot])
        ov2 = pr.op("vector", (lambda c: lambda v: v.tensor_scalar_mul(
            nmax[:, c:c + 1], nmax[:, c:c + 1], 0.125))(c), deps=[ov1, ov1b])
        os_ = pr.op("scalar", (lambda c, p: lambda s: s.activation(
            P_bf[:, c * 64:(c + 1) * 64], p[:], AF.Exp,
            bias=nmax[:, c:c + 1], scale=0.125, accum_out=ssum[:, c:c + 1]))(c, p),
            deps=[ov2])
        bld.setlu(id(p), os_)
        att.append(os_)

    orcp = pr.op("vector", lambda v: v.reciprocal(ssum[:, 0:4], ssum[:, 0:4]), deps=att)
    onrm = []
    for c in range(4):
        onrm.append(pr.op("vector", (lambda c: lambda v: v.tensor_scalar_mul(
            P_bf[:, c * 64:(c + 1) * 64], P_bf[:, c * 64:(c + 1) * 64],
            ssum[:, c:c + 1]))(c), deps=[orcp]))

    def av_cb(tile, p, i):
        h, n = tile
        fc, r0 = h // 2, (h % 2) * 64

        def fn(v):
            return v.tensor_copy(o_fm[r0:r0 + 64, fc * T + n * 512: fc * T + (n + 1) * 512],
                                 p[:])
        return ("vector", fn, [])

    pav = emit_proj(pr,
                    lambda k, tile: P_bf[(tile[0] % 2) * 64:(tile[0] % 2) * 64 + 64,
                                         (tile[0] // 2) * 64:(tile[0] // 2 + 1) * 64],
                    lambda k, tile: v_fm[(tile[0] % 2) * 64:(tile[0] % 2) * 64 + 64,
                                         (tile[0] // 2) * T + tile[1] * 512:(tile[0] // 2) * T + (tile[1] + 1) * 512],
                    [(h, n) for h in range(8) for n in range(4)], 1, pO, av_cb,
                    dep_ops=onrm + pv)

    stores = []
    for m in range(TC):
        slot = (m % 2) * D
        F = F_tm[:, slot:slot + D]
        G = G_tm[:, slot:slot + D]
        ev = []
        for n in range(2):
            i2 = m * 2 + n
            p = pA[i2 % 2]
            war = bld.war(id(p))

            def mk_t(m=m, n=n, p=p):
                def fn(t):
                    for k in range(4):
                        instr = t.matmul(p[:], o_fm[:, k * T + m * 128:k * T + (m + 1) * 128],
                                         wfc_sb[:, k * D + n * 512:k * D + (n + 1) * 512],
                                         start=(k == 0), stop=(k == 3))
                    return instr
                return fn
            ot = pr.op("tensor", mk_t(), deps=pav + [ld_wf] + war)

            sl = F_tm[:, slot + n * 512: slot + (n + 1) * 512]
            oe1 = pr.op("vector", (lambda sl, p, n: lambda v: v.tensor_add(
                sl, p[:], bfc_b[:, n * 512:(n + 1) * 512]))(sl, p, n),
                deps=[ot] + bc_bfc + bld.war(("F", m % 2)))
            oe = pr.op("vector", (lambda sl, m, n: lambda v: v.tensor_add(
                sl, sl, xn_tm[:, m * D + n * 512: m * D + (n + 1) * 512]))(sl, m, n),
                deps=[oe1])
            bld.setlu(id(p), oe)
            ev.append(oe)
        o7, o8 = ln_chunk_ops(pr, st2, m, F, ev, 1e-6)
        oap = ln_apply_chunk(pr, st2, m, F, G, [o7, o8], dep_dst=bld.war(("G", m % 2)))
        og1 = pr.op("vector", (lambda G: lambda v: v.tensor_mul(G, G, gam_b[:, 0:D]))(G),
                    deps=[oap] + bc_gam)
        og2 = pr.op("vector", (lambda G: lambda v: v.tensor_add(G, G, bet_b[:, 0:D]))(G),
                    deps=[og1] + bc_bet)
        p7, p8 = ln_chunk_ops(pr, st3, m, G, [og2], 1e-5)
        o_t = ln_apply_chunk(pr, st3, m, G, F, [p7, p8], dep_dst=ev)
        sl2 = m % 3
        xs2 = xch[:, sl2 * D:(sl2 + 1) * D]
        ld2 = pr.op("sync", (lambda m, xs2: lambda e: e.dma_start(out=xs2, in_=x_src[:, m, :]))(m, xs2),
                    deps=bld.war(("xch", sl2)), dma=True)
        o_add = pr.op("vector", (lambda F, xs2: lambda v: v.tensor_add(
            F, F, xs2))(F, xs2), deps=[o_t, ld2])
        bld.setlu(("xch", sl2), o_add)
        bld.setlu(("G", m % 2), o_add)

        ost = pr.op("sync", (lambda m, F: lambda e: e.dma_start(out=xo_dst[:, m, :], in_=F))(m, F),
                    deps=[o_add], dma=True)
        bld.setlu(("F", m % 2), ost)
        stores.append(ost)

    dbg("st1rstd", st1["rstd"], TC)
    dbg("st1nmr", st1["nmr"], TC)
    dbg("st1s1", st1["s1"], TC)
    dbg("st1s2", st1["s2"], TC)
    dbg("st2rstd", st2["rstd"], TC)
    dbg("st3rstd", st3["rstd"], TC)
    dbg("F_tm", F_tm, 2 * D)
    dbg("G_tm", G_tm, 2 * D)
    dbg("xn_tm", xn_tm, TC * D)
    dbg("xn_fm", xn_fm, DCH * T)
    dbg("q_tm", q_tm, TC * 512)
    dbg("k_tm", k_tm, TC * 512)
    dbg("v_fm", v_fm, 4 * T)
    dbg("P_bf", P_bf, 4 * 64)
    dbg("nmax", nmax, 4)
    dbg("ssum", ssum, 4)
    dbg("vcol", vcol, 4)
    for name, sb, cols in DEBUG["bufs"]:
        d = bld.nc.dram_tensor("dbg_" + name, [128, cols], sb.dtype if hasattr(sb, "dtype") else F32,
                               kind="ExternalOutput").ap()
        pr.op("sync", (lambda d, sb, cols: lambda e: e.dma_start(out=d, in_=sb[:, 0:cols]))(d, sb, cols),
              deps=stores, dma=True)
    pr.op("sync", lambda e: e.nop(), deps=stores)


def build_block3():
    bld = Bld()
    nc = bld.nc
    dram = (
        nc.dram_tensor("x", [T, D], F32, kind="ExternalInput").ap(),
        nc.dram_tensor("w_qkv", [D, 1536], F32, kind="ExternalInput").ap(),
        nc.dram_tensor("b_qkv", [1536], F32, kind="ExternalInput").ap(),
        nc.dram_tensor("w_fc", [512, D], F32, kind="ExternalInput").ap(),
        nc.dram_tensor("b_fc", [D], F32, kind="ExternalInput").ap(),
        nc.dram_tensor("gamma", [D], F32, kind="ExternalInput").ap(),
        nc.dram_tensor("beta", [D], F32, kind="ExternalInput").ap(),
        nc.dram_tensor("iden", [128, 128], F32, kind="ExternalInput").ap(),
        nc.dram_tensor("xo", [T, D], F32, kind="ExternalOutput").ap(),
    )
    with bld.es:
        pr = Prog(bld, "b3")
        body_block3(bld, pr, dram)
        pr.run()
    return bld.nc




def emit_sandwich_tail(bld, pr, get_F, st, xch, x_src, xo_dst, eps=1e-5, nsl=3):
    stores = []
    for m in range(TC):
        F, depF = get_F(m)
        tmp = F
        o7, o8 = ln_chunk_ops(pr, st, m, F, depF, eps)
        o_t = ln_apply_chunk(pr, st, m, F, tmp, [o7, o8])
        sl2 = m % nsl
        xs2 = xch[:, sl2 * D:(sl2 + 1) * D]
        ld2 = pr.op("sync", (lambda m, xs2: lambda e: e.dma_start(out=xs2, in_=x_src[:, m, :]))(m, xs2),
                    deps=bld.war(("xch", sl2)), dma=True)
        o_add = pr.op("vector", (lambda tmp, xs2: lambda v: v.tensor_add(tmp, tmp, xs2))(tmp, xs2),
                      deps=[o_t, ld2])
        bld.setlu(("xch", sl2), o_add)
        ost = pr.op("sync", (lambda m, tmp: lambda e: e.dma_start(out=xo_dst[:, m, :], in_=tmp))(m, tmp),
                    deps=[o_add], dma=True)
        bld.setlu(("swtmp", m % 2), ost)
        stores.append(ost)
    return stores


def emit_sandwich_ln1(bld, pr, x_src, xch, xn_tm, st1, eps=1e-5, nsl=3):
    ln1 = []
    for c in range(TC):
        sl = c % nsl
        xs = xch[:, sl * D:(sl + 1) * D]
        ld = pr.op("sync", (lambda c, xs: lambda e: e.dma_start(out=xs, in_=x_src[:, c, :]))(c, xs),
                   deps=bld.war(("xch", sl)), dma=True)
        o7, o8 = ln_chunk_ops(pr, st1, c, xs, [ld], eps)
        oap = ln_apply_chunk(pr, st1, c, xs, xn_tm[:, c * D:(c + 1) * D], [o7, o8])
        bld.setlu(("xch", sl), [oap, o7])
        ln1.append(oap)
    return ln1


def body_block4(bld, pr, dram):
    nc = bld.nc
    NE, MI, SI = 16, 512, 2048
    x_d, gw_d, be_d, up_d, dn_d, su_d, sd_d, idn_d, xo_d = dram
    x_src = x_d.rearrange("(c p) d -> p c d", p=128)
    xo_dst = xo_d.rearrange("(c p) d -> p c d", p=128)

    xch = bld.sbuf("xch", [128, 3 * D])
    xn_tm = bld.sbuf("xn_tm", [128, TC * D], BF16)
    xn_fm = bld.sbuf("xn_fm", [128, DCH * T], BF16)
    acc = bld.sbuf("acc", [128, TC * D])
    stage_ = [xn_tm[:, 8192:8192 + 4096], xn_tm[:, 12288:12288 + 4096]]
    sustage = bld.sbuf("sustage", [128, 1024], BF16)
    rb_buf = bld.sbuf("rb_buf", [128, 1024], BF16)
    wuT = bld.sbuf("wuT", [128, DCH * 512], BF16)
    wdT = bld.sbuf("wdT", [128, 4 * D], BF16)
    sdT = bld.sbuf("sdT", [128, 16 * D], BF16)
    suTe = bld.sbuf("suTe", [128, DCH * 128], BF16)
    gw_sb = bld.sbuf("gw_sb", [16, D], BF16)
    gwT = bld.sbuf("gwT", [128, DCH * 16], BF16)
    be_row = bld.sbuf("be_row", [1, 16], BF16)
    beb = bld.sbuf("beb", [128, 16])
    idn = bld.sbuf("idn", [128, 128], BF16)
    ones_row = bld.sbuf("ones_row", [1, 128], BF16)
    scr = bld.sbuf("scr", [128, TC * 16])
    comb = bld.sbuf("comb", [128, TC * 16])
    rt = {k: bld.sbuf("rt" + k, [128, w]) for k, w in
          [("sc", 16), ("prs", 24), ("g2", 4), ("pmin", 6), ("tg", 1), ("gm", 4),
           ("msk", 16), ("top8", 8), ("wsel", 16), ("wsum", 1)]}
    junk = bld.sbuf("junk", [128, D], BF16)
    st1 = alloc_stats(bld, "b4ln1", junk=junk)
    st2 = alloc_stats(bld, "b4ln2", junk=junk)
    he = xn_tm
    hs = xn_tm

    pA = [bld.psum("pA", [128, 512]) for _ in range(3)]
    pT = [bld.psum("pT", [128, 1024], BF16) for _ in range(2)]
    pR = [bld.psum("pR", [128, 16]) for _ in range(2)]

    ld_id = pr.op("gpsimd", lambda e: e.dma_start(out=idn[:], in_=idn_d), dma=True)
    o_ones = pr.op("gpsimd", lambda e: e.memset(ones_row[:], 1.0))
    ld_gw = pr.op("gpsimd", lambda e: e.dma_start(out=gw_sb[:], in_=gw_d), dma=True)
    ld_be = pr.op("gpsimd", lambda e: e.dma_start(out=be_row[:], in_=be_d[None, :]), dma=True)

    gw_t = []
    for k in range(DCH):
        def mk_t(k=k):
            return lambda t: t.transpose(pT[0][:, k * 16:(k + 1) * 16],
                                         gw_sb[0:16, k * 128:(k + 1) * 128], idn[0:16, 0:16])
        gw_t.append(pr.op("tensor", mk_t(), deps=[ld_gw, ld_id] + bld.war(id(pT[0]))))
    o_gwT = pr.op("vector", lambda v: v.tensor_copy(gwT[:, 0:DCH * 16].rearrange("p (k e) -> p k e", e=16),
                                                    pT[0][:, 0:DCH * 16].rearrange("p (k e) -> p k e", e=16)),
                  deps=gw_t)
    bld.setlu(id(pT[0]), o_gwT)
    o_beb_t = pr.op("tensor", lambda t: t.matmul(pR[0][:], ones_row[:], be_row[:],
                                                 start=True, stop=True),
                    deps=[ld_be, o_ones] + bld.war(id(pR[0])))
    o_beb = pr.op("vector", lambda v: v.tensor_copy(beb[:], pR[0][:]), deps=[o_beb_t])
    bld.setlu(id(pR[0]), o_beb)

    ln1 = emit_sandwich_ln1(bld, pr, x_src, xch, xn_tm, st1)

    tr1 = emit_transpose(pr, lambda c: xn_tm[:, c * D:(c + 1) * D],
                         lambda f: xn_fm[:, f * T:(f + 1) * T], TC, DCH, idn, pT,
                         dep_src=ln1, dep_idn=[ld_id])
    for key in (("stage", 0), ("stage", 1)):
        bld.setlu(key, tr1[-1])

    comb_ops = []
    for m in range(TC):
        p = pR[m % 2]
        war = bld.war(id(p))

        def mk_t(m=m, p=p):
            def fn(t):
                for k in range(DCH):
                    instr = t.matmul(p[:], xn_fm[:, k * T + m * 128:k * T + (m + 1) * 128],
                                     gwT[:, k * 16:(k + 1) * 16],
                                     start=(k == 0), stop=(k == DCH - 1))
                return instr
            return fn
        ot = pr.op("tensor", mk_t(), deps=tr1 + [o_gwT] + war)
        scm = scr[:, m * 16:(m + 1) * 16]
        o_sig = pr.op("scalar", (lambda p, scm: lambda s: s.activation(scm, p[:], AF.Sigmoid))(p, scm),
                      deps=[ot])
        bld.setlu(id(p), o_sig)
        o_sc = pr.op("vector", (lambda scm: lambda v: v.tensor_add(rt["sc"][:], scm, beb[:]))(scm),
                     deps=[o_sig, o_beb] + bld.war("rtbufs"))
        sc3 = rt["sc"][:].rearrange("p (g i) -> p g i", i=4)
        prs3 = rt["prs"][:].rearrange("p (g j) -> p g j", j=6)
        o_pairs = []
        for j, (i1, i2) in enumerate([(0, 1), (0, 2), (0, 3), (1, 2), (1, 3), (2, 3)]):
            o_pairs.append(pr.op("vector", (lambda j, i1, i2, sc3, prs3: lambda v: v.tensor_add(
                prs3[:, :, j:j + 1], sc3[:, :, i1:i1 + 1], sc3[:, :, i2:i2 + 1]))(j, i1, i2, sc3, prs3),
                deps=[o_sc]))
        o_g2 = pr.op("vector", (lambda prs3: lambda v: v.tensor_reduce(
            rt["g2"][:], prs3, AX.X, ALU.max))(prs3), deps=o_pairs)
        o_pm = []
        for j, (i1, i2) in enumerate([(0, 1), (0, 2), (0, 3), (1, 2), (1, 3), (2, 3)]):
            o_pm.append(pr.op("vector", (lambda j, i1, i2: lambda v: v.tensor_tensor(
                rt["pmin"][:, j:j + 1], rt["g2"][:, i1:i1 + 1], rt["g2"][:, i2:i2 + 1],
                ALU.min))(j, i1, i2), deps=[o_g2]))
        o_tg = pr.op("vector", lambda v: v.tensor_reduce(rt["tg"][:], rt["pmin"][:], AX.X, ALU.max),
                     deps=o_pm)
        o_gm = pr.op("vector", lambda v: v.tensor_single_scalar(rt["gm"][:], rt["g2"][:],
                                                                rt["tg"][:], ALU.is_ge),
                     deps=[o_tg])
        o_msk = []
        for g in range(4):
            o_msk.append(pr.op("vector", (lambda g: lambda v: v.tensor_single_scalar(
                rt["msk"][:, 4 * g:4 * g + 4], rt["sc"][:, 4 * g:4 * g + 4],
                rt["gm"][:, g:g + 1], ALU.mult))(g), deps=[o_gm, o_sc]))
        o_t8 = pr.op("vector", lambda v: v.max(rt["top8"][:], rt["msk"][:]), deps=o_msk, special=True)
        o_sel = pr.op("vector", lambda v: v.tensor_single_scalar(rt["wsel"][:], rt["msk"][:],
                                                                 rt["top8"][:, 3:4], ALU.is_ge),
                      deps=[o_t8])
        o_ws = pr.op("vector", (lambda scm: lambda v: v.tensor_mul(rt["wsel"][:], rt["wsel"][:], scm))(scm),
                     deps=[o_sel])
        o_sum = pr.op("vector", lambda v: v.tensor_reduce(rt["wsum"][:], rt["wsel"][:], AX.X, ALU.add),
                      deps=[o_ws])
        o_eps = pr.op("vector", lambda v: v.tensor_scalar_add(rt["wsum"][:], rt["wsum"][:], 1e-20),
                      deps=[o_sum])
        o_rcp = pr.op("vector", lambda v: v.reciprocal(rt["wsum"][:], rt["wsum"][:]), deps=[o_eps])
        o_r2 = pr.op("vector", lambda v: v.tensor_scalar_mul(rt["wsum"][:], rt["wsum"][:], 2.5),
                     deps=[o_rcp])
        cbm = comb[:, m * 16:(m + 1) * 16]
        o_cb = pr.op("vector", (lambda cbm: lambda v: v.tensor_single_scalar(
            cbm, rt["wsel"][:], rt["wsum"][:], ALU.mult))(cbm), deps=[o_r2])
        bld.setlu("rtbufs", o_cb)
        comb_ops.append(o_cb)

    for e in range(NE):
        stage = stage_[e % 2]
        ld_wu = pr.op("gpsimd", (lambda e, stage: lambda g: g.dma_start(
            out=stage.rearrange("p (r c) -> p r c", c=D),
            in_=up_d[e].rearrange("(r p) c -> p r c", p=128)))(e, stage),
            deps=bld.war(("stage", e % 2)), dma=True)
        twu = emit_transpose(pr, (lambda stage: lambda rc: stage[:, rc * D:(rc + 1) * D])(stage),
                             lambda dc: wuT[:, dc * 512:(dc + 1) * 512], 4, DCH, idn, pT,
                             dep_src=[ld_wu] + bld.war("wuT"), dep_idn=[ld_id])
        bld.setlu(("stage", e % 2), twu[-1])
        up_ev = []
        for ec in range(4):
            for tb in range(4):
                p = pA[(ec * 4 + tb) % 3]
                war = bld.war(id(p))

                def mk_t(ec=ec, tb=tb, p=p):
                    def fn(t):
                        for k in range(DCH):
                            instr = t.matmul(p[:], wuT[:, k * 512 + ec * 128:k * 512 + (ec + 1) * 128],
                                             xn_fm[:, k * T + tb * 512:k * T + (tb + 1) * 512],
                                             start=(k == 0), stop=(k == DCH - 1))
                        return instr
                    return fn
                ot = pr.op("tensor", mk_t(), deps=twu + war + bld.war(("he", ec, tb)))
                rb = rb_buf[:, (tb % 2) * 512:(tb % 2) * 512 + 512]
                o_rl = pr.op("scalar", (lambda p, rb: lambda s: s.activation(rb, p[:], AF.Relu))(p, rb),
                             deps=[ot] + bld.war(("rb", tb % 2)))
                hesl = he[:, ec * T + tb * 512:ec * T + (tb + 1) * 512]
                o_he = pr.op("vector", (lambda hesl, rb, p: lambda v: v.tensor_mul(hesl, rb, p[:]))(hesl, rb, p),
                             deps=[o_rl] + ([tr1[-1]] if e == 0 else []))
                bld.setlu(id(p), o_he)
                bld.setlu(("rb", tb % 2), o_he)
                up_ev.append(o_he)
                bld.setlu(("heC", ec, tb), o_he)
        ld_wd = pr.op("gpsimd", (lambda e, stage: lambda g: g.dma_start(
            out=stage.rearrange("p (r c) -> p r c", c=512),
            in_=dn_d[e].rearrange("(r p) c -> p r c", p=128)))(e, stage),
            deps=bld.war(("stage", e % 2)) + [twu[-1]], dma=True)
        twd = emit_transpose(pr, (lambda stage: lambda rc: stage[:, rc * 512:(rc + 1) * 512])(stage),
                             lambda ic: wdT[:, ic * D:(ic + 1) * D], 8, 4, idn, pT,
                             dep_src=[ld_wd] + bld.war("wdT"), dep_idn=[ld_id])
        bld.setlu(("stage", e % 2), twd[-1])
        dn_ev = []
        for m in range(TC):
            for eb in range(2):
                p = pA[(m * 2 + eb) % 3]
                war = bld.war(id(p))
                tb = m // 4

                def mk_t(m=m, eb=eb, p=p, tb=tb):
                    def fn(t):
                        for k in range(4):
                            instr = t.matmul(p[:], he[:, k * T + m * 128:k * T + (m + 1) * 128],
                                             wdT[:, k * D + eb * 512:k * D + (eb + 1) * 512],
                                             start=(k == 0), stop=(k == 3))
                        return instr
                    return fn
                hed = [bld.lu.get(("heC", kk, tb)) for kk in range(4)]
                ot = pr.op("tensor", mk_t(), deps=twd + war + [d for d in hed if d is not None])
                accsl = acc[:, m * D + eb * 512:m * D + (eb + 1) * 512]
                if e == 0:
                    o_ac = pr.op("vector", (lambda accsl, p, m, e: lambda v: v.tensor_single_scalar(
                        accsl, p[:], comb[:, m * 16 + e:m * 16 + e + 1], ALU.mult))(accsl, p, m, e),
                        deps=[ot, comb_ops[m]])
                else:
                    o_ac = pr.op("vector", (lambda accsl, p, m, e: lambda v: v.scalar_tensor_tensor(
                        accsl, p[:], comb[:, m * 16 + e:m * 16 + e + 1], accsl,
                        ALU.mult, ALU.add))(accsl, p, m, e),
                        deps=[ot, comb_ops[m]] + bld.war(("acc", m, eb)))
                bld.setlu(id(p), o_ac)
                bld.setlu(("acc", m, eb), o_ac)
                dn_ev.append(o_ac)
        bld.setlu("wuT", up_ev[-1])
        bld.setlu("wdT", dn_ev[-1])
        for ec in range(4):
            for tb in range(4):
                bld.setlu(("he", ec, tb), dn_ev[-1])

    for ec in range(16):
        bld.setlu(("hs", ec), dn_ev[-1])

    for piece in range(4):
        sstage = stage_[piece % 2]
        ld_sd = pr.op("gpsimd", (lambda piece, sstage: lambda g: g.dma_start(
            out=sstage.rearrange("p (r c) -> p r c", c=SI),
            in_=sd_d[piece * 256:(piece + 1) * 256, :].rearrange("(r p) c -> p r c", p=128)))(piece, sstage),
            deps=bld.war(("stage", piece % 2)), dma=True)
        tsd = emit_transpose(pr, (lambda sstage: lambda rc: sstage[:, rc * SI:(rc + 1) * SI])(sstage),
                             lambda ic: sdT[:, ic * D:(ic + 1) * D], 2, 16, idn, pT,
                             dep_src=[ld_sd], dep_idn=[ld_id], dst_off=piece * 256)
        bld.setlu(("stage", piece % 2), tsd[-1])
    sdT_done = tsd[-1]

    sh_ev = {}
    for tb in range(4):
        hs_ops = []
        for ec in range(16):
            ld_su = pr.op("gpsimd", (lambda ec: lambda g: g.dma_start(
                out=sustage[:],
                in_=su_d[ec * 128:(ec + 1) * 128, :]))(ec),
                deps=bld.war("sustage"), dma=True)
            tsu = emit_transpose(pr, lambda rc: sustage[:, 0:D],
                                 lambda dc: suTe[:, dc * 128:(dc + 1) * 128], 1, DCH, idn, pT,
                                 dep_src=[ld_su] + bld.war("suTe"), dep_idn=[ld_id])
            bld.setlu("sustage", tsu[-1])
            p = pA[ec % 2]
            war = bld.war(id(p))

            def mk_t(ec=ec, tb=tb, p=p):
                def fn(t):
                    for k in range(DCH):
                        instr = t.matmul(p[:], suTe[:, k * 128:(k + 1) * 128],
                                         xn_fm[:, k * T + tb * 512:k * T + (tb + 1) * 512],
                                         start=(k == 0), stop=(k == DCH - 1))
                    return instr
                return fn
            ot = pr.op("tensor", mk_t(), deps=tsu + war + bld.war(("hs", ec)))
            bld.setlu("suTe", ot)
            rb = rb_buf[:, (ec % 2) * 512:(ec % 2) * 512 + 512]
            o_rl = pr.op("scalar", (lambda p, rb: lambda s: s.activation(rb, p[:], AF.Relu))(p, rb),
                         deps=[ot] + bld.war(("rb", ec % 2)))
            hssl = hs[:, ec * 512:(ec + 1) * 512]
            o_hs = pr.op("vector", (lambda hssl, rb, p: lambda v: v.tensor_mul(hssl, rb, p[:]))(hssl, rb, p),
                         deps=[o_rl])
            bld.setlu(id(p), o_hs)
            bld.setlu(("rb", ec % 2), o_hs)
            hs_ops.append(o_hs)
        for mm in range(4):
            m = tb * 4 + mm
            for eb in range(2):
                p = pA[(mm * 2 + eb) % 3]
                war = bld.war(id(p))

                def mk_t(m=m, mm=mm, eb=eb, p=p):
                    def fn(t):
                        for k in range(16):
                            instr = t.matmul(p[:], hs[:, k * 512 + mm * 128:k * 512 + (mm + 1) * 128],
                                             sdT[:, k * D + eb * 512:k * D + (eb + 1) * 512],
                                             start=(k == 0), stop=(k == 15))
                        return instr
                    return fn
                ot = pr.op("tensor", mk_t(), deps=hs_ops + [sdT_done] + war)
                accsl = acc[:, m * D + eb * 512:m * D + (eb + 1) * 512]
                o_ac = pr.op("vector", (lambda accsl, p: lambda v: v.tensor_add(
                    accsl, accsl, p[:]))(accsl, p), deps=[ot] + bld.war(("acc", m, eb)))
                bld.setlu(id(p), o_ac)
                bld.setlu(("acc", m, eb), o_ac)
                sh_ev[(m, eb)] = o_ac
        for ec in range(16):
            bld.setlu(("hs", ec), sh_ev[(tb * 4 + 3, 1)])

    def get_F(m):
        F = acc[:, m * D:(m + 1) * D]
        return F, [sh_ev[(m, 0)], sh_ev[(m, 1)]]

    if DEBUG.get("notail"):
        stores = []
        alldeps = [sh_ev[(m, eb)] for m in range(TC) for eb in range(2)]
        for name, sb, cols in [("acc", acc, TC * D), ("xn_fm", xn_fm, DCH * T),
                               ("scr", scr, TC * 16), ("comb", comb, TC * 16),
                               ("wuT", wuT, DCH * 512), ("wdT", wdT, 4 * D),
                               ("sdT", sdT, 16 * D), ("henow", xn_tm, TC * D)] + [
                               ("rt" + k, rt[k], rt[k].shape[1]) for k in rt]:
            dt_out = BF16 if sb.dtype == BF16 else F32
            d = bld.nc.dram_tensor("dbg_" + name, [128, cols], dt_out,
                                   kind="ExternalOutput").ap()
            stores.append(pr.op("sync", (lambda d, sb, cols: lambda e: e.dma_start(
                out=d, in_=sb[:, 0:cols]))(d, sb, cols), deps=alldeps, dma=True))
        z = pr.op("vector", lambda v: v.memset(xch[:, 0:D], 0.0), deps=alldeps)
        for m in range(TC):
            stores.append(pr.op("sync", (lambda m: lambda e: e.dma_start(
                out=xo_dst[:, m, :], in_=xch[:, 0:D]))(m), deps=[z], dma=True))
    else:
        stores = emit_sandwich_tail(bld, pr, get_F, st2, xch, x_src, xo_dst)
    pr.op("sync", lambda e: e.nop(), deps=stores)


def build_block4():
    bld = Bld()
    nc = bld.nc
    dram = (
        nc.dram_tensor("x", [T, D], F32, kind="ExternalInput").ap(),
        nc.dram_tensor("gate_w", [16, D], F32, kind="ExternalInput").ap(),
        nc.dram_tensor("bias_e", [16], F32, kind="ExternalInput").ap(),
        nc.dram_tensor("up_w", [16, 512, D], F32, kind="ExternalInput").ap(),
        nc.dram_tensor("down_w", [16, D, 512], F32, kind="ExternalInput").ap(),
        nc.dram_tensor("shared_up", [2048, D], F32, kind="ExternalInput").ap(),
        nc.dram_tensor("shared_down", [D, 2048], F32, kind="ExternalInput").ap(),
        nc.dram_tensor("iden", [128, 128], F32, kind="ExternalInput").ap(),
        nc.dram_tensor("xo", [T, D], F32, kind="ExternalOutput").ap(),
    )
    with bld.es:
        pr = Prog(bld, "b4")
        body_block4(bld, pr, dram)
        pr.run()
    return bld.nc




def body_block1(bld, pr, dram):
    nc = bld.nc
    H = 16
    (x_d, wq_d, wk_d, wv_d, wo_d, bo_d, aq_d, bq_d, ak_d, bk_d, av_d, bv_d,
     ao_d, bo2_d, idn_d, xo_d) = dram
    x_src = x_d.rearrange("(c p) d -> p c d", p=128)
    xo_dst = xo_d.rearrange("(c p) d -> p c d", p=128)

    xch = bld.sbuf("xch", [128, 2 * D])
    xn_tm = bld.sbuf("xn_tm", [128, TC * D], BF16)
    xn_fm = bld.sbuf("xn_fm", [128, DCH * T], BF16)
    k_fm = bld.sbuf("k_fm", [128, DCH * T], BF16)
    v_tm = bld.sbuf("v_tm", [128, TC * D], BF16)
    WoT = bld.sbuf("WoT", [128, DCH * D], BF16)
    WT = bld.sbuf("WT", [128, DCH * D], BF16)
    WqT = WT
    q_qb = bld.sbuf("q_qb", [128, DCH * 512], BF16)
    o_qb = bld.sbuf("o_qb", [128, DCH * 512], BF16)
    stage = xn_tm[:, 0:8192]
    Pt_ = [xn_tm[:, 8192:8192 + 2048], xn_tm[:, 10240:10240 + 2048]]
    PT = xn_tm[:, 0:8192]
    An = xn_tm[0:16, 8192:8192 + 4096]
    Bn = bld.sbuf("Bn", [128, DCH * 4 * 16], BF16)
    AT = bld.sbuf("AT", [128, 4 * DCH * 16], BF16)
    BT = xn_tm[0:16, 12288:12288 + 4096]
    u_q = bld.sbuf("u_q", [16, T], BF16)
    u_k = xn_tm[0:16, 8192:8192 + 2048]
    u_v = xn_tm[0:16, 10240:10240 + 2048]
    u_bufs = [u_q, u_k, u_v]
    u_o = bld.sbuf("u_o", [16, 512], BF16)
    rows = bld.sbuf("rows", [1, D], BF16)
    bo_b = bld.sbuf("bo_b", [128, D], BF16)
    idn = bld.sbuf("idn", [128, 128], BF16)
    ones_row = bld.sbuf("ones_row", [1, 128], BF16)
    onescol_b1 = bld.sbuf("onescol_b1", [128, 1], BF16)
    attn_sc = bld.sbuf("attn_sc", [128, 16])
    km2 = bld.sbuf("km2", [1, 16])
    qm2 = bld.sbuf("qm2", [1, 16])
    mrow = bld.sbuf("mrow", [1, 16], BF16)
    mcols = bld.sbuf("mcols", [128, 16])
    sqb = bld.sbuf("sqb", [128, 512], BF16)
    nrmrow = bld.sbuf("nrmrow", [1, 4])
    F2 = bld.sbuf("F2", [128, 2 * D])
    junk = bld.sbuf("junk", [128, D], BF16)
    st1 = alloc_stats(bld, "b1ln1", junk=junk)
    st2 = alloc_stats(bld, "b1ln2", junk=junk)

    pS = bld.psum("pS", [128, T])
    pT = [bld.psum("pT", [128, 1024], BF16) for _ in range(2)]
    pA = [bld.psum("pA", [128, 512]) for _ in range(2)]

    ld_id = pr.op("gpsimd", lambda e: e.dma_start(out=idn[:], in_=idn_d), dma=True)
    o_ones = pr.op("gpsimd", lambda e: e.memset(ones_row[:], 1.0))
    o_onec_b1 = pr.op("gpsimd", lambda e: e.memset(onescol_b1[:], 1.0))
    ld_bo = pr.op("gpsimd", lambda e: e.dma_start(out=rows[:, 0:D], in_=bo_d[None, :]), dma=True)
    ld_A = None
    ld_B = []
    for i, bd in enumerate([bq_d, bk_d, bv_d, bo2_d]):
        ld_B.append(pr.op("gpsimd", (lambda i, bd: lambda e: e.dma_start(
            out=Bn[:, i * DCH * 16:(i + 1) * DCH * 16].rearrange("p (c e) -> p c e", e=16),
            in_=bd.rearrange("(c p) e -> p c e", p=128)))(i, bd), dma=True))

    bcs = emit_bcast_rows(pr, [(rows[:, 0:D], bo_b, D)], ones_row, pA, [ld_bo, o_ones])
    bc_bo = bcs[0]

    ln1 = emit_sandwich_ln1(bld, pr, x_src, xch, xn_tm, st1, nsl=2)
    tr1 = emit_transpose(pr, lambda c: xn_tm[:, c * D:(c + 1) * D],
                         lambda f: xn_fm[:, f * T:(f + 1) * T], TC, DCH, idn, pT,
                         dep_src=ln1, dep_idn=[ld_id])
    for key in ("stage", ("Pt", 0), ("Pt", 1)):
        bld.setlu(key, tr1[-1])
    ld_A = []
    for i, ad in enumerate([aq_d, ak_d, av_d, ao_d]):
        ld_A.append(pr.op("gpsimd", (lambda i, ad: lambda e: e.dma_start(
            out=An[:, i * D:(i + 1) * D], in_=ad))(i, ad), deps=[tr1[-1]], dma=True))

    at_ops = []
    for i in range(4):
        def mk_t(i=i):
            def fn(t):
                for k in range(DCH):
                    instr = t.transpose(pT[0][:, k * 16:(k + 1) * 16],
                                        An[0:16, i * D + k * 128:i * D + (k + 1) * 128],
                                        idn[0:16, 0:16])
                return instr
            return fn
        ot = pr.op("tensor", mk_t(), deps=[ld_A[i], ld_id] + bld.war(id(pT[0])))
        oe = pr.op("vector", (lambda i: lambda v: v.tensor_copy(
            AT[:, i * DCH * 16:(i + 1) * DCH * 16], pT[0][:, 0:DCH * 16]))(i), deps=[ot])
        bld.setlu(id(pT[0]), oe)
        at_ops.append(oe)
    bt_ops = []
    for i in range(4):
        def mk_t(i=i):
            def fn(t):
                for k in range(DCH):
                    instr = t.transpose(pT[0][0:16, k * 128:(k + 1) * 128],
                                        Bn[:, (i * DCH + k) * 16:(i * DCH + k) * 16 + 16],
                                        idn[:])
                return instr
            return fn
        ot = pr.op("tensor", mk_t(), deps=[ld_B[i], ld_id] + bld.war(id(pT[0])))
        oe = pr.op("vector", (lambda i: lambda v: v.tensor_copy(
            BT[0:16, i * D:(i + 1) * D], pT[0][0:16, 0:D]))(i), deps=[ot])
        bld.setlu(id(pT[0]), oe)
        bt_ops.append(oe)

    for key in ("Pt",):
        bld.setlu(key, at_ops)

    u_ops = [[], [], []]
    for i in range(3):
        for nb in range(4):
            p = pA[nb % 2]
            war = bld.war(id(p))

            def mk_t(i=i, nb=nb, p=p):
                def fn(t):
                    for k in range(DCH):
                        instr = t.matmul(p[0:16, :], AT[:, (i * DCH + k) * 16:(i * DCH + k) * 16 + 16],
                                         xn_fm[:, k * T + nb * 512:k * T + (nb + 1) * 512],
                                         start=(k == 0), stop=(k == DCH - 1))
                    return instr
                return fn
            ot = pr.op("tensor", mk_t(), deps=[at_ops[i]] + tr1 + war + ([at_ops[3]] if i > 0 else []))
            oe = pr.op("scalar", (lambda i, nb, p: lambda s: s.activation(
                u_bufs[i][0:16, nb * 512:(nb + 1) * 512], p[0:16, :],
                AF.Identity, scale=1.0 / 16))(i, nb, p), deps=[ot])
            bld.setlu(id(p), oe)
            u_ops[i].append(oe)

    def load_wt(w_d, dstT, tag):
        ld = pr.op("gpsimd", (lambda w_d: lambda g: g.dma_start(
            out=stage.rearrange("p (r c) -> p r c", c=D),
            in_=w_d.rearrange("(r p) c -> p r c", p=128)))(w_d),
            deps=bld.war("stage"), dma=True)
        tw = emit_transpose(pr, lambda rc: stage[:, rc * D:(rc + 1) * D],
                            lambda dc: dstT[:, dc * D:(dc + 1) * D], DCH, DCH, idn, pT,
                            dep_src=[ld] + bld.war(tag), dep_idn=[ld_id], evac_eng="vector")
        bld.setlu("stage", tw[-1])
        return tw

    twk = load_wt(wk_d, WT, "WT")
    pk = []
    for e in range(DCH):
        for nb in range(4):
            p = pA[(e * 4 + nb) % 2]
            war = bld.war(id(p))

            def mk_t(e=e, nb=nb, p=p):
                def fn(t):
                    for k in range(DCH):
                        instr = t.matmul(p[:], WT[:, k * D + e * 128:k * D + (e + 1) * 128],
                                         xn_fm[:, k * T + nb * 512:k * T + (nb + 1) * 512],
                                         start=(k == 0), stop=False)
                    instr = t.matmul(p[:], BT[0:16, D + e * 128:D + (e + 1) * 128],
                                     u_k[0:16, nb * 512:(nb + 1) * 512],
                                     start=False, stop=True)
                    return instr
                return fn
            ot = pr.op("tensor", mk_t(), deps=twk + [bt_ops[1]] + u_ops[1] + war)
            oe = pr.op("vector", (lambda e, nb, p: lambda v: ecopy(v,
                k_fm[:, e * T + nb * 512:e * T + (nb + 1) * 512], p[:]))(e, nb, p), deps=[ot])
            bld.setlu(id(p), oe)
            pk.append(oe)

    bld.setlu(("Pt", 0), pk[-1])

    kn_ops = []
    o_z = pr.op("vector", lambda v: v.memset(km2[:], 0.0))
    for e in range(DCH):
        for nb in range(4):
            osq = pr.op("scalar", (lambda e, nb: lambda s: s.activation(
                sqb[:], k_fm[:, e * T + nb * 512:e * T + (nb + 1) * 512], AF.Square))(e, nb),
                deps=[pk[e * 4 + nb]] + bld.war("sqb"))
            for half in range(2):
                h = 2 * e + half
                r0 = half * 64
                p = pA[half]
                war = bld.war(id(p))
                omm = pr.op("tensor", (lambda r0: lambda t: t.matmul(
                    p[0:1, :] if False else pA[0][0:1, 0:512], onescol_b1[r0:r0 + 64, :],
                    sqb[r0:r0 + 64, :], start=True, stop=True))(r0),
                    deps=[osq, o_onec_b1] + bld.war(id(pA[0])))
                orm = pr.op("vector", (lambda: lambda v: v.tensor_reduce(
                    nrmrow[0:1, 0:1], pA[0][0:1, 0:512], AX.X, ALU.max))(),
                    deps=[omm] + bld.war("nrmrow"))
                bld.setlu(id(pA[0]), orm)
                omx = pr.op("vector", (lambda h: lambda v: v.tensor_max(
                    km2[0:1, h:h + 1], km2[0:1, h:h + 1], nrmrow[0:1, 0:1]))(h),
                    deps=[orm, o_z])
                bld.setlu("nrmrow", omx)
                kn_ops.append(omx)
            bld.setlu("sqb", kn_ops[-1])

    twv = load_wt(wv_d, WT, "WT2")
    bld.setlu("WT", pk[-1])
    pv = []
    for m in range(TC):
        for eb in range(2):
            p = pA[(m * 2 + eb) % 2]
            war = bld.war(id(p))

            def mk_t(m=m, eb=eb, p=p):
                def fn(t):
                    for k in range(DCH):
                        instr = t.matmul(p[:], xn_fm[:, k * T + m * 128:k * T + (m + 1) * 128],
                                         WT[:, k * D + eb * 512:k * D + (eb + 1) * 512],
                                         start=(k == 0), stop=False)
                    instr = t.matmul(p[:], u_v[0:16, m * 128:(m + 1) * 128],
                                     BT[0:16, 2 * D + eb * 512:2 * D + (eb + 1) * 512],
                                     start=False, stop=True)
                    return instr
                return fn
            ot = pr.op("tensor", mk_t(), deps=twv + [bt_ops[2]] + u_ops[2] + war)
            oe = pr.op("vector", (lambda m, eb, p: lambda v: ecopy(v,
                v_tm[:, m * D + eb * 512:m * D + (eb + 1) * 512], p[:]))(m, eb, p), deps=[ot])
            bld.setlu(id(p), oe)
            pv.append(oe)

    bld.setlu(("Pt", 1), pv[-1])

    two = load_wt(wo_d, WoT, "WoT")
    bld.setlu("WT2", pv[-1])
    twq = load_wt(wq_d, WqT, "WqT")
    for qt in range(4):
        for g in range(2):
            bld.setlu(("PT", qt, g), twq[-1])

    stores = []
    for qb in range(4):
        pq = []
        for e in range(DCH):
            p = pA[e % 2]
            war = bld.war(id(p))

            def mk_t(e=e, qb=qb, p=p):
                def fn(t):
                    for k in range(DCH):
                        instr = t.matmul(p[:], WqT[:, k * D + e * 128:k * D + (e + 1) * 128],
                                         xn_fm[:, k * T + qb * 512:k * T + (qb + 1) * 512],
                                         start=(k == 0), stop=False)
                    instr = t.matmul(p[:], BT[0:16, e * 128:(e + 1) * 128],
                                     u_q[0:16, qb * 512:(qb + 1) * 512],
                                     start=False, stop=True)
                    return instr
                return fn
            ot = pr.op("tensor", mk_t(), deps=twq + [bt_ops[0]] + u_ops[0] + war + bld.war(("q_qb", e)))
            oe = pr.op("vector", (lambda e, p: lambda v: ecopy(v,
                q_qb[:, e * 512:(e + 1) * 512], p[:]))(e, p), deps=[ot])
            bld.setlu(id(p), oe)
            bld.setlu(("q_qb", e), None)
            pq.append(oe)

        qn_ops = []
        o_zq = pr.op("vector", lambda v: v.memset(qm2[:], 0.0), deps=bld.war("qm2"))
        for e in range(DCH):
            osq = pr.op("scalar", (lambda e: lambda s: s.activation(
                sqb[:], q_qb[:, e * 512:(e + 1) * 512], AF.Square))(e),
                deps=[pq[e]] + bld.war("sqb"))
            for half in range(2):
                h = 2 * e + half
                r0 = half * 64
                omm = pr.op("tensor", (lambda r0: lambda t: t.matmul(
                    pA[0][0:1, 0:512], onescol_b1[r0:r0 + 64, :],
                    sqb[r0:r0 + 64, :], start=True, stop=True))(r0),
                    deps=[osq, o_onec_b1] + bld.war(id(pA[0])))
                orm = pr.op("vector", (lambda: lambda v: v.tensor_reduce(
                    nrmrow[0:1, 0:1], pA[0][0:1, 0:512], AX.X, ALU.max))(),
                    deps=[omm] + bld.war("nrmrow"))
                bld.setlu(id(pA[0]), orm)
                omx = pr.op("vector", (lambda h: lambda v: v.tensor_max(
                    qm2[0:1, h:h + 1], qm2[0:1, h:h + 1], nrmrow[0:1, 0:1]))(h),
                    deps=[orm, o_zq])
                bld.setlu("nrmrow", omx)
                qn_ops.append(omx)
            bld.setlu("sqb", qn_ops[-1])
        bld.setlu("qm2", qn_ops)
        o_mt = pr.op("vector", lambda v: v.tensor_mul(qm2[:], qm2[:], km2[:]),
                     deps=qn_ops + kn_ops)
        bld.setlu("qm2", o_mt)
        o_ms = pr.op("scalar", lambda s: s.activation(mrow[:], qm2[:], AF.Sqrt), deps=[o_mt] + bld.war("mrow"))
        o_mb = pr.op("tensor", lambda t: t.matmul(pA[0][0:128, 0:16], ones_row[:], mrow[:],
                                                  start=True, stop=True),
                     deps=[o_ms, o_ones] + bld.war(id(pA[0])))
        o_mc = pr.op("vector", lambda v: v.tensor_single_scalar(
            mcols[:], pA[0][0:128, 0:16], -0.125, ALU.mult),
            deps=[o_mb] + bld.war("mcols"))
        bld.setlu(id(pA[0]), o_mc)
        bld.setlu("mrow", o_mb)

        pend = []
        last_av = [None]

        def flush_pend(force=False):
            rest = []
            while pend:
                d, fn = pend.pop(0)
                if d <= 0 or force:
                    fn()
                else:
                    rest.append((d - 1, fn))
            pend.extend(rest)

        for h in range(H):
            fc, r0 = h // 2, (h % 2) * 64
            pt_evacs = []
            for qt in range(4):
                it = (h * 4 + qt) % 2
                a0 = it * 8
                Pt = Pt_[it]
                mms = []
                for kb in range(4):
                    omm = pr.op("tensor", (lambda h, qt, fc, r0, kb: lambda t: t.matmul(
                        pS[:, kb * 512:(kb + 1) * 512],
                        q_qb[r0:r0 + 64, fc * 512 + qt * 128:fc * 512 + (qt + 1) * 128],
                        k_fm[r0:r0 + 64, fc * T + kb * 512:fc * T + (kb + 1) * 512],
                        start=True, stop=True))(h, qt, fc, r0, kb),
                        deps=pq + pk + bld.war(("pSb", kb)))
                    mms.append(omm)
                flush_pend()
                exps = []
                for kb in range(4):
                    oex = pr.op("scalar", (lambda h, kb, a0, Pt=Pt: lambda s: s.activation(
                        Pt[:, kb * 512:(kb + 1) * 512], pS[:, kb * 512:(kb + 1) * 512],
                        AF.Exp, bias=mcols[:, h:h + 1], scale=0.125,
                        accum_out=attn_sc[:, a0 + kb:a0 + kb + 1]))(h, kb, a0),
                        deps=[mms[kb], o_mc] + bld.war(("Pt", it)) + bld.war(("attq", it)))
                    bld.setlu(("pSb", kb), oex)
                    exps.append(oex)
                osum = pr.op("vector", (lambda a0: lambda v: v.tensor_reduce(
                    attn_sc[:, a0 + 4:a0 + 5], attn_sc[:, a0:a0 + 4], AX.X, ALU.add))(a0),
                    deps=exps)
                ocl = pr.op("vector", (lambda a0: lambda v: v.tensor_scalar_max(
                    attn_sc[:, a0 + 4:a0 + 5], attn_sc[:, a0 + 4:a0 + 5], 1e-35))(a0), deps=[osum])
                orc = pr.op("vector", (lambda a0: lambda v: v.reciprocal(
                    attn_sc[:, a0 + 5:a0 + 6], attn_sc[:, a0 + 4:a0 + 5]))(a0), deps=[ocl])
                if it == 0:
                    onr = pr.op("scalar", (lambda a0, Pt=Pt: lambda s: s.activation(
                        Pt, Pt, AF.Identity, scale=attn_sc[:, a0 + 5:a0 + 6]))(a0), deps=[orc])
                else:
                    onr = pr.op("vector", (lambda a0, Pt=Pt: lambda v: v.tensor_scalar_mul(
                        Pt, Pt, attn_sc[:, a0 + 5:a0 + 6]))(a0), deps=[orc])
                bld.setlu(("attq", it), onr)

                def emit_trans(qt=qt, it=it, onr=onr, Pt=Pt, pt_evacs=pt_evacs):
                    PT3 = PT.rearrange("p (k c) -> p k c", c=512)
                    for g in range(2):
                        p = pT[g]
                        war = bld.war(id(p))

                        def mk_tt(g=g, p=p, Pt=Pt):
                            def fn(t):
                                for j in range(8):
                                    kc = g * 8 + j
                                    instr = t.transpose(p[:, j * 128:(j + 1) * 128],
                                                        Pt[:, kc * 128:(kc + 1) * 128], idn[:])
                                return instr
                            return fn
                        ott = pr.op("tensor", mk_tt(), deps=[onr] + war)
                        p3 = p[:].rearrange("p (k c) -> p k c", c=128)

                        def mk_tv(g=g, p3=p3, qt=qt):
                            dst = PT3[:, g * 8:(g + 1) * 8, qt * 128:(qt + 1) * 128]
                            return lambda v: ecopy(v, dst, p3)
                        oev = pr.op("vector" if it == 0 else "scalar", mk_tv(),
                                    deps=[ott] + bld.war(("PT", qt, g)))
                        bld.setlu(id(p), oev)
                        pt_evacs.append(oev)
                        bld.setlu(("Pt", it), ott)
                pend.append((1, emit_trans))

            def emit_av(h=h, fc=fc, r0=r0, pt_evacs=pt_evacs):
                p = pA[h % 2]
                war = bld.war(id(p))

                def mk_av(h=h, p=p):
                    def fn(t):
                        for kc in range(TC):
                            instr = t.matmul(p[0:64, :], v_tm[:, kc * D + h * 64:kc * D + h * 64 + 64],
                                             PT[:, kc * 512:(kc + 1) * 512],
                                             start=(kc == 0), stop=(kc == TC - 1))
                        return instr
                    return fn
                oav = pr.op("tensor", mk_av(), deps=pt_evacs + pv + war)
                oeav = pr.op("vector", (lambda h, p, fc, r0: lambda v: ecopy(v,
                    o_qb[r0:r0 + 64, fc * 512:(fc + 1) * 512], p[0:64, :]))(h, p, fc, r0),
                    deps=[oav] + bld.war(("o_qb", fc, r0)))
                bld.setlu(id(p), oeav)
                bld.setlu(("o_qb", fc, r0), None)
                for qt in range(4):
                    for g in range(2):
                        bld.setlu(("PT", qt, g), oav)
                last_av[0] = oav
                bld.setlu(("oqbh", h), oeav)
            pend.append((2, emit_av))
        flush_pend(force=True)

        oqb_all = [bld.lu[("oqbh", h)] for h in range(H)]
        uo_ops = []
        p = pA[0]
        war = bld.war(id(p))

        def mk_uo(qb=qb, p=p):
            def fn(t):
                for k in range(DCH):
                    instr = t.matmul(p[0:16, :], AT[:, (3 * DCH + k) * 16:(3 * DCH + k) * 16 + 16],
                                     o_qb[:, k * 512:(k + 1) * 512],
                                     start=(k == 0), stop=(k == DCH - 1))
                return instr
            return fn
        ot = pr.op("tensor", mk_uo(), deps=[at_ops[3]] + oqb_all + war)
        ouo = pr.op("scalar", (lambda p: lambda s: s.activation(
            u_o[0:16, :], p[0:16, :], AF.Identity, scale=1.0 / 16))(p), deps=[ot] + bld.war("u_o"))
        bld.setlu(id(p), ouo)

        for mm in range(4):
            m = qb * 4 + mm
            Fsl = F2[:, (mm % 2) * D:(mm % 2) * D + D]
            ev = []
            for eb in range(2):
                p = pA[(mm * 2 + eb) % 2]
                war = bld.war(id(p))

                def mk_t(mm=mm, eb=eb, p=p, qb=qb):
                    def fn(t):
                        for k in range(DCH):
                            instr = t.matmul(p[:], o_qb[:, k * 512 + mm * 128:k * 512 + (mm + 1) * 128],
                                             WoT[:, k * D + eb * 512:k * D + (eb + 1) * 512],
                                             start=(k == 0), stop=False)
                        instr = t.matmul(p[:], u_o[0:16, mm * 128:(mm + 1) * 128],
                                         BT[0:16, 3 * D + eb * 512:3 * D + (eb + 1) * 512],
                                         start=False, stop=True)
                        return instr
                    return fn
                ot = pr.op("tensor", mk_t(), deps=two + [bt_ops[3], ouo] + oqb_all + war)
                oe = pr.op("vector", (lambda Fsl, eb, p: lambda v: v.tensor_add(
                    Fsl[:, eb * 512:(eb + 1) * 512], p[:], bo_b[:, eb * 512:(eb + 1) * 512]))(Fsl, eb, p),
                    deps=[ot] + bc_bo + bld.war(("F2", mm % 2)))
                bld.setlu(id(p), oe)
                ev.append(oe)
            o7, o8 = ln_chunk_ops(pr, st2, m, Fsl, ev, 1e-5)
            o_t = ln_apply_chunk(pr, st2, m, Fsl, Fsl, [o7, o8])
            sl2 = m % 2
            xs2 = xch[:, sl2 * D:(sl2 + 1) * D]
            ld2 = pr.op("sync", (lambda m, xs2: lambda e: e.dma_start(
                out=xs2, in_=x_src[:, m, :]))(m, xs2), deps=bld.war(("xch", sl2)), dma=True)
            o_add = pr.op("vector", (lambda Fsl, xs2: lambda v: v.tensor_add(
                Fsl, Fsl, xs2))(Fsl, xs2), deps=[o_t, ld2])
            bld.setlu(("xch", sl2), o_add)
            ost = pr.op("sync", (lambda m, Fsl: lambda e: e.dma_start(
                out=xo_dst[:, m, :], in_=Fsl))(m, Fsl), deps=[o_add], dma=True)
            bld.setlu(("F2", mm % 2), ost)
            stores.append(ost)
        bld.setlu("u_o", ot)
        for h in range(H):
            fc, r0 = h // 2, (h % 2) * 64
            bld.setlu(("o_qb", fc, r0), ot)
        for e in range(DCH):
            bld.setlu(("q_qb", e), last_av[0])

    pr.op("sync", lambda e: e.nop(), deps=stores)


def build_block1():
    bld = Bld()
    nc = bld.nc
    dram = (
        nc.dram_tensor("x", [T, D], F32, kind="ExternalInput").ap(),
        nc.dram_tensor("Wq", [D, D], F32, kind="ExternalInput").ap(),
        nc.dram_tensor("Wk", [D, D], F32, kind="ExternalInput").ap(),
        nc.dram_tensor("Wv", [D, D], F32, kind="ExternalInput").ap(),
        nc.dram_tensor("Wo", [D, D], F32, kind="ExternalInput").ap(),
        nc.dram_tensor("bo", [D], F32, kind="ExternalInput").ap(),
        nc.dram_tensor("Aq", [16, D], F32, kind="ExternalInput").ap(),
        nc.dram_tensor("Bq", [D, 16], F32, kind="ExternalInput").ap(),
        nc.dram_tensor("Ak", [16, D], F32, kind="ExternalInput").ap(),
        nc.dram_tensor("Bk", [D, 16], F32, kind="ExternalInput").ap(),
        nc.dram_tensor("Av", [16, D], F32, kind="ExternalInput").ap(),
        nc.dram_tensor("Bv", [D, 16], F32, kind="ExternalInput").ap(),
        nc.dram_tensor("Ao", [16, D], F32, kind="ExternalInput").ap(),
        nc.dram_tensor("Bo", [D, 16], F32, kind="ExternalInput").ap(),
        nc.dram_tensor("iden", [128, 128], F32, kind="ExternalInput").ap(),
        nc.dram_tensor("xo", [T, D], F32, kind="ExternalOutput").ap(),
    )
    with bld.es:
        pr = Prog(bld, "b1")
        body_block1(bld, pr, dram)
        pr.run()
    return bld.nc










































def body_block2(bld, pr, dram):
    nc = bld.nc
    NH, WIN, HID = 8, 512, 2730
    (x_d, wqkv_d, qs_d, ks_d, wo_d, lng_d, lnb_d, w1_d, b1_d, w2_d, b2_d,
     idn_d, c1q_d, c2q_d, c1k_d, c2k_d, b2c_d, mu_d, ml_d, rot_d, xo_d) = dram
    x_src = x_d.rearrange("(c p) d -> p c d", p=128)
    xo_dst = xo_d.rearrange("(c p) d -> p c d", p=128)
    YW = 2052

    xch = bld.sbuf("xch", [128, 2 * D])
    Y = bld.sbuf("Y", [128, TC * D])
    Y_fm = bld.sbuf("Y_fm", [128, DCH * YW], BF16)
    arena = bld.sbuf("arena", [128, 29696], BF16)
    qrot = arena[:, 0:8192]
    kAv = arena[:, 8192:16384]
    v_tm = arena[:, 16384:24576]
    PT = arena[:, 24576:28672]
    Pt = arena[:, 28672:29696]
    w1buf_ = [arena[:, 0:2048], arena[:, 26624:28672]]
    w2buf = arena[:, 2048:13312]
    h4 = arena[:, 13312:24576]
    agst = arena[:, 24576:25600]
    h2st = arena[:, 25600:26624]
    tab_buf = arena[:, 24576:25600]
    wqkv_sb = bld.sbuf("wqkv_sb", [128, DCH * 512], BF16)
    wo_sb = bld.sbuf("wo_sb", [128, 4 * D], BF16)
    o_w = bld.sbuf("o_w", [128, 4 * 512], BF16)
    ybf = bld.sbuf("ybf", [128, D], BF16)
    sqs = bld.sbuf("sqs", [128, 512], BF16)
    s12 = bld.sbuf("s12", [128, 2 * 512], BF16)
    Pt2_ = s12[:, 0:1024]
    t2t = bld.sbuf("t2t", [128, 512], BF16)
    rotm = bld.sbuf("rotm", [128, 128], BF16)
    rotmq = bld.sbuf("rotmq", [128, 128], BF16)
    rotmk = bld.sbuf("rotmk", [128, 128], BF16)
    c1m = bld.sbuf("c1m", [128, 512], BF16)
    rawb = s12[:, 0:512]
    rkrow = bld.sbuf("rkrow", [1, 2 * 512], BF16)
    rq8 = bld.sbuf("rq8", [128, NH * TC])
    qb2 = bld.sbuf("qb2", [128, 512], BF16)
    b2col = bld.sbuf("b2col", [128, 1])
    qscol = bld.sbuf("qscol", [128, 1])
    kscol = bld.sbuf("kscol", [128, 1])
    maskU = bld.sbuf("maskU", [128, 128], BF16)
    maskL = bld.sbuf("maskL", [128, 128], BF16)
    onescol = bld.sbuf("onescol", [128, 1], BF16)
    ones_row = bld.sbuf("ones_row", [1, 128], BF16)
    idn = bld.sbuf("idn", [128, 128], BF16)
    rows = bld.sbuf("rows", [1, 1408], BF16)
    b1c = bld.sbuf("b1c", [128, 44])
    b2r = bld.sbuf("b2r", [128, D], BF16)
    gamr = bld.sbuf("gamr", [128, D], BF16)
    betr = bld.sbuf("betr", [128, D], BF16)
    att = bld.sbuf("attsc", [128, 8])
    gbc = bld.sbuf("gbc", [128, 16])
    hs = {k: bld.sbuf("hx" + k, [128, TC]) for k in
          ("s1a", "s1b", "s2a", "s2b", "sh1", "sh2", "rsJ", "nmJ")}
    n0b = bld.sbuf("n0b", [128, 1])
    nm0b = bld.sbuf("nm0b", [1, 1], BF16)
    junk = bld.sbuf("junk", [128, D], BF16)
    st1 = alloc_stats(bld, "b2ln1", junk=junk)
    st2 = alloc_stats(bld, "b2ln2", junk=junk)
    st3 = st1

    pS2 = [bld.psum("pS", [128, 1024]) for _ in range(2)]
    pA = [bld.psum("pA", [128, 512]) for _ in range(2)]
    pT = [bld.psum("pT", [128, 1024], BF16) for _ in range(2)]
    pB = pS2[0][:, 0:512]
    pN = pS2[1][0:1, 0:512]

    ld_id = pr.op("gpsimd", lambda e: e.dma_start(out=idn[:], in_=idn_d), dma=True)
    o_ones = pr.op("gpsimd", lambda e: e.memset(ones_row[:], 1.0))
    o_onec = pr.op("gpsimd", lambda e: e.memset(onescol[:], 1.0))
    ld_b2c = pr.op("gpsimd", lambda e: e.dma_start(out=b2col[:], in_=b2c_d), dma=True)
    ld_mu = pr.op("gpsimd", lambda e: e.dma_start(out=maskU[:], in_=mu_d), dma=True)
    ld_ml = pr.op("gpsimd", lambda e: e.dma_start(out=maskL[:], in_=ml_d), dma=True)
    ld_rot = pr.op("gpsimd", lambda e: e.dma_start(out=rotm[:], in_=rot_d), dma=True)

    for c in range(TC):
        sl = c % 2
        xs = xch[:, sl * D:(sl + 1) * D]
        ld = pr.op("sync", (lambda c, xs: lambda e: e.dma_start(out=xs, in_=x_src[:, c, :]))(c, xs),
                   deps=bld.war(("xch", sl)), dma=True)
        o7, o8 = ln_chunk_ops(pr, st1, c, xs, [ld], 1e-5)
        oap = ln_apply_chunk(pr, st1, c, xs, Y[:, c * D:(c + 1) * D], [o7, o8])
        bld.setlu(("xch", sl), [oap, o7])
        bld.setlu(("Yc", c), oap)

    def build_Y_fm():
        zops = []
        for f in range(DCH):
            zops.append(pr.op("vector", (lambda f: lambda v: v.memset(
                Y_fm[:, f * YW:f * YW + 1], 0.0))(f), deps=bld.war("Yfm_rd")))
        evs = []
        for tc in range(TC):
            yb = ybf[:, 0:D]
            oc = pr.op("scalar", (lambda tc, yb: lambda s: s.activation(
                yb, Y[:, tc * D:(tc + 1) * D], AF.Identity))(tc, yb),
                deps=bld.war(("Yc", tc)) + bld.war("ybf"))
            p = pT[tc % 2]
            war = bld.war(id(p))

            def mk_t(yb=yb, p=p):
                def fn(t):
                    for f in range(DCH):
                        instr = t.transpose(p[:, f * 128:(f + 1) * 128],
                                            yb[:, f * 128:(f + 1) * 128], idn[:])
                    return instr
                return fn
            ot = pr.op("tensor", mk_t(), deps=[oc, ld_id] + war)
            bld.setlu("ybf", ot)

            def mk_v(tc=tc, p=p):
                def fn(v):
                    for f in range(DCH):
                        instr = v.tensor_copy(
                            Y_fm[:, f * YW + 1 + tc * 128:f * YW + 1 + (tc + 1) * 128],
                            p[:, f * 128:(f + 1) * 128])
                    return instr
                return fn
            oe = pr.op("vector", mk_v(), deps=[ot] + zops + bld.war("Yfm_rd"))
            bld.setlu(id(p), oe)
            evs.append(oe)
        bld.setlu("Yfm_wr", evs)
        return evs

    for L in range(2):
        ld_qs = pr.op("gpsimd", (lambda L: lambda e: e.dma_start(
            out=rows[:, 0:64], in_=qs_d[L][None, :]))(L), deps=bld.war("rows"), dma=True)
        ld_ks = pr.op("gpsimd", (lambda L: lambda e: e.dma_start(
            out=rows[:, 64:128], in_=ks_d[L][None, :]))(L), deps=bld.war("rows"), dma=True)

        def mkqsc(t):
            t.matmul(pB[0:64, 0:1], rows[:, 0:64], onescol[0:1, :], start=True, stop=True)
            return t.matmul(pB[0:64, 1:2], rows[:, 64:128], onescol[0:1, :], start=True, stop=True)
        oqs_t = pr.op("tensor", mkqsc, deps=[ld_qs, ld_ks, o_onec] + bld.war(id(pB)))
        oc1 = pr.op("vector", lambda v: v.tensor_copy(qscol[0:64, :], pB[0:64, 0:1]),
                    deps=[oqs_t] + bld.war("qksc"))
        oc2 = pr.op("vector", lambda v: v.tensor_copy(qscol[64:128, :], pB[0:64, 0:1]), deps=[oqs_t])
        oc3 = pr.op("vector", lambda v: v.tensor_copy(kscol[0:64, :], pB[0:64, 1:2]), deps=[oqs_t])
        oc4 = pr.op("vector", lambda v: v.tensor_copy(kscol[64:128, :], pB[0:64, 1:2]), deps=[oqs_t])
        oqsc = [oc1, oc2]
        oksc = [oc3, oc4]
        bld.setlu(id(pB), oqsc + oksc)
        bld.setlu("rows", [oqs_t])
        o_rmq = pr.op("vector", lambda v: v.tensor_single_scalar(rotmq[:], rotm[:], qscol[:], ALU.mult),
                      deps=[ld_rot] + oqsc + bld.war("rotmq"))
        o_rmk = pr.op("vector", lambda v: v.tensor_single_scalar(rotmk[:], rotm[:], kscol[:], ALU.mult),
                      deps=[ld_rot] + oksc + bld.war("rotmk"))
        bld.setlu("rotmq", o_rmq)
        bld.setlu("rotmk", o_rmk)

        yfm = build_Y_fm()

        def shifted(kc, t0, wdt):
            off = 1 if kc < 4 else 0
            return Y_fm[:, kc * YW + off + t0: kc * YW + off + t0 + wdt]

        pq_done = []
        ldw_prev = None
        for part in range(2):
            ldw = pr.op("gpsimd", (lambda L, part: lambda e: e.dma_start(
                out=wqkv_sb[:].rearrange("p (c e) -> p c e", e=512),
                in_=wqkv_d[L, :, part * 512:(part + 1) * 512].rearrange("(c p) e -> p c e", p=128)))(L, part),
                deps=bld.war("wqkv_sb"), dma=True)
            dst = qrot if part == 0 else kAv
            scol = qscol if part == 0 else kscol
            osc = oqsc if part == 0 else oksc
            c1d = c1q_d if part == 0 else c1k_d
            c2d = c2q_d if part == 0 else c2k_d
            mm_ops = []
            for tb in range(4):
                ld_t1 = pr.op("gpsimd", (lambda tb, c1d: lambda e: e.dma_start(
                    out=tab_buf[:, 0:512], in_=c1d[:, tb * 512:(tb + 1) * 512]))(tb, c1d),
                    deps=bld.war("tab_buf"), dma=True)
                ld_t2 = pr.op("gpsimd", (lambda tb, c2d: lambda e: e.dma_start(
                    out=tab_buf[:, 512:1024], in_=c2d[:, tb * 512:(tb + 1) * 512]))(tb, c2d),
                    deps=bld.war("tab_buf"), dma=True)
                o_c1m = pr.op("vector", (lambda scol: lambda v: v.tensor_single_scalar(
                    c1m[:], tab_buf[:, 0:512], scol[:], ALU.mult))(scol),
                    deps=[ld_t1] + osc + bld.war("c1m"))
                tabdep = [ld_t1, ld_t2, o_c1m]
                for ec in range(4):
                    p = pA[(ec * 4 + tb) % 2]
                    war = bld.war(id(p))

                    def mk_t(ec=ec, tb=tb, p=p):
                        def fn(t):
                            for kc in range(DCH):
                                instr = t.matmul(p[:],
                                                 wqkv_sb[:, kc * 512 + ec * 128:kc * 512 + (ec + 1) * 128],
                                                 shifted(kc, tb * 512, 512),
                                                 start=(kc == 0), stop=(kc == DCH - 1))
                            return instr
                        return fn
                    ot = pr.op("tensor", mk_t(), deps=[ldw] + yfm + war + bld.war("arena_ep"))
                    mm_ops.append(ot)
                    osq = pr.op("scalar", (lambda p: lambda s: s.activation(
                        sqs[:], p[:], AF.Square))(p), deps=[ot] + bld.war("sqs"))
                    if part == 0:
                        lastn = []
                        for half in range(2):
                            h = 2 * ec + half
                            r0 = half * 64
                            for qt in range(4):
                                gqt = tb * 4 + qt
                                omm = pr.op("tensor", (lambda r0, qt: lambda t: t.matmul(
                                    pB[:, 4:5], sqs[r0:r0 + 64, qt * 128:(qt + 1) * 128],
                                    onescol[r0:r0 + 64, :], start=True, stop=True))(r0, qt),
                                    deps=[osq] + bld.war(id(pB)))
                                osr = pr.op("scalar", (lambda h, gqt: lambda s: s.activation(
                                    rq8[:, h * TC + gqt:h * TC + gqt + 1], pB[:, 4:5],
                                    AF.Sqrt, scale=1.0 / 64))(h, gqt), deps=[omm])
                                bld.setlu(id(pB), osr)
                                om1 = pr.op("vector", (lambda h, gqt: lambda v: v.tensor_scalar_max(
                                    rq8[:, h * TC + gqt:h * TC + gqt + 1],
                                    rq8[:, h * TC + gqt:h * TC + gqt + 1], 1.25e-13))(h, gqt), deps=[osr])
                                om2 = pr.op("vector", (lambda h, gqt: lambda v: v.reciprocal(
                                    rq8[:, h * TC + gqt:h * TC + gqt + 1],
                                    rq8[:, h * TC + gqt:h * TC + gqt + 1]))(h, gqt), deps=[om1])
                                lastn.append(om2)
                        bld.setlu(("rqd", ec, tb), lastn)
                    else:
                        rk_done = []
                        for half in range(2):
                            r0 = half * 64
                            omm = pr.op("tensor", (lambda r0: lambda t: t.matmul(
                                pN, onescol[r0:r0 + 64, :], sqs[r0:r0 + 64, :],
                                start=True, stop=True))(r0),
                                deps=[osq] + bld.war("pNk") + bld.war(id(pB)))
                            osr = pr.op("scalar", (lambda half: lambda s: s.activation(
                                rkrow[:, half * 512:(half + 1) * 512], pN, AF.Sqrt,
                                scale=1.0))(half), deps=[omm] + bld.war(("rkr", half)))
                            bld.setlu("pNk", osr)
                            bld.setlu(id(pB), bld.war(id(pB)) + [osr])
                            om1 = pr.op("vector", (lambda half: lambda v: v.tensor_scalar_max(
                                rkrow[:, half * 512:(half + 1) * 512],
                                rkrow[:, half * 512:(half + 1) * 512], 1e-12))(half), deps=[osr])
                            def mkrcp(half=half):
                                def fn(v):
                                    with bld.nc.allow_low_precision(reason="rk row recip feeds bf16 matmul operand"):
                                        return v.reciprocal(rkrow[:, half * 512:(half + 1) * 512],
                                                            rkrow[:, half * 512:(half + 1) * 512])
                                return fn
                            om2 = pr.op("vector", mkrcp(), deps=[om1])
                            rk_done.append(om2)
                    c2 = tab_buf[:, 512:1024]
                    orawc = pr.op("scalar", (lambda p: lambda s: s.activation(
                        rawb[:], p[:], AF.Identity))(p), deps=[ot, osq] + bld.war("rawb"))
                    o1 = pr.op("vector", (lambda p: lambda v: v.tensor_mul(
                        s12[:, 512:1024], p[:], c1m[:]))(p), deps=[ot, osq] + tabdep + bld.war("s12"))
                    bld.setlu(id(p), [o1, orawc])
                    rotm_u = rotmq if part == 0 else rotmk
                    orot = pr.op("tensor", (lambda rotm_u: lambda t: t.matmul(
                        pB[:, 0:512], rotm_u[:], rawb[:], start=True, stop=True))(rotm_u),
                        deps=[orawc, o_rmq, o_rmk] + bld.war(id(pB)))
                    bld.setlu("rawb", orot)
                    o2 = pr.op("vector", (lambda c2: lambda v: v.tensor_mul(
                        t2t[:], pB[:, 0:512], c2))(c2), deps=[orot] + tabdep + bld.war("t2t"))
                    bld.setlu(id(pB), o2)
                    o3 = pr.op("vector", lambda v: v.tensor_add(
                        s12[:, 512:1024], s12[:, 512:1024], t2t[:]), deps=[o1, o2])
                    bld.setlu("t2t", o3)
                    oadds = [o3]
                    dsl = dst[:, ec * T + tb * 512:ec * T + (tb + 1) * 512]
                    if part == 0:
                        ost_ = pr.op("vector", (lambda dsl: lambda v: v.tensor_copy(
                            dsl, s12[:, 512:1024]))(dsl), deps=oadds + bld.war("arena_ep"))
                        bld.setlu(("qwr", ec, tb), ost_)
                    else:
                        def mkbc():
                            def fn(t):
                                t.matmul(pB[0:64, 0:512], ones_row[:, 0:64], rkrow[:, 0:512],
                                         start=True, stop=True)
                                return t.matmul(pB[64:128, 0:512], ones_row[:, 0:64],
                                                rkrow[:, 512:1024], start=True, stop=True)
                            return fn
                        obc = pr.op("tensor", mkbc(), deps=rk_done + [o_ones] + bld.war(id(pB)))
                        ost_ = pr.op("vector", (lambda dsl: lambda v: v.scalar_tensor_tensor(
                            dsl, s12[:, 512:1024], 1.0, pB[:, 0:512], ALU.mult, ALU.mult))(dsl),
                            deps=oadds + [obc] + bld.war("arena_ep"))
                        bld.setlu(id(pB), ost_)
                        bld.setlu(("rkr", 0), ost_)
                        bld.setlu(("rkr", 1), ost_)
                        bld.setlu(("kwr", ec, tb), ost_)
                    bld.setlu("s12", ost_)
                    bld.setlu("sqs", [ost_])
                    bld.setlu("tab_buf", bld.war("tab_buf") + [o1] + [o for o in oadds])
                    pq_done.append(ost_)
            bld.setlu("wqkv_sb", mm_ops)
        bld.setlu("PTz", pq_done[-8:])

        ld_v = pr.op("gpsimd", (lambda L: lambda e: e.dma_start(
            out=wqkv_sb[:].rearrange("p (c e) -> p c e", e=512),
            in_=wqkv_d[L, :, 1024:1536].rearrange("(c p) e -> p c e", p=128)))(L),
            deps=bld.war("wqkv_sb"), dma=True)
        ld_wo = pr.op("gpsimd", (lambda L: lambda e: e.dma_start(
            out=wo_sb[:].rearrange("p (c e) -> p c e", e=D),
            in_=wo_d[L].rearrange("(c p) e -> p c e", p=128)))(L),
            deps=bld.war("wo_sb"), dma=True)
        pv_done = []
        for m in range(TC):
            p = pA[m % 2]
            war = bld.war(id(p))

            def mk_t(m=m, p=p):
                def fn(t):
                    for kc in range(DCH):
                        instr = t.matmul(p[:], shifted(kc, m * 128, 128),
                                         wqkv_sb[:, kc * 512:(kc + 1) * 512],
                                         start=(kc == 0), stop=(kc == DCH - 1))
                    return instr
                return fn
            ot = pr.op("tensor", mk_t(), deps=[ld_v] + yfm + war + bld.war("arena_ep"))
            oe = pr.op("scalar", (lambda m, p: lambda v: ecopy(v,
                v_tm[:, m * 512:(m + 1) * 512], p[:]))(m, p), deps=[ot])
            bld.setlu(id(p), oe)
            pv_done.append(oe)
        bld.setlu("wqkv_sb", [pv_done[-1]])
        bld.setlu("Yfm_rd", pq_done + pv_done)

        wo_res = []
        last_wo_mm = None
        for w in range(4):
            ow_ops = []
            qb2_bld = {}
            pend = []
            ozh = {0: pr.op("gpsimd", lambda e: e.memset(PT, 0.0),
                            deps=bld.war("PTz") + bld.war("arena_ep2"))}

            def flush_pend(force=False):
                rest = []
                while pend:
                    d, fn = pend.pop(0)
                    if d <= 0 or force:
                        fn()
                    else:
                        rest.append((d - 1, fn))
                pend.extend(rest)

            for h in range(NH):
                fc, r0 = h // 2, (h % 2) * 64
                pt_ev = []
                for qt in range(4):
                    pS = pS2[qt % 2]
                    Pt_q = Pt if qt % 2 == 0 else Pt2_
                    gqt = 4 * w + qt
                    kb_lo = qt if w > 0 else 4
                    kb_hi = qt + 4
                    scol0, scol1 = 128 * kb_lo, 128 * (kb_hi + 1)
                    if w > 0 and fc not in qb2_bld:
                        qb2_bld[fc] = pr.op("vector", (lambda fc, w: lambda v: v.tensor_single_scalar(
                            qb2[:], qrot[:, fc * T + w * 512:fc * T + (w + 1) * 512],
                            b2col[:], ALU.mult))(fc, w),
                            deps=[ld_b2c, bld.lu[("qwr", fc, w)]] + bld.war("qb2"))
                    oqb2 = qb2_bld.get(fc) if w > 0 else None

                    def mk_t(fc=fc, r0=r0, qt=qt, w=w, gqt=gqt, kb_lo=kb_lo, kb_hi=kb_hi, pS=pS):
                        def fn(t):
                            for kb in range(kb_lo, kb_hi + 1):
                                gk = 4 * (w - 1) + kb
                                prev = kb < 4
                                lhs = qb2[r0:r0 + 64, qt * 128:(qt + 1) * 128] if prev else \
                                    qrot[r0:r0 + 64, fc * T + gqt * 128:fc * T + (gqt + 1) * 128]
                                msk = maskU if (w > 0 and kb == kb_lo) else (
                                    maskL if kb == kb_hi else None)
                                instr = t.matmul(pS[:, kb * 128:(kb + 1) * 128], lhs,
                                                 kAv[r0:r0 + 64, fc * T + gk * 128:fc * T + (gk + 1) * 128],
                                                 start=True, stop=msk is None)
                                if msk is not None:
                                    instr = t.matmul(pS[:, kb * 128:(kb + 1) * 128],
                                                     idn[:], msk[:], start=False, stop=True)
                            return instr
                        return fn
                    deps = (pq_done + pv_done + bld.war(("pS", qt % 2)) +
                            bld.war(id(pB)) + bld.war("pNk") + [ld_mu, ld_ml, ld_id] +
                            ([oqb2] if oqb2 else []))
                    ot = pr.op("tensor", mk_t(), deps=deps)
                    if oqb2 is not None:
                        bld.setlu("qb2", bld.war("qb2") + [ot])
                    flush_pend()
                    a0 = (qt % 2) * 4
                    onx = pr.op("vector", (lambda scol0, scol1, pS, a0: lambda v: v.tensor_reduce(
                        att[:, a0:a0 + 1], pS[:, scol0:scol1], AX.X, ALU.max, negate=True))(scol0, scol1, pS, a0),
                        deps=[ot] + bld.war(("attsc", qt % 2)))
                    rq8c = rq8[:, h * TC + gqt:h * TC + gqt + 1]
                    obx = pr.op("vector", (lambda rq8c, a0: lambda v: v.tensor_mul(
                        att[:, a0:a0 + 1], att[:, a0:a0 + 1], rq8c))(rq8c, a0),
                        deps=[onx] + bld.war(("rqd", fc, gqt // 4)))
                    oex = pr.op("scalar", (lambda scol0, scol1, rq8c, pS, a0, Pt_q=Pt_q: lambda s: s.activation(
                        Pt_q[:, scol0:scol1], pS[:, scol0:scol1], AF.Exp,
                        bias=att[:, a0:a0 + 1], scale=rq8c, accum_out=att[:, a0 + 2:a0 + 3]))(scol0, scol1, rq8c, pS, a0),
                        deps=[obx] + bld.war(("Pt", qt % 2)) + bld.war("arena_ep2") +
                        (bld.war("s12") if qt % 2 else []))
                    bld.setlu(("pS", qt % 2), oex)
                    orc = pr.op("vector", (lambda a0: lambda v: v.reciprocal(
                        att[:, a0 + 3:a0 + 4], att[:, a0 + 2:a0 + 3]))(a0), deps=[oex])
                    if qt % 2 == 0:
                        onr = pr.op("scalar", (lambda scol0, scol1, a0, Pt_q=Pt_q: lambda s: s.activation(
                            Pt_q[:, scol0:scol1], Pt_q[:, scol0:scol1], AF.Identity,
                            scale=att[:, a0 + 3:a0 + 4]))(scol0, scol1, a0), deps=[orc])
                    else:
                        onr = pr.op("vector", (lambda scol0, scol1, a0, Pt_q=Pt_q: lambda v: v.tensor_scalar_mul(
                            Pt_q[:, scol0:scol1], Pt_q[:, scol0:scol1], att[:, a0 + 3:a0 + 4]))(scol0, scol1, a0), deps=[orc])
                    bld.setlu(("attsc", qt % 2), onr)

                    def emit_trans(qt=qt, kb_lo=kb_lo, kb_hi=kb_hi, onr=onr, h=h,
                                   Pt_q=Pt_q, pt_ev=pt_ev):
                        ptp = pT[qt % 2]
                        warp = bld.war(id(ptp))

                        def mk_tt(kb_lo=kb_lo, kb_hi=kb_hi, ptp=ptp, Pt_q=Pt_q):
                            def fn(t):
                                for j, kb in enumerate(range(kb_lo, kb_hi + 1)):
                                    instr = t.transpose(ptp[:, j * 128:(j + 1) * 128],
                                                        Pt_q[:, kb * 128:(kb + 1) * 128], idn[:])
                                return instr
                            return fn
                        ott = pr.op("tensor", mk_tt(), deps=[onr] + warp)
                        bld.setlu(("Pt", qt % 2), ott)
                        if qt % 2:
                            bld.setlu("s12", ott)
                        PT3 = PT.rearrange("p (k c) -> p k c", c=512)

                        def mk_tv(qt=qt, kb_lo=kb_lo, kb_hi=kb_hi, ptp=ptp, PT3=PT3):
                            nkb = kb_hi - kb_lo + 1
                            dst = PT3[:, kb_lo:kb_hi + 1, qt * 128:(qt + 1) * 128]
                            p3 = ptp[:].rearrange("p (k c) -> p k c", c=128)[:, 0:nkb, :]
                            return lambda v: ecopy(v, dst, p3)
                        oev = pr.op("scalar" if qt % 2 else "vector", mk_tv(),
                                    deps=[ott, ozh[h]] + bld.war("PTz2"))
                        bld.setlu(id(ptp), oev)
                        pt_ev.append(oev)
                    pend.append((1, emit_trans))

                def emit_av(h=h, w=w, fc=fc, r0=r0, pt_ev=pt_ev):
                    pav = pA[h % 2]
                    warA = bld.war(id(pav))
                    kbs = (4, 5, 6, 7) if w == 0 else (3, 4, 0, 1, 2, 5, 6, 7)

                    def mk_av(w=w, h=h, pav=pav, kbs=kbs):
                        def fn(t):
                            for i, kb in enumerate(kbs):
                                gk = 4 * (w - 1) + kb
                                if w == 0:
                                    c0, c1_ = (kb - 4) * 128, 512
                                else:
                                    c0, c1_ = max(0, kb - 4) * 128, min(3, kb) * 128 + 128
                                instr = t.matmul(pav[0:64, c0:c1_],
                                                 v_tm[:, gk * 512 + h * 64:gk * 512 + h * 64 + 64],
                                                 PT[:, kb * 512 + c0:kb * 512 + c1_],
                                                 start=(i == 0), stop=(i == len(kbs) - 1))
                            return instr
                        return fn
                    oav = pr.op("tensor", mk_av(), deps=list(pt_ev) + pv_done + warA)
                    oevav = pr.op("scalar", (lambda fc, r0, pav: lambda v: ecopy(v,
                        o_w[r0:r0 + 64, fc * 512:(fc + 1) * 512], pav[0:64, :]))(fc, r0, pav),
                        deps=[oav] + bld.war(("o_w", fc, r0)))
                    bld.setlu(id(pav), oevav)
                    bld.setlu("PTz", oav)
                    ow_ops.append(oevav)
                    if h + 1 < NH:
                        ozh[h + 1] = pr.op("gpsimd", lambda e: e.memset(PT, 0.0),
                                           deps=bld.war("PTz") + bld.war("arena_ep2"))
                pend.append((2, emit_av))
            flush_pend(force=True)
            for mm in range(4):
                m = 4 * w + mm
                for eb in range(2):
                    p = pA[(mm * 2 + eb) % 2]
                    war = bld.war(id(p))

                    def mk_t(mm=mm, eb=eb, p=p):
                        def fn(t):
                            for kc in range(4):
                                instr = t.matmul(p[:],
                                                 o_w[:, kc * 512 + mm * 128:kc * 512 + (mm + 1) * 128],
                                                 wo_sb[:, kc * D + eb * 512:kc * D + (eb + 1) * 512],
                                                 start=(kc == 0), stop=(kc == 3))
                            return instr
                        return fn
                    ot = pr.op("tensor", mk_t(), deps=ow_ops + [ld_wo] + war)
                    last_wo_mm = ot
                    oe = pr.op("vector", (lambda m, eb, p: lambda v: v.tensor_add(
                        Y[:, m * D + eb * 512:m * D + (eb + 1) * 512],
                        Y[:, m * D + eb * 512:m * D + (eb + 1) * 512], p[:]))(m, eb, p),
                        deps=[ot] + bld.war(("Yc", m)))
                    bld.setlu(id(p), oe)
                    bld.setlu(("Yc", m), oe)
                    wo_res.append(oe)
            for h in range(NH):
                fc, r0 = h // 2, (h % 2) * 64
                bld.setlu(("o_w", fc, r0), last_wo_mm)
        fence = wo_res[-1]
        bld.setlu(id(pB), bld.war(id(pB)) + bld.war(("pS", 0)) + bld.war(("pS", 1)))
        bld.setlu("pNk", bld.war("pNk") + bld.war(("pS", 1)))

        ld_g = pr.op("gpsimd", (lambda L: lambda e: e.dma_start(
            out=rows[:, 0:D], in_=lng_d[L][None, :]))(L), deps=bld.war("rows"), dma=True)
        bcs = emit_bcast_rows(pr, [(rows[:, 0:D], gamr, D)], ones_row, pA, [ld_g, o_ones])
        bc_g = bcs[0]
        gb_ops = []
        for f in range(DCH):
            og = pr.op("tensor", (lambda f: lambda t: t.matmul(
                pB[:, 6:7], rows[:, f * 128:(f + 1) * 128], onescol[0:1, :],
                start=True, stop=True))(f), deps=[ld_g, o_onec] + bld.war(id(pB)))
            oe1 = pr.op("vector", (lambda f: lambda v: v.tensor_copy(
                gbc[:, f:f + 1], pB[:, 6:7]))(f), deps=[og])
            bld.setlu(id(pB), oe1)
            gb_ops.append(oe1)
        ld_b = pr.op("gpsimd", (lambda L: lambda e: e.dma_start(
            out=rows[:, 0:D], in_=lnb_d[L][None, :]))(L),
            deps=bc_g + gb_ops, dma=True)
        bcs = emit_bcast_rows(pr, [(rows[:, 0:D], betr, D)], ones_row, pA, [ld_b, o_ones])
        bc_b = bcs[0]
        for f in range(DCH):
            ob_ = pr.op("tensor", (lambda f: lambda t: t.matmul(
                pB[:, 6:7], rows[:, f * 128:(f + 1) * 128], onescol[0:1, :],
                start=True, stop=True))(f), deps=[ld_b, o_onec] + bld.war(id(pB)))
            oe2 = pr.op("vector", (lambda f: lambda v: v.tensor_copy(
                gbc[:, 8 + f:9 + f], pB[:, 6:7]))(f), deps=[ob_])
            bld.setlu(id(pB), oe2)
            gb_ops.append(oe2)
        bld.setlu("rows", bc_b + gb_ops)
        hops = []
        for c in range(TC):
            Yc = Y[:, c * D:(c + 1) * D]
            dep = bld.war(("Yc", c))
            o1 = pr.op("vector", (lambda c, Yc: lambda v: v.tensor_reduce(
                hs["s1a"][:, c:c + 1], Yc[:, 0:512], AX.X, ALU.add))(c, Yc), deps=dep)
            o2 = pr.op("vector", (lambda c, Yc: lambda v: v.tensor_reduce(
                hs["s1b"][:, c:c + 1], Yc[:, 512:1024], AX.X, ALU.add))(c, Yc), deps=dep)
            o3 = pr.op("scalar", (lambda c, Yc: lambda s: s.activation(
                junk[:, 0:512], Yc[:, 0:512], AF.Square, accum_out=hs["s2a"][:, c:c + 1]))(c, Yc),
                deps=dep)
            o4 = pr.op("scalar", (lambda c, Yc: lambda s: s.activation(
                junk[:, 0:512], Yc[:, 512:1024], AF.Square, accum_out=hs["s2b"][:, c:c + 1]))(c, Yc),
                deps=dep)
            hops += [o1, o2, o3, o4]
        osz = pr.op("gpsimd", lambda e: e.memset(hs["sh1"][0:1, 0:1], 0.0), deps=bld.war("hsh"))
        osz2 = pr.op("gpsimd", lambda e: e.memset(hs["sh2"][0:1, 0:1], 0.0), deps=bld.war("hsh"))
        od1 = pr.op("gpsimd", lambda e: e.dma_start(out=hs["sh1"][1:128, :], in_=hs["s1b"][0:127, :]),
                    deps=hops + bld.war("hsh"), dma=True)
        od2 = pr.op("gpsimd", lambda e: e.dma_start(out=hs["sh1"][0:1, 1:TC], in_=hs["s1b"][127:128, 0:TC - 1]),
                    deps=hops, dma=True)
        od3 = pr.op("gpsimd", lambda e: e.dma_start(out=hs["sh2"][1:128, :], in_=hs["s2b"][0:127, :]),
                    deps=hops, dma=True)
        od4 = pr.op("gpsimd", lambda e: e.dma_start(out=hs["sh2"][0:1, 1:TC], in_=hs["s2b"][127:128, 0:TC - 1]),
                    deps=hops, dma=True)
        shd = [osz, osz2, od1, od2, od3, od4]
        n = TC
        s1 = pr.op("vector", lambda v: v.tensor_add(st2["s1"][:, 0:n], hs["s1a"][:, 0:n], hs["sh1"][:, 0:n]),
                   deps=shd + hops)
        s2 = pr.op("vector", lambda v: v.tensor_add(st2["s2"][:, 0:n], hs["s2a"][:, 0:n], hs["sh2"][:, 0:n]),
                   deps=shd + hops)
        o3b = pr.op("vector", lambda v: v.tensor_scalar_mul(st2["mu"][:, 0:n], st2["s1"][:, 0:n], 1.0 / D),
                    deps=[s1])
        o4b = pr.op("vector", lambda v: v.tensor_mul(st2["var"][:, 0:n], st2["mu"][:, 0:n], st2["mu"][:, 0:n]),
                    deps=[o3b])
        o5b = pr.op("vector", lambda v: v.scalar_tensor_tensor(st2["var"][:, 0:n], st2["s2"][:, 0:n],
                                                               1.0 / D, st2["var"][:, 0:n],
                                                               ALU.mult, ALU.subtract), deps=[s2, o4b])
        eps_ap, eps_op = bld.constcol(pr, 1e-5)
        o6b = pr.op("scalar", lambda s: s.activation(st2["sd"][:, 0:n], st2["var"][:, 0:n], AF.Sqrt,
                                                     bias=eps_ap), deps=[o5b, eps_op])
        o7b = pr.op("vector", lambda v: v.reciprocal(st2["rstd"][:, 0:n], st2["sd"][:, 0:n]), deps=[o6b])
        o8b = pr.op("vector", lambda v: v.scalar_tensor_tensor(st2["nmr"][:, 0:n], st2["mu"][:, 0:n],
                                                               -1.0, st2["rstd"][:, 0:n],
                                                               ALU.mult, ALU.mult), deps=[o7b])
        oj1 = pr.op("gpsimd", lambda e: e.dma_start(out=hs["rsJ"][0:127, :], in_=st2["rstd"][1:128, :]),
                    deps=[o7b, o8b] + bld.war("hsJ"), dma=True)
        oj2 = pr.op("gpsimd", lambda e: e.dma_start(out=hs["rsJ"][127:128, 0:TC - 1], in_=st2["rstd"][0:1, 1:TC]),
                    deps=[o7b, o8b], dma=True)
        oj3 = pr.op("gpsimd", lambda e: e.dma_start(out=hs["nmJ"][0:127, :], in_=st2["nmr"][1:128, :]),
                    deps=[o8b], dma=True)
        oj4 = pr.op("gpsimd", lambda e: e.dma_start(out=hs["nmJ"][127:128, 0:TC - 1], in_=st2["nmr"][0:1, 1:TC]),
                    deps=[o8b], dma=True)
        jd = [oj1, oj2, oj3, oj4, o8b]
        ocp0 = pr.op("vector", lambda v: v.tensor_copy(nm0b[:], st2["nmr"][0:1, 0:1]),
                     deps=[o8b] + bld.war("nm0b"))
        bld.setlu("nm0b", ocp0)
        ob0 = pr.op("tensor", lambda t: t.matmul(pB[:, 5:6], ones_row[:], nm0b[:],
                                                 start=True, stop=True),
                    deps=[ocp0, o_ones] + bld.war(id(pB)))
        ob0e = pr.op("vector", lambda v: v.tensor_copy(n0b[:], pB[:, 5:6]), deps=[ob0])
        bld.setlu(id(pB), ob0e)

        h2ops = []
        for tc in range(TC):
            hst = h2st
            depw = bld.war("h2st")
            oa = pr.op("scalar", (lambda tc, hst: lambda s: s.activation(
                hst[:, 0:512], Y[:, tc * D:tc * D + 512], AF.Identity,
                bias=st2["nmr"][:, tc:tc + 1], scale=st2["rstd"][:, tc:tc + 1]))(tc, hst),
                deps=[o8b] + depw + [fence])
            ob = pr.op("scalar", (lambda tc, hst: lambda s: s.activation(
                hst[:, 512:1024], Y[:, tc * D + 512:tc * D + 1024], AF.Identity,
                bias=hs["nmJ"][:, tc:tc + 1], scale=hs["rsJ"][:, tc:tc + 1]))(tc, hst),
                deps=jd + depw)
            og1 = pr.op("vector", (lambda hst: lambda v: v.tensor_mul(
                hst[:, 0:1024], hst[:, 0:1024], gamr[:, 0:1024]))(hst), deps=[oa, ob] + bc_g)
            og2 = pr.op("vector", (lambda hst: lambda v: v.tensor_add(
                hst[:, 0:1024], hst[:, 0:1024], betr[:, 0:1024]))(hst), deps=[og1] + bc_b)
            p = pT[tc % 2]
            war = bld.war(id(p))

            def mk_t(hst=hst, p=p):
                def fn(t):
                    for f in range(DCH):
                        instr = t.transpose(p[:, f * 128:(f + 1) * 128],
                                            hst[:, f * 128:(f + 1) * 128], idn[:])
                    return instr
                return fn
            ot = pr.op("tensor", mk_t(), deps=[og2] + war)
            bld.setlu("h2st", ot)

            def mk_v(tc=tc, p=p):
                def fn(v):
                    for f in range(DCH):
                        if f < 4:
                            instr = v.tensor_copy(
                                Y_fm[:, f * YW + 1 + tc * 128:f * YW + 1 + (tc + 1) * 128],
                                p[:, f * 128:(f + 1) * 128])
                        else:
                            wdt = 128 if tc < TC - 1 else 127
                            instr = v.tensor_copy(
                                Y_fm[:, f * YW + 2 + tc * 128:f * YW + 2 + tc * 128 + wdt],
                                p[:, f * 128:f * 128 + wdt])
                    return instr
                return fn
            oe = pr.op("vector", mk_v(), deps=[ot] + bld.war("Yfm_rd2") +
                       (bld.war("Yfm_rd") if tc == 0 else []))
            bld.setlu(id(p), oe)
            h2ops.append(oe)
        pad_ops = []
        for f in range(4, DCH):
            opd = pr.op("vector", (lambda f: lambda v: v.scalar_tensor_tensor(
                Y_fm[:, f * YW + 1:f * YW + 2], n0b[:], 1.0, junk[:, 0:1], ALU.mult, ALU.bypass))(f),
                deps=[ob0e] + h2ops) if False else None
            op1 = pr.op("vector", (lambda f: lambda v: v.tensor_single_scalar(
                Y_fm[:, f * YW + 1:f * YW + 2], n0b[:], gbc[:, f:f + 1], ALU.mult))(f),
                deps=[ob0e] + gb_ops + h2ops)
            op2 = pr.op("vector", (lambda f: lambda v: v.tensor_single_scalar(
                Y_fm[:, f * YW + 1:f * YW + 2], Y_fm[:, f * YW + 1:f * YW + 2],
                gbc[:, 8 + f:9 + f], ALU.add))(f), deps=[op1])
            pad_ops.append(op2)
        y2done = h2ops + pad_ops
        bld.setlu("hsh", [s1, s2])
        bld.setlu("hsJ", y2done)

        ld_b1 = pr.op("gpsimd", (lambda L: lambda e: e.dma_start(
            out=rows[:, 0:1408], in_=b1_d[L][None, 0:1408]))(L), deps=bld.war("rows"), dma=True)
        ld_b2r = pr.op("gpsimd", (lambda L: lambda e: e.dma_start(
            out=rows[:, 0:D] if False else junk[0:1, 0:D], in_=b2_d[L][None, :]))(L),
            deps=bld.war("junkrow"), dma=True)
        b1_ops = []
        widths = [128] * 21 + [42]
        segs = [(0, 0, 11), (1408, 11, 22), (2730, 22, 33), (2730 + 1408, 33, 44)]
        for base, c0, c1_ in segs:
            ldp = pr.op("gpsimd", (lambda L, base: lambda e: e.dma_start(
                out=rows[:, 0:min(1408, 5460 - base)],
                in_=b1_d[L][None, base:base + min(1408, 5460 - base)]))(L, base),
                deps=bld.war("rows") + list(b1_ops), dma=True)
            for ci in range(c0, c1_):
                e = ci % 22
                w_ = widths[e]
                off = (ci - c0) * 128
                oga = pr.op("tensor", (lambda off, w_: lambda t: t.matmul(
                    pB[0:w_, 7:8], rows[:, off:off + w_], onescol[0:1, :],
                    start=True, stop=True))(off, w_), deps=[ldp, o_onec] + bld.war(id(pB)))
                oea = pr.op("vector", (lambda ci, w_: lambda v: v.tensor_copy(
                    b1c[0:w_, ci:ci + 1], pB[0:w_, 7:8]))(ci, w_), deps=[oga])
                bld.setlu(id(pB), oea)
                b1_ops.append(oea)
            bld.setlu("rows", list(b1_ops))
        bld.setlu("rows", b1_ops)
        bcs2 = emit_bcast_rows(pr, [(junk[0:1, 0:D], b2r, D)], ones_row, pA, [ld_b2r, o_ones])
        bc_b2 = bcs2[0]
        bld.setlu("junkrow", bc_b2)

        ffn_last = []
        for tb in range(4):
            h4_ops = {}
            for e in range(22):
                me = widths[e]
                w1buf = w1buf_[e % 2]
                lda = pr.op("gpsimd", (lambda L, e, me, w1buf: lambda g: g.dma_start(
                    out=w1buf[:, 0:me * 8].rearrange("p (c q) -> p c q", q=me),
                    in_=w1_d[L, :, e * 128:e * 128 + me].rearrange("(c p) q -> p c q", p=128)))(L, e, me, w1buf),
                    deps=bld.war(("w1b", e % 2, 0)) + ([fence] if tb == 0 and e == 0 else []), dma=True)
                ldg = pr.op("gpsimd", (lambda L, e, me, w1buf: lambda g: g.dma_start(
                    out=w1buf[:, 1024:1024 + me * 8].rearrange("p (c q) -> p c q", q=me),
                    in_=w1_d[L, :, 2730 + e * 128:2730 + e * 128 + me].rearrange("(c p) q -> p c q", p=128)))(L, e, me, w1buf),
                    deps=bld.war(("w1b", e % 2, 1)), dma=True)
                pa_ = pA[e % 2]
                war = bld.war(id(pa_))

                def mk_ta(e=e, me=me, tb=tb, pa_=pa_, w1buf=w1buf):
                    def fn(t):
                        for kc in range(DCH):
                            instr = t.matmul(pa_[0:me, :],
                                             w1buf[:, kc * me:(kc + 1) * me],
                                             Y_fm[:, kc * YW + 1 + tb * 512:kc * YW + 1 + (tb + 1) * 512],
                                             start=(kc == 0), stop=(kc == DCH - 1))
                        return instr
                    return fn
                ota = pr.op("tensor", mk_ta(), deps=[lda] + y2done + war)
                oea = pr.op("scalar", (lambda e, me, pa_: lambda s: s.activation(
                    agst[:, 0:512][0:me, :], pa_[0:me, :], AF.Identity,
                    bias=b1c[0:me, e:e + 1]))(e, me, pa_),
                    deps=[ota] + b1_ops + bld.war(("ag", 0)))
                bld.setlu(id(pa_), oea)
                war = bld.war(id(pa_))

                def mk_tg(e=e, me=me, tb=tb, pa_=pa_, w1buf=w1buf):
                    def fn(t):
                        for kc in range(DCH):
                            instr = t.matmul(pa_[0:me, :],
                                             w1buf[:, 1024 + kc * me:1024 + (kc + 1) * me],
                                             Y_fm[:, kc * YW + 1 + tb * 512:kc * YW + 1 + (tb + 1) * 512],
                                             start=(kc == 0), stop=(kc == DCH - 1))
                        return instr
                    return fn
                otg = pr.op("tensor", mk_tg(), deps=[ldg] + y2done + war)
                oeg = pr.op("scalar", (lambda e, me, pa_: lambda s: s.activation(
                    agst[:, 512:1024][0:me, :], pa_[0:me, :], AF.Gelu,
                    bias=b1c[0:me, 22 + e:23 + e]))(e, me, pa_),
                    deps=[otg] + b1_ops + bld.war(("ag", 1)))
                bld.setlu(id(pa_), oeg)
                bld.setlu(("w1b", e % 2, 0), ota)
                bld.setlu(("w1b", e % 2, 1), otg)
                oh = pr.op("vector", (lambda e, me: lambda v: v.tensor_mul(
                    h4[:, e * 512:(e + 1) * 512][0:me, :], agst[0:me, 0:512],
                    agst[0:me, 512:1024]))(e, me), deps=[oea, oeg] + bld.war(("h4", e)))
                bld.setlu(("ag", 0), oh)
                bld.setlu(("ag", 1), oh)
                h4_ops[e] = oh
            for eb in range(2):
                ldw2a = pr.op("gpsimd", (lambda L, eb: lambda g: g.dma_start(
                    out=w2buf[:, 0:11 * 512].rearrange("p (c q) -> p c q", q=512),
                    in_=w2_d[L, 0:1408, eb * 512:(eb + 1) * 512].rearrange("(c p) q -> p c q", p=128)))(L, eb),
                    deps=bld.war("w2ba"), dma=True)
                ldw2b = pr.op("gpsimd", (lambda L, eb: lambda g: g.dma_start(
                    out=w2buf[:, 11 * 512:21 * 512].rearrange("p (c q) -> p c q", q=512),
                    in_=w2_d[L, 1408:2688, eb * 512:(eb + 1) * 512].rearrange("(c p) q -> p c q", p=128)))(L, eb),
                    deps=bld.war("w2bb"), dma=True)
                ldw2c = pr.op("gpsimd", (lambda L, eb: lambda g: g.dma_start(
                    out=w2buf[0:42, 21 * 512:22 * 512],
                    in_=w2_d[L, 2688:2730, eb * 512:(eb + 1) * 512]))(L, eb),
                    deps=bld.war("w2bb"), dma=True)
                w2mm = []
                w2mma = []
                for mm in range(4):
                    m = 4 * tb + mm
                    p = pA[mm % 2]
                    war = bld.war(id(p))

                    def mk_ta(mm=mm, p=p):
                        def fn(t):
                            for kc in range(11):
                                me = widths[kc]
                                instr = t.matmul(p[:],
                                                 h4[:, kc * 512 + mm * 128:kc * 512 + (mm + 1) * 128][0:me, :],
                                                 w2buf[:, kc * 512:(kc + 1) * 512][0:me, :],
                                                 start=(kc == 0), stop=False)
                            return instr
                        return fn
                    ota = pr.op("tensor", mk_ta(), deps=[ldw2a] + list(h4_ops.values()) + war)
                    w2mma.append(ota)

                    def mk_tb(mm=mm, p=p):
                        def fn(t):
                            for kc in range(11, 22):
                                me = widths[kc]
                                instr = t.matmul(p[:],
                                                 h4[:, kc * 512 + mm * 128:kc * 512 + (mm + 1) * 128][0:me, :],
                                                 w2buf[:, kc * 512:(kc + 1) * 512][0:me, :],
                                                 start=False, stop=(kc == 21))
                            return instr
                        return fn
                    ot = pr.op("tensor", mk_tb(), deps=[ota, ldw2b, ldw2c], )
                    w2mm.append(ot)
                    oe1 = pr.op("vector", (lambda m, eb, p: lambda v: v.tensor_add(
                        Y[:, m * D + eb * 512:m * D + (eb + 1) * 512],
                        Y[:, m * D + eb * 512:m * D + (eb + 1) * 512], p[:]))(m, eb, p),
                        deps=[ot] + bld.war(("Yc", m)))
                    oe2 = pr.op("vector", (lambda m, eb: lambda v: v.tensor_add(
                        Y[:, m * D + eb * 512:m * D + (eb + 1) * 512],
                        Y[:, m * D + eb * 512:m * D + (eb + 1) * 512],
                        b2r[:, eb * 512:(eb + 1) * 512]))(m, eb), deps=[oe1] + bc_b2)
                    bld.setlu(id(p), oe2)
                    bld.setlu(("Yc", m), oe2)
                    ffn_last.append(oe2)
                bld.setlu("w2ba", w2mma)
                bld.setlu("w2bb", w2mm)
            for e in range(22):
                bld.setlu(("h4", e), w2mm)
        bld.setlu("arena_ep", ffn_last)
        bld.setlu("arena_ep2", ffn_last)
        bld.setlu("Yfm_rd", ffn_last)
        bld.setlu("Yfm_rd2", None)
        bld.setlu("qksc", ffn_last)

    def get_F(m):
        return Y[:, m * D:(m + 1) * D], bld.war(("Yc", m))

    stores = emit_sandwich_tail(bld, pr, get_F, st3, xch, x_src, xo_dst, nsl=2)
    pr.op("sync", lambda e: e.nop(), deps=stores)


def block2_tables():
    HD = 64
    t = np.arange(T, dtype=np.float64)
    half = np.arange(0, HD, 2, dtype=np.float64)
    inv_freq = 1.0 / (10000.0 ** (half / HD))
    base = (half + 0.4 * HD) / (1.4 * HD)
    fm = np.concatenate([np.arange(32), np.arange(32)])
    freqs = t[None, :] * inv_freq[fm][:, None]
    cos = np.cos(freqs)
    sin = np.sin(freqs)

    tm = (t % 512) / 256.0
    xq = base[fm][:, None] ** tm[None, :]
    xk = base[fm][:, None] ** (-tm[None, :])
    c1q = (cos * xq)
    c2q = (sin * xq)
    c1k = (cos * xk)
    c2k = (sin * xk)
    b2c = (base[fm] ** 2.0)
    dup = lambda a: np.concatenate([a, a], 0).astype(np.float32)
    c1q, c2q, c1k, c2k = dup(c1q), dup(c2q), dup(c1k), dup(c2k)
    b2col = np.concatenate([b2c, b2c]).astype(np.float32)[:, None]
    PT_ = np.zeros((128, 128))
    for f in range(128):
        h0 = (f // 64) * 64
        fl = f % 64
        if fl < 32:
            PT_[h0 + fl + 32, f] = -1.0
        else:
            PT_[h0 + fl - 32, f] = 1.0
    ii, jj = np.mgrid[0:128, 0:128]
    mU = np.where(jj >= ii, 0.0, -1e9).astype(np.float32)
    mL = np.where(jj <= ii, 0.0, -1e9).astype(np.float32)
    return c1q, c2q, c1k, c2k, b2col, mU, mL, PT_.astype(np.float32)


def build_block2():
    bld = Bld()
    nc = bld.nc
    dram = (
        nc.dram_tensor("x", [T, D], F32, kind="ExternalInput").ap(),
        nc.dram_tensor("wqkv", [2, D, 1536], F32, kind="ExternalInput").ap(),
        nc.dram_tensor("q_scale", [2, 64], F32, kind="ExternalInput").ap(),
        nc.dram_tensor("k_scale", [2, 64], F32, kind="ExternalInput").ap(),
        nc.dram_tensor("wo", [2, 512, D], F32, kind="ExternalInput").ap(),
        nc.dram_tensor("ln_g", [2, D], F32, kind="ExternalInput").ap(),
        nc.dram_tensor("ln_b", [2, D], F32, kind="ExternalInput").ap(),
        nc.dram_tensor("w1", [2, D, 5460], F32, kind="ExternalInput").ap(),
        nc.dram_tensor("b1", [2, 5460], F32, kind="ExternalInput").ap(),
        nc.dram_tensor("w2", [2, 2730, D], F32, kind="ExternalInput").ap(),
        nc.dram_tensor("b2", [2, D], F32, kind="ExternalInput").ap(),
        nc.dram_tensor("iden", [128, 128], F32, kind="ExternalInput").ap(),
        nc.dram_tensor("c1q", [128, T], F32, kind="ExternalInput").ap(),
        nc.dram_tensor("c2q", [128, T], F32, kind="ExternalInput").ap(),
        nc.dram_tensor("c1k", [128, T], F32, kind="ExternalInput").ap(),
        nc.dram_tensor("c2k", [128, T], F32, kind="ExternalInput").ap(),
        nc.dram_tensor("b2c", [128, 1], F32, kind="ExternalInput").ap(),
        nc.dram_tensor("maskU", [128, 128], F32, kind="ExternalInput").ap(),
        nc.dram_tensor("maskL", [128, 128], F32, kind="ExternalInput").ap(),
        nc.dram_tensor("rotm", [128, 128], F32, kind="ExternalInput").ap(),
        nc.dram_tensor("xo", [T, D], F32, kind="ExternalOutput").ap(),
    )
    with bld.es:
        pr = Prog(bld, "b2")
        body_block2(bld, pr, dram)
        pr.run()
    return bld.nc


def build_fused():
    bld = Bld()
    nc = bld.nc

    def din(name, shape):
        return nc.dram_tensor(name, shape, F32, kind="ExternalInput").ap()

    x_in = din("x", [T, D])
    h1 = nc.dram_tensor("h1", [T, D], F32, kind="Internal").ap()
    h2 = nc.dram_tensor("h2", [T, D], F32, kind="Internal").ap()
    h3 = nc.dram_tensor("h3", [T, D], F32, kind="Internal").ap()
    xo = nc.dram_tensor("out", [T, D], F32, kind="ExternalOutput").ap()
    iden = din("iden", [128, 128])

    d1 = (x_in, din("Wq", [D, D]), din("Wk", [D, D]), din("Wv", [D, D]),
          din("Wo", [D, D]), din("bo", [D]),
          din("Aq", [16, D]), din("Bq", [D, 16]), din("Ak", [16, D]), din("Bk", [D, 16]),
          din("Av", [16, D]), din("Bv", [D, 16]), din("Ao", [16, D]), din("Bo", [D, 16]),
          iden, h1)
    d2 = (h1, din("wqkv", [2, D, 1536]), din("q_scale", [2, 64]), din("k_scale", [2, 64]),
          din("wo2", [2, 512, D]), din("ln_g", [2, D]), din("ln_b", [2, D]),
          din("w1", [2, D, 5460]), din("b1", [2, 5460]), din("w2", [2, 2730, D]),
          din("b2", [2, D]), iden, din("c1q", [128, T]), din("c2q", [128, T]),
          din("c1k", [128, T]), din("c2k", [128, T]), din("b2c", [128, 1]),
          din("maskU", [128, 128]), din("maskL", [128, 128]), din("rotm", [128, 128]), h2)
    d3 = (h2, din("w_qkv", [D, 1536]), din("b_qkv", [1536]), din("w_fc", [512, D]),
          din("b_fc", [D]), din("gamma", [D]), din("beta", [D]), iden, h3)
    d4 = (h3, din("gate_w", [16, D]), din("bias_e", [16]), din("up_w", [16, 512, D]),
          din("down_w", [16, D, 512]), din("shared_up", [2048, D]),
          din("shared_down", [D, 2048]), iden, xo)

    with bld.es:
        for body, dd in ((body_block1, d1), (body_block2, d2),
                         (body_block3, d3), (body_block4, d4)):
            bld.push_scope()
            pr = Prog(bld)
            body(bld, pr, dd)
            pr.run(barrier=True)
            bld.pop_scope()
    return bld.nc


FUSED = True
_NC_CACHE = {}


def _ncs():
    if "b" not in _NC_CACHE:
        _NC_CACHE["b"] = (build_block1(), build_block2(), build_block3(), build_block4())
    return _NC_CACHE["b"]


def kernel_fused(**inputs):
    g = {k: np.ascontiguousarray(np.asarray(v, dtype=np.float32)) for k, v in inputs.items()}
    iden = np.eye(128, dtype=np.float32)
    c1q, c2q, c1k, c2k, b2c, mU, mL, rotm = block2_tables()
    if "f" not in _NC_CACHE:
        _NC_CACHE["f"] = build_fused()
    nc = _NC_CACHE["f"]
    x = g["L0_attn_x"]
    ims = []
    for c in range(NCORE):
        ims.append({
            "x": np.ascontiguousarray(x[c]), "iden": iden,
            "Wq": g["L0_attn_Wq"], "Wk": g["L0_attn_Wk"], "Wv": g["L0_attn_Wv"],
            "Wo": g["L0_attn_Wo"], "bo": g["L0_attn_bo"],
            "Aq": g["L0_attn_Aq"], "Bq": g["L0_attn_Bq"], "Ak": g["L0_attn_Ak"],
            "Bk": g["L0_attn_Bk"], "Av": g["L0_attn_Av"], "Bv": g["L0_attn_Bv"],
            "Ao": g["L0_attn_Ao"], "Bo": g["L0_attn_Bo"],
            "wqkv": g["L0_ffn_wqkv"], "q_scale": g["L0_ffn_q_scale"],
            "k_scale": g["L0_ffn_k_scale"], "wo2": g["L0_ffn_wo"],
            "ln_g": g["L0_ffn_ln_g"], "ln_b": g["L0_ffn_ln_b"],
            "w1": g["L0_ffn_w1"], "b1": g["L0_ffn_b1"], "w2": g["L0_ffn_w2"],
            "b2": g["L0_ffn_b2"], "c1q": c1q, "c2q": c2q, "c1k": c1k, "c2k": c2k,
            "b2c": b2c, "maskU": mU, "maskL": mL, "rotm": rotm,
            "w_qkv": g["L1_attn_w_qkv"], "b_qkv": g["L1_attn_b_qkv"],
            "w_fc": g["L1_attn_w_fc"], "b_fc": g["L1_attn_b_fc"],
            "gamma": g["L1_attn_gamma"], "beta": g["L1_attn_beta"],
            "gate_w": g["L1_ffn_gate_w"], "bias_e": g["L1_ffn_bias_e"],
            "up_w": g["L1_ffn_up_w"], "down_w": g["L1_ffn_down_w"],
            "shared_up": g["L1_ffn_shared_up"], "shared_down": g["L1_ffn_shared_down"],
        })
    res = run_bass_kernel_spmd(nc, ims, core_ids=list(range(NCORE)))
    return np.stack([np.asarray(res.results[c]["out"], dtype=np.float32)
                     for c in range(NCORE)], axis=0)


def kernel(**inputs):
    if FUSED:
        return kernel_fused(**inputs)
    g = {k: np.ascontiguousarray(np.asarray(v, dtype=np.float32)) for k, v in inputs.items()}
    iden = np.eye(128, dtype=np.float32)
    c1q, c2q, c1k, c2k, b2c, mU, mL, rotm = block2_tables()
    nc1, nc2, nc3, nc4 = _ncs()

    x = g["L0_attn_x"]
    xs = [np.ascontiguousarray(x[c]) for c in range(NCORE)]

    im1 = [{"x": xs[c], "Wq": g["L0_attn_Wq"], "Wk": g["L0_attn_Wk"],
            "Wv": g["L0_attn_Wv"], "Wo": g["L0_attn_Wo"], "bo": g["L0_attn_bo"],
            "Aq": g["L0_attn_Aq"], "Bq": g["L0_attn_Bq"], "Ak": g["L0_attn_Ak"],
            "Bk": g["L0_attn_Bk"], "Av": g["L0_attn_Av"], "Bv": g["L0_attn_Bv"],
            "Ao": g["L0_attn_Ao"], "Bo": g["L0_attn_Bo"], "iden": iden}
           for c in range(NCORE)]
    r1 = run_bass_kernel_spmd(nc1, im1, core_ids=list(range(NCORE)))
    xs = [np.ascontiguousarray(np.asarray(r1.results[c]["xo"], dtype=np.float32))
          for c in range(NCORE)]

    im2 = [{"x": xs[c], "wqkv": g["L0_ffn_wqkv"], "q_scale": g["L0_ffn_q_scale"],
            "k_scale": g["L0_ffn_k_scale"], "wo": g["L0_ffn_wo"], "ln_g": g["L0_ffn_ln_g"],
            "ln_b": g["L0_ffn_ln_b"], "w1": g["L0_ffn_w1"], "b1": g["L0_ffn_b1"],
            "w2": g["L0_ffn_w2"], "b2": g["L0_ffn_b2"], "iden": iden,
            "c1q": c1q, "c2q": c2q, "c1k": c1k, "c2k": c2k, "b2c": b2c,
            "maskU": mU, "maskL": mL, "rotm": rotm} for c in range(NCORE)]
    r2 = run_bass_kernel_spmd(nc2, im2, core_ids=list(range(NCORE)))
    xs = [np.ascontiguousarray(np.asarray(r2.results[c]["xo"], dtype=np.float32))
          for c in range(NCORE)]

    im3 = [{"x": xs[c], "w_qkv": g["L1_attn_w_qkv"], "b_qkv": g["L1_attn_b_qkv"],
            "w_fc": g["L1_attn_w_fc"], "b_fc": g["L1_attn_b_fc"],
            "gamma": g["L1_attn_gamma"], "beta": g["L1_attn_beta"], "iden": iden}
           for c in range(NCORE)]
    r3 = run_bass_kernel_spmd(nc3, im3, core_ids=list(range(NCORE)))
    xs = [np.ascontiguousarray(np.asarray(r3.results[c]["xo"], dtype=np.float32))
          for c in range(NCORE)]

    im4 = [{"x": xs[c], "gate_w": g["L1_ffn_gate_w"], "bias_e": g["L1_ffn_bias_e"],
            "up_w": g["L1_ffn_up_w"], "down_w": g["L1_ffn_down_w"],
            "shared_up": g["L1_ffn_shared_up"], "shared_down": g["L1_ffn_shared_down"],
            "iden": iden} for c in range(NCORE)]
    r4 = run_bass_kernel_spmd(nc4, im4, core_ids=list(range(NCORE)))
    out = np.stack([np.asarray(r4.results[c]["xo"], dtype=np.float32)
                    for c in range(NCORE)], axis=0)
    return out


# version 17
# speedup vs baseline: 1.2170x; 1.2170x over previous
import contextlib
import math
import numpy as np
import concourse.bass as bass
import concourse.mybir as mybir
from concourse.bass_utils import run_bass_kernel_spmd

F32 = mybir.dt.float32
BF16 = mybir.dt.bfloat16
AF = mybir.ActivationFunctionType
ALU = mybir.AluOpType
AX = mybir.AxisListType

T, D = 2048, 1024
TC, DCH = T // 128, D // 128
NCORE = 8
ENGS = ("sync", "gpsimd", "tensor", "vector", "scalar")


class Op:
    __slots__ = ("i", "eng", "fn", "deps", "dma", "special", "inc_amt", "cum", "needed", "sem_name")

    def __init__(self, i, eng, fn, deps, dma, special):
        self.i, self.eng, self.fn = i, eng, fn
        self.deps = [d for d in deps if d is not None]
        self.dma = dma
        self.special = special
        self.inc_amt = 16 if dma else 1
        self.cum = 0
        self.needed = False
        self.sem_name = None


class Prog:

    def __init__(self, bld, name="p"):
        self.bld = bld
        self.name = name
        self.ops = []

    def op(self, eng, fn, deps=(), dma=False, special=False):
        o = Op(len(self.ops), eng, fn, deps, dma, special)
        self.ops.append(o)
        return o.i

    def run(self, barrier=False):
        bld = self.bld
        ops = self.ops
        entry = {sn: v for sn, v in bld.cnt.items() if v > 0} if barrier else {}
        NDMASEM = 6
        for o in ops:
            if o.dma:
                o.needed = True
            for d in o.deps:
                p = ops[d]
                if p.eng != o.eng or p.special or p.dma or p.eng in ("vector", "scalar", "gpsimd"):
                    p.needed = True
        cum = {e: bld.cnt.get("S" + e, 0) for e in ENGS}
        for o in ops:
            if o.dma:
                slot = bld.dslot.get(o.eng, 0)
                bld.dslot[o.eng] = slot + 1
                sn = f"D{o.eng}{slot % NDMASEM}"
                k = bld.cnt.get(sn, 0) // 16 + 1
                bld.cnt[sn] = 16 * k
                o.sem_name = sn
                o.cum = 16 * k
            elif o.needed:
                cum[o.eng] += o.inc_amt
                o.sem_name = "S" + o.eng
                o.cum = cum[o.eng]

        def mk(eng_name):
            mine = [o for o in ops if o.eng == eng_name]
            if not mine:
                return None

            def section(eng):
                waited = {}
                for sn, v in entry.items():
                    eng.wait_ge(bld.sem(sn), v)
                    waited[sn] = v
                for o in mine:
                    for d in o.deps:
                        p = ops[d]
                        if p.eng == eng_name and not (p.special or p.dma) and p.eng not in ("vector", "scalar", "gpsimd"):
                            continue
                        if p.cum == 0:
                            continue
                        sn = p.sem_name
                        if waited.get(sn, -1) < p.cum:
                            eng.wait_ge(bld.sem(sn), p.cum)
                            waited[sn] = p.cum
                    if o.dma and o.cum > 16:
                        sn = o.sem_name
                        if waited.get(sn, -1) < o.cum - 16:
                            eng.wait_ge(bld.sem(sn), o.cum - 16)
                            waited[sn] = o.cum - 16
                    instr = o.fn(eng)
                    if o.needed:
                        assert instr is not None, f"needed op returned None on {eng_name} (op {o.i})"
                        instr.then_inc(bld.sem(o.sem_name), o.inc_amt)
            return section

        sections = {e: mk(e) for e in ENGS}
        with bld.nc.Block() as blk:
            for e in ENGS:
                if sections[e] is not None:
                    getattr(blk, e)(sections[e])
        for e in ENGS:
            bld.cnt["S" + e] = cum[e]


class Bld:
    def __init__(self):
        self.nc = bass.Bass("TRN2", target_bir_lowering=False)
        self.es = contextlib.ExitStack()
        self._sems = {}
        self.cnt = {}
        self._uid = 0
        self.lu = {}
        self.dslot = {}

    def uid(self):
        self._uid += 1
        return self._uid

    def sem(self, name):
        if name not in self._sems:
            self._sems[name] = self.es.enter_context(self.nc.semaphore(name))
            self.cnt.setdefault(name, 0)
        return self._sems[name]

    def _stack(self):
        sc = getattr(self, "_scopes", None)
        return sc[-1] if sc else self.es

    def push_scope(self):
        if not hasattr(self, "_scopes"):
            self._scopes = []
        self._scopes.append(self.es.enter_context(contextlib.ExitStack()))

    def pop_scope(self):
        self._scopes.pop().close()
        self.lu = {}
        for a in ("_ctile", "_cmap"):
            if hasattr(self, a):
                delattr(self, a)

    def sbuf(self, name, shape, dt=F32, **kw):
        return self._stack().enter_context(self.nc.sbuf_tensor(f"{name}_{self.uid()}", shape, dt, **kw))

    def psum(self, name, shape, dt=F32):
        return self._stack().enter_context(self.nc.psum_tensor(f"{name}_{self.uid()}", shape, dt))

    def constcol(self, pr, value):
        if not hasattr(self, "_ctile"):
            self._ctile = self.sbuf("consts", [128, 16])
            self._cmap = {}
        if value not in self._cmap:
            col = len(self._cmap)
            assert col < 16
            ap = self._ctile[:, col:col + 1]
            opid = pr.op("gpsimd", (lambda ap, value: lambda e: e.memset(ap, value))(ap, value))
            self._cmap[value] = (ap, opid)
        return self._cmap[value]

    def war(self, key):
        v = self.lu.get(key)
        if v is None:
            return []
        return list(v) if isinstance(v, (list, tuple)) else [v]

    def setlu(self, key, opid):
        self.lu[key] = opid


def ecopy(e, out, in_):
    if hasattr(e, "tensor_copy"):
        return e.tensor_copy(out, in_)
    return e.copy(out, in_)


def dma_load_tm(pr, dram, sb, nchunks=TC, width=D, eng="sync", deps=()):
    src = dram.rearrange("(c p) d -> p c d", p=128)
    dst = sb[:, 0:nchunks * width].rearrange("p (c d) -> p c d", d=width)
    return pr.op(eng, lambda e: e.dma_start(out=dst, in_=src), deps=deps, dma=True)


def ln_chunk_ops(pr, st, col, src_ap, dep_src, eps):
    c = col
    width = src_ap.shape[-1]
    o1 = pr.op("vector", lambda v: v.tensor_reduce(st["s1"][:, c:c + 1], src_ap, AX.X, ALU.add),
               deps=dep_src)
    o2 = pr.op("scalar", lambda s: s.activation(st["junk"][:, 0:width], src_ap, AF.Square,
                                                accum_out=st["s2"][:, c:c + 1]),
               deps=dep_src)
    o3 = pr.op("vector", lambda v: v.tensor_scalar_mul(st["mu"][:, c:c + 1], st["s1"][:, c:c + 1], 1.0 / width),
               deps=[o1])
    o4 = pr.op("vector", lambda v: v.tensor_mul(st["var"][:, c:c + 1], st["mu"][:, c:c + 1], st["mu"][:, c:c + 1]),
               deps=[o3])
    o5 = pr.op("vector", lambda v: v.scalar_tensor_tensor(st["var"][:, c:c + 1], st["s2"][:, c:c + 1],
                                                          1.0 / width, st["var"][:, c:c + 1],
                                                          ALU.mult, ALU.subtract),
               deps=[o2, o4])
    eps_ap, eps_op = pr.bld.constcol(pr, eps)
    o6 = pr.op("scalar", lambda s: s.activation(st["sd"][:, c:c + 1], st["var"][:, c:c + 1], AF.Sqrt, bias=eps_ap),
               deps=[o5, eps_op])
    o7 = pr.op("vector", lambda v: v.reciprocal(st["rstd"][:, c:c + 1], st["sd"][:, c:c + 1]),
               deps=[o6])
    o8 = pr.op("vector", lambda v: v.scalar_tensor_tensor(st["nmr"][:, c:c + 1], st["mu"][:, c:c + 1],
                                                          -1.0, st["rstd"][:, c:c + 1], ALU.mult, ALU.mult),
               deps=[o7])
    return o7, o8


def ln_apply_chunk(pr, st, col, src_ap, dst_ap, dep_stats, dep_dst=()):
    c = col
    return pr.op("scalar", lambda s: s.activation(dst_ap, src_ap, AF.Identity,
                                                  bias=st["nmr"][:, c:c + 1],
                                                  scale=st["rstd"][:, c:c + 1]),
                 deps=list(dep_stats) + list(dep_dst))


def alloc_stats(bld, tag, nchunks=TC, width=D, junk=None):
    if junk is None:
        junk = bld.sbuf(f"{tag}junk", [128, width], BF16)
    return {
        "junk": junk,
        "s1": bld.sbuf(f"{tag}s1", [128, nchunks]),
        "s2": bld.sbuf(f"{tag}s2", [128, nchunks]),
        "mu": bld.sbuf(f"{tag}mu", [128, nchunks]),
        "var": bld.sbuf(f"{tag}var", [128, nchunks]),
        "sd": bld.sbuf(f"{tag}sd", [128, nchunks]),
        "rstd": bld.sbuf(f"{tag}rstd", [128, nchunks]),
        "nmr": bld.sbuf(f"{tag}nmr", [128, nchunks]),
    }


def emit_ln_full(pr, st, src_sb, dst_sb, dep_src, nchunks=TC, width=D, eps=1e-5):
    outs = []
    for c in range(nchunks):
        src = src_sb[:, c * width:(c + 1) * width]
        dst = dst_sb[:, c * width:(c + 1) * width]
        o7, o8 = ln_chunk_ops(pr, st, c, src, dep_src, eps)
        outs.append(ln_apply_chunk(pr, st, c, src, dst, [o7, o8]))
    return outs


def emit_transpose(pr, src_chunk, dst_chunk, tchunks, fchunks, idn, pT, dep_src,
                   dst_off=0, evac_eng="scalar", dep_idn=()):
    bld = pr.bld
    nb = len(pT)
    all_evacs = []
    gi = 0
    for f in range(fchunks):
        for g in range((tchunks + 7) // 8):
            tcs = list(range(g * 8, min(g * 8 + 8, tchunks)))
            p = pT[gi % nb]
            war = bld.war(id(p))

            def mk_t(p=p, tcs=tcs, f=f):
                def fn(t):
                    for j, tc in enumerate(tcs):
                        instr = t.transpose(p[:, j * 128:(j + 1) * 128],
                                            src_chunk(tc)[:, f * 128:(f + 1) * 128], idn[:])
                    return instr
                return fn
            ot = pr.op("tensor", mk_t(), deps=list(dep_src) + list(dep_idn) + war)

            def mk_v(p=p, tcs=tcs, f=f):
                dst = dst_chunk(f)
                a = dst_off + tcs[0] * 128
                b = dst_off + (tcs[-1] + 1) * 128
                n = len(tcs) * 128
                return lambda v: ecopy(v, dst[:, a:b], p[:, 0:n])
            oe = pr.op(evac_eng, mk_v(), deps=[ot])
            bld.setlu(id(p), oe)
            all_evacs.append(oe)
            gi += 1
    return all_evacs


def emit_proj(pr, lhs, rhs, tiles, kchunks, psums, out_cb, dep_ops):
    bld = pr.bld
    nb = len(psums)
    evacs = []
    for i, tile in enumerate(tiles):
        p = psums[i % nb]
        war = bld.war(id(p))

        def mk_t(p=p, tile=tile):
            def fn(t):
                for k in range(kchunks):
                    instr = t.matmul(p[:], lhs(k, tile), rhs(k, tile),
                                     start=(k == 0), stop=(k == kchunks - 1))
                return instr
            return fn
        ot = pr.op("tensor", mk_t(), deps=list(dep_ops) + war)
        eng, fn, extra = out_cb(tile, p, i)
        oe = pr.op(eng, fn, deps=[ot] + list(extra))
        bld.setlu(id(p), oe)
        evacs.append(oe)
    return evacs


def emit_bcast_rows(pr, specs, ones_row, psums, dep_rows):
    bld = pr.bld
    out = []
    i = 0
    for row_ap, dst, width in specs:
        outs = []
        for b in range((width + 511) // 512):
            w = min(512, width - b * 512)
            p = psums[i % len(psums)]
            war = bld.war(id(p))

            def mk_t(p=p, b=b, w=w, row_ap=row_ap):
                return lambda t: t.matmul(p[:], ones_row[:],
                                          row_ap[:, b * 512:b * 512 + w], start=True, stop=True)
            ot = pr.op("tensor", mk_t(), deps=list(dep_rows) + war)

            def mk_v(p=p, b=b, w=w, dst=dst):
                return lambda v: v.tensor_copy(dst[:, b * 512:b * 512 + w], p[:])
            oe = pr.op("vector", mk_v(), deps=[ot])
            bld.setlu(id(p), oe)
            outs.append(oe)
            i += 1
        out.append(outs)
    return out


def emit_row_to_cols(pr, row_ap, dst_cols, ncols, ones_row, pTs, dep, dep_ones=()):
    bld = pr.bld
    p = pTs[0]
    war = bld.war(id(p))

    def mk_t():
        def fn(t):
            for e in range(ncols):
                instr = t.matmul(p[:, e:e + 1], row_ap[:, e * 128:(e + 1) * 128],
                                 ones_row[:, 0:1], start=True, stop=True)
            return instr
        return fn
    ot = pr.op("tensor", mk_t(), deps=list(dep) + list(dep_ones) + war)
    oe = pr.op("vector", lambda v: v.tensor_copy(dst_cols[:, 0:ncols], p[:, 0:ncols]), deps=[ot])
    bld.setlu(id(p), oe)
    return oe


DEBUG = {"on": False, "bufs": []}


def dbg(name, sb, cols):
    if DEBUG["on"]:
        DEBUG["bufs"].append((name, sb, cols))


def body_block3(bld, pr, dram):
    nc = bld.nc
    H, DX = 8, 64
    x_d, wqkv_d, bqkv_d, wfc_d, bfc_d, gam_d, bet_d, idn_d, xo_d = dram

    xch = bld.sbuf("xch", [128, 3 * D])
    xn_tm = bld.sbuf("xn_tm", [128, TC * D], BF16)
    xn_fm = bld.sbuf("xn_fm", [128, DCH * T], BF16)
    q_tm = bld.sbuf("q_tm", [128, TC * 512], BF16)
    k_tm = bld.sbuf("k_tm", [128, TC * 512], BF16)
    v_fm = bld.sbuf("v_fm", [128, 4 * T], BF16)
    o_fm = xn_fm
    wq_sb = bld.sbuf("wq_sb", [128, DCH * 1536], BF16)
    wfc_sb = bld.sbuf("wfc_sb", [128, 4 * D], BF16)
    idn = bld.sbuf("idn", [128, 128], BF16)
    ones_row = bld.sbuf("ones_row", [1, 128], BF16)
    bqk_b = bld.sbuf("bqk_b", [128, 1536], BF16)
    bfc_b = bld.sbuf("bfc_b", [128, D], BF16)
    gam_b = bld.sbuf("gam_b", [128, D], BF16)
    bet_b = bld.sbuf("bet_b", [128, D], BF16)
    rows = bld.sbuf("rows", [1, 1536 + 3 * D], BF16)
    vcol = bld.sbuf("vcol", [128, 4])
    P_bf = bld.sbuf("P_bf", [128, 4 * 64], BF16)
    nmax = bld.sbuf("nmax", [128, 4])
    ssum = bld.sbuf("ssum", [128, 4])
    F_tm = bld.sbuf("F_tm", [128, 2 * D])
    G_tm = bld.sbuf("G_tm", [128, 2 * D])
    junk = bld.sbuf("junk", [128, D], BF16)
    st1 = alloc_stats(bld, "b3ln1", junk=junk)
    st2 = alloc_stats(bld, "b3ln2", junk=junk)
    st3 = alloc_stats(bld, "b3ln3", junk=junk)

    pA = [bld.psum("pA", [128, 512]) for _ in range(2)]
    pT = [bld.psum("pT", [128, 1024], BF16) for _ in range(2)]
    pS = [bld.psum("pS", [128, 64]) for _ in range(2)]
    pO = [bld.psum("pO", [64, 512]) for _ in range(2)]

    x_src = x_d.rearrange("(c p) d -> p c d", p=128)
    xo_dst = xo_d.rearrange("(c p) d -> p c d", p=128)
    ld_wq = pr.op("gpsimd", lambda e: e.dma_start(
        out=wq_sb[:].rearrange("p (c e) -> p c e", e=1536),
        in_=wqkv_d.rearrange("(c p) e -> p c e", p=128)), dma=True)
    ld_wf = pr.op("gpsimd", lambda e: e.dma_start(
        out=wfc_sb[:].rearrange("p (c e) -> p c e", e=D),
        in_=wfc_d.rearrange("(c p) e -> p c e", p=128)), dma=True)
    ld_r1 = pr.op("gpsimd", lambda e: e.dma_start(out=rows[:, 0:1536], in_=bqkv_d[None, :]), dma=True)
    ld_r2 = pr.op("gpsimd", lambda e: e.dma_start(out=rows[:, 1536:1536 + D], in_=bfc_d[None, :]), dma=True)
    ld_r3 = pr.op("gpsimd", lambda e: e.dma_start(out=rows[:, 1536 + D:1536 + 2 * D], in_=gam_d[None, :]), dma=True)
    ld_r4 = pr.op("gpsimd", lambda e: e.dma_start(out=rows[:, 1536 + 2 * D:1536 + 3 * D], in_=bet_d[None, :]), dma=True)
    ld_id = pr.op("gpsimd", lambda e: e.dma_start(out=idn[:], in_=idn_d), dma=True)
    o_ones = pr.op("gpsimd", lambda e: e.memset(ones_row[:], 1.0))
    ld_rows = [ld_r1, ld_r2, ld_r3, ld_r4, o_ones]

    bcs = emit_bcast_rows(pr, [
        (rows[:, 0:1536], bqk_b, 1536),
        (rows[:, 1536:1536 + D], bfc_b, D),
        (rows[:, 1536 + D:1536 + 2 * D], gam_b, D),
        (rows[:, 1536 + 2 * D:1536 + 3 * D], bet_b, D),
    ], ones_row, pA, ld_rows)
    bc_bqk, bc_bfc, bc_gam, bc_bet = bcs

    o_vcol = emit_row_to_cols(pr, rows[:, 1024:1536], vcol, 4, ones_row, pA, [ld_r1], dep_ones=[o_ones])

    ln1 = []
    for c in range(TC):
        sl = c % 3
        xs = xch[:, sl * D:(sl + 1) * D]
        ld = pr.op("sync", (lambda c, xs: lambda e: e.dma_start(out=xs, in_=x_src[:, c, :]))(c, xs),
                   deps=bld.war(("xch", sl)), dma=True)
        o7, o8 = ln_chunk_ops(pr, st1, c, xs, [ld], 1e-5)
        oap = ln_apply_chunk(pr, st1, c, xs, xn_tm[:, c * D:(c + 1) * D], [o7, o8])
        bld.setlu(("xch", sl), [oap, ln1[-1] if False else o7])
        bld.setlu(("xch", sl), [oap, o7])
        ln1.append(oap)

    tr1 = emit_transpose(pr, lambda c: xn_tm[:, c * D:(c + 1) * D],
                         lambda f: xn_fm[:, f * T:(f + 1) * T], TC, DCH, idn, pT,
                         dep_src=ln1, dep_idn=[ld_id])

    def lhs1(k, m):
        return xn_fm[:, k * T + m * 128: k * T + (m + 1) * 128]

    def qk_cb(dst, bias0, bc):
        def cb(tile, p, i):
            m = tile

            def fn(v):
                return v.tensor_add(dst[:, m * 512:(m + 1) * 512], p[:],
                                    bqk_b[:, bias0:bias0 + 512])
            return ("vector", fn, bc)
        return cb

    pq = emit_proj(pr, lhs1, lambda k, m: wq_sb[:, k * 1536:k * 1536 + 512],
                   list(range(TC)), DCH, pA, qk_cb(q_tm, 0, bc_bqk),
                   dep_ops=tr1 + [ld_wq])
    pk = emit_proj(pr, lhs1, lambda k, m: wq_sb[:, k * 1536 + 512:k * 1536 + 1024],
                   list(range(TC)), DCH, pA, qk_cb(k_tm, 512, bc_bqk),
                   dep_ops=tr1 + [ld_wq])

    def v_cb(tile, p, i):
        e, n = tile

        def fn(v):
            return v.tensor_scalar_add(v_fm[:, e * T + n * 512: e * T + n * 512 + 512],
                                       p[:], vcol[:, e:e + 1])
        return ("vector", fn, [o_vcol])

    pv = emit_proj(pr,
                   lambda k, tile: wq_sb[:, k * 1536 + 1024 + tile[0] * 128: k * 1536 + 1024 + (tile[0] + 1) * 128],
                   lambda k, tile: xn_fm[:, k * T + tile[1] * 512: k * T + (tile[1] + 1) * 512],
                   [(e, n) for e in range(4) for n in range(4)], DCH, pA, v_cb,
                   dep_ops=tr1 + [ld_wq])

    att = []
    for c in range(4):
        p = pS[c % 2]
        war = bld.war(id(p))

        def mk_t(c=c, p=p):
            def fn(t):
                for half in range(2):
                    h = 2 * c + half
                    r0 = half * 64
                    for m in range(TC):
                        instr = t.matmul(p[r0:r0 + 64, :],
                                         q_tm[:, m * 512 + h * 64: m * 512 + (h + 1) * 64],
                                         k_tm[:, m * 512 + h * 64: m * 512 + (h + 1) * 64],
                                         start=(m == 0), stop=(m == TC - 1))
                return instr
            return fn
        ot = pr.op("tensor", mk_t(), deps=pq + pk + war)
        ov1 = pr.op("vector", (lambda c, p: lambda v: v.tensor_reduce(
            nmax[0:64, c:c + 1], p[0:64, :], AX.X, ALU.max, negate=True))(c, p), deps=[ot])
        ov1b = pr.op("vector", (lambda c, p: lambda v: v.tensor_reduce(
            nmax[64:128, c:c + 1], p[64:128, :], AX.X, ALU.max, negate=True))(c, p), deps=[ot])
        ov2 = pr.op("vector", (lambda c: lambda v: v.tensor_scalar_mul(
            nmax[:, c:c + 1], nmax[:, c:c + 1], 0.125))(c), deps=[ov1, ov1b])
        os_ = pr.op("scalar", (lambda c, p: lambda s: s.activation(
            P_bf[:, c * 64:(c + 1) * 64], p[:], AF.Exp,
            bias=nmax[:, c:c + 1], scale=0.125, accum_out=ssum[:, c:c + 1]))(c, p),
            deps=[ov2])
        bld.setlu(id(p), os_)
        att.append(os_)

    orcp = pr.op("vector", lambda v: v.reciprocal(ssum[:, 0:4], ssum[:, 0:4]), deps=att)
    onrm = []
    for c in range(4):
        onrm.append(pr.op("vector", (lambda c: lambda v: v.tensor_scalar_mul(
            P_bf[:, c * 64:(c + 1) * 64], P_bf[:, c * 64:(c + 1) * 64],
            ssum[:, c:c + 1]))(c), deps=[orcp]))

    def av_cb(tile, p, i):
        h, n = tile
        fc, r0 = h // 2, (h % 2) * 64

        def fn(v):
            return v.tensor_copy(o_fm[r0:r0 + 64, fc * T + n * 512: fc * T + (n + 1) * 512],
                                 p[:])
        return ("vector", fn, [])

    pav = emit_proj(pr,
                    lambda k, tile: P_bf[(tile[0] % 2) * 64:(tile[0] % 2) * 64 + 64,
                                         (tile[0] // 2) * 64:(tile[0] // 2 + 1) * 64],
                    lambda k, tile: v_fm[(tile[0] % 2) * 64:(tile[0] % 2) * 64 + 64,
                                         (tile[0] // 2) * T + tile[1] * 512:(tile[0] // 2) * T + (tile[1] + 1) * 512],
                    [(h, n) for h in range(8) for n in range(4)], 1, pO, av_cb,
                    dep_ops=onrm + pv)

    stores = []
    for m in range(TC):
        slot = (m % 2) * D
        F = F_tm[:, slot:slot + D]
        G = G_tm[:, slot:slot + D]
        ev = []
        for n in range(2):
            i2 = m * 2 + n
            p = pA[i2 % 2]
            war = bld.war(id(p))

            def mk_t(m=m, n=n, p=p):
                def fn(t):
                    for k in range(4):
                        instr = t.matmul(p[:], o_fm[:, k * T + m * 128:k * T + (m + 1) * 128],
                                         wfc_sb[:, k * D + n * 512:k * D + (n + 1) * 512],
                                         start=(k == 0), stop=(k == 3))
                    return instr
                return fn
            ot = pr.op("tensor", mk_t(), deps=pav + [ld_wf] + war)

            sl = F_tm[:, slot + n * 512: slot + (n + 1) * 512]
            oe1 = pr.op("vector", (lambda sl, p, n: lambda v: v.tensor_add(
                sl, p[:], bfc_b[:, n * 512:(n + 1) * 512]))(sl, p, n),
                deps=[ot] + bc_bfc + bld.war(("F", m % 2)))
            oe = pr.op("vector", (lambda sl, m, n: lambda v: v.tensor_add(
                sl, sl, xn_tm[:, m * D + n * 512: m * D + (n + 1) * 512]))(sl, m, n),
                deps=[oe1])
            bld.setlu(id(p), oe)
            ev.append(oe)
        o7, o8 = ln_chunk_ops(pr, st2, m, F, ev, 1e-6)
        oap = ln_apply_chunk(pr, st2, m, F, G, [o7, o8], dep_dst=bld.war(("G", m % 2)))
        og1 = pr.op("vector", (lambda G: lambda v: v.tensor_mul(G, G, gam_b[:, 0:D]))(G),
                    deps=[oap] + bc_gam)
        og2 = pr.op("vector", (lambda G: lambda v: v.tensor_add(G, G, bet_b[:, 0:D]))(G),
                    deps=[og1] + bc_bet)
        p7, p8 = ln_chunk_ops(pr, st3, m, G, [og2], 1e-5)
        o_t = ln_apply_chunk(pr, st3, m, G, F, [p7, p8], dep_dst=ev)
        sl2 = m % 3
        xs2 = xch[:, sl2 * D:(sl2 + 1) * D]
        ld2 = pr.op("sync", (lambda m, xs2: lambda e: e.dma_start(out=xs2, in_=x_src[:, m, :]))(m, xs2),
                    deps=bld.war(("xch", sl2)), dma=True)
        o_add = pr.op("vector", (lambda F, xs2: lambda v: v.tensor_add(
            F, F, xs2))(F, xs2), deps=[o_t, ld2])
        bld.setlu(("xch", sl2), o_add)
        bld.setlu(("G", m % 2), o_add)

        ost = pr.op("sync", (lambda m, F: lambda e: e.dma_start(out=xo_dst[:, m, :], in_=F))(m, F),
                    deps=[o_add], dma=True)
        bld.setlu(("F", m % 2), ost)
        stores.append(ost)

    dbg("st1rstd", st1["rstd"], TC)
    dbg("st1nmr", st1["nmr"], TC)
    dbg("st1s1", st1["s1"], TC)
    dbg("st1s2", st1["s2"], TC)
    dbg("st2rstd", st2["rstd"], TC)
    dbg("st3rstd", st3["rstd"], TC)
    dbg("F_tm", F_tm, 2 * D)
    dbg("G_tm", G_tm, 2 * D)
    dbg("xn_tm", xn_tm, TC * D)
    dbg("xn_fm", xn_fm, DCH * T)
    dbg("q_tm", q_tm, TC * 512)
    dbg("k_tm", k_tm, TC * 512)
    dbg("v_fm", v_fm, 4 * T)
    dbg("P_bf", P_bf, 4 * 64)
    dbg("nmax", nmax, 4)
    dbg("ssum", ssum, 4)
    dbg("vcol", vcol, 4)
    for name, sb, cols in DEBUG["bufs"]:
        d = bld.nc.dram_tensor("dbg_" + name, [128, cols], sb.dtype if hasattr(sb, "dtype") else F32,
                               kind="ExternalOutput").ap()
        pr.op("sync", (lambda d, sb, cols: lambda e: e.dma_start(out=d, in_=sb[:, 0:cols]))(d, sb, cols),
              deps=stores, dma=True)
    pr.op("sync", lambda e: e.nop(), deps=stores)


def build_block3():
    bld = Bld()
    nc = bld.nc
    dram = (
        nc.dram_tensor("x", [T, D], F32, kind="ExternalInput").ap(),
        nc.dram_tensor("w_qkv", [D, 1536], F32, kind="ExternalInput").ap(),
        nc.dram_tensor("b_qkv", [1536], F32, kind="ExternalInput").ap(),
        nc.dram_tensor("w_fc", [512, D], F32, kind="ExternalInput").ap(),
        nc.dram_tensor("b_fc", [D], F32, kind="ExternalInput").ap(),
        nc.dram_tensor("gamma", [D], F32, kind="ExternalInput").ap(),
        nc.dram_tensor("beta", [D], F32, kind="ExternalInput").ap(),
        nc.dram_tensor("iden", [128, 128], F32, kind="ExternalInput").ap(),
        nc.dram_tensor("xo", [T, D], F32, kind="ExternalOutput").ap(),
    )
    with bld.es:
        pr = Prog(bld, "b3")
        body_block3(bld, pr, dram)
        pr.run()
    return bld.nc




def emit_sandwich_tail(bld, pr, get_F, st, xch, x_src, xo_dst, eps=1e-5, nsl=3):
    stores = []
    for m in range(TC):
        F, depF = get_F(m)
        tmp = F
        o7, o8 = ln_chunk_ops(pr, st, m, F, depF, eps)
        o_t = ln_apply_chunk(pr, st, m, F, tmp, [o7, o8])
        sl2 = m % nsl
        xs2 = xch[:, sl2 * D:(sl2 + 1) * D]
        ld2 = pr.op("sync", (lambda m, xs2: lambda e: e.dma_start(out=xs2, in_=x_src[:, m, :]))(m, xs2),
                    deps=bld.war(("xch", sl2)), dma=True)
        o_add = pr.op("vector", (lambda tmp, xs2: lambda v: v.tensor_add(tmp, tmp, xs2))(tmp, xs2),
                      deps=[o_t, ld2])
        bld.setlu(("xch", sl2), o_add)
        ost = pr.op("sync", (lambda m, tmp: lambda e: e.dma_start(out=xo_dst[:, m, :], in_=tmp))(m, tmp),
                    deps=[o_add], dma=True)
        bld.setlu(("swtmp", m % 2), ost)
        stores.append(ost)
    return stores


def emit_sandwich_ln1(bld, pr, x_src, xch, xn_tm, st1, eps=1e-5, nsl=3):
    ln1 = []
    for c in range(TC):
        sl = c % nsl
        xs = xch[:, sl * D:(sl + 1) * D]
        ld = pr.op("sync", (lambda c, xs: lambda e: e.dma_start(out=xs, in_=x_src[:, c, :]))(c, xs),
                   deps=bld.war(("xch", sl)), dma=True)
        o7, o8 = ln_chunk_ops(pr, st1, c, xs, [ld], eps)
        oap = ln_apply_chunk(pr, st1, c, xs, xn_tm[:, c * D:(c + 1) * D], [o7, o8])
        bld.setlu(("xch", sl), [oap, o7])
        ln1.append(oap)
    return ln1


def body_block4(bld, pr, dram):
    nc = bld.nc
    NE, MI, SI = 16, 512, 2048
    x_d, gw_d, be_d, up_d, dn_d, su_d, sd_d, idn_d, xo_d = dram
    x_src = x_d.rearrange("(c p) d -> p c d", p=128)
    xo_dst = xo_d.rearrange("(c p) d -> p c d", p=128)

    xch = bld.sbuf("xch", [128, 3 * D])
    xn_tm = bld.sbuf("xn_tm", [128, TC * D], BF16)
    xn_fm = bld.sbuf("xn_fm", [128, DCH * T], BF16)
    acc = bld.sbuf("acc", [128, TC * D])
    stage_ = [xn_tm[:, 8192:8192 + 4096], xn_tm[:, 12288:12288 + 4096]]
    sustage = bld.sbuf("sustage", [128, 1024], BF16)
    rb_buf = bld.sbuf("rb_buf", [128, 1024], BF16)
    wuT = bld.sbuf("wuT", [128, DCH * 512], BF16)
    wdT = bld.sbuf("wdT", [128, 4 * D], BF16)
    sdT = bld.sbuf("sdT", [128, 16 * D], BF16)
    suTe = bld.sbuf("suTe", [128, DCH * 128], BF16)
    gw_sb = bld.sbuf("gw_sb", [16, D], BF16)
    gwT = bld.sbuf("gwT", [128, DCH * 16], BF16)
    be_row = bld.sbuf("be_row", [1, 16], BF16)
    beb = bld.sbuf("beb", [128, 16])
    idn = bld.sbuf("idn", [128, 128], BF16)
    ones_row = bld.sbuf("ones_row", [1, 128], BF16)
    scr = bld.sbuf("scr", [128, TC * 16])
    comb = bld.sbuf("comb", [128, TC * 16])
    rt = {k: bld.sbuf("rt" + k, [128, w]) for k, w in
          [("sc", 16), ("prs", 24), ("g2", 4), ("pmin", 6), ("tg", 1), ("gm", 4),
           ("msk", 16), ("top8", 8), ("wsel", 16), ("wsum", 1)]}
    junk = bld.sbuf("junk", [128, D], BF16)
    st1 = alloc_stats(bld, "b4ln1", junk=junk)
    st2 = alloc_stats(bld, "b4ln2", junk=junk)
    he = xn_tm
    hs = xn_tm

    pA = [bld.psum("pA", [128, 512]) for _ in range(3)]
    pT = [bld.psum("pT", [128, 1024], BF16) for _ in range(2)]
    pR = [bld.psum("pR", [128, 16]) for _ in range(2)]

    ld_id = pr.op("gpsimd", lambda e: e.dma_start(out=idn[:], in_=idn_d), dma=True)
    o_ones = pr.op("gpsimd", lambda e: e.memset(ones_row[:], 1.0))
    ld_gw = pr.op("gpsimd", lambda e: e.dma_start(out=gw_sb[:], in_=gw_d), dma=True)
    ld_be = pr.op("gpsimd", lambda e: e.dma_start(out=be_row[:], in_=be_d[None, :]), dma=True)

    gw_t = []
    for k in range(DCH):
        def mk_t(k=k):
            return lambda t: t.transpose(pT[0][:, k * 16:(k + 1) * 16],
                                         gw_sb[0:16, k * 128:(k + 1) * 128], idn[0:16, 0:16])
        gw_t.append(pr.op("tensor", mk_t(), deps=[ld_gw, ld_id] + bld.war(id(pT[0]))))
    o_gwT = pr.op("vector", lambda v: v.tensor_copy(gwT[:, 0:DCH * 16].rearrange("p (k e) -> p k e", e=16),
                                                    pT[0][:, 0:DCH * 16].rearrange("p (k e) -> p k e", e=16)),
                  deps=gw_t)
    bld.setlu(id(pT[0]), o_gwT)
    o_beb_t = pr.op("tensor", lambda t: t.matmul(pR[0][:], ones_row[:], be_row[:],
                                                 start=True, stop=True),
                    deps=[ld_be, o_ones] + bld.war(id(pR[0])))
    o_beb = pr.op("vector", lambda v: v.tensor_copy(beb[:], pR[0][:]), deps=[o_beb_t])
    bld.setlu(id(pR[0]), o_beb)

    ln1 = emit_sandwich_ln1(bld, pr, x_src, xch, xn_tm, st1)

    tr1 = emit_transpose(pr, lambda c: xn_tm[:, c * D:(c + 1) * D],
                         lambda f: xn_fm[:, f * T:(f + 1) * T], TC, DCH, idn, pT,
                         dep_src=ln1, dep_idn=[ld_id])
    for key in (("stage", 0), ("stage", 1)):
        bld.setlu(key, tr1[-1])
    for par in (0, 1):
        for j in range(8):
            bld.setlu(("stg", par, j), tr1[-1])

    comb_ops = []
    for m in range(TC):
        p = pR[m % 2]
        war = bld.war(id(p))

        def mk_t(m=m, p=p):
            def fn(t):
                for k in range(DCH):
                    instr = t.matmul(p[:], xn_fm[:, k * T + m * 128:k * T + (m + 1) * 128],
                                     gwT[:, k * 16:(k + 1) * 16],
                                     start=(k == 0), stop=(k == DCH - 1))
                return instr
            return fn
        ot = pr.op("tensor", mk_t(), deps=tr1 + [o_gwT] + war)
        scm = scr[:, m * 16:(m + 1) * 16]
        o_sig = pr.op("scalar", (lambda p, scm: lambda s: s.activation(scm, p[:], AF.Sigmoid))(p, scm),
                      deps=[ot])
        bld.setlu(id(p), o_sig)
        o_sc = pr.op("vector", (lambda scm: lambda v: v.tensor_add(rt["sc"][:], scm, beb[:]))(scm),
                     deps=[o_sig, o_beb] + bld.war("rtbufs"))
        sc3 = rt["sc"][:].rearrange("p (g i) -> p g i", i=4)
        prs3 = rt["prs"][:].rearrange("p (g j) -> p g j", j=6)
        o_pairs = []
        for j, (i1, i2) in enumerate([(0, 1), (0, 2), (0, 3), (1, 2), (1, 3), (2, 3)]):
            o_pairs.append(pr.op("vector", (lambda j, i1, i2, sc3, prs3: lambda v: v.tensor_add(
                prs3[:, :, j:j + 1], sc3[:, :, i1:i1 + 1], sc3[:, :, i2:i2 + 1]))(j, i1, i2, sc3, prs3),
                deps=[o_sc]))
        o_g2 = pr.op("vector", (lambda prs3: lambda v: v.tensor_reduce(
            rt["g2"][:], prs3, AX.X, ALU.max))(prs3), deps=o_pairs)
        o_pm = []
        for j, (i1, i2) in enumerate([(0, 1), (0, 2), (0, 3), (1, 2), (1, 3), (2, 3)]):
            o_pm.append(pr.op("vector", (lambda j, i1, i2: lambda v: v.tensor_tensor(
                rt["pmin"][:, j:j + 1], rt["g2"][:, i1:i1 + 1], rt["g2"][:, i2:i2 + 1],
                ALU.min))(j, i1, i2), deps=[o_g2]))
        o_tg = pr.op("vector", lambda v: v.tensor_reduce(rt["tg"][:], rt["pmin"][:], AX.X, ALU.max),
                     deps=o_pm)
        o_gm = pr.op("vector", lambda v: v.tensor_single_scalar(rt["gm"][:], rt["g2"][:],
                                                                rt["tg"][:], ALU.is_ge),
                     deps=[o_tg])
        o_msk = []
        for g in range(4):
            o_msk.append(pr.op("vector", (lambda g: lambda v: v.tensor_single_scalar(
                rt["msk"][:, 4 * g:4 * g + 4], rt["sc"][:, 4 * g:4 * g + 4],
                rt["gm"][:, g:g + 1], ALU.mult))(g), deps=[o_gm, o_sc]))
        o_t8 = pr.op("vector", lambda v: v.max(rt["top8"][:], rt["msk"][:]), deps=o_msk, special=True)
        o_sel = pr.op("vector", lambda v: v.tensor_single_scalar(rt["wsel"][:], rt["msk"][:],
                                                                 rt["top8"][:, 3:4], ALU.is_ge),
                      deps=[o_t8])
        o_ws = pr.op("vector", (lambda scm: lambda v: v.tensor_mul(rt["wsel"][:], rt["wsel"][:], scm))(scm),
                     deps=[o_sel])
        o_sum = pr.op("vector", lambda v: v.tensor_reduce(rt["wsum"][:], rt["wsel"][:], AX.X, ALU.add),
                      deps=[o_ws])
        o_eps = pr.op("vector", lambda v: v.tensor_scalar_add(rt["wsum"][:], rt["wsum"][:], 1e-20),
                      deps=[o_sum])
        o_rcp = pr.op("vector", lambda v: v.reciprocal(rt["wsum"][:], rt["wsum"][:]), deps=[o_eps])
        o_r2 = pr.op("vector", lambda v: v.tensor_scalar_mul(rt["wsum"][:], rt["wsum"][:], 2.5),
                     deps=[o_rcp])
        cbm = comb[:, m * 16:(m + 1) * 16]
        o_cb = pr.op("vector", (lambda cbm: lambda v: v.tensor_single_scalar(
            cbm, rt["wsel"][:], rt["wsum"][:], ALU.mult))(cbm), deps=[o_r2])
        bld.setlu("rtbufs", o_cb)
        comb_ops.append(o_cb)

    wuT4 = wuT[:].rearrange("p (dc c) -> p dc c", c=512)
    wdT4 = wdT[:].rearrange("p (ic c) -> p ic c", c=D)
    for e in range(NE):
        stage = stage_[e % 2]
        par = e % 2
        twu_g = {}
        for rc in range(4):
            ld = pr.op("gpsimd", (lambda e, stage, rc: lambda g: g.dma_start(
                out=stage[:, rc * D:(rc + 1) * D],
                in_=up_d[e][rc * 128:(rc + 1) * 128, :]))(e, stage, rc),
                deps=bld.war(("stg", par, 2 * rc)) + bld.war(("stg", par, 2 * rc + 1)), dma=True)
            p = pT[rc % 2]
            warp = bld.war(id(p))

            def mk_tt(stage=stage, rc=rc, p=p):
                def fn(t):
                    for f in range(DCH):
                        instr = t.transpose(p[:, f * 128:(f + 1) * 128],
                                            stage[:, rc * D + f * 128:rc * D + (f + 1) * 128],
                                            idn[:])
                    return instr
                return fn
            ott = pr.op("tensor", mk_tt(), deps=[ld, ld_id] + warp + bld.war(("wuTr", rc)))
            bld.setlu(("stg", par, 2 * rc), ott)
            bld.setlu(("stg", par, 2 * rc + 1), ott)

            def mk_tv(p=p, rc=rc):
                p3 = p[:].rearrange("p (k c) -> p k c", c=128)[:, 0:DCH, :]
                dst = wuT4[:, 0:DCH, rc * 128:(rc + 1) * 128]
                return lambda v: ecopy(v, dst, p3)
            oev = pr.op("scalar", mk_tv(), deps=[ott])
            bld.setlu(id(p), oev)
            twu_g[rc] = oev
        up_ev = []
        wuTr = {rc: [] for rc in range(4)}
        for ec in range(4):
            for tb in range(4):
                p = pA[(ec * 4 + tb) % 3]
                war = bld.war(id(p))

                def mk_t(ec=ec, tb=tb, p=p):
                    def fn(t):
                        for k in range(DCH):
                            instr = t.matmul(p[:], wuT[:, k * 512 + ec * 128:k * 512 + (ec + 1) * 128],
                                             xn_fm[:, k * T + tb * 512:k * T + (tb + 1) * 512],
                                             start=(k == 0), stop=(k == DCH - 1))
                        return instr
                    return fn
                ot = pr.op("tensor", mk_t(), deps=[twu_g[ec]] + war + bld.war(("he", ec, tb)))
                wuTr[ec].append(ot)
                rb = rb_buf[:, (tb % 2) * 512:(tb % 2) * 512 + 512]
                o_rl = pr.op("scalar", (lambda p, rb: lambda s: s.activation(rb, p[:], AF.Relu))(p, rb),
                             deps=[ot] + bld.war(("rb", tb % 2)))
                hesl = he[:, ec * T + tb * 512:ec * T + (tb + 1) * 512]
                o_he = pr.op("vector", (lambda hesl, rb, p: lambda v: v.tensor_mul(hesl, rb, p[:]))(hesl, rb, p),
                             deps=[o_rl] + ([tr1[-1]] if e == 0 else []))
                bld.setlu(id(p), o_he)
                bld.setlu(("rb", tb % 2), o_he)
                up_ev.append(o_he)
                bld.setlu(("heC", ec, tb), o_he)
        for rc in range(4):
            bld.setlu(("wuTr", rc), wuTr[rc])
        twd_g = {}
        for rc in range(8):
            ld = pr.op("gpsimd", (lambda e, stage, rc: lambda g: g.dma_start(
                out=stage[:, rc * 512:(rc + 1) * 512],
                in_=dn_d[e][rc * 128:(rc + 1) * 128, :]))(e, stage, rc),
                deps=bld.war(("stg", par, rc)), dma=True)
            p = pT[rc % 2]
            warp = bld.war(id(p))

            def mk_tt(stage=stage, rc=rc, p=p):
                def fn(t):
                    for ic in range(4):
                        instr = t.transpose(p[:, ic * 128:(ic + 1) * 128],
                                            stage[:, rc * 512 + ic * 128:rc * 512 + (ic + 1) * 128],
                                            idn[:])
                    return instr
                return fn
            ott = pr.op("tensor", mk_tt(), deps=[ld, ld_id] + warp + bld.war(("wdTr", rc // 4)))
            bld.setlu(("stg", par, rc), ott)

            def mk_tv(p=p, rc=rc):
                p3 = p[:].rearrange("p (k c) -> p k c", c=128)[:, 0:4, :]
                dst = wdT4[:, 0:4, rc * 128:(rc + 1) * 128]
                return lambda v: ecopy(v, dst, p3)
            oev = pr.op("scalar", mk_tv(), deps=[ott])
            bld.setlu(id(p), oev)
            twd_g[rc] = oev
        dn_ev = []
        wdTr = {0: [], 1: []}
        for m in range(TC):
            for eb in range(2):
                p = pA[(m * 2 + eb) % 3]
                war = bld.war(id(p))
                tb = m // 4

                def mk_t(m=m, eb=eb, p=p, tb=tb):
                    def fn(t):
                        for k in range(4):
                            instr = t.matmul(p[:], he[:, k * T + m * 128:k * T + (m + 1) * 128],
                                             wdT[:, k * D + eb * 512:k * D + (eb + 1) * 512],
                                             start=(k == 0), stop=(k == 3))
                        return instr
                    return fn
                hed = [bld.lu.get(("heC", kk, tb)) for kk in range(4)]
                ot = pr.op("tensor", mk_t(),
                           deps=[twd_g[rc] for rc in range(4 * eb, 4 * eb + 4)] + war +
                           [d for d in hed if d is not None])
                wdTr[eb].append(ot)
                accsl = acc[:, m * D + eb * 512:m * D + (eb + 1) * 512]
                if e == 0:
                    o_ac = pr.op("vector", (lambda accsl, p, m, e: lambda v: v.tensor_single_scalar(
                        accsl, p[:], comb[:, m * 16 + e:m * 16 + e + 1], ALU.mult))(accsl, p, m, e),
                        deps=[ot, comb_ops[m]])
                else:
                    o_ac = pr.op("vector", (lambda accsl, p, m, e: lambda v: v.scalar_tensor_tensor(
                        accsl, p[:], comb[:, m * 16 + e:m * 16 + e + 1], accsl,
                        ALU.mult, ALU.add))(accsl, p, m, e),
                        deps=[ot, comb_ops[m]] + bld.war(("acc", m, eb)))
                bld.setlu(id(p), o_ac)
                bld.setlu(("acc", m, eb), o_ac)
                dn_ev.append(o_ac)
        for eb in range(2):
            bld.setlu(("wdTr", eb), wdTr[eb])
        bld.setlu(("stage", par), [bld.lu[("stg", par, j)] for j in range(8)])
        for ec in range(4):
            for tb in range(4):
                bld.setlu(("he", ec, tb), dn_ev[-1])

    for ec in range(16):
        bld.setlu(("hs", ec), dn_ev[-1])

    for piece in range(4):
        sstage = stage_[piece % 2]
        ld_sd = pr.op("gpsimd", (lambda piece, sstage: lambda g: g.dma_start(
            out=sstage.rearrange("p (r c) -> p r c", c=SI),
            in_=sd_d[piece * 256:(piece + 1) * 256, :].rearrange("(r p) c -> p r c", p=128)))(piece, sstage),
            deps=bld.war(("stage", piece % 2)), dma=True)
        tsd = emit_transpose(pr, (lambda sstage: lambda rc: sstage[:, rc * SI:(rc + 1) * SI])(sstage),
                             lambda ic: sdT[:, ic * D:(ic + 1) * D], 2, 16, idn, pT,
                             dep_src=[ld_sd], dep_idn=[ld_id], dst_off=piece * 256)
        bld.setlu(("stage", piece % 2), tsd[-1])
    sdT_done = tsd[-1]

    sh_ev = {}
    for tb in range(4):
        hs_ops = []
        for ec in range(16):
            ld_su = pr.op("gpsimd", (lambda ec: lambda g: g.dma_start(
                out=sustage[:],
                in_=su_d[ec * 128:(ec + 1) * 128, :]))(ec),
                deps=bld.war("sustage"), dma=True)
            tsu = emit_transpose(pr, lambda rc: sustage[:, 0:D],
                                 lambda dc: suTe[:, dc * 128:(dc + 1) * 128], 1, DCH, idn, pT,
                                 dep_src=[ld_su] + bld.war("suTe"), dep_idn=[ld_id])
            bld.setlu("sustage", tsu[-1])
            p = pA[ec % 2]
            war = bld.war(id(p))

            def mk_t(ec=ec, tb=tb, p=p):
                def fn(t):
                    for k in range(DCH):
                        instr = t.matmul(p[:], suTe[:, k * 128:(k + 1) * 128],
                                         xn_fm[:, k * T + tb * 512:k * T + (tb + 1) * 512],
                                         start=(k == 0), stop=(k == DCH - 1))
                    return instr
                return fn
            ot = pr.op("tensor", mk_t(), deps=tsu + war + bld.war(("hs", ec)))
            bld.setlu("suTe", ot)
            rb = rb_buf[:, (ec % 2) * 512:(ec % 2) * 512 + 512]
            o_rl = pr.op("scalar", (lambda p, rb: lambda s: s.activation(rb, p[:], AF.Relu))(p, rb),
                         deps=[ot] + bld.war(("rb", ec % 2)))
            hssl = hs[:, ec * 512:(ec + 1) * 512]
            o_hs = pr.op("vector", (lambda hssl, rb, p: lambda v: v.tensor_mul(hssl, rb, p[:]))(hssl, rb, p),
                         deps=[o_rl])
            bld.setlu(id(p), o_hs)
            bld.setlu(("rb", ec % 2), o_hs)
            hs_ops.append(o_hs)
        for mm in range(4):
            m = tb * 4 + mm
            for eb in range(2):
                p = pA[(mm * 2 + eb) % 3]
                war = bld.war(id(p))

                def mk_t(m=m, mm=mm, eb=eb, p=p):
                    def fn(t):
                        for k in range(16):
                            instr = t.matmul(p[:], hs[:, k * 512 + mm * 128:k * 512 + (mm + 1) * 128],
                                             sdT[:, k * D + eb * 512:k * D + (eb + 1) * 512],
                                             start=(k == 0), stop=(k == 15))
                        return instr
                    return fn
                ot = pr.op("tensor", mk_t(), deps=hs_ops + [sdT_done] + war)
                accsl = acc[:, m * D + eb * 512:m * D + (eb + 1) * 512]
                o_ac = pr.op("vector", (lambda accsl, p: lambda v: v.tensor_add(
                    accsl, accsl, p[:]))(accsl, p), deps=[ot] + bld.war(("acc", m, eb)))
                bld.setlu(id(p), o_ac)
                bld.setlu(("acc", m, eb), o_ac)
                sh_ev[(m, eb)] = o_ac
        for ec in range(16):
            bld.setlu(("hs", ec), sh_ev[(tb * 4 + 3, 1)])

    def get_F(m):
        F = acc[:, m * D:(m + 1) * D]
        return F, [sh_ev[(m, 0)], sh_ev[(m, 1)]]

    if DEBUG.get("notail"):
        stores = []
        alldeps = [sh_ev[(m, eb)] for m in range(TC) for eb in range(2)]
        for name, sb, cols in [("acc", acc, TC * D), ("xn_fm", xn_fm, DCH * T),
                               ("scr", scr, TC * 16), ("comb", comb, TC * 16),
                               ("wuT", wuT, DCH * 512), ("wdT", wdT, 4 * D),
                               ("sdT", sdT, 16 * D), ("henow", xn_tm, TC * D)] + [
                               ("rt" + k, rt[k], rt[k].shape[1]) for k in rt]:
            dt_out = BF16 if sb.dtype == BF16 else F32
            d = bld.nc.dram_tensor("dbg_" + name, [128, cols], dt_out,
                                   kind="ExternalOutput").ap()
            stores.append(pr.op("sync", (lambda d, sb, cols: lambda e: e.dma_start(
                out=d, in_=sb[:, 0:cols]))(d, sb, cols), deps=alldeps, dma=True))
        z = pr.op("vector", lambda v: v.memset(xch[:, 0:D], 0.0), deps=alldeps)
        for m in range(TC):
            stores.append(pr.op("sync", (lambda m: lambda e: e.dma_start(
                out=xo_dst[:, m, :], in_=xch[:, 0:D]))(m), deps=[z], dma=True))
    else:
        stores = emit_sandwich_tail(bld, pr, get_F, st2, xch, x_src, xo_dst)
    pr.op("sync", lambda e: e.nop(), deps=stores)


def build_block4():
    bld = Bld()
    nc = bld.nc
    dram = (
        nc.dram_tensor("x", [T, D], F32, kind="ExternalInput").ap(),
        nc.dram_tensor("gate_w", [16, D], F32, kind="ExternalInput").ap(),
        nc.dram_tensor("bias_e", [16], F32, kind="ExternalInput").ap(),
        nc.dram_tensor("up_w", [16, 512, D], F32, kind="ExternalInput").ap(),
        nc.dram_tensor("down_w", [16, D, 512], F32, kind="ExternalInput").ap(),
        nc.dram_tensor("shared_up", [2048, D], F32, kind="ExternalInput").ap(),
        nc.dram_tensor("shared_down", [D, 2048], F32, kind="ExternalInput").ap(),
        nc.dram_tensor("iden", [128, 128], F32, kind="ExternalInput").ap(),
        nc.dram_tensor("xo", [T, D], F32, kind="ExternalOutput").ap(),
    )
    with bld.es:
        pr = Prog(bld, "b4")
        body_block4(bld, pr, dram)
        pr.run()
    return bld.nc




def body_block1(bld, pr, dram):
    nc = bld.nc
    H = 16
    (x_d, wq_d, wk_d, wv_d, wo_d, bo_d, aq_d, bq_d, ak_d, bk_d, av_d, bv_d,
     ao_d, bo2_d, idn_d, xo_d) = dram
    x_src = x_d.rearrange("(c p) d -> p c d", p=128)
    xo_dst = xo_d.rearrange("(c p) d -> p c d", p=128)

    xch = bld.sbuf("xch", [128, 2 * D])
    xn_tm = bld.sbuf("xn_tm", [128, TC * D], BF16)
    xn_fm = bld.sbuf("xn_fm", [128, DCH * T], BF16)
    k_fm = bld.sbuf("k_fm", [128, DCH * T], BF16)
    v_tm = bld.sbuf("v_tm", [128, TC * D], BF16)
    WoT = bld.sbuf("WoT", [128, DCH * D], BF16)
    WT = bld.sbuf("WT", [128, DCH * D], BF16)
    WqT = WT
    q_qb = bld.sbuf("q_qb", [128, DCH * 512], BF16)
    o_qb = bld.sbuf("o_qb", [128, DCH * 512], BF16)
    stage = xn_tm[:, 0:8192]
    Pt_ = [xn_tm[:, 8192:8192 + 2048], xn_tm[:, 10240:10240 + 2048]]
    PT = xn_tm[:, 0:8192]
    An = xn_tm[0:16, 8192:8192 + 4096]
    Bn = bld.sbuf("Bn", [128, DCH * 4 * 16], BF16)
    AT = bld.sbuf("AT", [128, 4 * DCH * 16], BF16)
    BT = xn_tm[0:16, 12288:12288 + 4096]
    u_q = bld.sbuf("u_q", [16, T], BF16)
    u_k = xn_tm[0:16, 8192:8192 + 2048]
    u_v = xn_tm[0:16, 10240:10240 + 2048]
    u_bufs = [u_q, u_k, u_v]
    u_o = bld.sbuf("u_o", [16, 512], BF16)
    rows = bld.sbuf("rows", [1, D], BF16)
    bo_b = bld.sbuf("bo_b", [128, D], BF16)
    idn = bld.sbuf("idn", [128, 128], BF16)
    ones_row = bld.sbuf("ones_row", [1, 128], BF16)
    onescol_b1 = bld.sbuf("onescol_b1", [128, 1], BF16)
    attn_sc = bld.sbuf("attn_sc", [128, 16])
    km2 = bld.sbuf("km2", [1, 16])
    qm2 = bld.sbuf("qm2", [1, 16])
    mrow = bld.sbuf("mrow", [1, 16], BF16)
    mcols = bld.sbuf("mcols", [128, 16])
    sqb = bld.sbuf("sqb", [128, 512], BF16)
    nrmrow = bld.sbuf("nrmrow", [1, 4])
    F2 = bld.sbuf("F2", [128, 2 * D])
    junk = bld.sbuf("junk", [128, D], BF16)
    st1 = alloc_stats(bld, "b1ln1", junk=junk)
    st2 = alloc_stats(bld, "b1ln2", junk=junk)

    pS = bld.psum("pS", [128, T])
    pT = [bld.psum("pT", [128, 1024], BF16) for _ in range(2)]
    pA = [bld.psum("pA", [128, 512]) for _ in range(2)]

    ld_id = pr.op("gpsimd", lambda e: e.dma_start(out=idn[:], in_=idn_d), dma=True)
    o_ones = pr.op("gpsimd", lambda e: e.memset(ones_row[:], 1.0))
    o_onec_b1 = pr.op("gpsimd", lambda e: e.memset(onescol_b1[:], 1.0))
    ld_bo = pr.op("gpsimd", lambda e: e.dma_start(out=rows[:, 0:D], in_=bo_d[None, :]), dma=True)
    ld_A = None
    ld_B = []
    for i, bd in enumerate([bq_d, bk_d, bv_d, bo2_d]):
        ld_B.append(pr.op("gpsimd", (lambda i, bd: lambda e: e.dma_start(
            out=Bn[:, i * DCH * 16:(i + 1) * DCH * 16].rearrange("p (c e) -> p c e", e=16),
            in_=bd.rearrange("(c p) e -> p c e", p=128)))(i, bd), dma=True))

    bcs = emit_bcast_rows(pr, [(rows[:, 0:D], bo_b, D)], ones_row, pA, [ld_bo, o_ones])
    bc_bo = bcs[0]

    ln1 = emit_sandwich_ln1(bld, pr, x_src, xch, xn_tm, st1, nsl=2)
    tr1 = emit_transpose(pr, lambda c: xn_tm[:, c * D:(c + 1) * D],
                         lambda f: xn_fm[:, f * T:(f + 1) * T], TC, DCH, idn, pT,
                         dep_src=ln1, dep_idn=[ld_id])
    for key in ("stage", ("Pt", 0), ("Pt", 1)):
        bld.setlu(key, tr1[-1])
    ld_A = []
    for i, ad in enumerate([aq_d, ak_d, av_d, ao_d]):
        ld_A.append(pr.op("gpsimd", (lambda i, ad: lambda e: e.dma_start(
            out=An[:, i * D:(i + 1) * D], in_=ad))(i, ad), deps=[tr1[-1]], dma=True))

    at_ops = []
    for i in range(4):
        def mk_t(i=i):
            def fn(t):
                for k in range(DCH):
                    instr = t.transpose(pT[0][:, k * 16:(k + 1) * 16],
                                        An[0:16, i * D + k * 128:i * D + (k + 1) * 128],
                                        idn[0:16, 0:16])
                return instr
            return fn
        ot = pr.op("tensor", mk_t(), deps=[ld_A[i], ld_id] + bld.war(id(pT[0])))
        oe = pr.op("vector", (lambda i: lambda v: v.tensor_copy(
            AT[:, i * DCH * 16:(i + 1) * DCH * 16], pT[0][:, 0:DCH * 16]))(i), deps=[ot])
        bld.setlu(id(pT[0]), oe)
        at_ops.append(oe)
    bt_ops = []
    for i in range(4):
        def mk_t(i=i):
            def fn(t):
                for k in range(DCH):
                    instr = t.transpose(pT[0][0:16, k * 128:(k + 1) * 128],
                                        Bn[:, (i * DCH + k) * 16:(i * DCH + k) * 16 + 16],
                                        idn[:])
                return instr
            return fn
        ot = pr.op("tensor", mk_t(), deps=[ld_B[i], ld_id] + bld.war(id(pT[0])))
        oe = pr.op("vector", (lambda i: lambda v: v.tensor_copy(
            BT[0:16, i * D:(i + 1) * D], pT[0][0:16, 0:D]))(i), deps=[ot])
        bld.setlu(id(pT[0]), oe)
        bt_ops.append(oe)

    for key in ("Pt",):
        bld.setlu(key, at_ops)

    u_ops = [[], [], []]
    for i in range(3):
        for nb in range(4):
            p = pA[nb % 2]
            war = bld.war(id(p))

            def mk_t(i=i, nb=nb, p=p):
                def fn(t):
                    for k in range(DCH):
                        instr = t.matmul(p[0:16, :], AT[:, (i * DCH + k) * 16:(i * DCH + k) * 16 + 16],
                                         xn_fm[:, k * T + nb * 512:k * T + (nb + 1) * 512],
                                         start=(k == 0), stop=(k == DCH - 1))
                    return instr
                return fn
            ot = pr.op("tensor", mk_t(), deps=[at_ops[i]] + tr1 + war + ([at_ops[3]] if i > 0 else []))
            oe = pr.op("scalar", (lambda i, nb, p: lambda s: s.activation(
                u_bufs[i][0:16, nb * 512:(nb + 1) * 512], p[0:16, :],
                AF.Identity, scale=1.0 / 16))(i, nb, p), deps=[ot])
            bld.setlu(id(p), oe)
            u_ops[i].append(oe)

    def load_wt(w_d, dstT, tag):
        ld = pr.op("gpsimd", (lambda w_d: lambda g: g.dma_start(
            out=stage.rearrange("p (r c) -> p r c", c=D),
            in_=w_d.rearrange("(r p) c -> p r c", p=128)))(w_d),
            deps=bld.war("stage"), dma=True)
        tw = emit_transpose(pr, lambda rc: stage[:, rc * D:(rc + 1) * D],
                            lambda dc: dstT[:, dc * D:(dc + 1) * D], DCH, DCH, idn, pT,
                            dep_src=[ld] + bld.war(tag), dep_idn=[ld_id], evac_eng="vector")
        bld.setlu("stage", tw[-1])
        return tw

    twk = load_wt(wk_d, WT, "WT")
    pk = []
    for e in range(DCH):
        for nb in range(4):
            p = pA[(e * 4 + nb) % 2]
            war = bld.war(id(p))

            def mk_t(e=e, nb=nb, p=p):
                def fn(t):
                    for k in range(DCH):
                        instr = t.matmul(p[:], WT[:, k * D + e * 128:k * D + (e + 1) * 128],
                                         xn_fm[:, k * T + nb * 512:k * T + (nb + 1) * 512],
                                         start=(k == 0), stop=False)
                    instr = t.matmul(p[:], BT[0:16, D + e * 128:D + (e + 1) * 128],
                                     u_k[0:16, nb * 512:(nb + 1) * 512],
                                     start=False, stop=True)
                    return instr
                return fn
            ot = pr.op("tensor", mk_t(), deps=twk + [bt_ops[1]] + u_ops[1] + war)
            oe = pr.op("vector", (lambda e, nb, p: lambda v: ecopy(v,
                k_fm[:, e * T + nb * 512:e * T + (nb + 1) * 512], p[:]))(e, nb, p), deps=[ot])
            bld.setlu(id(p), oe)
            pk.append(oe)

    bld.setlu(("Pt", 0), pk[-1])

    kn_ops = []
    o_z = pr.op("vector", lambda v: v.memset(km2[:], 0.0))
    for e in range(DCH):
        for nb in range(4):
            osq = pr.op("scalar", (lambda e, nb: lambda s: s.activation(
                sqb[:], k_fm[:, e * T + nb * 512:e * T + (nb + 1) * 512], AF.Square))(e, nb),
                deps=[pk[e * 4 + nb]] + bld.war("sqb"))
            for half in range(2):
                h = 2 * e + half
                r0 = half * 64
                p = pA[half]
                war = bld.war(id(p))
                omm = pr.op("tensor", (lambda r0: lambda t: t.matmul(
                    p[0:1, :] if False else pA[0][0:1, 0:512], onescol_b1[r0:r0 + 64, :],
                    sqb[r0:r0 + 64, :], start=True, stop=True))(r0),
                    deps=[osq, o_onec_b1] + bld.war(id(pA[0])))
                orm = pr.op("vector", (lambda: lambda v: v.tensor_reduce(
                    nrmrow[0:1, 0:1], pA[0][0:1, 0:512], AX.X, ALU.max))(),
                    deps=[omm] + bld.war("nrmrow"))
                bld.setlu(id(pA[0]), orm)
                omx = pr.op("vector", (lambda h: lambda v: v.tensor_max(
                    km2[0:1, h:h + 1], km2[0:1, h:h + 1], nrmrow[0:1, 0:1]))(h),
                    deps=[orm, o_z])
                bld.setlu("nrmrow", omx)
                kn_ops.append(omx)
            bld.setlu("sqb", kn_ops[-1])

    twv = load_wt(wv_d, WT, "WT2")
    bld.setlu("WT", pk[-1])
    pv = []
    for m in range(TC):
        for eb in range(2):
            p = pA[(m * 2 + eb) % 2]
            war = bld.war(id(p))

            def mk_t(m=m, eb=eb, p=p):
                def fn(t):
                    for k in range(DCH):
                        instr = t.matmul(p[:], xn_fm[:, k * T + m * 128:k * T + (m + 1) * 128],
                                         WT[:, k * D + eb * 512:k * D + (eb + 1) * 512],
                                         start=(k == 0), stop=False)
                    instr = t.matmul(p[:], u_v[0:16, m * 128:(m + 1) * 128],
                                     BT[0:16, 2 * D + eb * 512:2 * D + (eb + 1) * 512],
                                     start=False, stop=True)
                    return instr
                return fn
            ot = pr.op("tensor", mk_t(), deps=twv + [bt_ops[2]] + u_ops[2] + war)
            oe = pr.op("vector", (lambda m, eb, p: lambda v: ecopy(v,
                v_tm[:, m * D + eb * 512:m * D + (eb + 1) * 512], p[:]))(m, eb, p), deps=[ot])
            bld.setlu(id(p), oe)
            pv.append(oe)

    bld.setlu(("Pt", 1), pv[-1])

    two = load_wt(wo_d, WoT, "WoT")
    bld.setlu("WT2", pv[-1])
    twq = load_wt(wq_d, WqT, "WqT")
    for qt in range(4):
        for g in range(2):
            bld.setlu(("PT", qt, g), twq[-1])

    stores = []
    for qb in range(4):
        pq = []
        for e in range(DCH):
            p = pA[e % 2]
            war = bld.war(id(p))

            def mk_t(e=e, qb=qb, p=p):
                def fn(t):
                    for k in range(DCH):
                        instr = t.matmul(p[:], WqT[:, k * D + e * 128:k * D + (e + 1) * 128],
                                         xn_fm[:, k * T + qb * 512:k * T + (qb + 1) * 512],
                                         start=(k == 0), stop=False)
                    instr = t.matmul(p[:], BT[0:16, e * 128:(e + 1) * 128],
                                     u_q[0:16, qb * 512:(qb + 1) * 512],
                                     start=False, stop=True)
                    return instr
                return fn
            ot = pr.op("tensor", mk_t(), deps=twq + [bt_ops[0]] + u_ops[0] + war + bld.war(("q_qb", e)))
            oe = pr.op("vector", (lambda e, p: lambda v: ecopy(v,
                q_qb[:, e * 512:(e + 1) * 512], p[:]))(e, p), deps=[ot])
            bld.setlu(id(p), oe)
            bld.setlu(("q_qb", e), None)
            pq.append(oe)

        qn_ops = []
        o_zq = pr.op("vector", lambda v: v.memset(qm2[:], 0.0), deps=bld.war("qm2"))
        for e in range(DCH):
            osq = pr.op("scalar", (lambda e: lambda s: s.activation(
                sqb[:], q_qb[:, e * 512:(e + 1) * 512], AF.Square))(e),
                deps=[pq[e]] + bld.war("sqb"))
            for half in range(2):
                h = 2 * e + half
                r0 = half * 64
                omm = pr.op("tensor", (lambda r0: lambda t: t.matmul(
                    pA[0][0:1, 0:512], onescol_b1[r0:r0 + 64, :],
                    sqb[r0:r0 + 64, :], start=True, stop=True))(r0),
                    deps=[osq, o_onec_b1] + bld.war(id(pA[0])))
                orm = pr.op("vector", (lambda: lambda v: v.tensor_reduce(
                    nrmrow[0:1, 0:1], pA[0][0:1, 0:512], AX.X, ALU.max))(),
                    deps=[omm] + bld.war("nrmrow"))
                bld.setlu(id(pA[0]), orm)
                omx = pr.op("vector", (lambda h: lambda v: v.tensor_max(
                    qm2[0:1, h:h + 1], qm2[0:1, h:h + 1], nrmrow[0:1, 0:1]))(h),
                    deps=[orm, o_zq])
                bld.setlu("nrmrow", omx)
                qn_ops.append(omx)
            bld.setlu("sqb", qn_ops[-1])
        bld.setlu("qm2", qn_ops)
        o_mt = pr.op("vector", lambda v: v.tensor_mul(qm2[:], qm2[:], km2[:]),
                     deps=qn_ops + kn_ops)
        bld.setlu("qm2", o_mt)
        o_ms = pr.op("scalar", lambda s: s.activation(mrow[:], qm2[:], AF.Sqrt), deps=[o_mt] + bld.war("mrow"))
        o_mb = pr.op("tensor", lambda t: t.matmul(pA[0][0:128, 0:16], ones_row[:], mrow[:],
                                                  start=True, stop=True),
                     deps=[o_ms, o_ones] + bld.war(id(pA[0])))
        o_mc = pr.op("vector", lambda v: v.tensor_single_scalar(
            mcols[:], pA[0][0:128, 0:16], -0.125, ALU.mult),
            deps=[o_mb] + bld.war("mcols"))
        bld.setlu(id(pA[0]), o_mc)
        bld.setlu("mrow", o_mb)

        pend = []
        last_av = [None]

        def flush_pend(force=False):
            rest = []
            while pend:
                d, fn = pend.pop(0)
                if d <= 0 or force:
                    fn()
                else:
                    rest.append((d - 1, fn))
            pend.extend(rest)

        for h in range(H):
            fc, r0 = h // 2, (h % 2) * 64
            pt_evacs = []
            for qt in range(4):
                it = (h * 4 + qt) % 2
                a0 = it * 8
                Pt = Pt_[it]
                mms = []
                for kb in range(4):
                    omm = pr.op("tensor", (lambda h, qt, fc, r0, kb: lambda t: t.matmul(
                        pS[:, kb * 512:(kb + 1) * 512],
                        q_qb[r0:r0 + 64, fc * 512 + qt * 128:fc * 512 + (qt + 1) * 128],
                        k_fm[r0:r0 + 64, fc * T + kb * 512:fc * T + (kb + 1) * 512],
                        start=True, stop=True))(h, qt, fc, r0, kb),
                        deps=pq + pk + bld.war(("pSb", kb)))
                    mms.append(omm)
                flush_pend()
                exps = []
                for kb in range(4):
                    oex = pr.op("scalar", (lambda h, kb, a0, Pt=Pt: lambda s: s.activation(
                        Pt[:, kb * 512:(kb + 1) * 512], pS[:, kb * 512:(kb + 1) * 512],
                        AF.Exp, bias=mcols[:, h:h + 1], scale=0.125,
                        accum_out=attn_sc[:, a0 + kb:a0 + kb + 1]))(h, kb, a0),
                        deps=[mms[kb], o_mc] + bld.war(("Pt", it)) + bld.war(("attq", it)))
                    bld.setlu(("pSb", kb), oex)
                    exps.append(oex)
                osum = pr.op("vector", (lambda a0: lambda v: v.tensor_reduce(
                    attn_sc[:, a0 + 4:a0 + 5], attn_sc[:, a0:a0 + 4], AX.X, ALU.add))(a0),
                    deps=exps)
                ocl = pr.op("vector", (lambda a0: lambda v: v.tensor_scalar_max(
                    attn_sc[:, a0 + 4:a0 + 5], attn_sc[:, a0 + 4:a0 + 5], 1e-35))(a0), deps=[osum])
                orc = pr.op("vector", (lambda a0: lambda v: v.reciprocal(
                    attn_sc[:, a0 + 5:a0 + 6], attn_sc[:, a0 + 4:a0 + 5]))(a0), deps=[ocl])
                onr = pr.op("vector", (lambda a0, Pt=Pt: lambda v: v.tensor_scalar_mul(
                    Pt, Pt, attn_sc[:, a0 + 5:a0 + 6]))(a0), deps=[orc])
                bld.setlu(("attq", it), onr)

                def emit_trans(qt=qt, it=it, onr=onr, Pt=Pt, pt_evacs=pt_evacs):
                    PT3 = PT.rearrange("p (k c) -> p k c", c=512)
                    for g in range(2):
                        p = pT[g]
                        war = bld.war(id(p))

                        def mk_tt(g=g, p=p, Pt=Pt):
                            def fn(t):
                                for j in range(8):
                                    kc = g * 8 + j
                                    instr = t.transpose(p[:, j * 128:(j + 1) * 128],
                                                        Pt[:, kc * 128:(kc + 1) * 128], idn[:])
                                return instr
                            return fn
                        ott = pr.op("tensor", mk_tt(), deps=[onr] + war)
                        p3 = p[:].rearrange("p (k c) -> p k c", c=128)

                        def mk_tv(g=g, p3=p3, qt=qt):
                            dst = PT3[:, g * 8:(g + 1) * 8, qt * 128:(qt + 1) * 128]
                            return lambda v: ecopy(v, dst, p3)
                        oev = pr.op("vector", mk_tv(),
                                    deps=[ott] + bld.war(("PT", qt, g)))
                        bld.setlu(id(p), oev)
                        pt_evacs.append(oev)
                        bld.setlu(("Pt", it), ott)
                pend.append((1, emit_trans))

            def emit_av(h=h, fc=fc, r0=r0, pt_evacs=pt_evacs):
                p = pA[h % 2]
                war = bld.war(id(p))

                def mk_av(h=h, p=p):
                    def fn(t):
                        for kc in range(TC):
                            instr = t.matmul(p[0:64, :], v_tm[:, kc * D + h * 64:kc * D + h * 64 + 64],
                                             PT[:, kc * 512:(kc + 1) * 512],
                                             start=(kc == 0), stop=(kc == TC - 1))
                        return instr
                    return fn
                oav = pr.op("tensor", mk_av(), deps=pt_evacs + pv + war)
                oeav = pr.op("vector", (lambda h, p, fc, r0: lambda v: ecopy(v,
                    o_qb[r0:r0 + 64, fc * 512:(fc + 1) * 512], p[0:64, :]))(h, p, fc, r0),
                    deps=[oav] + bld.war(("o_qb", fc, r0)))
                bld.setlu(id(p), oeav)
                bld.setlu(("o_qb", fc, r0), None)
                for qt in range(4):
                    for g in range(2):
                        bld.setlu(("PT", qt, g), oav)
                last_av[0] = oav
                bld.setlu(("oqbh", h), oeav)
            pend.append((2, emit_av))
        flush_pend(force=True)

        oqb_all = [bld.lu[("oqbh", h)] for h in range(H)]
        uo_ops = []
        p = pA[0]
        war = bld.war(id(p))

        def mk_uo(qb=qb, p=p):
            def fn(t):
                for k in range(DCH):
                    instr = t.matmul(p[0:16, :], AT[:, (3 * DCH + k) * 16:(3 * DCH + k) * 16 + 16],
                                     o_qb[:, k * 512:(k + 1) * 512],
                                     start=(k == 0), stop=(k == DCH - 1))
                return instr
            return fn
        ot = pr.op("tensor", mk_uo(), deps=[at_ops[3]] + oqb_all + war)
        ouo = pr.op("scalar", (lambda p: lambda s: s.activation(
            u_o[0:16, :], p[0:16, :], AF.Identity, scale=1.0 / 16))(p), deps=[ot] + bld.war("u_o"))
        bld.setlu(id(p), ouo)

        for mm in range(4):
            m = qb * 4 + mm
            Fsl = F2[:, (mm % 2) * D:(mm % 2) * D + D]
            ev = []
            for eb in range(2):
                p = pA[(mm * 2 + eb) % 2]
                war = bld.war(id(p))

                def mk_t(mm=mm, eb=eb, p=p, qb=qb):
                    def fn(t):
                        for k in range(DCH):
                            instr = t.matmul(p[:], o_qb[:, k * 512 + mm * 128:k * 512 + (mm + 1) * 128],
                                             WoT[:, k * D + eb * 512:k * D + (eb + 1) * 512],
                                             start=(k == 0), stop=False)
                        instr = t.matmul(p[:], u_o[0:16, mm * 128:(mm + 1) * 128],
                                         BT[0:16, 3 * D + eb * 512:3 * D + (eb + 1) * 512],
                                         start=False, stop=True)
                        return instr
                    return fn
                ot = pr.op("tensor", mk_t(), deps=two + [bt_ops[3], ouo] + oqb_all + war)
                oe = pr.op("vector", (lambda Fsl, eb, p: lambda v: v.tensor_add(
                    Fsl[:, eb * 512:(eb + 1) * 512], p[:], bo_b[:, eb * 512:(eb + 1) * 512]))(Fsl, eb, p),
                    deps=[ot] + bc_bo + bld.war(("F2", mm % 2)))
                bld.setlu(id(p), oe)
                ev.append(oe)
            o7, o8 = ln_chunk_ops(pr, st2, m, Fsl, ev, 1e-5)
            o_t = ln_apply_chunk(pr, st2, m, Fsl, Fsl, [o7, o8])
            sl2 = m % 2
            xs2 = xch[:, sl2 * D:(sl2 + 1) * D]
            ld2 = pr.op("sync", (lambda m, xs2: lambda e: e.dma_start(
                out=xs2, in_=x_src[:, m, :]))(m, xs2), deps=bld.war(("xch", sl2)), dma=True)
            o_add = pr.op("vector", (lambda Fsl, xs2: lambda v: v.tensor_add(
                Fsl, Fsl, xs2))(Fsl, xs2), deps=[o_t, ld2])
            bld.setlu(("xch", sl2), o_add)
            ost = pr.op("sync", (lambda m, Fsl: lambda e: e.dma_start(
                out=xo_dst[:, m, :], in_=Fsl))(m, Fsl), deps=[o_add], dma=True)
            bld.setlu(("F2", mm % 2), ost)
            stores.append(ost)
        bld.setlu("u_o", ot)
        for h in range(H):
            fc, r0 = h // 2, (h % 2) * 64
            bld.setlu(("o_qb", fc, r0), ot)
        for e in range(DCH):
            bld.setlu(("q_qb", e), last_av[0])

    pr.op("sync", lambda e: e.nop(), deps=stores)


def build_block1():
    bld = Bld()
    nc = bld.nc
    dram = (
        nc.dram_tensor("x", [T, D], F32, kind="ExternalInput").ap(),
        nc.dram_tensor("Wq", [D, D], F32, kind="ExternalInput").ap(),
        nc.dram_tensor("Wk", [D, D], F32, kind="ExternalInput").ap(),
        nc.dram_tensor("Wv", [D, D], F32, kind="ExternalInput").ap(),
        nc.dram_tensor("Wo", [D, D], F32, kind="ExternalInput").ap(),
        nc.dram_tensor("bo", [D], F32, kind="ExternalInput").ap(),
        nc.dram_tensor("Aq", [16, D], F32, kind="ExternalInput").ap(),
        nc.dram_tensor("Bq", [D, 16], F32, kind="ExternalInput").ap(),
        nc.dram_tensor("Ak", [16, D], F32, kind="ExternalInput").ap(),
        nc.dram_tensor("Bk", [D, 16], F32, kind="ExternalInput").ap(),
        nc.dram_tensor("Av", [16, D], F32, kind="ExternalInput").ap(),
        nc.dram_tensor("Bv", [D, 16], F32, kind="ExternalInput").ap(),
        nc.dram_tensor("Ao", [16, D], F32, kind="ExternalInput").ap(),
        nc.dram_tensor("Bo", [D, 16], F32, kind="ExternalInput").ap(),
        nc.dram_tensor("iden", [128, 128], F32, kind="ExternalInput").ap(),
        nc.dram_tensor("xo", [T, D], F32, kind="ExternalOutput").ap(),
    )
    with bld.es:
        pr = Prog(bld, "b1")
        body_block1(bld, pr, dram)
        pr.run()
    return bld.nc










































def body_block2(bld, pr, dram):
    nc = bld.nc
    NH, WIN, HID = 8, 512, 2730
    (x_d, wqkv_d, qs_d, ks_d, wo_d, lng_d, lnb_d, w1_d, b1_d, w2_d, b2_d,
     idn_d, c1q_d, c2q_d, c1k_d, c2k_d, b2c_d, mu_d, ml_d, rot_d, xo_d) = dram
    x_src = x_d.rearrange("(c p) d -> p c d", p=128)
    xo_dst = xo_d.rearrange("(c p) d -> p c d", p=128)
    YW = 2052

    xch = bld.sbuf("xch", [128, 2 * D])
    Y = bld.sbuf("Y", [128, TC * D])
    Y_fm = bld.sbuf("Y_fm", [128, DCH * YW], BF16)
    arena = bld.sbuf("arena", [128, 29696], BF16)
    qrot = arena[:, 0:8192]
    kAv = arena[:, 8192:16384]
    v_tm = arena[:, 16384:24576]
    PT = arena[:, 24576:28672]
    Pt = arena[:, 28672:29696]
    w1buf_ = [arena[:, 0:2048], arena[:, 26624:28672]]
    w2buf = arena[:, 2048:13312]
    h4 = arena[:, 13312:24576]
    agst = arena[:, 24576:25600]
    h2st = arena[:, 25600:26624]
    tab_buf = arena[:, 24576:25600]
    wqkv_sb = bld.sbuf("wqkv_sb", [128, DCH * 512], BF16)
    wo_sb = bld.sbuf("wo_sb", [128, 4 * D], BF16)
    o_w = bld.sbuf("o_w", [128, 4 * 512], BF16)
    ybf = bld.sbuf("ybf", [128, D], BF16)
    sqs = bld.sbuf("sqs", [128, 512], BF16)
    s12 = bld.sbuf("s12", [128, 2 * 512], BF16)
    Pt2_ = s12[:, 0:1024]
    t2t = bld.sbuf("t2t", [128, 512], BF16)
    rotm = bld.sbuf("rotm", [128, 128], BF16)
    rotmq = bld.sbuf("rotmq", [128, 128], BF16)
    rotmk = bld.sbuf("rotmk", [128, 128], BF16)
    c1m = bld.sbuf("c1m", [128, 512], BF16)
    rawb = s12[:, 0:512]
    rkrow = bld.sbuf("rkrow", [1, 2 * 512], BF16)
    rq8 = bld.sbuf("rq8", [128, NH * TC])
    qb2 = bld.sbuf("qb2", [128, 512], BF16)
    b2col = bld.sbuf("b2col", [128, 1])
    qscol = bld.sbuf("qscol", [128, 1])
    kscol = bld.sbuf("kscol", [128, 1])
    maskU = bld.sbuf("maskU", [128, 128], BF16)
    maskL = bld.sbuf("maskL", [128, 128], BF16)
    onescol = bld.sbuf("onescol", [128, 1], BF16)
    ones_row = bld.sbuf("ones_row", [1, 128], BF16)
    idn = bld.sbuf("idn", [128, 128], BF16)
    rows = bld.sbuf("rows", [1, 1408], BF16)
    b1c = bld.sbuf("b1c", [128, 44])
    b2r = bld.sbuf("b2r", [128, D], BF16)
    gamr = bld.sbuf("gamr", [128, D], BF16)
    betr = bld.sbuf("betr", [128, D], BF16)
    att = bld.sbuf("attsc", [128, 8])
    gbc = bld.sbuf("gbc", [128, 16])
    hs = {k: bld.sbuf("hx" + k, [128, TC]) for k in
          ("s1a", "s1b", "s2a", "s2b", "sh1", "sh2", "rsJ", "nmJ")}
    n0b = bld.sbuf("n0b", [128, 1])
    nm0b = bld.sbuf("nm0b", [1, 1], BF16)
    junk = bld.sbuf("junk", [128, D], BF16)
    st1 = alloc_stats(bld, "b2ln1", junk=junk)
    st2 = alloc_stats(bld, "b2ln2", junk=junk)
    st3 = st1

    pS2 = [bld.psum("pS", [128, 1024]) for _ in range(2)]
    pA = [bld.psum("pA", [128, 512]) for _ in range(2)]
    pT = [bld.psum("pT", [128, 1024], BF16) for _ in range(2)]
    pB = pS2[0][:, 0:512]
    pN = pS2[1][0:1, 0:512]

    ld_id = pr.op("gpsimd", lambda e: e.dma_start(out=idn[:], in_=idn_d), dma=True)
    o_ones = pr.op("gpsimd", lambda e: e.memset(ones_row[:], 1.0))
    o_onec = pr.op("gpsimd", lambda e: e.memset(onescol[:], 1.0))
    ld_b2c = pr.op("gpsimd", lambda e: e.dma_start(out=b2col[:], in_=b2c_d), dma=True)
    ld_mu = pr.op("gpsimd", lambda e: e.dma_start(out=maskU[:], in_=mu_d), dma=True)
    ld_ml = pr.op("gpsimd", lambda e: e.dma_start(out=maskL[:], in_=ml_d), dma=True)
    ld_rot = pr.op("gpsimd", lambda e: e.dma_start(out=rotm[:], in_=rot_d), dma=True)

    for c in range(TC):
        sl = c % 2
        xs = xch[:, sl * D:(sl + 1) * D]
        ld = pr.op("sync", (lambda c, xs: lambda e: e.dma_start(out=xs, in_=x_src[:, c, :]))(c, xs),
                   deps=bld.war(("xch", sl)), dma=True)
        o7, o8 = ln_chunk_ops(pr, st1, c, xs, [ld], 1e-5)
        oap = ln_apply_chunk(pr, st1, c, xs, Y[:, c * D:(c + 1) * D], [o7, o8])
        bld.setlu(("xch", sl), [oap, o7])
        bld.setlu(("Yc", c), oap)

    def build_Y_fm():
        zops = []
        for f in range(DCH):
            zops.append(pr.op("vector", (lambda f: lambda v: v.memset(
                Y_fm[:, f * YW:f * YW + 1], 0.0))(f), deps=bld.war("Yfm_rd")))
        evs = []
        for tc in range(TC):
            yb = ybf[:, 0:D]
            oc = pr.op("scalar", (lambda tc, yb: lambda s: s.activation(
                yb, Y[:, tc * D:(tc + 1) * D], AF.Identity))(tc, yb),
                deps=bld.war(("Yc", tc)) + bld.war("ybf"))
            p = pT[tc % 2]
            war = bld.war(id(p))

            def mk_t(yb=yb, p=p):
                def fn(t):
                    for f in range(DCH):
                        instr = t.transpose(p[:, f * 128:(f + 1) * 128],
                                            yb[:, f * 128:(f + 1) * 128], idn[:])
                    return instr
                return fn
            ot = pr.op("tensor", mk_t(), deps=[oc, ld_id] + war)
            bld.setlu("ybf", ot)

            def mk_v(tc=tc, p=p):
                def fn(v):
                    for f in range(DCH):
                        instr = v.tensor_copy(
                            Y_fm[:, f * YW + 1 + tc * 128:f * YW + 1 + (tc + 1) * 128],
                            p[:, f * 128:(f + 1) * 128])
                    return instr
                return fn
            oe = pr.op("vector", mk_v(), deps=[ot] + zops + bld.war("Yfm_rd"))
            bld.setlu(id(p), oe)
            evs.append(oe)
        bld.setlu("Yfm_wr", evs)
        return evs

    for L in range(2):
        ld_qs = pr.op("gpsimd", (lambda L: lambda e: e.dma_start(
            out=rows[:, 0:64], in_=qs_d[L][None, :]))(L), deps=bld.war("rows"), dma=True)
        ld_ks = pr.op("gpsimd", (lambda L: lambda e: e.dma_start(
            out=rows[:, 64:128], in_=ks_d[L][None, :]))(L), deps=bld.war("rows"), dma=True)

        def mkqsc(t):
            t.matmul(pB[0:64, 0:1], rows[:, 0:64], onescol[0:1, :], start=True, stop=True)
            return t.matmul(pB[0:64, 1:2], rows[:, 64:128], onescol[0:1, :], start=True, stop=True)
        oqs_t = pr.op("tensor", mkqsc, deps=[ld_qs, ld_ks, o_onec] + bld.war(id(pB)))
        oc1 = pr.op("vector", lambda v: v.tensor_copy(qscol[0:64, :], pB[0:64, 0:1]),
                    deps=[oqs_t] + bld.war("qksc"))
        oc2 = pr.op("vector", lambda v: v.tensor_copy(qscol[64:128, :], pB[0:64, 0:1]), deps=[oqs_t])
        oc3 = pr.op("vector", lambda v: v.tensor_copy(kscol[0:64, :], pB[0:64, 1:2]), deps=[oqs_t])
        oc4 = pr.op("vector", lambda v: v.tensor_copy(kscol[64:128, :], pB[0:64, 1:2]), deps=[oqs_t])
        oqsc = [oc1, oc2]
        oksc = [oc3, oc4]
        bld.setlu(id(pB), oqsc + oksc)
        bld.setlu("rows", [oqs_t])
        o_rmq = pr.op("vector", lambda v: v.tensor_single_scalar(rotmq[:], rotm[:], qscol[:], ALU.mult),
                      deps=[ld_rot] + oqsc + bld.war("rotmq"))
        o_rmk = pr.op("vector", lambda v: v.tensor_single_scalar(rotmk[:], rotm[:], kscol[:], ALU.mult),
                      deps=[ld_rot] + oksc + bld.war("rotmk"))
        bld.setlu("rotmq", o_rmq)
        bld.setlu("rotmk", o_rmk)

        yfm = build_Y_fm()

        def shifted(kc, t0, wdt):
            off = 1 if kc < 4 else 0
            return Y_fm[:, kc * YW + off + t0: kc * YW + off + t0 + wdt]

        pq_done = []
        ldw_prev = None
        for part in range(2):
            ldw = pr.op("gpsimd", (lambda L, part: lambda e: e.dma_start(
                out=wqkv_sb[:].rearrange("p (c e) -> p c e", e=512),
                in_=wqkv_d[L, :, part * 512:(part + 1) * 512].rearrange("(c p) e -> p c e", p=128)))(L, part),
                deps=bld.war("wqkv_sb"), dma=True)
            dst = qrot if part == 0 else kAv
            scol = qscol if part == 0 else kscol
            osc = oqsc if part == 0 else oksc
            c1d = c1q_d if part == 0 else c1k_d
            c2d = c2q_d if part == 0 else c2k_d
            mm_ops = []
            for tb in range(4):
                ld_t1 = pr.op("gpsimd", (lambda tb, c1d: lambda e: e.dma_start(
                    out=tab_buf[:, 0:512], in_=c1d[:, tb * 512:(tb + 1) * 512]))(tb, c1d),
                    deps=bld.war("tab_buf"), dma=True)
                ld_t2 = pr.op("gpsimd", (lambda tb, c2d: lambda e: e.dma_start(
                    out=tab_buf[:, 512:1024], in_=c2d[:, tb * 512:(tb + 1) * 512]))(tb, c2d),
                    deps=bld.war("tab_buf"), dma=True)
                o_c1m = pr.op("vector", (lambda scol: lambda v: v.tensor_single_scalar(
                    c1m[:], tab_buf[:, 0:512], scol[:], ALU.mult))(scol),
                    deps=[ld_t1] + osc + bld.war("c1m"))
                tabdep = [ld_t1, ld_t2, o_c1m]
                for ec in range(4):
                    p = pA[(ec * 4 + tb) % 2]
                    war = bld.war(id(p))

                    def mk_t(ec=ec, tb=tb, p=p):
                        def fn(t):
                            for kc in range(DCH):
                                instr = t.matmul(p[:],
                                                 wqkv_sb[:, kc * 512 + ec * 128:kc * 512 + (ec + 1) * 128],
                                                 shifted(kc, tb * 512, 512),
                                                 start=(kc == 0), stop=(kc == DCH - 1))
                            return instr
                        return fn
                    ot = pr.op("tensor", mk_t(), deps=[ldw] + yfm + war + bld.war("arena_ep"))
                    mm_ops.append(ot)
                    osq = pr.op("scalar", (lambda p: lambda s: s.activation(
                        sqs[:], p[:], AF.Square))(p), deps=[ot] + bld.war("sqs"))
                    if part == 0:
                        lastn = []
                        for half in range(2):
                            h = 2 * ec + half
                            r0 = half * 64
                            for qt in range(4):
                                gqt = tb * 4 + qt
                                omm = pr.op("tensor", (lambda r0, qt: lambda t: t.matmul(
                                    pB[:, 4:5], sqs[r0:r0 + 64, qt * 128:(qt + 1) * 128],
                                    onescol[r0:r0 + 64, :], start=True, stop=True))(r0, qt),
                                    deps=[osq] + bld.war(id(pB)))
                                osr = pr.op("scalar", (lambda h, gqt: lambda s: s.activation(
                                    rq8[:, h * TC + gqt:h * TC + gqt + 1], pB[:, 4:5],
                                    AF.Sqrt, scale=1.0 / 64))(h, gqt), deps=[omm])
                                bld.setlu(id(pB), osr)
                                om1 = pr.op("vector", (lambda h, gqt: lambda v: v.tensor_scalar_max(
                                    rq8[:, h * TC + gqt:h * TC + gqt + 1],
                                    rq8[:, h * TC + gqt:h * TC + gqt + 1], 1.25e-13))(h, gqt), deps=[osr])
                                om2 = pr.op("vector", (lambda h, gqt: lambda v: v.reciprocal(
                                    rq8[:, h * TC + gqt:h * TC + gqt + 1],
                                    rq8[:, h * TC + gqt:h * TC + gqt + 1]))(h, gqt), deps=[om1])
                                lastn.append(om2)
                        bld.setlu(("rqd", ec, tb), lastn)
                    else:
                        rk_done = []
                        for half in range(2):
                            r0 = half * 64
                            omm = pr.op("tensor", (lambda r0: lambda t: t.matmul(
                                pN, onescol[r0:r0 + 64, :], sqs[r0:r0 + 64, :],
                                start=True, stop=True))(r0),
                                deps=[osq] + bld.war("pNk") + bld.war(id(pB)))
                            osr = pr.op("scalar", (lambda half: lambda s: s.activation(
                                rkrow[:, half * 512:(half + 1) * 512], pN, AF.Sqrt,
                                scale=1.0))(half), deps=[omm] + bld.war(("rkr", half)))
                            bld.setlu("pNk", osr)
                            bld.setlu(id(pB), bld.war(id(pB)) + [osr])
                            om1 = pr.op("vector", (lambda half: lambda v: v.tensor_scalar_max(
                                rkrow[:, half * 512:(half + 1) * 512],
                                rkrow[:, half * 512:(half + 1) * 512], 1e-12))(half), deps=[osr])
                            def mkrcp(half=half):
                                def fn(v):
                                    with bld.nc.allow_low_precision(reason="rk row recip feeds bf16 matmul operand"):
                                        return v.reciprocal(rkrow[:, half * 512:(half + 1) * 512],
                                                            rkrow[:, half * 512:(half + 1) * 512])
                                return fn
                            om2 = pr.op("vector", mkrcp(), deps=[om1])
                            rk_done.append(om2)
                    c2 = tab_buf[:, 512:1024]
                    orawc = pr.op("scalar", (lambda p: lambda s: s.activation(
                        rawb[:], p[:], AF.Identity))(p), deps=[ot, osq] + bld.war("rawb"))
                    o1 = pr.op("vector", (lambda p: lambda v: v.tensor_mul(
                        s12[:, 512:1024], p[:], c1m[:]))(p), deps=[ot, osq] + tabdep + bld.war("s12"))
                    bld.setlu(id(p), [o1, orawc])
                    rotm_u = rotmq if part == 0 else rotmk
                    orot = pr.op("tensor", (lambda rotm_u: lambda t: t.matmul(
                        pB[:, 0:512], rotm_u[:], rawb[:], start=True, stop=True))(rotm_u),
                        deps=[orawc, o_rmq, o_rmk] + bld.war(id(pB)))
                    bld.setlu("rawb", orot)
                    o2 = pr.op("vector", (lambda c2: lambda v: v.tensor_mul(
                        t2t[:], pB[:, 0:512], c2))(c2), deps=[orot] + tabdep + bld.war("t2t"))
                    bld.setlu(id(pB), o2)
                    o3 = pr.op("vector", lambda v: v.tensor_add(
                        s12[:, 512:1024], s12[:, 512:1024], t2t[:]), deps=[o1, o2])
                    bld.setlu("t2t", o3)
                    oadds = [o3]
                    dsl = dst[:, ec * T + tb * 512:ec * T + (tb + 1) * 512]
                    if part == 0:
                        ost_ = pr.op("vector", (lambda dsl: lambda v: v.tensor_copy(
                            dsl, s12[:, 512:1024]))(dsl), deps=oadds + bld.war("arena_ep"))
                        bld.setlu(("qwr", ec, tb), ost_)
                    else:
                        def mkbc():
                            def fn(t):
                                t.matmul(pB[0:64, 0:512], ones_row[:, 0:64], rkrow[:, 0:512],
                                         start=True, stop=True)
                                return t.matmul(pB[64:128, 0:512], ones_row[:, 0:64],
                                                rkrow[:, 512:1024], start=True, stop=True)
                            return fn
                        obc = pr.op("tensor", mkbc(), deps=rk_done + [o_ones] + bld.war(id(pB)))
                        ost_ = pr.op("vector", (lambda dsl: lambda v: v.scalar_tensor_tensor(
                            dsl, s12[:, 512:1024], 1.0, pB[:, 0:512], ALU.mult, ALU.mult))(dsl),
                            deps=oadds + [obc] + bld.war("arena_ep"))
                        bld.setlu(id(pB), ost_)
                        bld.setlu(("rkr", 0), ost_)
                        bld.setlu(("rkr", 1), ost_)
                        bld.setlu(("kwr", ec, tb), ost_)
                    bld.setlu("s12", ost_)
                    bld.setlu("sqs", [ost_])
                    bld.setlu("tab_buf", bld.war("tab_buf") + [o1] + [o for o in oadds])
                    pq_done.append(ost_)
            bld.setlu("wqkv_sb", mm_ops)
        bld.setlu("PTz", pq_done[-8:])

        ld_v = pr.op("gpsimd", (lambda L: lambda e: e.dma_start(
            out=wqkv_sb[:].rearrange("p (c e) -> p c e", e=512),
            in_=wqkv_d[L, :, 1024:1536].rearrange("(c p) e -> p c e", p=128)))(L),
            deps=bld.war("wqkv_sb"), dma=True)
        ld_wo = pr.op("gpsimd", (lambda L: lambda e: e.dma_start(
            out=wo_sb[:].rearrange("p (c e) -> p c e", e=D),
            in_=wo_d[L].rearrange("(c p) e -> p c e", p=128)))(L),
            deps=bld.war("wo_sb"), dma=True)
        pv_done = []
        for m in range(TC):
            p = pA[m % 2]
            war = bld.war(id(p))

            def mk_t(m=m, p=p):
                def fn(t):
                    for kc in range(DCH):
                        instr = t.matmul(p[:], shifted(kc, m * 128, 128),
                                         wqkv_sb[:, kc * 512:(kc + 1) * 512],
                                         start=(kc == 0), stop=(kc == DCH - 1))
                    return instr
                return fn
            ot = pr.op("tensor", mk_t(), deps=[ld_v] + yfm + war + bld.war("arena_ep"))
            oe = pr.op("scalar", (lambda m, p: lambda v: ecopy(v,
                v_tm[:, m * 512:(m + 1) * 512], p[:]))(m, p), deps=[ot])
            bld.setlu(id(p), oe)
            pv_done.append(oe)
        bld.setlu("wqkv_sb", [pv_done[-1]])
        bld.setlu("Yfm_rd", pq_done + pv_done)

        wo_res = []
        last_wo_mm = None
        for w in range(4):
            ow_ops = []
            qb2_bld = {}
            pend = []
            ozh = {0: pr.op("gpsimd", lambda e: e.memset(PT, 0.0),
                            deps=bld.war("PTz") + bld.war("arena_ep2"))}

            def flush_pend(force=False):
                rest = []
                while pend:
                    d, fn = pend.pop(0)
                    if d <= 0 or force:
                        fn()
                    else:
                        rest.append((d - 1, fn))
                pend.extend(rest)

            for h in range(NH):
                fc, r0 = h // 2, (h % 2) * 64
                pt_ev = []
                for qt in range(4):
                    pS = pS2[qt % 2]
                    Pt_q = Pt if qt % 2 == 0 else Pt2_
                    gqt = 4 * w + qt
                    kb_lo = qt if w > 0 else 4
                    kb_hi = qt + 4
                    scol0, scol1 = 128 * kb_lo, 128 * (kb_hi + 1)
                    if w > 0 and fc not in qb2_bld:
                        qb2_bld[fc] = pr.op("vector", (lambda fc, w: lambda v: v.tensor_single_scalar(
                            qb2[:], qrot[:, fc * T + w * 512:fc * T + (w + 1) * 512],
                            b2col[:], ALU.mult))(fc, w),
                            deps=[ld_b2c, bld.lu[("qwr", fc, w)]] + bld.war("qb2"))
                    oqb2 = qb2_bld.get(fc) if w > 0 else None

                    def mk_t(fc=fc, r0=r0, qt=qt, w=w, gqt=gqt, kb_lo=kb_lo, kb_hi=kb_hi, pS=pS):
                        def fn(t):
                            for kb in range(kb_lo, kb_hi + 1):
                                gk = 4 * (w - 1) + kb
                                prev = kb < 4
                                lhs = qb2[r0:r0 + 64, qt * 128:(qt + 1) * 128] if prev else \
                                    qrot[r0:r0 + 64, fc * T + gqt * 128:fc * T + (gqt + 1) * 128]
                                msk = maskU if (w > 0 and kb == kb_lo) else (
                                    maskL if kb == kb_hi else None)
                                instr = t.matmul(pS[:, kb * 128:(kb + 1) * 128], lhs,
                                                 kAv[r0:r0 + 64, fc * T + gk * 128:fc * T + (gk + 1) * 128],
                                                 start=True, stop=msk is None)
                                if msk is not None:
                                    instr = t.matmul(pS[:, kb * 128:(kb + 1) * 128],
                                                     idn[:], msk[:], start=False, stop=True)
                            return instr
                        return fn
                    deps = (pq_done + pv_done + bld.war(("pS", qt % 2)) +
                            bld.war(id(pB)) + bld.war("pNk") + [ld_mu, ld_ml, ld_id] +
                            ([oqb2] if oqb2 else []))
                    ot = pr.op("tensor", mk_t(), deps=deps)
                    if oqb2 is not None:
                        bld.setlu("qb2", bld.war("qb2") + [ot])
                    flush_pend()
                    a0 = (qt % 2) * 4
                    onx = pr.op("vector", (lambda scol0, scol1, pS, a0: lambda v: v.tensor_reduce(
                        att[:, a0:a0 + 1], pS[:, scol0:scol1], AX.X, ALU.max, negate=True))(scol0, scol1, pS, a0),
                        deps=[ot] + bld.war(("attsc", qt % 2)))
                    rq8c = rq8[:, h * TC + gqt:h * TC + gqt + 1]
                    obx = pr.op("vector", (lambda rq8c, a0: lambda v: v.tensor_mul(
                        att[:, a0:a0 + 1], att[:, a0:a0 + 1], rq8c))(rq8c, a0),
                        deps=[onx] + bld.war(("rqd", fc, gqt // 4)))
                    oex = pr.op("scalar", (lambda scol0, scol1, rq8c, pS, a0, Pt_q=Pt_q: lambda s: s.activation(
                        Pt_q[:, scol0:scol1], pS[:, scol0:scol1], AF.Exp,
                        bias=att[:, a0:a0 + 1], scale=rq8c, accum_out=att[:, a0 + 2:a0 + 3]))(scol0, scol1, rq8c, pS, a0),
                        deps=[obx] + bld.war(("Pt", qt % 2)) + bld.war("arena_ep2") +
                        (bld.war("s12") if qt % 2 else []))
                    bld.setlu(("pS", qt % 2), oex)
                    orc = pr.op("vector", (lambda a0: lambda v: v.reciprocal(
                        att[:, a0 + 3:a0 + 4], att[:, a0 + 2:a0 + 3]))(a0), deps=[oex])
                    if qt % 2 == 0:
                        onr = pr.op("scalar", (lambda scol0, scol1, a0, Pt_q=Pt_q: lambda s: s.activation(
                            Pt_q[:, scol0:scol1], Pt_q[:, scol0:scol1], AF.Identity,
                            scale=att[:, a0 + 3:a0 + 4]))(scol0, scol1, a0), deps=[orc])
                    else:
                        onr = pr.op("vector", (lambda scol0, scol1, a0, Pt_q=Pt_q: lambda v: v.tensor_scalar_mul(
                            Pt_q[:, scol0:scol1], Pt_q[:, scol0:scol1], att[:, a0 + 3:a0 + 4]))(scol0, scol1, a0), deps=[orc])
                    bld.setlu(("attsc", qt % 2), onr)

                    def emit_trans(qt=qt, kb_lo=kb_lo, kb_hi=kb_hi, onr=onr, h=h,
                                   Pt_q=Pt_q, pt_ev=pt_ev):
                        ptp = pT[qt % 2]
                        warp = bld.war(id(ptp))

                        def mk_tt(kb_lo=kb_lo, kb_hi=kb_hi, ptp=ptp, Pt_q=Pt_q):
                            def fn(t):
                                for j, kb in enumerate(range(kb_lo, kb_hi + 1)):
                                    instr = t.transpose(ptp[:, j * 128:(j + 1) * 128],
                                                        Pt_q[:, kb * 128:(kb + 1) * 128], idn[:])
                                return instr
                            return fn
                        ott = pr.op("tensor", mk_tt(), deps=[onr] + warp)
                        bld.setlu(("Pt", qt % 2), ott)
                        if qt % 2:
                            bld.setlu("s12", ott)
                        PT3 = PT.rearrange("p (k c) -> p k c", c=512)

                        def mk_tv(qt=qt, kb_lo=kb_lo, kb_hi=kb_hi, ptp=ptp, PT3=PT3):
                            nkb = kb_hi - kb_lo + 1
                            dst = PT3[:, kb_lo:kb_hi + 1, qt * 128:(qt + 1) * 128]
                            p3 = ptp[:].rearrange("p (k c) -> p k c", c=128)[:, 0:nkb, :]
                            return lambda v: ecopy(v, dst, p3)
                        oev = pr.op("scalar" if qt % 2 else "vector", mk_tv(),
                                    deps=[ott, ozh[h]] + bld.war("PTz2"))
                        bld.setlu(id(ptp), oev)
                        pt_ev.append(oev)
                    pend.append((1, emit_trans))

                def emit_av(h=h, w=w, fc=fc, r0=r0, pt_ev=pt_ev):
                    pav = pA[h % 2]
                    warA = bld.war(id(pav))
                    kbs = (4, 5, 6, 7) if w == 0 else (3, 4, 0, 1, 2, 5, 6, 7)

                    def mk_av(w=w, h=h, pav=pav, kbs=kbs):
                        def fn(t):
                            for i, kb in enumerate(kbs):
                                gk = 4 * (w - 1) + kb
                                if w == 0:
                                    c0, c1_ = (kb - 4) * 128, 512
                                else:
                                    c0, c1_ = max(0, kb - 4) * 128, min(3, kb) * 128 + 128
                                instr = t.matmul(pav[0:64, c0:c1_],
                                                 v_tm[:, gk * 512 + h * 64:gk * 512 + h * 64 + 64],
                                                 PT[:, kb * 512 + c0:kb * 512 + c1_],
                                                 start=(i == 0), stop=(i == len(kbs) - 1))
                            return instr
                        return fn
                    oav = pr.op("tensor", mk_av(), deps=list(pt_ev) + pv_done + warA)
                    oevav = pr.op("scalar", (lambda fc, r0, pav: lambda v: ecopy(v,
                        o_w[r0:r0 + 64, fc * 512:(fc + 1) * 512], pav[0:64, :]))(fc, r0, pav),
                        deps=[oav] + bld.war(("o_w", fc, r0)))
                    bld.setlu(id(pav), oevav)
                    bld.setlu("PTz", oav)
                    ow_ops.append(oevav)
                    if h + 1 < NH:
                        ozh[h + 1] = pr.op("gpsimd", lambda e: e.memset(PT, 0.0),
                                           deps=bld.war("PTz") + bld.war("arena_ep2"))
                pend.append((2, emit_av))
            flush_pend(force=True)
            for mm in range(4):
                m = 4 * w + mm
                for eb in range(2):
                    p = pA[(mm * 2 + eb) % 2]
                    war = bld.war(id(p))

                    def mk_t(mm=mm, eb=eb, p=p):
                        def fn(t):
                            for kc in range(4):
                                instr = t.matmul(p[:],
                                                 o_w[:, kc * 512 + mm * 128:kc * 512 + (mm + 1) * 128],
                                                 wo_sb[:, kc * D + eb * 512:kc * D + (eb + 1) * 512],
                                                 start=(kc == 0), stop=(kc == 3))
                            return instr
                        return fn
                    ot = pr.op("tensor", mk_t(), deps=ow_ops + [ld_wo] + war)
                    last_wo_mm = ot
                    oe = pr.op("vector", (lambda m, eb, p: lambda v: v.tensor_add(
                        Y[:, m * D + eb * 512:m * D + (eb + 1) * 512],
                        Y[:, m * D + eb * 512:m * D + (eb + 1) * 512], p[:]))(m, eb, p),
                        deps=[ot] + bld.war(("Yc", m)))
                    bld.setlu(id(p), oe)
                    bld.setlu(("Yc", m), oe)
                    wo_res.append(oe)
            for h in range(NH):
                fc, r0 = h // 2, (h % 2) * 64
                bld.setlu(("o_w", fc, r0), last_wo_mm)
        fence = wo_res[-1]
        bld.setlu(id(pB), bld.war(id(pB)) + bld.war(("pS", 0)) + bld.war(("pS", 1)))
        bld.setlu("pNk", bld.war("pNk") + bld.war(("pS", 1)))

        ld_g = pr.op("gpsimd", (lambda L: lambda e: e.dma_start(
            out=rows[:, 0:D], in_=lng_d[L][None, :]))(L), deps=bld.war("rows"), dma=True)
        bcs = emit_bcast_rows(pr, [(rows[:, 0:D], gamr, D)], ones_row, pA, [ld_g, o_ones])
        bc_g = bcs[0]
        gb_ops = []
        for f in range(DCH):
            og = pr.op("tensor", (lambda f: lambda t: t.matmul(
                pB[:, 6:7], rows[:, f * 128:(f + 1) * 128], onescol[0:1, :],
                start=True, stop=True))(f), deps=[ld_g, o_onec] + bld.war(id(pB)))
            oe1 = pr.op("vector", (lambda f: lambda v: v.tensor_copy(
                gbc[:, f:f + 1], pB[:, 6:7]))(f), deps=[og])
            bld.setlu(id(pB), oe1)
            gb_ops.append(oe1)
        ld_b = pr.op("gpsimd", (lambda L: lambda e: e.dma_start(
            out=rows[:, 0:D], in_=lnb_d[L][None, :]))(L),
            deps=bc_g + gb_ops, dma=True)
        bcs = emit_bcast_rows(pr, [(rows[:, 0:D], betr, D)], ones_row, pA, [ld_b, o_ones])
        bc_b = bcs[0]
        for f in range(DCH):
            ob_ = pr.op("tensor", (lambda f: lambda t: t.matmul(
                pB[:, 6:7], rows[:, f * 128:(f + 1) * 128], onescol[0:1, :],
                start=True, stop=True))(f), deps=[ld_b, o_onec] + bld.war(id(pB)))
            oe2 = pr.op("vector", (lambda f: lambda v: v.tensor_copy(
                gbc[:, 8 + f:9 + f], pB[:, 6:7]))(f), deps=[ob_])
            bld.setlu(id(pB), oe2)
            gb_ops.append(oe2)
        bld.setlu("rows", bc_b + gb_ops)
        hops = []
        for c in range(TC):
            Yc = Y[:, c * D:(c + 1) * D]
            dep = bld.war(("Yc", c))
            o1 = pr.op("vector", (lambda c, Yc: lambda v: v.tensor_reduce(
                hs["s1a"][:, c:c + 1], Yc[:, 0:512], AX.X, ALU.add))(c, Yc), deps=dep)
            o2 = pr.op("vector", (lambda c, Yc: lambda v: v.tensor_reduce(
                hs["s1b"][:, c:c + 1], Yc[:, 512:1024], AX.X, ALU.add))(c, Yc), deps=dep)
            o3 = pr.op("scalar", (lambda c, Yc: lambda s: s.activation(
                junk[:, 0:512], Yc[:, 0:512], AF.Square, accum_out=hs["s2a"][:, c:c + 1]))(c, Yc),
                deps=dep)
            o4 = pr.op("scalar", (lambda c, Yc: lambda s: s.activation(
                junk[:, 0:512], Yc[:, 512:1024], AF.Square, accum_out=hs["s2b"][:, c:c + 1]))(c, Yc),
                deps=dep)
            hops += [o1, o2, o3, o4]
        osz = pr.op("gpsimd", lambda e: e.memset(hs["sh1"][0:1, 0:1], 0.0), deps=bld.war("hsh"))
        osz2 = pr.op("gpsimd", lambda e: e.memset(hs["sh2"][0:1, 0:1], 0.0), deps=bld.war("hsh"))
        od1 = pr.op("gpsimd", lambda e: e.dma_start(out=hs["sh1"][1:128, :], in_=hs["s1b"][0:127, :]),
                    deps=hops + bld.war("hsh"), dma=True)
        od2 = pr.op("gpsimd", lambda e: e.dma_start(out=hs["sh1"][0:1, 1:TC], in_=hs["s1b"][127:128, 0:TC - 1]),
                    deps=hops, dma=True)
        od3 = pr.op("gpsimd", lambda e: e.dma_start(out=hs["sh2"][1:128, :], in_=hs["s2b"][0:127, :]),
                    deps=hops, dma=True)
        od4 = pr.op("gpsimd", lambda e: e.dma_start(out=hs["sh2"][0:1, 1:TC], in_=hs["s2b"][127:128, 0:TC - 1]),
                    deps=hops, dma=True)
        shd = [osz, osz2, od1, od2, od3, od4]
        n = TC
        s1 = pr.op("vector", lambda v: v.tensor_add(st2["s1"][:, 0:n], hs["s1a"][:, 0:n], hs["sh1"][:, 0:n]),
                   deps=shd + hops)
        s2 = pr.op("vector", lambda v: v.tensor_add(st2["s2"][:, 0:n], hs["s2a"][:, 0:n], hs["sh2"][:, 0:n]),
                   deps=shd + hops)
        o3b = pr.op("vector", lambda v: v.tensor_scalar_mul(st2["mu"][:, 0:n], st2["s1"][:, 0:n], 1.0 / D),
                    deps=[s1])
        o4b = pr.op("vector", lambda v: v.tensor_mul(st2["var"][:, 0:n], st2["mu"][:, 0:n], st2["mu"][:, 0:n]),
                    deps=[o3b])
        o5b = pr.op("vector", lambda v: v.scalar_tensor_tensor(st2["var"][:, 0:n], st2["s2"][:, 0:n],
                                                               1.0 / D, st2["var"][:, 0:n],
                                                               ALU.mult, ALU.subtract), deps=[s2, o4b])
        eps_ap, eps_op = bld.constcol(pr, 1e-5)
        o6b = pr.op("scalar", lambda s: s.activation(st2["sd"][:, 0:n], st2["var"][:, 0:n], AF.Sqrt,
                                                     bias=eps_ap), deps=[o5b, eps_op])
        o7b = pr.op("vector", lambda v: v.reciprocal(st2["rstd"][:, 0:n], st2["sd"][:, 0:n]), deps=[o6b])
        o8b = pr.op("vector", lambda v: v.scalar_tensor_tensor(st2["nmr"][:, 0:n], st2["mu"][:, 0:n],
                                                               -1.0, st2["rstd"][:, 0:n],
                                                               ALU.mult, ALU.mult), deps=[o7b])
        oj1 = pr.op("gpsimd", lambda e: e.dma_start(out=hs["rsJ"][0:127, :], in_=st2["rstd"][1:128, :]),
                    deps=[o7b, o8b] + bld.war("hsJ"), dma=True)
        oj2 = pr.op("gpsimd", lambda e: e.dma_start(out=hs["rsJ"][127:128, 0:TC - 1], in_=st2["rstd"][0:1, 1:TC]),
                    deps=[o7b, o8b], dma=True)
        oj3 = pr.op("gpsimd", lambda e: e.dma_start(out=hs["nmJ"][0:127, :], in_=st2["nmr"][1:128, :]),
                    deps=[o8b], dma=True)
        oj4 = pr.op("gpsimd", lambda e: e.dma_start(out=hs["nmJ"][127:128, 0:TC - 1], in_=st2["nmr"][0:1, 1:TC]),
                    deps=[o8b], dma=True)
        jd = [oj1, oj2, oj3, oj4, o8b]
        ocp0 = pr.op("vector", lambda v: v.tensor_copy(nm0b[:], st2["nmr"][0:1, 0:1]),
                     deps=[o8b] + bld.war("nm0b"))
        bld.setlu("nm0b", ocp0)
        ob0 = pr.op("tensor", lambda t: t.matmul(pB[:, 5:6], ones_row[:], nm0b[:],
                                                 start=True, stop=True),
                    deps=[ocp0, o_ones] + bld.war(id(pB)))
        ob0e = pr.op("vector", lambda v: v.tensor_copy(n0b[:], pB[:, 5:6]), deps=[ob0])
        bld.setlu(id(pB), ob0e)

        h2ops = []
        for tc in range(TC):
            hst = h2st
            depw = bld.war("h2st")
            oa = pr.op("scalar", (lambda tc, hst: lambda s: s.activation(
                hst[:, 0:512], Y[:, tc * D:tc * D + 512], AF.Identity,
                bias=st2["nmr"][:, tc:tc + 1], scale=st2["rstd"][:, tc:tc + 1]))(tc, hst),
                deps=[o8b] + depw + [fence])
            ob = pr.op("scalar", (lambda tc, hst: lambda s: s.activation(
                hst[:, 512:1024], Y[:, tc * D + 512:tc * D + 1024], AF.Identity,
                bias=hs["nmJ"][:, tc:tc + 1], scale=hs["rsJ"][:, tc:tc + 1]))(tc, hst),
                deps=jd + depw)
            og1 = pr.op("vector", (lambda hst: lambda v: v.tensor_mul(
                hst[:, 0:1024], hst[:, 0:1024], gamr[:, 0:1024]))(hst), deps=[oa, ob] + bc_g)
            og2 = pr.op("vector", (lambda hst: lambda v: v.tensor_add(
                hst[:, 0:1024], hst[:, 0:1024], betr[:, 0:1024]))(hst), deps=[og1] + bc_b)
            p = pT[tc % 2]
            war = bld.war(id(p))

            def mk_t(hst=hst, p=p):
                def fn(t):
                    for f in range(DCH):
                        instr = t.transpose(p[:, f * 128:(f + 1) * 128],
                                            hst[:, f * 128:(f + 1) * 128], idn[:])
                    return instr
                return fn
            ot = pr.op("tensor", mk_t(), deps=[og2] + war)
            bld.setlu("h2st", ot)

            def mk_v(tc=tc, p=p):
                def fn(v):
                    for f in range(DCH):
                        if f < 4:
                            instr = v.tensor_copy(
                                Y_fm[:, f * YW + 1 + tc * 128:f * YW + 1 + (tc + 1) * 128],
                                p[:, f * 128:(f + 1) * 128])
                        else:
                            wdt = 128 if tc < TC - 1 else 127
                            instr = v.tensor_copy(
                                Y_fm[:, f * YW + 2 + tc * 128:f * YW + 2 + tc * 128 + wdt],
                                p[:, f * 128:f * 128 + wdt])
                    return instr
                return fn
            oe = pr.op("vector", mk_v(), deps=[ot] + bld.war("Yfm_rd2") +
                       (bld.war("Yfm_rd") if tc == 0 else []))
            bld.setlu(id(p), oe)
            h2ops.append(oe)
        pad_ops = []
        for f in range(4, DCH):
            opd = pr.op("vector", (lambda f: lambda v: v.scalar_tensor_tensor(
                Y_fm[:, f * YW + 1:f * YW + 2], n0b[:], 1.0, junk[:, 0:1], ALU.mult, ALU.bypass))(f),
                deps=[ob0e] + h2ops) if False else None
            op1 = pr.op("vector", (lambda f: lambda v: v.tensor_single_scalar(
                Y_fm[:, f * YW + 1:f * YW + 2], n0b[:], gbc[:, f:f + 1], ALU.mult))(f),
                deps=[ob0e] + gb_ops + h2ops)
            op2 = pr.op("vector", (lambda f: lambda v: v.tensor_single_scalar(
                Y_fm[:, f * YW + 1:f * YW + 2], Y_fm[:, f * YW + 1:f * YW + 2],
                gbc[:, 8 + f:9 + f], ALU.add))(f), deps=[op1])
            pad_ops.append(op2)
        y2done = h2ops + pad_ops
        bld.setlu("hsh", [s1, s2])
        bld.setlu("hsJ", y2done)

        ld_b1 = pr.op("gpsimd", (lambda L: lambda e: e.dma_start(
            out=rows[:, 0:1408], in_=b1_d[L][None, 0:1408]))(L), deps=bld.war("rows"), dma=True)
        ld_b2r = pr.op("gpsimd", (lambda L: lambda e: e.dma_start(
            out=rows[:, 0:D] if False else junk[0:1, 0:D], in_=b2_d[L][None, :]))(L),
            deps=bld.war("junkrow"), dma=True)
        b1_ops = []
        widths = [128] * 21 + [42]
        segs = [(0, 0, 11), (1408, 11, 22), (2730, 22, 33), (2730 + 1408, 33, 44)]
        for base, c0, c1_ in segs:
            ldp = pr.op("gpsimd", (lambda L, base: lambda e: e.dma_start(
                out=rows[:, 0:min(1408, 5460 - base)],
                in_=b1_d[L][None, base:base + min(1408, 5460 - base)]))(L, base),
                deps=bld.war("rows") + list(b1_ops), dma=True)
            for ci in range(c0, c1_):
                e = ci % 22
                w_ = widths[e]
                off = (ci - c0) * 128
                oga = pr.op("tensor", (lambda off, w_: lambda t: t.matmul(
                    pB[0:w_, 7:8], rows[:, off:off + w_], onescol[0:1, :],
                    start=True, stop=True))(off, w_), deps=[ldp, o_onec] + bld.war(id(pB)))
                oea = pr.op("vector", (lambda ci, w_: lambda v: v.tensor_copy(
                    b1c[0:w_, ci:ci + 1], pB[0:w_, 7:8]))(ci, w_), deps=[oga])
                bld.setlu(id(pB), oea)
                b1_ops.append(oea)
            bld.setlu("rows", list(b1_ops))
        bld.setlu("rows", b1_ops)
        bcs2 = emit_bcast_rows(pr, [(junk[0:1, 0:D], b2r, D)], ones_row, pA, [ld_b2r, o_ones])
        bc_b2 = bcs2[0]
        bld.setlu("junkrow", bc_b2)

        ffn_last = []
        for tb in range(4):
            h4_ops = {}
            for e in range(22):
                me = widths[e]
                w1buf = w1buf_[e % 2]
                lda = pr.op("gpsimd", (lambda L, e, me, w1buf: lambda g: g.dma_start(
                    out=w1buf[:, 0:me * 8].rearrange("p (c q) -> p c q", q=me),
                    in_=w1_d[L, :, e * 128:e * 128 + me].rearrange("(c p) q -> p c q", p=128)))(L, e, me, w1buf),
                    deps=bld.war(("w1b", e % 2, 0)) + ([fence] if tb == 0 and e == 0 else []), dma=True)
                ldg = pr.op("gpsimd", (lambda L, e, me, w1buf: lambda g: g.dma_start(
                    out=w1buf[:, 1024:1024 + me * 8].rearrange("p (c q) -> p c q", q=me),
                    in_=w1_d[L, :, 2730 + e * 128:2730 + e * 128 + me].rearrange("(c p) q -> p c q", p=128)))(L, e, me, w1buf),
                    deps=bld.war(("w1b", e % 2, 1)), dma=True)
                pa_ = pA[e % 2]
                war = bld.war(id(pa_))

                def mk_ta(e=e, me=me, tb=tb, pa_=pa_, w1buf=w1buf):
                    def fn(t):
                        for kc in range(DCH):
                            instr = t.matmul(pa_[0:me, :],
                                             w1buf[:, kc * me:(kc + 1) * me],
                                             Y_fm[:, kc * YW + 1 + tb * 512:kc * YW + 1 + (tb + 1) * 512],
                                             start=(kc == 0), stop=(kc == DCH - 1))
                        return instr
                    return fn
                ota = pr.op("tensor", mk_ta(), deps=[lda] + y2done + war)
                oea = pr.op("scalar", (lambda e, me, pa_: lambda s: s.activation(
                    agst[:, 0:512][0:me, :], pa_[0:me, :], AF.Identity,
                    bias=b1c[0:me, e:e + 1]))(e, me, pa_),
                    deps=[ota] + b1_ops + bld.war(("ag", 0)))
                bld.setlu(id(pa_), oea)
                war = bld.war(id(pa_))

                def mk_tg(e=e, me=me, tb=tb, pa_=pa_, w1buf=w1buf):
                    def fn(t):
                        for kc in range(DCH):
                            instr = t.matmul(pa_[0:me, :],
                                             w1buf[:, 1024 + kc * me:1024 + (kc + 1) * me],
                                             Y_fm[:, kc * YW + 1 + tb * 512:kc * YW + 1 + (tb + 1) * 512],
                                             start=(kc == 0), stop=(kc == DCH - 1))
                        return instr
                    return fn
                otg = pr.op("tensor", mk_tg(), deps=[ldg] + y2done + war)
                oeg = pr.op("scalar", (lambda e, me, pa_: lambda s: s.activation(
                    agst[:, 512:1024][0:me, :], pa_[0:me, :], AF.Gelu,
                    bias=b1c[0:me, 22 + e:23 + e]))(e, me, pa_),
                    deps=[otg] + b1_ops + bld.war(("ag", 1)))
                bld.setlu(id(pa_), oeg)
                bld.setlu(("w1b", e % 2, 0), ota)
                bld.setlu(("w1b", e % 2, 1), otg)
                oh = pr.op("vector", (lambda e, me: lambda v: v.tensor_mul(
                    h4[:, e * 512:(e + 1) * 512][0:me, :], agst[0:me, 0:512],
                    agst[0:me, 512:1024]))(e, me), deps=[oea, oeg] + bld.war(("h4", e)))
                bld.setlu(("ag", 0), oh)
                bld.setlu(("ag", 1), oh)
                h4_ops[e] = oh
            for eb in range(2):
                ldw2a = pr.op("gpsimd", (lambda L, eb: lambda g: g.dma_start(
                    out=w2buf[:, 0:11 * 512].rearrange("p (c q) -> p c q", q=512),
                    in_=w2_d[L, 0:1408, eb * 512:(eb + 1) * 512].rearrange("(c p) q -> p c q", p=128)))(L, eb),
                    deps=bld.war("w2ba"), dma=True)
                ldw2b = pr.op("gpsimd", (lambda L, eb: lambda g: g.dma_start(
                    out=w2buf[:, 11 * 512:21 * 512].rearrange("p (c q) -> p c q", q=512),
                    in_=w2_d[L, 1408:2688, eb * 512:(eb + 1) * 512].rearrange("(c p) q -> p c q", p=128)))(L, eb),
                    deps=bld.war("w2bb"), dma=True)
                ldw2c = pr.op("gpsimd", (lambda L, eb: lambda g: g.dma_start(
                    out=w2buf[0:42, 21 * 512:22 * 512],
                    in_=w2_d[L, 2688:2730, eb * 512:(eb + 1) * 512]))(L, eb),
                    deps=bld.war("w2bb"), dma=True)
                w2mm = []
                w2mma = []
                for mm in range(4):
                    m = 4 * tb + mm
                    p = pA[mm % 2]
                    war = bld.war(id(p))

                    def mk_ta(mm=mm, p=p):
                        def fn(t):
                            for kc in range(11):
                                me = widths[kc]
                                instr = t.matmul(p[:],
                                                 h4[:, kc * 512 + mm * 128:kc * 512 + (mm + 1) * 128][0:me, :],
                                                 w2buf[:, kc * 512:(kc + 1) * 512][0:me, :],
                                                 start=(kc == 0), stop=False)
                            return instr
                        return fn
                    ota = pr.op("tensor", mk_ta(), deps=[ldw2a] + list(h4_ops.values()) + war)
                    w2mma.append(ota)

                    def mk_tb(mm=mm, p=p):
                        def fn(t):
                            for kc in range(11, 22):
                                me = widths[kc]
                                instr = t.matmul(p[:],
                                                 h4[:, kc * 512 + mm * 128:kc * 512 + (mm + 1) * 128][0:me, :],
                                                 w2buf[:, kc * 512:(kc + 1) * 512][0:me, :],
                                                 start=False, stop=(kc == 21))
                            return instr
                        return fn
                    ot = pr.op("tensor", mk_tb(), deps=[ota, ldw2b, ldw2c], )
                    w2mm.append(ot)
                    oe1 = pr.op("vector", (lambda m, eb, p: lambda v: v.tensor_add(
                        Y[:, m * D + eb * 512:m * D + (eb + 1) * 512],
                        Y[:, m * D + eb * 512:m * D + (eb + 1) * 512], p[:]))(m, eb, p),
                        deps=[ot] + bld.war(("Yc", m)))
                    oe2 = pr.op("vector", (lambda m, eb: lambda v: v.tensor_add(
                        Y[:, m * D + eb * 512:m * D + (eb + 1) * 512],
                        Y[:, m * D + eb * 512:m * D + (eb + 1) * 512],
                        b2r[:, eb * 512:(eb + 1) * 512]))(m, eb), deps=[oe1] + bc_b2)
                    bld.setlu(id(p), oe2)
                    bld.setlu(("Yc", m), oe2)
                    ffn_last.append(oe2)
                bld.setlu("w2ba", w2mma)
                bld.setlu("w2bb", w2mm)
            for e in range(22):
                bld.setlu(("h4", e), w2mm)
        bld.setlu("arena_ep", ffn_last)
        bld.setlu("arena_ep2", ffn_last)
        bld.setlu("Yfm_rd", ffn_last)
        bld.setlu("Yfm_rd2", None)
        bld.setlu("qksc", ffn_last)

    def get_F(m):
        return Y[:, m * D:(m + 1) * D], bld.war(("Yc", m))

    stores = emit_sandwich_tail(bld, pr, get_F, st3, xch, x_src, xo_dst, nsl=2)
    pr.op("sync", lambda e: e.nop(), deps=stores)


def block2_tables():
    HD = 64
    t = np.arange(T, dtype=np.float64)
    half = np.arange(0, HD, 2, dtype=np.float64)
    inv_freq = 1.0 / (10000.0 ** (half / HD))
    base = (half + 0.4 * HD) / (1.4 * HD)
    fm = np.concatenate([np.arange(32), np.arange(32)])
    freqs = t[None, :] * inv_freq[fm][:, None]
    cos = np.cos(freqs)
    sin = np.sin(freqs)

    tm = (t % 512) / 256.0
    xq = base[fm][:, None] ** tm[None, :]
    xk = base[fm][:, None] ** (-tm[None, :])
    c1q = (cos * xq)
    c2q = (sin * xq)
    c1k = (cos * xk)
    c2k = (sin * xk)
    b2c = (base[fm] ** 2.0)
    dup = lambda a: np.concatenate([a, a], 0).astype(np.float32)
    c1q, c2q, c1k, c2k = dup(c1q), dup(c2q), dup(c1k), dup(c2k)
    b2col = np.concatenate([b2c, b2c]).astype(np.float32)[:, None]
    PT_ = np.zeros((128, 128))
    for f in range(128):
        h0 = (f // 64) * 64
        fl = f % 64
        if fl < 32:
            PT_[h0 + fl + 32, f] = -1.0
        else:
            PT_[h0 + fl - 32, f] = 1.0
    ii, jj = np.mgrid[0:128, 0:128]
    mU = np.where(jj >= ii, 0.0, -1e9).astype(np.float32)
    mL = np.where(jj <= ii, 0.0, -1e9).astype(np.float32)
    return c1q, c2q, c1k, c2k, b2col, mU, mL, PT_.astype(np.float32)


def build_block2():
    bld = Bld()
    nc = bld.nc
    dram = (
        nc.dram_tensor("x", [T, D], F32, kind="ExternalInput").ap(),
        nc.dram_tensor("wqkv", [2, D, 1536], F32, kind="ExternalInput").ap(),
        nc.dram_tensor("q_scale", [2, 64], F32, kind="ExternalInput").ap(),
        nc.dram_tensor("k_scale", [2, 64], F32, kind="ExternalInput").ap(),
        nc.dram_tensor("wo", [2, 512, D], F32, kind="ExternalInput").ap(),
        nc.dram_tensor("ln_g", [2, D], F32, kind="ExternalInput").ap(),
        nc.dram_tensor("ln_b", [2, D], F32, kind="ExternalInput").ap(),
        nc.dram_tensor("w1", [2, D, 5460], F32, kind="ExternalInput").ap(),
        nc.dram_tensor("b1", [2, 5460], F32, kind="ExternalInput").ap(),
        nc.dram_tensor("w2", [2, 2730, D], F32, kind="ExternalInput").ap(),
        nc.dram_tensor("b2", [2, D], F32, kind="ExternalInput").ap(),
        nc.dram_tensor("iden", [128, 128], F32, kind="ExternalInput").ap(),
        nc.dram_tensor("c1q", [128, T], F32, kind="ExternalInput").ap(),
        nc.dram_tensor("c2q", [128, T], F32, kind="ExternalInput").ap(),
        nc.dram_tensor("c1k", [128, T], F32, kind="ExternalInput").ap(),
        nc.dram_tensor("c2k", [128, T], F32, kind="ExternalInput").ap(),
        nc.dram_tensor("b2c", [128, 1], F32, kind="ExternalInput").ap(),
        nc.dram_tensor("maskU", [128, 128], F32, kind="ExternalInput").ap(),
        nc.dram_tensor("maskL", [128, 128], F32, kind="ExternalInput").ap(),
        nc.dram_tensor("rotm", [128, 128], F32, kind="ExternalInput").ap(),
        nc.dram_tensor("xo", [T, D], F32, kind="ExternalOutput").ap(),
    )
    with bld.es:
        pr = Prog(bld, "b2")
        body_block2(bld, pr, dram)
        pr.run()
    return bld.nc


def build_fused():
    bld = Bld()
    nc = bld.nc

    def din(name, shape):
        return nc.dram_tensor(name, shape, F32, kind="ExternalInput").ap()

    x_in = din("x", [T, D])
    h1 = nc.dram_tensor("h1", [T, D], F32, kind="Internal").ap()
    h2 = nc.dram_tensor("h2", [T, D], F32, kind="Internal").ap()
    h3 = nc.dram_tensor("h3", [T, D], F32, kind="Internal").ap()
    xo = nc.dram_tensor("out", [T, D], F32, kind="ExternalOutput").ap()
    iden = din("iden", [128, 128])

    d1 = (x_in, din("Wq", [D, D]), din("Wk", [D, D]), din("Wv", [D, D]),
          din("Wo", [D, D]), din("bo", [D]),
          din("Aq", [16, D]), din("Bq", [D, 16]), din("Ak", [16, D]), din("Bk", [D, 16]),
          din("Av", [16, D]), din("Bv", [D, 16]), din("Ao", [16, D]), din("Bo", [D, 16]),
          iden, h1)
    d2 = (h1, din("wqkv", [2, D, 1536]), din("q_scale", [2, 64]), din("k_scale", [2, 64]),
          din("wo2", [2, 512, D]), din("ln_g", [2, D]), din("ln_b", [2, D]),
          din("w1", [2, D, 5460]), din("b1", [2, 5460]), din("w2", [2, 2730, D]),
          din("b2", [2, D]), iden, din("c1q", [128, T]), din("c2q", [128, T]),
          din("c1k", [128, T]), din("c2k", [128, T]), din("b2c", [128, 1]),
          din("maskU", [128, 128]), din("maskL", [128, 128]), din("rotm", [128, 128]), h2)
    d3 = (h2, din("w_qkv", [D, 1536]), din("b_qkv", [1536]), din("w_fc", [512, D]),
          din("b_fc", [D]), din("gamma", [D]), din("beta", [D]), iden, h3)
    d4 = (h3, din("gate_w", [16, D]), din("bias_e", [16]), din("up_w", [16, 512, D]),
          din("down_w", [16, D, 512]), din("shared_up", [2048, D]),
          din("shared_down", [D, 2048]), iden, xo)

    with bld.es:
        for body, dd in ((body_block1, d1), (body_block2, d2),
                         (body_block3, d3), (body_block4, d4)):
            bld.push_scope()
            pr = Prog(bld)
            body(bld, pr, dd)
            pr.run(barrier=True)
            bld.pop_scope()
    return bld.nc


FUSED = True
_NC_CACHE = {}


def _ncs():
    if "b" not in _NC_CACHE:
        _NC_CACHE["b"] = (build_block1(), build_block2(), build_block3(), build_block4())
    return _NC_CACHE["b"]


def kernel_fused(**inputs):
    g = {k: np.ascontiguousarray(np.asarray(v, dtype=np.float32)) for k, v in inputs.items()}
    iden = np.eye(128, dtype=np.float32)
    c1q, c2q, c1k, c2k, b2c, mU, mL, rotm = block2_tables()
    if "f" not in _NC_CACHE:
        _NC_CACHE["f"] = build_fused()
    nc = _NC_CACHE["f"]
    x = g["L0_attn_x"]
    ims = []
    for c in range(NCORE):
        ims.append({
            "x": np.ascontiguousarray(x[c]), "iden": iden,
            "Wq": g["L0_attn_Wq"], "Wk": g["L0_attn_Wk"], "Wv": g["L0_attn_Wv"],
            "Wo": g["L0_attn_Wo"], "bo": g["L0_attn_bo"],
            "Aq": g["L0_attn_Aq"], "Bq": g["L0_attn_Bq"], "Ak": g["L0_attn_Ak"],
            "Bk": g["L0_attn_Bk"], "Av": g["L0_attn_Av"], "Bv": g["L0_attn_Bv"],
            "Ao": g["L0_attn_Ao"], "Bo": g["L0_attn_Bo"],
            "wqkv": g["L0_ffn_wqkv"], "q_scale": g["L0_ffn_q_scale"],
            "k_scale": g["L0_ffn_k_scale"], "wo2": g["L0_ffn_wo"],
            "ln_g": g["L0_ffn_ln_g"], "ln_b": g["L0_ffn_ln_b"],
            "w1": g["L0_ffn_w1"], "b1": g["L0_ffn_b1"], "w2": g["L0_ffn_w2"],
            "b2": g["L0_ffn_b2"], "c1q": c1q, "c2q": c2q, "c1k": c1k, "c2k": c2k,
            "b2c": b2c, "maskU": mU, "maskL": mL, "rotm": rotm,
            "w_qkv": g["L1_attn_w_qkv"], "b_qkv": g["L1_attn_b_qkv"],
            "w_fc": g["L1_attn_w_fc"], "b_fc": g["L1_attn_b_fc"],
            "gamma": g["L1_attn_gamma"], "beta": g["L1_attn_beta"],
            "gate_w": g["L1_ffn_gate_w"], "bias_e": g["L1_ffn_bias_e"],
            "up_w": g["L1_ffn_up_w"], "down_w": g["L1_ffn_down_w"],
            "shared_up": g["L1_ffn_shared_up"], "shared_down": g["L1_ffn_shared_down"],
        })
    res = run_bass_kernel_spmd(nc, ims, core_ids=list(range(NCORE)))
    return np.stack([np.asarray(res.results[c]["out"], dtype=np.float32)
                     for c in range(NCORE)], axis=0)


def kernel(**inputs):
    if FUSED:
        return kernel_fused(**inputs)
    g = {k: np.ascontiguousarray(np.asarray(v, dtype=np.float32)) for k, v in inputs.items()}
    iden = np.eye(128, dtype=np.float32)
    c1q, c2q, c1k, c2k, b2c, mU, mL, rotm = block2_tables()
    nc1, nc2, nc3, nc4 = _ncs()

    x = g["L0_attn_x"]
    xs = [np.ascontiguousarray(x[c]) for c in range(NCORE)]

    im1 = [{"x": xs[c], "Wq": g["L0_attn_Wq"], "Wk": g["L0_attn_Wk"],
            "Wv": g["L0_attn_Wv"], "Wo": g["L0_attn_Wo"], "bo": g["L0_attn_bo"],
            "Aq": g["L0_attn_Aq"], "Bq": g["L0_attn_Bq"], "Ak": g["L0_attn_Ak"],
            "Bk": g["L0_attn_Bk"], "Av": g["L0_attn_Av"], "Bv": g["L0_attn_Bv"],
            "Ao": g["L0_attn_Ao"], "Bo": g["L0_attn_Bo"], "iden": iden}
           for c in range(NCORE)]
    r1 = run_bass_kernel_spmd(nc1, im1, core_ids=list(range(NCORE)))
    xs = [np.ascontiguousarray(np.asarray(r1.results[c]["xo"], dtype=np.float32))
          for c in range(NCORE)]

    im2 = [{"x": xs[c], "wqkv": g["L0_ffn_wqkv"], "q_scale": g["L0_ffn_q_scale"],
            "k_scale": g["L0_ffn_k_scale"], "wo": g["L0_ffn_wo"], "ln_g": g["L0_ffn_ln_g"],
            "ln_b": g["L0_ffn_ln_b"], "w1": g["L0_ffn_w1"], "b1": g["L0_ffn_b1"],
            "w2": g["L0_ffn_w2"], "b2": g["L0_ffn_b2"], "iden": iden,
            "c1q": c1q, "c2q": c2q, "c1k": c1k, "c2k": c2k, "b2c": b2c,
            "maskU": mU, "maskL": mL, "rotm": rotm} for c in range(NCORE)]
    r2 = run_bass_kernel_spmd(nc2, im2, core_ids=list(range(NCORE)))
    xs = [np.ascontiguousarray(np.asarray(r2.results[c]["xo"], dtype=np.float32))
          for c in range(NCORE)]

    im3 = [{"x": xs[c], "w_qkv": g["L1_attn_w_qkv"], "b_qkv": g["L1_attn_b_qkv"],
            "w_fc": g["L1_attn_w_fc"], "b_fc": g["L1_attn_b_fc"],
            "gamma": g["L1_attn_gamma"], "beta": g["L1_attn_beta"], "iden": iden}
           for c in range(NCORE)]
    r3 = run_bass_kernel_spmd(nc3, im3, core_ids=list(range(NCORE)))
    xs = [np.ascontiguousarray(np.asarray(r3.results[c]["xo"], dtype=np.float32))
          for c in range(NCORE)]

    im4 = [{"x": xs[c], "gate_w": g["L1_ffn_gate_w"], "bias_e": g["L1_ffn_bias_e"],
            "up_w": g["L1_ffn_up_w"], "down_w": g["L1_ffn_down_w"],
            "shared_up": g["L1_ffn_shared_up"], "shared_down": g["L1_ffn_shared_down"],
            "iden": iden} for c in range(NCORE)]
    r4 = run_bass_kernel_spmd(nc4, im4, core_ids=list(range(NCORE)))
    out = np.stack([np.asarray(r4.results[c]["xo"], dtype=np.float32)
                    for c in range(NCORE)], axis=0)
    return out


# version 18
# speedup vs baseline: 1.2498x; 1.0270x over previous
import contextlib
import math
import numpy as np
import concourse.bass as bass
import concourse.mybir as mybir
from concourse.bass_utils import run_bass_kernel_spmd

F32 = mybir.dt.float32
BF16 = mybir.dt.bfloat16
AF = mybir.ActivationFunctionType
ALU = mybir.AluOpType
AX = mybir.AxisListType

T, D = 2048, 1024
TC, DCH = T // 128, D // 128
NCORE = 8
ENGS = ("sync", "gpsimd", "tensor", "vector", "scalar")


class Op:
    __slots__ = ("i", "eng", "fn", "deps", "dma", "special", "inc_amt", "cum", "needed", "sem_name")

    def __init__(self, i, eng, fn, deps, dma, special):
        self.i, self.eng, self.fn = i, eng, fn
        self.deps = [d for d in deps if d is not None]
        self.dma = dma
        self.special = special
        self.inc_amt = 16 if dma else 1
        self.cum = 0
        self.needed = False
        self.sem_name = None


class Prog:

    def __init__(self, bld, name="p"):
        self.bld = bld
        self.name = name
        self.ops = []

    def op(self, eng, fn, deps=(), dma=False, special=False):
        o = Op(len(self.ops), eng, fn, deps, dma, special)
        self.ops.append(o)
        return o.i

    def run(self, barrier=False):
        bld = self.bld
        ops = self.ops
        entry = {sn: v for sn, v in bld.cnt.items() if v > 0} if barrier else {}
        NDMASEM = 6
        for o in ops:
            if o.dma:
                o.needed = True
            for d in o.deps:
                p = ops[d]
                if p.eng != o.eng or p.special or p.dma or p.eng in ("vector", "scalar", "gpsimd"):
                    p.needed = True
        cum = {e: bld.cnt.get("S" + e, 0) for e in ENGS}
        for o in ops:
            if o.dma:
                slot = bld.dslot.get(o.eng, 0)
                bld.dslot[o.eng] = slot + 1
                sn = f"D{o.eng}{slot % NDMASEM}"
                k = bld.cnt.get(sn, 0) // 16 + 1
                bld.cnt[sn] = 16 * k
                o.sem_name = sn
                o.cum = 16 * k
            elif o.needed:
                cum[o.eng] += o.inc_amt
                o.sem_name = "S" + o.eng
                o.cum = cum[o.eng]

        def mk(eng_name):
            mine = [o for o in ops if o.eng == eng_name]
            if not mine:
                return None

            def section(eng):
                waited = {}
                for sn, v in entry.items():
                    eng.wait_ge(bld.sem(sn), v)
                    waited[sn] = v
                for o in mine:
                    for d in o.deps:
                        p = ops[d]
                        if p.eng == eng_name and not (p.special or p.dma) and p.eng not in ("vector", "scalar", "gpsimd"):
                            continue
                        if p.cum == 0:
                            continue
                        sn = p.sem_name
                        if waited.get(sn, -1) < p.cum:
                            eng.wait_ge(bld.sem(sn), p.cum)
                            waited[sn] = p.cum
                    if o.dma and o.cum > 16:
                        sn = o.sem_name
                        if waited.get(sn, -1) < o.cum - 16:
                            eng.wait_ge(bld.sem(sn), o.cum - 16)
                            waited[sn] = o.cum - 16
                    instr = o.fn(eng)
                    if o.needed:
                        assert instr is not None, f"needed op returned None on {eng_name} (op {o.i})"
                        instr.then_inc(bld.sem(o.sem_name), o.inc_amt)
            return section

        sections = {e: mk(e) for e in ENGS}
        with bld.nc.Block() as blk:
            for e in ENGS:
                if sections[e] is not None:
                    getattr(blk, e)(sections[e])
        for e in ENGS:
            bld.cnt["S" + e] = cum[e]


class Bld:
    def __init__(self):
        self.nc = bass.Bass("TRN2", target_bir_lowering=False)
        self.es = contextlib.ExitStack()
        self._sems = {}
        self.cnt = {}
        self._uid = 0
        self.lu = {}
        self.dslot = {}

    def uid(self):
        self._uid += 1
        return self._uid

    def sem(self, name):
        if name not in self._sems:
            self._sems[name] = self.es.enter_context(self.nc.semaphore(name))
            self.cnt.setdefault(name, 0)
        return self._sems[name]

    def _stack(self):
        sc = getattr(self, "_scopes", None)
        return sc[-1] if sc else self.es

    def push_scope(self):
        if not hasattr(self, "_scopes"):
            self._scopes = []
        self._scopes.append(self.es.enter_context(contextlib.ExitStack()))

    def pop_scope(self):
        self._scopes.pop().close()
        self.lu = {}
        for a in ("_ctile", "_cmap"):
            if hasattr(self, a):
                delattr(self, a)

    def sbuf(self, name, shape, dt=F32, **kw):
        return self._stack().enter_context(self.nc.sbuf_tensor(f"{name}_{self.uid()}", shape, dt, **kw))

    def psum(self, name, shape, dt=F32):
        return self._stack().enter_context(self.nc.psum_tensor(f"{name}_{self.uid()}", shape, dt))

    def constcol(self, pr, value):
        if not hasattr(self, "_ctile"):
            self._ctile = self.sbuf("consts", [128, 16])
            self._cmap = {}
        if value not in self._cmap:
            col = len(self._cmap)
            assert col < 16
            ap = self._ctile[:, col:col + 1]
            opid = pr.op("gpsimd", (lambda ap, value: lambda e: e.memset(ap, value))(ap, value))
            self._cmap[value] = (ap, opid)
        return self._cmap[value]

    def war(self, key):
        v = self.lu.get(key)
        if v is None:
            return []
        return list(v) if isinstance(v, (list, tuple)) else [v]

    def setlu(self, key, opid):
        self.lu[key] = opid


def ecopy(e, out, in_):
    if hasattr(e, "tensor_copy"):
        return e.tensor_copy(out, in_)
    return e.copy(out, in_)


def dma_load_tm(pr, dram, sb, nchunks=TC, width=D, eng="sync", deps=()):
    src = dram.rearrange("(c p) d -> p c d", p=128)
    dst = sb[:, 0:nchunks * width].rearrange("p (c d) -> p c d", d=width)
    return pr.op(eng, lambda e: e.dma_start(out=dst, in_=src), deps=deps, dma=True)


def ln_chunk_ops(pr, st, col, src_ap, dep_src, eps):
    c = col
    width = src_ap.shape[-1]
    o1 = pr.op("vector", lambda v: v.tensor_reduce(st["s1"][:, c:c + 1], src_ap, AX.X, ALU.add),
               deps=dep_src)
    o2 = pr.op("scalar", lambda s: s.activation(st["junk"][:, 0:width], src_ap, AF.Square,
                                                accum_out=st["s2"][:, c:c + 1]),
               deps=dep_src)
    o3 = pr.op("vector", lambda v: v.tensor_scalar_mul(st["mu"][:, c:c + 1], st["s1"][:, c:c + 1], 1.0 / width),
               deps=[o1])
    o4 = pr.op("vector", lambda v: v.tensor_mul(st["var"][:, c:c + 1], st["mu"][:, c:c + 1], st["mu"][:, c:c + 1]),
               deps=[o3])
    o5 = pr.op("vector", lambda v: v.scalar_tensor_tensor(st["var"][:, c:c + 1], st["s2"][:, c:c + 1],
                                                          1.0 / width, st["var"][:, c:c + 1],
                                                          ALU.mult, ALU.subtract),
               deps=[o2, o4])
    eps_ap, eps_op = pr.bld.constcol(pr, eps)
    o6 = pr.op("scalar", lambda s: s.activation(st["sd"][:, c:c + 1], st["var"][:, c:c + 1], AF.Sqrt, bias=eps_ap),
               deps=[o5, eps_op])
    o7 = pr.op("vector", lambda v: v.reciprocal(st["rstd"][:, c:c + 1], st["sd"][:, c:c + 1]),
               deps=[o6])
    o8 = pr.op("vector", lambda v: v.scalar_tensor_tensor(st["nmr"][:, c:c + 1], st["mu"][:, c:c + 1],
                                                          -1.0, st["rstd"][:, c:c + 1], ALU.mult, ALU.mult),
               deps=[o7])
    return o7, o8


def ln_apply_chunk(pr, st, col, src_ap, dst_ap, dep_stats, dep_dst=()):
    c = col
    return pr.op("scalar", lambda s: s.activation(dst_ap, src_ap, AF.Identity,
                                                  bias=st["nmr"][:, c:c + 1],
                                                  scale=st["rstd"][:, c:c + 1]),
                 deps=list(dep_stats) + list(dep_dst))


def alloc_stats(bld, tag, nchunks=TC, width=D, junk=None):
    if junk is None:
        junk = bld.sbuf(f"{tag}junk", [128, width], BF16)
    return {
        "junk": junk,
        "s1": bld.sbuf(f"{tag}s1", [128, nchunks]),
        "s2": bld.sbuf(f"{tag}s2", [128, nchunks]),
        "mu": bld.sbuf(f"{tag}mu", [128, nchunks]),
        "var": bld.sbuf(f"{tag}var", [128, nchunks]),
        "sd": bld.sbuf(f"{tag}sd", [128, nchunks]),
        "rstd": bld.sbuf(f"{tag}rstd", [128, nchunks]),
        "nmr": bld.sbuf(f"{tag}nmr", [128, nchunks]),
    }


def emit_ln_full(pr, st, src_sb, dst_sb, dep_src, nchunks=TC, width=D, eps=1e-5):
    outs = []
    for c in range(nchunks):
        src = src_sb[:, c * width:(c + 1) * width]
        dst = dst_sb[:, c * width:(c + 1) * width]
        o7, o8 = ln_chunk_ops(pr, st, c, src, dep_src, eps)
        outs.append(ln_apply_chunk(pr, st, c, src, dst, [o7, o8]))
    return outs


def emit_transpose(pr, src_chunk, dst_chunk, tchunks, fchunks, idn, pT, dep_src,
                   dst_off=0, evac_eng="scalar", dep_idn=()):
    bld = pr.bld
    nb = len(pT)
    all_evacs = []
    gi = 0
    for f in range(fchunks):
        for g in range((tchunks + 7) // 8):
            tcs = list(range(g * 8, min(g * 8 + 8, tchunks)))
            p = pT[gi % nb]
            war = bld.war(id(p))

            def mk_t(p=p, tcs=tcs, f=f):
                def fn(t):
                    for j, tc in enumerate(tcs):
                        instr = t.transpose(p[:, j * 128:(j + 1) * 128],
                                            src_chunk(tc)[:, f * 128:(f + 1) * 128], idn[:])
                    return instr
                return fn
            ot = pr.op("tensor", mk_t(), deps=list(dep_src) + list(dep_idn) + war)

            def mk_v(p=p, tcs=tcs, f=f):
                dst = dst_chunk(f)
                a = dst_off + tcs[0] * 128
                b = dst_off + (tcs[-1] + 1) * 128
                n = len(tcs) * 128
                return lambda v: ecopy(v, dst[:, a:b], p[:, 0:n])
            oe = pr.op(evac_eng, mk_v(), deps=[ot])
            bld.setlu(id(p), oe)
            all_evacs.append(oe)
            gi += 1
    return all_evacs


def emit_proj(pr, lhs, rhs, tiles, kchunks, psums, out_cb, dep_ops):
    bld = pr.bld
    nb = len(psums)
    evacs = []
    for i, tile in enumerate(tiles):
        p = psums[i % nb]
        war = bld.war(id(p))

        def mk_t(p=p, tile=tile):
            def fn(t):
                for k in range(kchunks):
                    instr = t.matmul(p[:], lhs(k, tile), rhs(k, tile),
                                     start=(k == 0), stop=(k == kchunks - 1))
                return instr
            return fn
        ot = pr.op("tensor", mk_t(), deps=list(dep_ops) + war)
        eng, fn, extra = out_cb(tile, p, i)
        oe = pr.op(eng, fn, deps=[ot] + list(extra))
        bld.setlu(id(p), oe)
        evacs.append(oe)
    return evacs


def emit_bcast_rows(pr, specs, ones_row, psums, dep_rows):
    bld = pr.bld
    out = []
    i = 0
    for row_ap, dst, width in specs:
        outs = []
        for b in range((width + 511) // 512):
            w = min(512, width - b * 512)
            p = psums[i % len(psums)]
            war = bld.war(id(p))

            def mk_t(p=p, b=b, w=w, row_ap=row_ap):
                return lambda t: t.matmul(p[:], ones_row[:],
                                          row_ap[:, b * 512:b * 512 + w], start=True, stop=True)
            ot = pr.op("tensor", mk_t(), deps=list(dep_rows) + war)

            def mk_v(p=p, b=b, w=w, dst=dst):
                return lambda v: v.tensor_copy(dst[:, b * 512:b * 512 + w], p[:])
            oe = pr.op("vector", mk_v(), deps=[ot])
            bld.setlu(id(p), oe)
            outs.append(oe)
            i += 1
        out.append(outs)
    return out


def emit_row_to_cols(pr, row_ap, dst_cols, ncols, ones_row, pTs, dep, dep_ones=()):
    bld = pr.bld
    p = pTs[0]
    war = bld.war(id(p))

    def mk_t():
        def fn(t):
            for e in range(ncols):
                instr = t.matmul(p[:, e:e + 1], row_ap[:, e * 128:(e + 1) * 128],
                                 ones_row[:, 0:1], start=True, stop=True)
            return instr
        return fn
    ot = pr.op("tensor", mk_t(), deps=list(dep) + list(dep_ones) + war)
    oe = pr.op("vector", lambda v: v.tensor_copy(dst_cols[:, 0:ncols], p[:, 0:ncols]), deps=[ot])
    bld.setlu(id(p), oe)
    return oe


DEBUG = {"on": False, "bufs": []}


def dbg(name, sb, cols):
    if DEBUG["on"]:
        DEBUG["bufs"].append((name, sb, cols))


def body_block3(bld, pr, dram):
    nc = bld.nc
    H, DX = 8, 64
    x_d, wqkv_d, bqkv_d, wfc_d, bfc_d, gam_d, bet_d, idn_d, xo_d = dram

    xch = bld.sbuf("xch", [128, 3 * D])
    xn_tm = bld.sbuf("xn_tm", [128, TC * D], BF16)
    xn_fm = bld.sbuf("xn_fm", [128, DCH * T], BF16)
    q_tm = bld.sbuf("q_tm", [128, TC * 512], BF16)
    k_tm = bld.sbuf("k_tm", [128, TC * 512], BF16)
    v_fm = bld.sbuf("v_fm", [128, 4 * T], BF16)
    o_fm = xn_fm
    wq_sb = bld.sbuf("wq_sb", [128, DCH * 1536], BF16)
    wfc_sb = bld.sbuf("wfc_sb", [128, 4 * D], BF16)
    idn = bld.sbuf("idn", [128, 128], BF16)
    ones_row = bld.sbuf("ones_row", [1, 128], BF16)
    bqk_b = bld.sbuf("bqk_b", [128, 1536], BF16)
    bfc_b = bld.sbuf("bfc_b", [128, D], BF16)
    gam_b = bld.sbuf("gam_b", [128, D], BF16)
    bet_b = bld.sbuf("bet_b", [128, D], BF16)
    rows = bld.sbuf("rows", [1, 1536 + 3 * D], BF16)
    vcol = bld.sbuf("vcol", [128, 4])
    P_bf = bld.sbuf("P_bf", [128, 4 * 64], BF16)
    nmax = bld.sbuf("nmax", [128, 4])
    ssum = bld.sbuf("ssum", [128, 4])
    F_tm = bld.sbuf("F_tm", [128, 2 * D])
    G_tm = bld.sbuf("G_tm", [128, 2 * D])
    junk = bld.sbuf("junk", [128, D], BF16)
    st1 = alloc_stats(bld, "b3ln1", junk=junk)
    st2 = alloc_stats(bld, "b3ln2", junk=junk)
    st3 = alloc_stats(bld, "b3ln3", junk=junk)

    pA = [bld.psum("pA", [128, 512]) for _ in range(2)]
    pT = [bld.psum("pT", [128, 1024], BF16) for _ in range(2)]
    pS = [bld.psum("pS", [128, 64]) for _ in range(2)]
    pO = [bld.psum("pO", [64, 512]) for _ in range(2)]

    x_src = x_d.rearrange("(c p) d -> p c d", p=128)
    xo_dst = xo_d.rearrange("(c p) d -> p c d", p=128)
    ld_wq = pr.op("gpsimd", lambda e: e.dma_start(
        out=wq_sb[:].rearrange("p (c e) -> p c e", e=1536),
        in_=wqkv_d.rearrange("(c p) e -> p c e", p=128)), dma=True)
    ld_wf = pr.op("gpsimd", lambda e: e.dma_start(
        out=wfc_sb[:].rearrange("p (c e) -> p c e", e=D),
        in_=wfc_d.rearrange("(c p) e -> p c e", p=128)), dma=True)
    ld_r1 = pr.op("gpsimd", lambda e: e.dma_start(out=rows[:, 0:1536], in_=bqkv_d[None, :]), dma=True)
    ld_r2 = pr.op("gpsimd", lambda e: e.dma_start(out=rows[:, 1536:1536 + D], in_=bfc_d[None, :]), dma=True)
    ld_r3 = pr.op("gpsimd", lambda e: e.dma_start(out=rows[:, 1536 + D:1536 + 2 * D], in_=gam_d[None, :]), dma=True)
    ld_r4 = pr.op("gpsimd", lambda e: e.dma_start(out=rows[:, 1536 + 2 * D:1536 + 3 * D], in_=bet_d[None, :]), dma=True)
    ld_id = pr.op("gpsimd", lambda e: e.dma_start(out=idn[:], in_=idn_d), dma=True)
    o_ones = pr.op("gpsimd", lambda e: e.memset(ones_row[:], 1.0))
    ld_rows = [ld_r1, ld_r2, ld_r3, ld_r4, o_ones]

    bcs = emit_bcast_rows(pr, [
        (rows[:, 0:1536], bqk_b, 1536),
        (rows[:, 1536:1536 + D], bfc_b, D),
        (rows[:, 1536 + D:1536 + 2 * D], gam_b, D),
        (rows[:, 1536 + 2 * D:1536 + 3 * D], bet_b, D),
    ], ones_row, pA, ld_rows)
    bc_bqk, bc_bfc, bc_gam, bc_bet = bcs

    o_vcol = emit_row_to_cols(pr, rows[:, 1024:1536], vcol, 4, ones_row, pA, [ld_r1], dep_ones=[o_ones])

    ln1 = []
    for c in range(TC):
        sl = c % 3
        xs = xch[:, sl * D:(sl + 1) * D]
        ld = pr.op("sync", (lambda c, xs: lambda e: e.dma_start(out=xs, in_=x_src[:, c, :]))(c, xs),
                   deps=bld.war(("xch", sl)), dma=True)
        o7, o8 = ln_chunk_ops(pr, st1, c, xs, [ld], 1e-5)
        oap = ln_apply_chunk(pr, st1, c, xs, xn_tm[:, c * D:(c + 1) * D], [o7, o8])
        bld.setlu(("xch", sl), [oap, ln1[-1] if False else o7])
        bld.setlu(("xch", sl), [oap, o7])
        ln1.append(oap)

    tr1 = emit_transpose(pr, lambda c: xn_tm[:, c * D:(c + 1) * D],
                         lambda f: xn_fm[:, f * T:(f + 1) * T], TC, DCH, idn, pT,
                         dep_src=ln1, dep_idn=[ld_id])

    def lhs1(k, m):
        return xn_fm[:, k * T + m * 128: k * T + (m + 1) * 128]

    def qk_cb(dst, bias0, bc):
        def cb(tile, p, i):
            m = tile

            def fn(v):
                return v.tensor_add(dst[:, m * 512:(m + 1) * 512], p[:],
                                    bqk_b[:, bias0:bias0 + 512])
            return ("vector", fn, bc)
        return cb

    pq = emit_proj(pr, lhs1, lambda k, m: wq_sb[:, k * 1536:k * 1536 + 512],
                   list(range(TC)), DCH, pA, qk_cb(q_tm, 0, bc_bqk),
                   dep_ops=tr1 + [ld_wq])
    pk = emit_proj(pr, lhs1, lambda k, m: wq_sb[:, k * 1536 + 512:k * 1536 + 1024],
                   list(range(TC)), DCH, pA, qk_cb(k_tm, 512, bc_bqk),
                   dep_ops=tr1 + [ld_wq])

    def v_cb(tile, p, i):
        e, n = tile

        def fn(v):
            return v.tensor_scalar_add(v_fm[:, e * T + n * 512: e * T + n * 512 + 512],
                                       p[:], vcol[:, e:e + 1])
        return ("vector", fn, [o_vcol])

    pv = emit_proj(pr,
                   lambda k, tile: wq_sb[:, k * 1536 + 1024 + tile[0] * 128: k * 1536 + 1024 + (tile[0] + 1) * 128],
                   lambda k, tile: xn_fm[:, k * T + tile[1] * 512: k * T + (tile[1] + 1) * 512],
                   [(e, n) for e in range(4) for n in range(4)], DCH, pA, v_cb,
                   dep_ops=tr1 + [ld_wq])

    att = []
    for c in range(4):
        p = pS[c % 2]
        war = bld.war(id(p))

        def mk_t(c=c, p=p):
            def fn(t):
                for half in range(2):
                    h = 2 * c + half
                    r0 = half * 64
                    for m in range(TC):
                        instr = t.matmul(p[r0:r0 + 64, :],
                                         q_tm[:, m * 512 + h * 64: m * 512 + (h + 1) * 64],
                                         k_tm[:, m * 512 + h * 64: m * 512 + (h + 1) * 64],
                                         start=(m == 0), stop=(m == TC - 1))
                return instr
            return fn
        ot = pr.op("tensor", mk_t(), deps=pq + pk + war)
        ov1 = pr.op("vector", (lambda c, p: lambda v: v.tensor_reduce(
            nmax[0:64, c:c + 1], p[0:64, :], AX.X, ALU.max, negate=True))(c, p), deps=[ot])
        ov1b = pr.op("vector", (lambda c, p: lambda v: v.tensor_reduce(
            nmax[64:128, c:c + 1], p[64:128, :], AX.X, ALU.max, negate=True))(c, p), deps=[ot])
        ov2 = pr.op("vector", (lambda c: lambda v: v.tensor_scalar_mul(
            nmax[:, c:c + 1], nmax[:, c:c + 1], 0.125))(c), deps=[ov1, ov1b])
        os_ = pr.op("scalar", (lambda c, p: lambda s: s.activation(
            P_bf[:, c * 64:(c + 1) * 64], p[:], AF.Exp,
            bias=nmax[:, c:c + 1], scale=0.125, accum_out=ssum[:, c:c + 1]))(c, p),
            deps=[ov2])
        bld.setlu(id(p), os_)
        att.append(os_)

    orcp = pr.op("vector", lambda v: v.reciprocal(ssum[:, 0:4], ssum[:, 0:4]), deps=att)
    onrm = []
    for c in range(4):
        onrm.append(pr.op("vector", (lambda c: lambda v: v.tensor_scalar_mul(
            P_bf[:, c * 64:(c + 1) * 64], P_bf[:, c * 64:(c + 1) * 64],
            ssum[:, c:c + 1]))(c), deps=[orcp]))

    def av_cb(tile, p, i):
        h, n = tile
        fc, r0 = h // 2, (h % 2) * 64

        def fn(v):
            return v.tensor_copy(o_fm[r0:r0 + 64, fc * T + n * 512: fc * T + (n + 1) * 512],
                                 p[:])
        return ("vector", fn, [])

    pav = emit_proj(pr,
                    lambda k, tile: P_bf[(tile[0] % 2) * 64:(tile[0] % 2) * 64 + 64,
                                         (tile[0] // 2) * 64:(tile[0] // 2 + 1) * 64],
                    lambda k, tile: v_fm[(tile[0] % 2) * 64:(tile[0] % 2) * 64 + 64,
                                         (tile[0] // 2) * T + tile[1] * 512:(tile[0] // 2) * T + (tile[1] + 1) * 512],
                    [(h, n) for h in range(8) for n in range(4)], 1, pO, av_cb,
                    dep_ops=onrm + pv)

    stores = []
    for m in range(TC):
        slot = (m % 2) * D
        F = F_tm[:, slot:slot + D]
        G = G_tm[:, slot:slot + D]
        ev = []
        for n in range(2):
            i2 = m * 2 + n
            p = pA[i2 % 2]
            war = bld.war(id(p))

            def mk_t(m=m, n=n, p=p):
                def fn(t):
                    for k in range(4):
                        instr = t.matmul(p[:], o_fm[:, k * T + m * 128:k * T + (m + 1) * 128],
                                         wfc_sb[:, k * D + n * 512:k * D + (n + 1) * 512],
                                         start=(k == 0), stop=(k == 3))
                    return instr
                return fn
            ot = pr.op("tensor", mk_t(), deps=pav + [ld_wf] + war)

            sl = F_tm[:, slot + n * 512: slot + (n + 1) * 512]
            oe1 = pr.op("vector", (lambda sl, p, n: lambda v: v.tensor_add(
                sl, p[:], bfc_b[:, n * 512:(n + 1) * 512]))(sl, p, n),
                deps=[ot] + bc_bfc + bld.war(("F", m % 2)))
            oe = pr.op("vector", (lambda sl, m, n: lambda v: v.tensor_add(
                sl, sl, xn_tm[:, m * D + n * 512: m * D + (n + 1) * 512]))(sl, m, n),
                deps=[oe1])
            bld.setlu(id(p), oe)
            ev.append(oe)
        o7, o8 = ln_chunk_ops(pr, st2, m, F, ev, 1e-6)
        oap = ln_apply_chunk(pr, st2, m, F, G, [o7, o8], dep_dst=bld.war(("G", m % 2)))
        og1 = pr.op("vector", (lambda G: lambda v: v.tensor_mul(G, G, gam_b[:, 0:D]))(G),
                    deps=[oap] + bc_gam)
        og2 = pr.op("vector", (lambda G: lambda v: v.tensor_add(G, G, bet_b[:, 0:D]))(G),
                    deps=[og1] + bc_bet)
        p7, p8 = ln_chunk_ops(pr, st3, m, G, [og2], 1e-5)
        o_t = ln_apply_chunk(pr, st3, m, G, F, [p7, p8], dep_dst=ev)
        sl2 = m % 3
        xs2 = xch[:, sl2 * D:(sl2 + 1) * D]
        ld2 = pr.op("sync", (lambda m, xs2: lambda e: e.dma_start(out=xs2, in_=x_src[:, m, :]))(m, xs2),
                    deps=bld.war(("xch", sl2)), dma=True)
        o_add = pr.op("vector", (lambda F, xs2: lambda v: v.tensor_add(
            F, F, xs2))(F, xs2), deps=[o_t, ld2])
        bld.setlu(("xch", sl2), o_add)
        bld.setlu(("G", m % 2), o_add)

        ost = pr.op("sync", (lambda m, F: lambda e: e.dma_start(out=xo_dst[:, m, :], in_=F))(m, F),
                    deps=[o_add], dma=True)
        bld.setlu(("F", m % 2), ost)
        stores.append(ost)

    dbg("st1rstd", st1["rstd"], TC)
    dbg("st1nmr", st1["nmr"], TC)
    dbg("st1s1", st1["s1"], TC)
    dbg("st1s2", st1["s2"], TC)
    dbg("st2rstd", st2["rstd"], TC)
    dbg("st3rstd", st3["rstd"], TC)
    dbg("F_tm", F_tm, 2 * D)
    dbg("G_tm", G_tm, 2 * D)
    dbg("xn_tm", xn_tm, TC * D)
    dbg("xn_fm", xn_fm, DCH * T)
    dbg("q_tm", q_tm, TC * 512)
    dbg("k_tm", k_tm, TC * 512)
    dbg("v_fm", v_fm, 4 * T)
    dbg("P_bf", P_bf, 4 * 64)
    dbg("nmax", nmax, 4)
    dbg("ssum", ssum, 4)
    dbg("vcol", vcol, 4)
    for name, sb, cols in DEBUG["bufs"]:
        d = bld.nc.dram_tensor("dbg_" + name, [128, cols], sb.dtype if hasattr(sb, "dtype") else F32,
                               kind="ExternalOutput").ap()
        pr.op("sync", (lambda d, sb, cols: lambda e: e.dma_start(out=d, in_=sb[:, 0:cols]))(d, sb, cols),
              deps=stores, dma=True)
    pr.op("sync", lambda e: e.nop(), deps=stores)


def build_block3():
    bld = Bld()
    nc = bld.nc
    dram = (
        nc.dram_tensor("x", [T, D], F32, kind="ExternalInput").ap(),
        nc.dram_tensor("w_qkv", [D, 1536], F32, kind="ExternalInput").ap(),
        nc.dram_tensor("b_qkv", [1536], F32, kind="ExternalInput").ap(),
        nc.dram_tensor("w_fc", [512, D], F32, kind="ExternalInput").ap(),
        nc.dram_tensor("b_fc", [D], F32, kind="ExternalInput").ap(),
        nc.dram_tensor("gamma", [D], F32, kind="ExternalInput").ap(),
        nc.dram_tensor("beta", [D], F32, kind="ExternalInput").ap(),
        nc.dram_tensor("iden", [128, 128], F32, kind="ExternalInput").ap(),
        nc.dram_tensor("xo", [T, D], F32, kind="ExternalOutput").ap(),
    )
    with bld.es:
        pr = Prog(bld, "b3")
        body_block3(bld, pr, dram)
        pr.run()
    return bld.nc




def emit_sandwich_tail(bld, pr, get_F, st, xch, x_src, xo_dst, eps=1e-5, nsl=3):
    stores = []
    for m in range(TC):
        F, depF = get_F(m)
        tmp = F
        o7, o8 = ln_chunk_ops(pr, st, m, F, depF, eps)
        o_t = ln_apply_chunk(pr, st, m, F, tmp, [o7, o8])
        sl2 = m % nsl
        xs2 = xch[:, sl2 * D:(sl2 + 1) * D]
        ld2 = pr.op("sync", (lambda m, xs2: lambda e: e.dma_start(out=xs2, in_=x_src[:, m, :]))(m, xs2),
                    deps=bld.war(("xch", sl2)), dma=True)
        o_add = pr.op("vector", (lambda tmp, xs2: lambda v: v.tensor_add(tmp, tmp, xs2))(tmp, xs2),
                      deps=[o_t, ld2])
        bld.setlu(("xch", sl2), o_add)
        ost = pr.op("sync", (lambda m, tmp: lambda e: e.dma_start(out=xo_dst[:, m, :], in_=tmp))(m, tmp),
                    deps=[o_add], dma=True)
        bld.setlu(("swtmp", m % 2), ost)
        stores.append(ost)
    return stores


def emit_sandwich_ln1(bld, pr, x_src, xch, xn_tm, st1, eps=1e-5, nsl=3):
    ln1 = []
    for c in range(TC):
        sl = c % nsl
        xs = xch[:, sl * D:(sl + 1) * D]
        ld = pr.op("sync", (lambda c, xs: lambda e: e.dma_start(out=xs, in_=x_src[:, c, :]))(c, xs),
                   deps=bld.war(("xch", sl)), dma=True)
        o7, o8 = ln_chunk_ops(pr, st1, c, xs, [ld], eps)
        oap = ln_apply_chunk(pr, st1, c, xs, xn_tm[:, c * D:(c + 1) * D], [o7, o8])
        bld.setlu(("xch", sl), [oap, o7])
        ln1.append(oap)
    return ln1


def body_block4(bld, pr, dram):
    nc = bld.nc
    NE, MI, SI = 16, 512, 2048
    x_d, gw_d, be_d, up_d, dn_d, su_d, sd_d, idn_d, xo_d = dram
    x_src = x_d.rearrange("(c p) d -> p c d", p=128)
    xo_dst = xo_d.rearrange("(c p) d -> p c d", p=128)

    xch = bld.sbuf("xch", [128, 3 * D])
    xn_tm = bld.sbuf("xn_tm", [128, TC * D], BF16)
    xn_fm = bld.sbuf("xn_fm", [128, DCH * T], BF16)
    acc = bld.sbuf("acc", [128, TC * D])
    stage_ = [xn_tm[:, 8192:8192 + 4096], xn_tm[:, 12288:12288 + 4096]]
    sustage = bld.sbuf("sustage", [128, 1024], BF16)
    rb_buf = bld.sbuf("rb_buf", [128, 1024], BF16)
    wuT = bld.sbuf("wuT", [128, DCH * 512], BF16)
    wdT = bld.sbuf("wdT", [128, 4 * D], BF16)
    sdT = bld.sbuf("sdT", [128, 16 * D], BF16)
    suTe = bld.sbuf("suTe", [128, DCH * 128], BF16)
    gw_sb = bld.sbuf("gw_sb", [16, D], BF16)
    gwT = bld.sbuf("gwT", [128, DCH * 16], BF16)
    be_row = bld.sbuf("be_row", [1, 16], BF16)
    beb = bld.sbuf("beb", [128, 16])
    idn = bld.sbuf("idn", [128, 128], BF16)
    ones_row = bld.sbuf("ones_row", [1, 128], BF16)
    scr = bld.sbuf("scr", [128, TC * 16])
    comb = bld.sbuf("comb", [128, TC * 16])
    rt = {k: bld.sbuf("rt" + k, [128, w]) for k, w in
          [("sc", 16), ("prs", 24), ("g2", 4), ("pmin", 6), ("tg", 1), ("gm", 4),
           ("msk", 16), ("top8", 8), ("wsel", 16), ("wsum", 1)]}
    junk = bld.sbuf("junk", [128, D], BF16)
    st1 = alloc_stats(bld, "b4ln1", junk=junk)
    st2 = alloc_stats(bld, "b4ln2", junk=junk)
    he = xn_tm
    hs = xn_tm

    pA = [bld.psum("pA", [128, 512]) for _ in range(3)]
    pT = [bld.psum("pT", [128, 1024], BF16) for _ in range(2)]
    pR = [bld.psum("pR", [128, 16]) for _ in range(2)]

    ld_id = pr.op("gpsimd", lambda e: e.dma_start(out=idn[:], in_=idn_d), dma=True)
    o_ones = pr.op("gpsimd", lambda e: e.memset(ones_row[:], 1.0))
    ld_gw = pr.op("gpsimd", lambda e: e.dma_start(out=gw_sb[:], in_=gw_d), dma=True)
    ld_be = pr.op("gpsimd", lambda e: e.dma_start(out=be_row[:], in_=be_d[None, :]), dma=True)

    gw_t = []
    for k in range(DCH):
        def mk_t(k=k):
            return lambda t: t.transpose(pT[0][:, k * 16:(k + 1) * 16],
                                         gw_sb[0:16, k * 128:(k + 1) * 128], idn[0:16, 0:16])
        gw_t.append(pr.op("tensor", mk_t(), deps=[ld_gw, ld_id] + bld.war(id(pT[0]))))
    o_gwT = pr.op("vector", lambda v: v.tensor_copy(gwT[:, 0:DCH * 16].rearrange("p (k e) -> p k e", e=16),
                                                    pT[0][:, 0:DCH * 16].rearrange("p (k e) -> p k e", e=16)),
                  deps=gw_t)
    bld.setlu(id(pT[0]), o_gwT)
    o_beb_t = pr.op("tensor", lambda t: t.matmul(pR[0][:], ones_row[:], be_row[:],
                                                 start=True, stop=True),
                    deps=[ld_be, o_ones] + bld.war(id(pR[0])))
    o_beb = pr.op("vector", lambda v: v.tensor_copy(beb[:], pR[0][:]), deps=[o_beb_t])
    bld.setlu(id(pR[0]), o_beb)

    ln1 = emit_sandwich_ln1(bld, pr, x_src, xch, xn_tm, st1)

    tr1 = emit_transpose(pr, lambda c: xn_tm[:, c * D:(c + 1) * D],
                         lambda f: xn_fm[:, f * T:(f + 1) * T], TC, DCH, idn, pT,
                         dep_src=ln1, dep_idn=[ld_id])
    for key in (("stage", 0), ("stage", 1)):
        bld.setlu(key, tr1[-1])
    for par in (0, 1):
        for j in range(8):
            bld.setlu(("stg", par, j), tr1[-1])

    comb_ops = []
    for m in range(TC):
        p = pR[m % 2]
        war = bld.war(id(p))

        def mk_t(m=m, p=p):
            def fn(t):
                for k in range(DCH):
                    instr = t.matmul(p[:], xn_fm[:, k * T + m * 128:k * T + (m + 1) * 128],
                                     gwT[:, k * 16:(k + 1) * 16],
                                     start=(k == 0), stop=(k == DCH - 1))
                return instr
            return fn
        ot = pr.op("tensor", mk_t(), deps=tr1 + [o_gwT] + war)
        scm = scr[:, m * 16:(m + 1) * 16]
        o_sig = pr.op("scalar", (lambda p, scm: lambda s: s.activation(scm, p[:], AF.Sigmoid))(p, scm),
                      deps=[ot])
        bld.setlu(id(p), o_sig)
        o_sc = pr.op("vector", (lambda scm: lambda v: v.tensor_add(rt["sc"][:], scm, beb[:]))(scm),
                     deps=[o_sig, o_beb] + bld.war("rtbufs"))
        sc3 = rt["sc"][:].rearrange("p (g i) -> p g i", i=4)
        prs3 = rt["prs"][:].rearrange("p (g j) -> p g j", j=6)
        o_pairs = []
        for j, (i1, i2) in enumerate([(0, 1), (0, 2), (0, 3), (1, 2), (1, 3), (2, 3)]):
            o_pairs.append(pr.op("vector", (lambda j, i1, i2, sc3, prs3: lambda v: v.tensor_add(
                prs3[:, :, j:j + 1], sc3[:, :, i1:i1 + 1], sc3[:, :, i2:i2 + 1]))(j, i1, i2, sc3, prs3),
                deps=[o_sc]))
        o_g2 = pr.op("vector", (lambda prs3: lambda v: v.tensor_reduce(
            rt["g2"][:], prs3, AX.X, ALU.max))(prs3), deps=o_pairs)
        o_pm = []
        for j, (i1, i2) in enumerate([(0, 1), (0, 2), (0, 3), (1, 2), (1, 3), (2, 3)]):
            o_pm.append(pr.op("vector", (lambda j, i1, i2: lambda v: v.tensor_tensor(
                rt["pmin"][:, j:j + 1], rt["g2"][:, i1:i1 + 1], rt["g2"][:, i2:i2 + 1],
                ALU.min))(j, i1, i2), deps=[o_g2]))
        o_tg = pr.op("vector", lambda v: v.tensor_reduce(rt["tg"][:], rt["pmin"][:], AX.X, ALU.max),
                     deps=o_pm)
        o_gm = pr.op("vector", lambda v: v.tensor_single_scalar(rt["gm"][:], rt["g2"][:],
                                                                rt["tg"][:], ALU.is_ge),
                     deps=[o_tg])
        o_msk = []
        for g in range(4):
            o_msk.append(pr.op("vector", (lambda g: lambda v: v.tensor_single_scalar(
                rt["msk"][:, 4 * g:4 * g + 4], rt["sc"][:, 4 * g:4 * g + 4],
                rt["gm"][:, g:g + 1], ALU.mult))(g), deps=[o_gm, o_sc]))
        o_t8 = pr.op("vector", lambda v: v.max(rt["top8"][:], rt["msk"][:]), deps=o_msk, special=True)
        o_sel = pr.op("vector", lambda v: v.tensor_single_scalar(rt["wsel"][:], rt["msk"][:],
                                                                 rt["top8"][:, 3:4], ALU.is_ge),
                      deps=[o_t8])
        o_ws = pr.op("vector", (lambda scm: lambda v: v.tensor_mul(rt["wsel"][:], rt["wsel"][:], scm))(scm),
                     deps=[o_sel])
        o_sum = pr.op("vector", lambda v: v.tensor_reduce(rt["wsum"][:], rt["wsel"][:], AX.X, ALU.add),
                      deps=[o_ws])
        o_eps = pr.op("vector", lambda v: v.tensor_scalar_add(rt["wsum"][:], rt["wsum"][:], 1e-20),
                      deps=[o_sum])
        o_rcp = pr.op("vector", lambda v: v.reciprocal(rt["wsum"][:], rt["wsum"][:]), deps=[o_eps])
        o_r2 = pr.op("vector", lambda v: v.tensor_scalar_mul(rt["wsum"][:], rt["wsum"][:], 2.5),
                     deps=[o_rcp])
        cbm = comb[:, m * 16:(m + 1) * 16]
        o_cb = pr.op("vector", (lambda cbm: lambda v: v.tensor_single_scalar(
            cbm, rt["wsel"][:], rt["wsum"][:], ALU.mult))(cbm), deps=[o_r2])
        bld.setlu("rtbufs", o_cb)
        comb_ops.append(o_cb)

    wuT4 = wuT[:].rearrange("p (dc c) -> p dc c", c=512)
    wdT4 = wdT[:].rearrange("p (ic c) -> p ic c", c=D)
    for e in range(NE):
        stage = stage_[e % 2]
        par = e % 2
        twu_g = {}
        for rc in range(4):
            ld = pr.op("gpsimd", (lambda e, stage, rc: lambda g: g.dma_start(
                out=stage[:, rc * D:(rc + 1) * D],
                in_=up_d[e][rc * 128:(rc + 1) * 128, :]))(e, stage, rc),
                deps=bld.war(("stg", par, 2 * rc)) + bld.war(("stg", par, 2 * rc + 1)), dma=True)
            p = pT[rc % 2]
            warp = bld.war(id(p))

            def mk_tt(stage=stage, rc=rc, p=p):
                def fn(t):
                    for f in range(DCH):
                        instr = t.transpose(p[:, f * 128:(f + 1) * 128],
                                            stage[:, rc * D + f * 128:rc * D + (f + 1) * 128],
                                            idn[:])
                    return instr
                return fn
            ott = pr.op("tensor", mk_tt(), deps=[ld, ld_id] + warp + bld.war(("wuTr", rc)))
            bld.setlu(("stg", par, 2 * rc), ott)
            bld.setlu(("stg", par, 2 * rc + 1), ott)

            def mk_tv(p=p, rc=rc):
                p3 = p[:].rearrange("p (k c) -> p k c", c=128)[:, 0:DCH, :]
                dst = wuT4[:, 0:DCH, rc * 128:(rc + 1) * 128]
                return lambda v: ecopy(v, dst, p3)
            oev = pr.op("scalar" if rc % 2 else "vector", mk_tv(), deps=[ott])
            bld.setlu(id(p), oev)
            twu_g[rc] = oev
        up_ev = []
        wuTr = {rc: [] for rc in range(4)}
        for ec in range(4):
            for tb in range(4):
                p = pA[(ec * 4 + tb) % 3]
                war = bld.war(id(p))

                def mk_t(ec=ec, tb=tb, p=p):
                    def fn(t):
                        for k in range(DCH):
                            instr = t.matmul(p[:], wuT[:, k * 512 + ec * 128:k * 512 + (ec + 1) * 128],
                                             xn_fm[:, k * T + tb * 512:k * T + (tb + 1) * 512],
                                             start=(k == 0), stop=(k == DCH - 1))
                        return instr
                    return fn
                ot = pr.op("tensor", mk_t(), deps=[twu_g[ec]] + war + bld.war(("he", ec, tb)))
                wuTr[ec].append(ot)
                rb = rb_buf[:, (tb % 2) * 512:(tb % 2) * 512 + 512]
                o_rl = pr.op("scalar", (lambda p, rb: lambda s: s.activation(rb, p[:], AF.Relu))(p, rb),
                             deps=[ot] + bld.war(("rb", tb % 2)))
                hesl = he[:, ec * T + tb * 512:ec * T + (tb + 1) * 512]
                o_he = pr.op("vector", (lambda hesl, rb, p: lambda v: v.tensor_mul(hesl, rb, p[:]))(hesl, rb, p),
                             deps=[o_rl] + ([tr1[-1]] if e == 0 else []))
                bld.setlu(id(p), o_he)
                bld.setlu(("rb", tb % 2), o_he)
                up_ev.append(o_he)
                bld.setlu(("heC", ec, tb), o_he)
        for rc in range(4):
            bld.setlu(("wuTr", rc), wuTr[rc])
        twd_g = {}
        for rc in range(8):
            ld = pr.op("gpsimd", (lambda e, stage, rc: lambda g: g.dma_start(
                out=stage[:, rc * 512:(rc + 1) * 512],
                in_=dn_d[e][rc * 128:(rc + 1) * 128, :]))(e, stage, rc),
                deps=bld.war(("stg", par, rc)), dma=True)
            p = pT[rc % 2]
            warp = bld.war(id(p))

            def mk_tt(stage=stage, rc=rc, p=p):
                def fn(t):
                    for ic in range(4):
                        instr = t.transpose(p[:, ic * 128:(ic + 1) * 128],
                                            stage[:, rc * 512 + ic * 128:rc * 512 + (ic + 1) * 128],
                                            idn[:])
                    return instr
                return fn
            ott = pr.op("tensor", mk_tt(), deps=[ld, ld_id] + warp + bld.war(("wdTr", rc // 4)))
            bld.setlu(("stg", par, rc), ott)

            def mk_tv(p=p, rc=rc):
                p3 = p[:].rearrange("p (k c) -> p k c", c=128)[:, 0:4, :]
                dst = wdT4[:, 0:4, rc * 128:(rc + 1) * 128]
                return lambda v: ecopy(v, dst, p3)
            oev = pr.op("scalar" if rc % 2 else "vector", mk_tv(), deps=[ott])
            bld.setlu(id(p), oev)
            twd_g[rc] = oev
        dn_ev = []
        wdTr = {0: [], 1: []}
        for m in range(TC):
            for eb in range(2):
                p = pA[(m * 2 + eb) % 3]
                war = bld.war(id(p))
                tb = m // 4

                def mk_t(m=m, eb=eb, p=p, tb=tb):
                    def fn(t):
                        for k in range(4):
                            instr = t.matmul(p[:], he[:, k * T + m * 128:k * T + (m + 1) * 128],
                                             wdT[:, k * D + eb * 512:k * D + (eb + 1) * 512],
                                             start=(k == 0), stop=(k == 3))
                        return instr
                    return fn
                hed = [bld.lu.get(("heC", kk, tb)) for kk in range(4)]
                ot = pr.op("tensor", mk_t(),
                           deps=[twd_g[rc] for rc in range(4 * eb, 4 * eb + 4)] + war +
                           [d for d in hed if d is not None])
                wdTr[eb].append(ot)
                accsl = acc[:, m * D + eb * 512:m * D + (eb + 1) * 512]
                if e == 0:
                    o_ac = pr.op("vector", (lambda accsl, p, m, e: lambda v: v.tensor_single_scalar(
                        accsl, p[:], comb[:, m * 16 + e:m * 16 + e + 1], ALU.mult))(accsl, p, m, e),
                        deps=[ot, comb_ops[m]])
                else:
                    o_ac = pr.op("vector", (lambda accsl, p, m, e: lambda v: v.scalar_tensor_tensor(
                        accsl, p[:], comb[:, m * 16 + e:m * 16 + e + 1], accsl,
                        ALU.mult, ALU.add))(accsl, p, m, e),
                        deps=[ot, comb_ops[m]] + bld.war(("acc", m, eb)))
                bld.setlu(id(p), o_ac)
                bld.setlu(("acc", m, eb), o_ac)
                dn_ev.append(o_ac)
        for eb in range(2):
            bld.setlu(("wdTr", eb), wdTr[eb])
        bld.setlu(("stage", par), [bld.lu[("stg", par, j)] for j in range(8)])
        for ec in range(4):
            for tb in range(4):
                bld.setlu(("he", ec, tb), dn_ev[-1])

    for ec in range(16):
        bld.setlu(("hs", ec), dn_ev[-1])

    for piece in range(4):
        sstage = stage_[piece % 2]
        ld_sd = pr.op("gpsimd", (lambda piece, sstage: lambda g: g.dma_start(
            out=sstage.rearrange("p (r c) -> p r c", c=SI),
            in_=sd_d[piece * 256:(piece + 1) * 256, :].rearrange("(r p) c -> p r c", p=128)))(piece, sstage),
            deps=bld.war(("stage", piece % 2)), dma=True)
        tsd = emit_transpose(pr, (lambda sstage: lambda rc: sstage[:, rc * SI:(rc + 1) * SI])(sstage),
                             lambda ic: sdT[:, ic * D:(ic + 1) * D], 2, 16, idn, pT,
                             dep_src=[ld_sd], dep_idn=[ld_id], dst_off=piece * 256)
        bld.setlu(("stage", piece % 2), tsd[-1])
    sdT_done = tsd[-1]

    sh_ev = {}
    for tb in range(4):
        hs_ops = []
        for ec in range(16):
            ld_su = pr.op("gpsimd", (lambda ec: lambda g: g.dma_start(
                out=sustage[:],
                in_=su_d[ec * 128:(ec + 1) * 128, :]))(ec),
                deps=bld.war("sustage"), dma=True)
            tsu = emit_transpose(pr, lambda rc: sustage[:, 0:D],
                                 lambda dc: suTe[:, dc * 128:(dc + 1) * 128], 1, DCH, idn, pT,
                                 dep_src=[ld_su] + bld.war("suTe"), dep_idn=[ld_id])
            bld.setlu("sustage", tsu[-1])
            p = pA[ec % 2]
            war = bld.war(id(p))

            def mk_t(ec=ec, tb=tb, p=p):
                def fn(t):
                    for k in range(DCH):
                        instr = t.matmul(p[:], suTe[:, k * 128:(k + 1) * 128],
                                         xn_fm[:, k * T + tb * 512:k * T + (tb + 1) * 512],
                                         start=(k == 0), stop=(k == DCH - 1))
                    return instr
                return fn
            ot = pr.op("tensor", mk_t(), deps=tsu + war + bld.war(("hs", ec)))
            bld.setlu("suTe", ot)
            rb = rb_buf[:, (ec % 2) * 512:(ec % 2) * 512 + 512]
            o_rl = pr.op("scalar", (lambda p, rb: lambda s: s.activation(rb, p[:], AF.Relu))(p, rb),
                         deps=[ot] + bld.war(("rb", ec % 2)))
            hssl = hs[:, ec * 512:(ec + 1) * 512]
            o_hs = pr.op("vector", (lambda hssl, rb, p: lambda v: v.tensor_mul(hssl, rb, p[:]))(hssl, rb, p),
                         deps=[o_rl])
            bld.setlu(id(p), o_hs)
            bld.setlu(("rb", ec % 2), o_hs)
            hs_ops.append(o_hs)
        for mm in range(4):
            m = tb * 4 + mm
            for eb in range(2):
                p = pA[(mm * 2 + eb) % 3]
                war = bld.war(id(p))

                def mk_t(m=m, mm=mm, eb=eb, p=p):
                    def fn(t):
                        for k in range(16):
                            instr = t.matmul(p[:], hs[:, k * 512 + mm * 128:k * 512 + (mm + 1) * 128],
                                             sdT[:, k * D + eb * 512:k * D + (eb + 1) * 512],
                                             start=(k == 0), stop=(k == 15))
                        return instr
                    return fn
                ot = pr.op("tensor", mk_t(), deps=hs_ops + [sdT_done] + war)
                accsl = acc[:, m * D + eb * 512:m * D + (eb + 1) * 512]
                o_ac = pr.op("vector", (lambda accsl, p: lambda v: v.tensor_add(
                    accsl, accsl, p[:]))(accsl, p), deps=[ot] + bld.war(("acc", m, eb)))
                bld.setlu(id(p), o_ac)
                bld.setlu(("acc", m, eb), o_ac)
                sh_ev[(m, eb)] = o_ac
        for ec in range(16):
            bld.setlu(("hs", ec), sh_ev[(tb * 4 + 3, 1)])

    def get_F(m):
        F = acc[:, m * D:(m + 1) * D]
        return F, [sh_ev[(m, 0)], sh_ev[(m, 1)]]

    if DEBUG.get("notail"):
        stores = []
        alldeps = [sh_ev[(m, eb)] for m in range(TC) for eb in range(2)]
        for name, sb, cols in [("acc", acc, TC * D), ("xn_fm", xn_fm, DCH * T),
                               ("scr", scr, TC * 16), ("comb", comb, TC * 16),
                               ("wuT", wuT, DCH * 512), ("wdT", wdT, 4 * D),
                               ("sdT", sdT, 16 * D), ("henow", xn_tm, TC * D)] + [
                               ("rt" + k, rt[k], rt[k].shape[1]) for k in rt]:
            dt_out = BF16 if sb.dtype == BF16 else F32
            d = bld.nc.dram_tensor("dbg_" + name, [128, cols], dt_out,
                                   kind="ExternalOutput").ap()
            stores.append(pr.op("sync", (lambda d, sb, cols: lambda e: e.dma_start(
                out=d, in_=sb[:, 0:cols]))(d, sb, cols), deps=alldeps, dma=True))
        z = pr.op("vector", lambda v: v.memset(xch[:, 0:D], 0.0), deps=alldeps)
        for m in range(TC):
            stores.append(pr.op("sync", (lambda m: lambda e: e.dma_start(
                out=xo_dst[:, m, :], in_=xch[:, 0:D]))(m), deps=[z], dma=True))
    else:
        stores = emit_sandwich_tail(bld, pr, get_F, st2, xch, x_src, xo_dst)
    pr.op("sync", lambda e: e.nop(), deps=stores)


def build_block4():
    bld = Bld()
    nc = bld.nc
    dram = (
        nc.dram_tensor("x", [T, D], F32, kind="ExternalInput").ap(),
        nc.dram_tensor("gate_w", [16, D], F32, kind="ExternalInput").ap(),
        nc.dram_tensor("bias_e", [16], F32, kind="ExternalInput").ap(),
        nc.dram_tensor("up_w", [16, 512, D], F32, kind="ExternalInput").ap(),
        nc.dram_tensor("down_w", [16, D, 512], F32, kind="ExternalInput").ap(),
        nc.dram_tensor("shared_up", [2048, D], F32, kind="ExternalInput").ap(),
        nc.dram_tensor("shared_down", [D, 2048], F32, kind="ExternalInput").ap(),
        nc.dram_tensor("iden", [128, 128], F32, kind="ExternalInput").ap(),
        nc.dram_tensor("xo", [T, D], F32, kind="ExternalOutput").ap(),
    )
    with bld.es:
        pr = Prog(bld, "b4")
        body_block4(bld, pr, dram)
        pr.run()
    return bld.nc




def body_block1(bld, pr, dram):
    nc = bld.nc
    H = 16
    (x_d, wq_d, wk_d, wv_d, wo_d, bo_d, aq_d, bq_d, ak_d, bk_d, av_d, bv_d,
     ao_d, bo2_d, idn_d, xo_d) = dram
    x_src = x_d.rearrange("(c p) d -> p c d", p=128)
    xo_dst = xo_d.rearrange("(c p) d -> p c d", p=128)

    xch = bld.sbuf("xch", [128, 2 * D])
    xn_tm = bld.sbuf("xn_tm", [128, TC * D], BF16)
    xn_fm = bld.sbuf("xn_fm", [128, DCH * T], BF16)
    k_fm = bld.sbuf("k_fm", [128, DCH * T], BF16)
    v_tm = bld.sbuf("v_tm", [128, TC * D], BF16)
    WoT = bld.sbuf("WoT", [128, DCH * D], BF16)
    WT = bld.sbuf("WT", [128, DCH * D], BF16)
    WqT = WT
    q_qb = bld.sbuf("q_qb", [128, DCH * 512], BF16)
    o_qb = bld.sbuf("o_qb", [128, DCH * 512], BF16)
    stage = xn_tm[:, 0:8192]
    Pt_ = [xn_tm[:, 8192:8192 + 2048], xn_tm[:, 10240:10240 + 2048]]
    PT = xn_tm[:, 0:8192]
    An = xn_tm[0:16, 8192:8192 + 4096]
    Bn = bld.sbuf("Bn", [128, DCH * 4 * 16], BF16)
    AT = bld.sbuf("AT", [128, 4 * DCH * 16], BF16)
    BT = xn_tm[0:16, 12288:12288 + 4096]
    u_q = bld.sbuf("u_q", [16, T], BF16)
    u_k = xn_tm[0:16, 8192:8192 + 2048]
    u_v = xn_tm[0:16, 10240:10240 + 2048]
    u_bufs = [u_q, u_k, u_v]
    u_o = bld.sbuf("u_o", [16, 512], BF16)
    rows = bld.sbuf("rows", [1, D], BF16)
    bo_b = bld.sbuf("bo_b", [128, D], BF16)
    idn = bld.sbuf("idn", [128, 128], BF16)
    ones_row = bld.sbuf("ones_row", [1, 128], BF16)
    onescol_b1 = bld.sbuf("onescol_b1", [128, 1], BF16)
    attn_sc = bld.sbuf("attn_sc", [128, 16])
    km2 = bld.sbuf("km2", [1, 16])
    qm2 = bld.sbuf("qm2", [1, 16])
    mrow = bld.sbuf("mrow", [1, 16], BF16)
    mcols = bld.sbuf("mcols", [128, 16])
    sqb = bld.sbuf("sqb", [128, 512], BF16)
    F2 = bld.sbuf("F2", [128, 2 * D])
    junk = bld.sbuf("junk", [128, D], BF16)
    st1 = alloc_stats(bld, "b1ln1", junk=junk)
    st2 = alloc_stats(bld, "b1ln2", junk=junk)

    pS = bld.psum("pS", [128, T])
    pT = [bld.psum("pT", [128, 1024], BF16) for _ in range(2)]
    pA = [bld.psum("pA", [128, 512]) for _ in range(2)]

    ld_id = pr.op("gpsimd", lambda e: e.dma_start(out=idn[:], in_=idn_d), dma=True)
    o_ones = pr.op("gpsimd", lambda e: e.memset(ones_row[:], 1.0))
    o_onec_b1 = pr.op("gpsimd", lambda e: e.memset(onescol_b1[:], 1.0))
    ld_bo = pr.op("gpsimd", lambda e: e.dma_start(out=rows[:, 0:D], in_=bo_d[None, :]), dma=True)
    ld_A = None
    ld_B = []
    for i, bd in enumerate([bq_d, bk_d, bv_d, bo2_d]):
        ld_B.append(pr.op("gpsimd", (lambda i, bd: lambda e: e.dma_start(
            out=Bn[:, i * DCH * 16:(i + 1) * DCH * 16].rearrange("p (c e) -> p c e", e=16),
            in_=bd.rearrange("(c p) e -> p c e", p=128)))(i, bd), dma=True))

    bcs = emit_bcast_rows(pr, [(rows[:, 0:D], bo_b, D)], ones_row, pA, [ld_bo, o_ones])
    bc_bo = bcs[0]

    ln1 = emit_sandwich_ln1(bld, pr, x_src, xch, xn_tm, st1, nsl=2)
    tr1 = emit_transpose(pr, lambda c: xn_tm[:, c * D:(c + 1) * D],
                         lambda f: xn_fm[:, f * T:(f + 1) * T], TC, DCH, idn, pT,
                         dep_src=ln1, dep_idn=[ld_id])
    for key in ("stage", ("Pt", 0), ("Pt", 1)):
        bld.setlu(key, tr1[-1])
    ld_A = []
    for i, ad in enumerate([aq_d, ak_d, av_d, ao_d]):
        ld_A.append(pr.op("gpsimd", (lambda i, ad: lambda e: e.dma_start(
            out=An[:, i * D:(i + 1) * D], in_=ad))(i, ad), deps=[tr1[-1]], dma=True))

    at_ops = []
    for i in range(4):
        def mk_t(i=i):
            def fn(t):
                for k in range(DCH):
                    instr = t.transpose(pT[0][:, k * 16:(k + 1) * 16],
                                        An[0:16, i * D + k * 128:i * D + (k + 1) * 128],
                                        idn[0:16, 0:16])
                return instr
            return fn
        ot = pr.op("tensor", mk_t(), deps=[ld_A[i], ld_id] + bld.war(id(pT[0])))
        oe = pr.op("vector", (lambda i: lambda v: v.tensor_copy(
            AT[:, i * DCH * 16:(i + 1) * DCH * 16], pT[0][:, 0:DCH * 16]))(i), deps=[ot])
        bld.setlu(id(pT[0]), oe)
        at_ops.append(oe)
    bt_ops = []
    for i in range(4):
        def mk_t(i=i):
            def fn(t):
                for k in range(DCH):
                    instr = t.transpose(pT[0][0:16, k * 128:(k + 1) * 128],
                                        Bn[:, (i * DCH + k) * 16:(i * DCH + k) * 16 + 16],
                                        idn[:])
                return instr
            return fn
        ot = pr.op("tensor", mk_t(), deps=[ld_B[i], ld_id] + bld.war(id(pT[0])))
        oe = pr.op("vector", (lambda i: lambda v: v.tensor_copy(
            BT[0:16, i * D:(i + 1) * D], pT[0][0:16, 0:D]))(i), deps=[ot])
        bld.setlu(id(pT[0]), oe)
        bt_ops.append(oe)

    for key in ("Pt",):
        bld.setlu(key, at_ops)

    u_ops = [[], [], []]
    for i in range(3):
        for nb in range(4):
            p = pA[nb % 2]
            war = bld.war(id(p))

            def mk_t(i=i, nb=nb, p=p):
                def fn(t):
                    for k in range(DCH):
                        instr = t.matmul(p[0:16, :], AT[:, (i * DCH + k) * 16:(i * DCH + k) * 16 + 16],
                                         xn_fm[:, k * T + nb * 512:k * T + (nb + 1) * 512],
                                         start=(k == 0), stop=(k == DCH - 1))
                    return instr
                return fn
            ot = pr.op("tensor", mk_t(), deps=[at_ops[i]] + tr1 + war + ([at_ops[3]] if i > 0 else []))
            oe = pr.op("scalar", (lambda i, nb, p: lambda s: s.activation(
                u_bufs[i][0:16, nb * 512:(nb + 1) * 512], p[0:16, :],
                AF.Identity, scale=1.0 / 16))(i, nb, p), deps=[ot])
            bld.setlu(id(p), oe)
            u_ops[i].append(oe)

    def load_wt(w_d, dstT, tag):
        ld = pr.op("gpsimd", (lambda w_d: lambda g: g.dma_start(
            out=stage.rearrange("p (r c) -> p r c", c=D),
            in_=w_d.rearrange("(r p) c -> p r c", p=128)))(w_d),
            deps=bld.war("stage"), dma=True)
        tw = emit_transpose(pr, lambda rc: stage[:, rc * D:(rc + 1) * D],
                            lambda dc: dstT[:, dc * D:(dc + 1) * D], DCH, DCH, idn, pT,
                            dep_src=[ld] + bld.war(tag), dep_idn=[ld_id], evac_eng="vector")
        bld.setlu("stage", tw[-1])
        return tw

    twk = load_wt(wk_d, WT, "WT")
    pk = []
    for e in range(DCH):
        for nb in range(4):
            p = pA[(e * 4 + nb) % 2]
            war = bld.war(id(p))

            def mk_t(e=e, nb=nb, p=p):
                def fn(t):
                    for k in range(DCH):
                        instr = t.matmul(p[:], WT[:, k * D + e * 128:k * D + (e + 1) * 128],
                                         xn_fm[:, k * T + nb * 512:k * T + (nb + 1) * 512],
                                         start=(k == 0), stop=False)
                    instr = t.matmul(p[:], BT[0:16, D + e * 128:D + (e + 1) * 128],
                                     u_k[0:16, nb * 512:(nb + 1) * 512],
                                     start=False, stop=True)
                    return instr
                return fn
            ot = pr.op("tensor", mk_t(), deps=twk + [bt_ops[1]] + u_ops[1] + war)
            oe = pr.op("vector", (lambda e, nb, p: lambda v: ecopy(v,
                k_fm[:, e * T + nb * 512:e * T + (nb + 1) * 512], p[:]))(e, nb, p), deps=[ot])
            bld.setlu(id(p), oe)
            pk.append(oe)

    bld.setlu(("Pt", 0), pk[-1])

    kn_st = pS[0:1, 0:64]
    kn_ops = []
    for e in range(DCH):
        for nb in range(4):
            sl = 0
            sqs = sqb[:, 0:512]
            osq = pr.op("scalar", (lambda e, nb, sqs: lambda s: s.activation(
                sqs, k_fm[:, e * T + nb * 512:e * T + (nb + 1) * 512], AF.Square))(e, nb, sqs),
                deps=[pk[e * 4 + nb]] + bld.war(("sqb", sl)))
            omms = []
            for half in range(2):
                h = 2 * e + half
                r0 = half * 64
                p = pA[half]
                omm = pr.op("tensor", (lambda r0, p, sqs: lambda t: t.matmul(
                    p[0:1, 0:512], onescol_b1[r0:r0 + 64, :],
                    sqs[r0:r0 + 64, :], start=True, stop=True))(r0, p, sqs),
                    deps=[osq, o_onec_b1] + bld.war(id(p)))
                col = h * 4 + nb
                orm = pr.op("vector", (lambda col, p: lambda v: v.tensor_reduce(
                    kn_st[0:1, col:col + 1], p[0:1, 0:512], AX.X, ALU.max))(col, p),
                    deps=[omm])
                bld.setlu(id(p), orm)
                omms.append(omm)
                kn_ops.append(orm)
            bld.setlu(("sqb", sl), omms)
    okm = pr.op("vector", lambda v: v.tensor_reduce(
        km2[0:1, 0:16], kn_st[0:1, :].rearrange("p (h n) -> p h n", n=4),
        AX.X, ALU.max), deps=kn_ops)
    bld.setlu(("pSb", 0), [okm])
    kn_ops = [okm]

    twv = load_wt(wv_d, WT, "WT2")
    bld.setlu("WT", pk[-1])
    pv = []
    for m in range(TC):
        for eb in range(2):
            p = pA[(m * 2 + eb) % 2]
            war = bld.war(id(p))

            def mk_t(m=m, eb=eb, p=p):
                def fn(t):
                    for k in range(DCH):
                        instr = t.matmul(p[:], xn_fm[:, k * T + m * 128:k * T + (m + 1) * 128],
                                         WT[:, k * D + eb * 512:k * D + (eb + 1) * 512],
                                         start=(k == 0), stop=False)
                    instr = t.matmul(p[:], u_v[0:16, m * 128:(m + 1) * 128],
                                     BT[0:16, 2 * D + eb * 512:2 * D + (eb + 1) * 512],
                                     start=False, stop=True)
                    return instr
                return fn
            ot = pr.op("tensor", mk_t(), deps=twv + [bt_ops[2]] + u_ops[2] + war)
            oe = pr.op("vector", (lambda m, eb, p: lambda v: ecopy(v,
                v_tm[:, m * D + eb * 512:m * D + (eb + 1) * 512], p[:]))(m, eb, p), deps=[ot])
            bld.setlu(id(p), oe)
            pv.append(oe)

    bld.setlu(("Pt", 1), pv[-1])

    two = load_wt(wo_d, WoT, "WoT")
    bld.setlu("WT2", pv[-1])
    twq = load_wt(wq_d, WqT, "WqT")
    for qt in range(4):
        for g in range(2):
            bld.setlu(("PT", qt, g), twq[-1])

    stores = []
    for qb in range(4):
        pq = []
        for e in range(DCH):
            p = pA[e % 2]
            war = bld.war(id(p))

            def mk_t(e=e, qb=qb, p=p):
                def fn(t):
                    for k in range(DCH):
                        instr = t.matmul(p[:], WqT[:, k * D + e * 128:k * D + (e + 1) * 128],
                                         xn_fm[:, k * T + qb * 512:k * T + (qb + 1) * 512],
                                         start=(k == 0), stop=False)
                    instr = t.matmul(p[:], BT[0:16, e * 128:(e + 1) * 128],
                                     u_q[0:16, qb * 512:(qb + 1) * 512],
                                     start=False, stop=True)
                    return instr
                return fn
            ot = pr.op("tensor", mk_t(), deps=twq + [bt_ops[0]] + u_ops[0] + war + bld.war(("q_qb", e)))
            oe = pr.op("vector", (lambda e, p: lambda v: ecopy(v,
                q_qb[:, e * 512:(e + 1) * 512], p[:]))(e, p), deps=[ot])
            bld.setlu(id(p), oe)
            bld.setlu(("q_qb", e), None)
            pq.append(oe)

        qn_ops = []
        for e in range(DCH):
            sqs = sqb[:, 0:512]
            osq = pr.op("scalar", (lambda e, sqs: lambda s: s.activation(
                sqs, q_qb[:, e * 512:(e + 1) * 512], AF.Square))(e, sqs),
                deps=[pq[e]] + bld.war(("sqb", 0)))
            omms = []
            for half in range(2):
                h = 2 * e + half
                r0 = half * 64
                p = pA[half]
                omm = pr.op("tensor", (lambda r0, p, sqs: lambda t: t.matmul(
                    p[0:1, 0:512], onescol_b1[r0:r0 + 64, :],
                    sqs[r0:r0 + 64, :], start=True, stop=True))(r0, p, sqs),
                    deps=[osq, o_onec_b1] + bld.war(id(p)))
                orm = pr.op("vector", (lambda h, p: lambda v: v.tensor_reduce(
                    qm2[0:1, h:h + 1], p[0:1, 0:512], AX.X, ALU.max))(h, p),
                    deps=[omm] + bld.war("qm2"))
                bld.setlu(id(p), orm)
                omms.append(omm)
                qn_ops.append(orm)
            bld.setlu(("sqb", 0), omms)
        o_mt = pr.op("vector", lambda v: v.tensor_mul(qm2[:], qm2[:], km2[:]),
                     deps=qn_ops + kn_ops)
        o_ms = pr.op("scalar", lambda s: s.activation(mrow[:], qm2[:], AF.Sqrt), deps=[o_mt] + bld.war("mrow"))
        bld.setlu("qm2", o_ms)
        o_mb = pr.op("tensor", lambda t: t.matmul(pA[0][0:128, 0:16], ones_row[:], mrow[:],
                                                  start=True, stop=True),
                     deps=[o_ms, o_ones] + bld.war(id(pA[0])))
        o_mc = pr.op("vector", lambda v: v.tensor_single_scalar(
            mcols[:], pA[0][0:128, 0:16], -0.125, ALU.mult),
            deps=[o_mb] + bld.war("mcols"))
        bld.setlu(id(pA[0]), o_mc)
        bld.setlu("mrow", o_mb)

        pend = []
        last_av = [None]

        def flush_pend(force=False):
            rest = []
            while pend:
                d, fn = pend.pop(0)
                if d <= 0 or force:
                    fn()
                else:
                    rest.append((d - 1, fn))
            pend.extend(rest)

        for h in range(H):
            fc, r0 = h // 2, (h % 2) * 64
            pt_evacs = []
            for qt in range(4):
                it = (h * 4 + qt) % 2
                a0 = it * 8
                Pt = Pt_[it]
                mms = []
                for kb in range(4):
                    omm = pr.op("tensor", (lambda h, qt, fc, r0, kb: lambda t: t.matmul(
                        pS[:, kb * 512:(kb + 1) * 512],
                        q_qb[r0:r0 + 64, fc * 512 + qt * 128:fc * 512 + (qt + 1) * 128],
                        k_fm[r0:r0 + 64, fc * T + kb * 512:fc * T + (kb + 1) * 512],
                        start=True, stop=True))(h, qt, fc, r0, kb),
                        deps=pq + pk + bld.war(("pSb", kb)))
                    mms.append(omm)
                flush_pend()
                exps = []
                for kb in range(4):
                    oex = pr.op("scalar", (lambda h, kb, a0, Pt=Pt: lambda s: s.activation(
                        Pt[:, kb * 512:(kb + 1) * 512], pS[:, kb * 512:(kb + 1) * 512],
                        AF.Exp, bias=mcols[:, h:h + 1], scale=0.125,
                        accum_out=attn_sc[:, a0 + kb:a0 + kb + 1]))(h, kb, a0),
                        deps=[mms[kb], o_mc] + bld.war(("Pt", it)) + bld.war(("attq", it)))
                    bld.setlu(("pSb", kb), oex)
                    exps.append(oex)
                osum = pr.op("vector", (lambda a0: lambda v: v.tensor_reduce(
                    attn_sc[:, a0 + 4:a0 + 5], attn_sc[:, a0:a0 + 4], AX.X, ALU.add))(a0),
                    deps=exps)
                ocl = pr.op("vector", (lambda a0: lambda v: v.tensor_scalar_max(
                    attn_sc[:, a0 + 4:a0 + 5], attn_sc[:, a0 + 4:a0 + 5], 1e-35))(a0), deps=[osum])
                orc = pr.op("vector", (lambda a0: lambda v: v.reciprocal(
                    attn_sc[:, a0 + 5:a0 + 6], attn_sc[:, a0 + 4:a0 + 5]))(a0), deps=[ocl])
                onr = pr.op("vector", (lambda a0, Pt=Pt: lambda v: v.tensor_scalar_mul(
                    Pt, Pt, attn_sc[:, a0 + 5:a0 + 6]))(a0), deps=[orc])
                bld.setlu(("attq", it), onr)

                def emit_trans(qt=qt, it=it, onr=onr, Pt=Pt, pt_evacs=pt_evacs):
                    PT3 = PT.rearrange("p (k c) -> p k c", c=512)
                    for g in range(2):
                        p = pT[g]
                        war = bld.war(id(p))

                        def mk_tt(g=g, p=p, Pt=Pt):
                            def fn(t):
                                for j in range(8):
                                    kc = g * 8 + j
                                    instr = t.transpose(p[:, j * 128:(j + 1) * 128],
                                                        Pt[:, kc * 128:(kc + 1) * 128], idn[:])
                                return instr
                            return fn
                        ott = pr.op("tensor", mk_tt(), deps=[onr] + war)
                        p3 = p[:].rearrange("p (k c) -> p k c", c=128)

                        def mk_tv(g=g, p3=p3, qt=qt):
                            dst = PT3[:, g * 8:(g + 1) * 8, qt * 128:(qt + 1) * 128]
                            return lambda v: ecopy(v, dst, p3)
                        oev = pr.op("vector", mk_tv(),
                                    deps=[ott] + bld.war(("PT", qt, g)))
                        bld.setlu(id(p), oev)
                        pt_evacs.append(oev)
                        bld.setlu(("Pt", it), ott)
                pend.append((1, emit_trans))

            def emit_av(h=h, fc=fc, r0=r0, pt_evacs=pt_evacs):
                p = pA[h % 2]
                war = bld.war(id(p))

                def mk_av(h=h, p=p):
                    def fn(t):
                        for kc in range(TC):
                            instr = t.matmul(p[0:64, :], v_tm[:, kc * D + h * 64:kc * D + h * 64 + 64],
                                             PT[:, kc * 512:(kc + 1) * 512],
                                             start=(kc == 0), stop=(kc == TC - 1))
                        return instr
                    return fn
                oav = pr.op("tensor", mk_av(), deps=pt_evacs + pv + war)
                oeav = pr.op("vector", (lambda h, p, fc, r0: lambda v: ecopy(v,
                    o_qb[r0:r0 + 64, fc * 512:(fc + 1) * 512], p[0:64, :]))(h, p, fc, r0),
                    deps=[oav] + bld.war(("o_qb", fc, r0)))
                bld.setlu(id(p), oeav)
                bld.setlu(("o_qb", fc, r0), None)
                for qt in range(4):
                    for g in range(2):
                        bld.setlu(("PT", qt, g), oav)
                last_av[0] = oav
                bld.setlu(("oqbh", h), oeav)
            pend.append((2, emit_av))
        flush_pend(force=True)

        oqb_all = [bld.lu[("oqbh", h)] for h in range(H)]
        uo_ops = []
        p = pA[0]
        war = bld.war(id(p))

        def mk_uo(qb=qb, p=p):
            def fn(t):
                for k in range(DCH):
                    instr = t.matmul(p[0:16, :], AT[:, (3 * DCH + k) * 16:(3 * DCH + k) * 16 + 16],
                                     o_qb[:, k * 512:(k + 1) * 512],
                                     start=(k == 0), stop=(k == DCH - 1))
                return instr
            return fn
        ot = pr.op("tensor", mk_uo(), deps=[at_ops[3]] + oqb_all + war)
        ouo = pr.op("scalar", (lambda p: lambda s: s.activation(
            u_o[0:16, :], p[0:16, :], AF.Identity, scale=1.0 / 16))(p), deps=[ot] + bld.war("u_o"))
        bld.setlu(id(p), ouo)

        for mm in range(4):
            m = qb * 4 + mm
            Fsl = F2[:, (mm % 2) * D:(mm % 2) * D + D]
            ev = []
            for eb in range(2):
                p = pA[(mm * 2 + eb) % 2]
                war = bld.war(id(p))

                def mk_t(mm=mm, eb=eb, p=p, qb=qb):
                    def fn(t):
                        for k in range(DCH):
                            instr = t.matmul(p[:], o_qb[:, k * 512 + mm * 128:k * 512 + (mm + 1) * 128],
                                             WoT[:, k * D + eb * 512:k * D + (eb + 1) * 512],
                                             start=(k == 0), stop=False)
                        instr = t.matmul(p[:], u_o[0:16, mm * 128:(mm + 1) * 128],
                                         BT[0:16, 3 * D + eb * 512:3 * D + (eb + 1) * 512],
                                         start=False, stop=True)
                        return instr
                    return fn
                ot = pr.op("tensor", mk_t(), deps=two + [bt_ops[3], ouo] + oqb_all + war)
                oe = pr.op("vector", (lambda Fsl, eb, p: lambda v: v.tensor_add(
                    Fsl[:, eb * 512:(eb + 1) * 512], p[:], bo_b[:, eb * 512:(eb + 1) * 512]))(Fsl, eb, p),
                    deps=[ot] + bc_bo + bld.war(("F2", mm % 2)))
                bld.setlu(id(p), oe)
                ev.append(oe)
            o7, o8 = ln_chunk_ops(pr, st2, m, Fsl, ev, 1e-5)
            o_t = ln_apply_chunk(pr, st2, m, Fsl, Fsl, [o7, o8])
            sl2 = m % 2
            xs2 = xch[:, sl2 * D:(sl2 + 1) * D]
            ld2 = pr.op("sync", (lambda m, xs2: lambda e: e.dma_start(
                out=xs2, in_=x_src[:, m, :]))(m, xs2), deps=bld.war(("xch", sl2)), dma=True)
            o_add = pr.op("vector", (lambda Fsl, xs2: lambda v: v.tensor_add(
                Fsl, Fsl, xs2))(Fsl, xs2), deps=[o_t, ld2])
            bld.setlu(("xch", sl2), o_add)
            ost = pr.op("sync", (lambda m, Fsl: lambda e: e.dma_start(
                out=xo_dst[:, m, :], in_=Fsl))(m, Fsl), deps=[o_add], dma=True)
            bld.setlu(("F2", mm % 2), ost)
            stores.append(ost)
        bld.setlu("u_o", ot)
        for h in range(H):
            fc, r0 = h // 2, (h % 2) * 64
            bld.setlu(("o_qb", fc, r0), ot)
        for e in range(DCH):
            bld.setlu(("q_qb", e), last_av[0])

    pr.op("sync", lambda e: e.nop(), deps=stores)


def build_block1():
    bld = Bld()
    nc = bld.nc
    dram = (
        nc.dram_tensor("x", [T, D], F32, kind="ExternalInput").ap(),
        nc.dram_tensor("Wq", [D, D], F32, kind="ExternalInput").ap(),
        nc.dram_tensor("Wk", [D, D], F32, kind="ExternalInput").ap(),
        nc.dram_tensor("Wv", [D, D], F32, kind="ExternalInput").ap(),
        nc.dram_tensor("Wo", [D, D], F32, kind="ExternalInput").ap(),
        nc.dram_tensor("bo", [D], F32, kind="ExternalInput").ap(),
        nc.dram_tensor("Aq", [16, D], F32, kind="ExternalInput").ap(),
        nc.dram_tensor("Bq", [D, 16], F32, kind="ExternalInput").ap(),
        nc.dram_tensor("Ak", [16, D], F32, kind="ExternalInput").ap(),
        nc.dram_tensor("Bk", [D, 16], F32, kind="ExternalInput").ap(),
        nc.dram_tensor("Av", [16, D], F32, kind="ExternalInput").ap(),
        nc.dram_tensor("Bv", [D, 16], F32, kind="ExternalInput").ap(),
        nc.dram_tensor("Ao", [16, D], F32, kind="ExternalInput").ap(),
        nc.dram_tensor("Bo", [D, 16], F32, kind="ExternalInput").ap(),
        nc.dram_tensor("iden", [128, 128], F32, kind="ExternalInput").ap(),
        nc.dram_tensor("xo", [T, D], F32, kind="ExternalOutput").ap(),
    )
    with bld.es:
        pr = Prog(bld, "b1")
        body_block1(bld, pr, dram)
        pr.run()
    return bld.nc










































def body_block2(bld, pr, dram):
    nc = bld.nc
    NH, WIN, HID = 8, 512, 2730
    (x_d, wqkv_d, qs_d, ks_d, wo_d, lng_d, lnb_d, w1_d, b1_d, w2_d, b2_d,
     idn_d, c1q_d, c2q_d, c1k_d, c2k_d, b2c_d, mu_d, ml_d, rot_d, xo_d) = dram
    x_src = x_d.rearrange("(c p) d -> p c d", p=128)
    xo_dst = xo_d.rearrange("(c p) d -> p c d", p=128)
    YW = 2052

    xch = bld.sbuf("xch", [128, 2 * D])
    Y = bld.sbuf("Y", [128, TC * D])
    Y_fm = bld.sbuf("Y_fm", [128, DCH * YW], BF16)
    arena = bld.sbuf("arena", [128, 29696], BF16)
    qrot = arena[:, 0:8192]
    kAv = arena[:, 8192:16384]
    v_tm = arena[:, 16384:24576]
    PT = arena[:, 24576:28672]
    Pt = arena[:, 28672:29696]
    w1buf_ = [arena[:, 0:2048], arena[:, 26624:28672]]
    w2buf = arena[:, 2048:13312]
    h4 = arena[:, 13312:24576]
    agst = arena[:, 24576:25600]
    h2st = arena[:, 25600:26624]
    tab_buf = arena[:, 24576:25600]
    wqkv_sb = bld.sbuf("wqkv_sb", [128, DCH * 512], BF16)
    wo_sb = bld.sbuf("wo_sb", [128, 4 * D], BF16)
    o_w = bld.sbuf("o_w", [128, 4 * 512], BF16)
    ybf = bld.sbuf("ybf", [128, D], BF16)
    sqs = bld.sbuf("sqs", [128, 512], BF16)
    s12 = bld.sbuf("s12", [128, 2 * 512], BF16)
    Pt2_ = s12[:, 0:1024]
    t2t = bld.sbuf("t2t", [128, 512], BF16)
    rotm = bld.sbuf("rotm", [128, 128], BF16)
    rotmq = bld.sbuf("rotmq", [128, 128], BF16)
    rotmk = bld.sbuf("rotmk", [128, 128], BF16)
    c1m = bld.sbuf("c1m", [128, 512], BF16)
    rawb = s12[:, 0:512]
    rkrow = bld.sbuf("rkrow", [1, 2 * 512], BF16)
    rq8 = bld.sbuf("rq8", [128, NH * TC])
    qb2 = bld.sbuf("qb2", [128, 512], BF16)
    b2col = bld.sbuf("b2col", [128, 1])
    qscol = bld.sbuf("qscol", [128, 1])
    kscol = bld.sbuf("kscol", [128, 1])
    maskU = bld.sbuf("maskU", [128, 128], BF16)
    maskL = bld.sbuf("maskL", [128, 128], BF16)
    onescol = bld.sbuf("onescol", [128, 1], BF16)
    ones_row = bld.sbuf("ones_row", [1, 128], BF16)
    idn = bld.sbuf("idn", [128, 128], BF16)
    rows = bld.sbuf("rows", [1, 1408], BF16)
    b1c = bld.sbuf("b1c", [128, 44])
    b2r = bld.sbuf("b2r", [128, D], BF16)
    gamr = bld.sbuf("gamr", [128, D], BF16)
    betr = bld.sbuf("betr", [128, D], BF16)
    att = bld.sbuf("attsc", [128, 8])
    gbc = bld.sbuf("gbc", [128, 16])
    hs = {k: bld.sbuf("hx" + k, [128, TC]) for k in
          ("s1a", "s1b", "s2a", "s2b", "sh1", "sh2", "rsJ", "nmJ")}
    n0b = bld.sbuf("n0b", [128, 1])
    nm0b = bld.sbuf("nm0b", [1, 1], BF16)
    junk = bld.sbuf("junk", [128, D], BF16)
    st1 = alloc_stats(bld, "b2ln1", junk=junk)
    st2 = alloc_stats(bld, "b2ln2", junk=junk)
    st3 = st1

    pS2 = [bld.psum("pS", [128, 1024]) for _ in range(2)]
    pA = [bld.psum("pA", [128, 512]) for _ in range(2)]
    pT = [bld.psum("pT", [128, 1024], BF16) for _ in range(2)]
    pB = pS2[0][:, 0:512]
    pN = pS2[1][0:1, 0:512]

    ld_id = pr.op("gpsimd", lambda e: e.dma_start(out=idn[:], in_=idn_d), dma=True)
    o_ones = pr.op("gpsimd", lambda e: e.memset(ones_row[:], 1.0))
    o_onec = pr.op("gpsimd", lambda e: e.memset(onescol[:], 1.0))
    ld_b2c = pr.op("gpsimd", lambda e: e.dma_start(out=b2col[:], in_=b2c_d), dma=True)
    ld_mu = pr.op("gpsimd", lambda e: e.dma_start(out=maskU[:], in_=mu_d), dma=True)
    ld_ml = pr.op("gpsimd", lambda e: e.dma_start(out=maskL[:], in_=ml_d), dma=True)
    ld_rot = pr.op("gpsimd", lambda e: e.dma_start(out=rotm[:], in_=rot_d), dma=True)

    for c in range(TC):
        sl = c % 2
        xs = xch[:, sl * D:(sl + 1) * D]
        ld = pr.op("sync", (lambda c, xs: lambda e: e.dma_start(out=xs, in_=x_src[:, c, :]))(c, xs),
                   deps=bld.war(("xch", sl)), dma=True)
        o7, o8 = ln_chunk_ops(pr, st1, c, xs, [ld], 1e-5)
        oap = ln_apply_chunk(pr, st1, c, xs, Y[:, c * D:(c + 1) * D], [o7, o8])
        bld.setlu(("xch", sl), [oap, o7])
        bld.setlu(("Yc", c), oap)

    def build_Y_fm():
        zops = []
        for f in range(DCH):
            zops.append(pr.op("vector", (lambda f: lambda v: v.memset(
                Y_fm[:, f * YW:f * YW + 1], 0.0))(f), deps=bld.war("Yfm_rd")))
        evs = []
        for tc in range(TC):
            yb = ybf[:, 0:D]
            oc = pr.op("scalar", (lambda tc, yb: lambda s: s.activation(
                yb, Y[:, tc * D:(tc + 1) * D], AF.Identity))(tc, yb),
                deps=bld.war(("Yc", tc)) + bld.war("ybf"))
            p = pT[tc % 2]
            war = bld.war(id(p))

            def mk_t(yb=yb, p=p):
                def fn(t):
                    for f in range(DCH):
                        instr = t.transpose(p[:, f * 128:(f + 1) * 128],
                                            yb[:, f * 128:(f + 1) * 128], idn[:])
                    return instr
                return fn
            ot = pr.op("tensor", mk_t(), deps=[oc, ld_id] + war)
            bld.setlu("ybf", ot)

            def mk_v(tc=tc, p=p):
                def fn(v):
                    for f in range(DCH):
                        instr = v.tensor_copy(
                            Y_fm[:, f * YW + 1 + tc * 128:f * YW + 1 + (tc + 1) * 128],
                            p[:, f * 128:(f + 1) * 128])
                    return instr
                return fn
            oe = pr.op("vector", mk_v(), deps=[ot] + zops + bld.war("Yfm_rd"))
            bld.setlu(id(p), oe)
            evs.append(oe)
        bld.setlu("Yfm_wr", evs)
        return evs

    for L in range(2):
        ld_qs = pr.op("gpsimd", (lambda L: lambda e: e.dma_start(
            out=rows[:, 0:64], in_=qs_d[L][None, :]))(L), deps=bld.war("rows"), dma=True)
        ld_ks = pr.op("gpsimd", (lambda L: lambda e: e.dma_start(
            out=rows[:, 64:128], in_=ks_d[L][None, :]))(L), deps=bld.war("rows"), dma=True)

        def mkqsc(t):
            t.matmul(pB[0:64, 0:1], rows[:, 0:64], onescol[0:1, :], start=True, stop=True)
            return t.matmul(pB[0:64, 1:2], rows[:, 64:128], onescol[0:1, :], start=True, stop=True)
        oqs_t = pr.op("tensor", mkqsc, deps=[ld_qs, ld_ks, o_onec] + bld.war(id(pB)))
        oc1 = pr.op("vector", lambda v: v.tensor_copy(qscol[0:64, :], pB[0:64, 0:1]),
                    deps=[oqs_t] + bld.war("qksc"))
        oc2 = pr.op("vector", lambda v: v.tensor_copy(qscol[64:128, :], pB[0:64, 0:1]), deps=[oqs_t])
        oc3 = pr.op("vector", lambda v: v.tensor_copy(kscol[0:64, :], pB[0:64, 1:2]), deps=[oqs_t])
        oc4 = pr.op("vector", lambda v: v.tensor_copy(kscol[64:128, :], pB[0:64, 1:2]), deps=[oqs_t])
        oqsc = [oc1, oc2]
        oksc = [oc3, oc4]
        bld.setlu(id(pB), oqsc + oksc)
        bld.setlu("rows", [oqs_t])
        o_rmq = pr.op("vector", lambda v: v.tensor_single_scalar(rotmq[:], rotm[:], qscol[:], ALU.mult),
                      deps=[ld_rot] + oqsc + bld.war("rotmq"))
        o_rmk = pr.op("vector", lambda v: v.tensor_single_scalar(rotmk[:], rotm[:], kscol[:], ALU.mult),
                      deps=[ld_rot] + oksc + bld.war("rotmk"))
        bld.setlu("rotmq", o_rmq)
        bld.setlu("rotmk", o_rmk)

        yfm = build_Y_fm()

        def shifted(kc, t0, wdt):
            off = 1 if kc < 4 else 0
            return Y_fm[:, kc * YW + off + t0: kc * YW + off + t0 + wdt]

        pq_done = []
        ldw_prev = None
        for part in range(2):
            ldw = pr.op("gpsimd", (lambda L, part: lambda e: e.dma_start(
                out=wqkv_sb[:].rearrange("p (c e) -> p c e", e=512),
                in_=wqkv_d[L, :, part * 512:(part + 1) * 512].rearrange("(c p) e -> p c e", p=128)))(L, part),
                deps=bld.war("wqkv_sb"), dma=True)
            dst = qrot if part == 0 else kAv
            scol = qscol if part == 0 else kscol
            osc = oqsc if part == 0 else oksc
            c1d = c1q_d if part == 0 else c1k_d
            c2d = c2q_d if part == 0 else c2k_d
            mm_ops = []
            for tb in range(4):
                ld_t1 = pr.op("gpsimd", (lambda tb, c1d: lambda e: e.dma_start(
                    out=tab_buf[:, 0:512], in_=c1d[:, tb * 512:(tb + 1) * 512]))(tb, c1d),
                    deps=bld.war("tab_buf"), dma=True)
                ld_t2 = pr.op("gpsimd", (lambda tb, c2d: lambda e: e.dma_start(
                    out=tab_buf[:, 512:1024], in_=c2d[:, tb * 512:(tb + 1) * 512]))(tb, c2d),
                    deps=bld.war("tab_buf"), dma=True)
                o_c1m = pr.op("vector", (lambda scol: lambda v: v.tensor_single_scalar(
                    c1m[:], tab_buf[:, 0:512], scol[:], ALU.mult))(scol),
                    deps=[ld_t1] + osc + bld.war("c1m"))
                tabdep = [ld_t1, ld_t2, o_c1m]
                for ec in range(4):
                    p = pA[(ec * 4 + tb) % 2]
                    war = bld.war(id(p))

                    def mk_t(ec=ec, tb=tb, p=p):
                        def fn(t):
                            for kc in range(DCH):
                                instr = t.matmul(p[:],
                                                 wqkv_sb[:, kc * 512 + ec * 128:kc * 512 + (ec + 1) * 128],
                                                 shifted(kc, tb * 512, 512),
                                                 start=(kc == 0), stop=(kc == DCH - 1))
                            return instr
                        return fn
                    ot = pr.op("tensor", mk_t(), deps=[ldw] + yfm + war + bld.war("arena_ep"))
                    mm_ops.append(ot)
                    osq = pr.op("scalar", (lambda p: lambda s: s.activation(
                        sqs[:], p[:], AF.Square))(p), deps=[ot] + bld.war("sqs"))
                    if part == 0:
                        lastn = []
                        for half in range(2):
                            h = 2 * ec + half
                            r0 = half * 64
                            for qt in range(4):
                                gqt = tb * 4 + qt
                                omm = pr.op("tensor", (lambda r0, qt: lambda t: t.matmul(
                                    pB[:, 4:5], sqs[r0:r0 + 64, qt * 128:(qt + 1) * 128],
                                    onescol[r0:r0 + 64, :], start=True, stop=True))(r0, qt),
                                    deps=[osq] + bld.war(id(pB)))
                                osr = pr.op("scalar", (lambda h, gqt: lambda s: s.activation(
                                    rq8[:, h * TC + gqt:h * TC + gqt + 1], pB[:, 4:5],
                                    AF.Sqrt, scale=1.0 / 64))(h, gqt), deps=[omm])
                                bld.setlu(id(pB), osr)
                                om1 = pr.op("vector", (lambda h, gqt: lambda v: v.tensor_scalar_max(
                                    rq8[:, h * TC + gqt:h * TC + gqt + 1],
                                    rq8[:, h * TC + gqt:h * TC + gqt + 1], 1.25e-13))(h, gqt), deps=[osr])
                                om2 = pr.op("vector", (lambda h, gqt: lambda v: v.reciprocal(
                                    rq8[:, h * TC + gqt:h * TC + gqt + 1],
                                    rq8[:, h * TC + gqt:h * TC + gqt + 1]))(h, gqt), deps=[om1])
                                lastn.append(om2)
                        bld.setlu(("rqd", ec, tb), lastn)
                    else:
                        rk_done = []
                        for half in range(2):
                            r0 = half * 64
                            omm = pr.op("tensor", (lambda r0: lambda t: t.matmul(
                                pN, onescol[r0:r0 + 64, :], sqs[r0:r0 + 64, :],
                                start=True, stop=True))(r0),
                                deps=[osq] + bld.war("pNk") + bld.war(id(pB)))
                            osr = pr.op("scalar", (lambda half: lambda s: s.activation(
                                rkrow[:, half * 512:(half + 1) * 512], pN, AF.Sqrt,
                                scale=1.0))(half), deps=[omm] + bld.war(("rkr", half)))
                            bld.setlu("pNk", osr)
                            bld.setlu(id(pB), bld.war(id(pB)) + [osr])
                            om1 = pr.op("vector", (lambda half: lambda v: v.tensor_scalar_max(
                                rkrow[:, half * 512:(half + 1) * 512],
                                rkrow[:, half * 512:(half + 1) * 512], 1e-12))(half), deps=[osr])
                            def mkrcp(half=half):
                                def fn(v):
                                    with bld.nc.allow_low_precision(reason="rk row recip feeds bf16 matmul operand"):
                                        return v.reciprocal(rkrow[:, half * 512:(half + 1) * 512],
                                                            rkrow[:, half * 512:(half + 1) * 512])
                                return fn
                            om2 = pr.op("vector", mkrcp(), deps=[om1])
                            rk_done.append(om2)
                    c2 = tab_buf[:, 512:1024]
                    orawc = pr.op("scalar", (lambda p: lambda s: s.activation(
                        rawb[:], p[:], AF.Identity))(p), deps=[ot, osq] + bld.war("rawb"))
                    o1 = pr.op("vector", (lambda p: lambda v: v.tensor_mul(
                        s12[:, 512:1024], p[:], c1m[:]))(p), deps=[ot, osq] + tabdep + bld.war("s12"))
                    bld.setlu(id(p), [o1, orawc])
                    rotm_u = rotmq if part == 0 else rotmk
                    orot = pr.op("tensor", (lambda rotm_u: lambda t: t.matmul(
                        pB[:, 0:512], rotm_u[:], rawb[:], start=True, stop=True))(rotm_u),
                        deps=[orawc, o_rmq, o_rmk] + bld.war(id(pB)))
                    bld.setlu("rawb", orot)
                    o2 = pr.op("vector", (lambda c2: lambda v: v.tensor_mul(
                        t2t[:], pB[:, 0:512], c2))(c2), deps=[orot] + tabdep + bld.war("t2t"))
                    bld.setlu(id(pB), o2)
                    o3 = pr.op("vector", lambda v: v.tensor_add(
                        s12[:, 512:1024], s12[:, 512:1024], t2t[:]), deps=[o1, o2])
                    bld.setlu("t2t", o3)
                    oadds = [o3]
                    dsl = dst[:, ec * T + tb * 512:ec * T + (tb + 1) * 512]
                    if part == 0:
                        ost_ = pr.op("vector", (lambda dsl: lambda v: v.tensor_copy(
                            dsl, s12[:, 512:1024]))(dsl), deps=oadds + bld.war("arena_ep"))
                        bld.setlu(("qwr", ec, tb), ost_)
                    else:
                        def mkbc():
                            def fn(t):
                                t.matmul(pB[0:64, 0:512], ones_row[:, 0:64], rkrow[:, 0:512],
                                         start=True, stop=True)
                                return t.matmul(pB[64:128, 0:512], ones_row[:, 0:64],
                                                rkrow[:, 512:1024], start=True, stop=True)
                            return fn
                        obc = pr.op("tensor", mkbc(), deps=rk_done + [o_ones] + bld.war(id(pB)))
                        ost_ = pr.op("vector", (lambda dsl: lambda v: v.scalar_tensor_tensor(
                            dsl, s12[:, 512:1024], 1.0, pB[:, 0:512], ALU.mult, ALU.mult))(dsl),
                            deps=oadds + [obc] + bld.war("arena_ep"))
                        bld.setlu(id(pB), ost_)
                        bld.setlu(("rkr", 0), ost_)
                        bld.setlu(("rkr", 1), ost_)
                        bld.setlu(("kwr", ec, tb), ost_)
                    bld.setlu("s12", ost_)
                    bld.setlu("sqs", [ost_])
                    bld.setlu("tab_buf", bld.war("tab_buf") + [o1] + [o for o in oadds])
                    pq_done.append(ost_)
            bld.setlu("wqkv_sb", mm_ops)
        bld.setlu("PTz", pq_done[-8:])

        ld_v = pr.op("gpsimd", (lambda L: lambda e: e.dma_start(
            out=wqkv_sb[:].rearrange("p (c e) -> p c e", e=512),
            in_=wqkv_d[L, :, 1024:1536].rearrange("(c p) e -> p c e", p=128)))(L),
            deps=bld.war("wqkv_sb"), dma=True)
        ld_wo = pr.op("gpsimd", (lambda L: lambda e: e.dma_start(
            out=wo_sb[:].rearrange("p (c e) -> p c e", e=D),
            in_=wo_d[L].rearrange("(c p) e -> p c e", p=128)))(L),
            deps=bld.war("wo_sb"), dma=True)
        pv_done = []
        for m in range(TC):
            p = pA[m % 2]
            war = bld.war(id(p))

            def mk_t(m=m, p=p):
                def fn(t):
                    for kc in range(DCH):
                        instr = t.matmul(p[:], shifted(kc, m * 128, 128),
                                         wqkv_sb[:, kc * 512:(kc + 1) * 512],
                                         start=(kc == 0), stop=(kc == DCH - 1))
                    return instr
                return fn
            ot = pr.op("tensor", mk_t(), deps=[ld_v] + yfm + war + bld.war("arena_ep"))
            oe = pr.op("scalar", (lambda m, p: lambda v: ecopy(v,
                v_tm[:, m * 512:(m + 1) * 512], p[:]))(m, p), deps=[ot])
            bld.setlu(id(p), oe)
            pv_done.append(oe)
        bld.setlu("wqkv_sb", [pv_done[-1]])
        bld.setlu("Yfm_rd", pq_done + pv_done)

        wo_res = []
        last_wo_mm = None
        for w in range(4):
            ow_ops = []
            qb2_bld = {}
            pend = []
            ozh = {0: pr.op("gpsimd", lambda e: e.memset(PT, 0.0),
                            deps=bld.war("PTz") + bld.war("arena_ep2"))}

            def flush_pend(force=False):
                rest = []
                while pend:
                    d, fn = pend.pop(0)
                    if d <= 0 or force:
                        fn()
                    else:
                        rest.append((d - 1, fn))
                pend.extend(rest)

            for h in range(NH):
                fc, r0 = h // 2, (h % 2) * 64
                pt_ev = []
                for qt in range(4):
                    pS = pS2[qt % 2]
                    Pt_q = Pt if qt % 2 == 0 else Pt2_
                    gqt = 4 * w + qt
                    kb_lo = qt if w > 0 else 4
                    kb_hi = qt + 4
                    scol0, scol1 = 128 * kb_lo, 128 * (kb_hi + 1)
                    if w > 0 and fc not in qb2_bld:
                        qb2_bld[fc] = pr.op("vector", (lambda fc, w: lambda v: v.tensor_single_scalar(
                            qb2[:], qrot[:, fc * T + w * 512:fc * T + (w + 1) * 512],
                            b2col[:], ALU.mult))(fc, w),
                            deps=[ld_b2c, bld.lu[("qwr", fc, w)]] + bld.war("qb2"))
                    oqb2 = qb2_bld.get(fc) if w > 0 else None

                    def mk_t(fc=fc, r0=r0, qt=qt, w=w, gqt=gqt, kb_lo=kb_lo, kb_hi=kb_hi, pS=pS):
                        def fn(t):
                            for kb in range(kb_lo, kb_hi + 1):
                                gk = 4 * (w - 1) + kb
                                prev = kb < 4
                                lhs = qb2[r0:r0 + 64, qt * 128:(qt + 1) * 128] if prev else \
                                    qrot[r0:r0 + 64, fc * T + gqt * 128:fc * T + (gqt + 1) * 128]
                                msk = maskU if (w > 0 and kb == kb_lo) else (
                                    maskL if kb == kb_hi else None)
                                instr = t.matmul(pS[:, kb * 128:(kb + 1) * 128], lhs,
                                                 kAv[r0:r0 + 64, fc * T + gk * 128:fc * T + (gk + 1) * 128],
                                                 start=True, stop=msk is None)
                                if msk is not None:
                                    instr = t.matmul(pS[:, kb * 128:(kb + 1) * 128],
                                                     idn[:], msk[:], start=False, stop=True)
                            return instr
                        return fn
                    deps = (pq_done + pv_done + bld.war(("pS", qt % 2)) +
                            bld.war(id(pB)) + bld.war("pNk") + [ld_mu, ld_ml, ld_id] +
                            ([oqb2] if oqb2 else []))
                    ot = pr.op("tensor", mk_t(), deps=deps)
                    if oqb2 is not None:
                        bld.setlu("qb2", bld.war("qb2") + [ot])
                    flush_pend()
                    a0 = (qt % 2) * 4
                    onx = pr.op("vector", (lambda scol0, scol1, pS, a0: lambda v: v.tensor_reduce(
                        att[:, a0:a0 + 1], pS[:, scol0:scol1], AX.X, ALU.max, negate=True))(scol0, scol1, pS, a0),
                        deps=[ot] + bld.war(("attsc", qt % 2)))
                    rq8c = rq8[:, h * TC + gqt:h * TC + gqt + 1]
                    obx = pr.op("vector", (lambda rq8c, a0: lambda v: v.tensor_mul(
                        att[:, a0:a0 + 1], att[:, a0:a0 + 1], rq8c))(rq8c, a0),
                        deps=[onx] + bld.war(("rqd", fc, gqt // 4)))
                    oex = pr.op("scalar", (lambda scol0, scol1, rq8c, pS, a0, Pt_q=Pt_q: lambda s: s.activation(
                        Pt_q[:, scol0:scol1], pS[:, scol0:scol1], AF.Exp,
                        bias=att[:, a0:a0 + 1], scale=rq8c, accum_out=att[:, a0 + 2:a0 + 3]))(scol0, scol1, rq8c, pS, a0),
                        deps=[obx] + bld.war(("Pt", qt % 2)) + bld.war("arena_ep2") +
                        (bld.war("s12") if qt % 2 else []))
                    bld.setlu(("pS", qt % 2), oex)
                    orc = pr.op("vector", (lambda a0: lambda v: v.reciprocal(
                        att[:, a0 + 3:a0 + 4], att[:, a0 + 2:a0 + 3]))(a0), deps=[oex])
                    if qt % 2 == 0:
                        onr = pr.op("scalar", (lambda scol0, scol1, a0, Pt_q=Pt_q: lambda s: s.activation(
                            Pt_q[:, scol0:scol1], Pt_q[:, scol0:scol1], AF.Identity,
                            scale=att[:, a0 + 3:a0 + 4]))(scol0, scol1, a0), deps=[orc])
                    else:
                        onr = pr.op("vector", (lambda scol0, scol1, a0, Pt_q=Pt_q: lambda v: v.tensor_scalar_mul(
                            Pt_q[:, scol0:scol1], Pt_q[:, scol0:scol1], att[:, a0 + 3:a0 + 4]))(scol0, scol1, a0), deps=[orc])
                    bld.setlu(("attsc", qt % 2), onr)

                    def emit_trans(qt=qt, kb_lo=kb_lo, kb_hi=kb_hi, onr=onr, h=h,
                                   Pt_q=Pt_q, pt_ev=pt_ev):
                        ptp = pT[qt % 2]
                        warp = bld.war(id(ptp))

                        def mk_tt(kb_lo=kb_lo, kb_hi=kb_hi, ptp=ptp, Pt_q=Pt_q):
                            def fn(t):
                                for j, kb in enumerate(range(kb_lo, kb_hi + 1)):
                                    instr = t.transpose(ptp[:, j * 128:(j + 1) * 128],
                                                        Pt_q[:, kb * 128:(kb + 1) * 128], idn[:])
                                return instr
                            return fn
                        ott = pr.op("tensor", mk_tt(), deps=[onr] + warp)
                        bld.setlu(("Pt", qt % 2), ott)
                        if qt % 2:
                            bld.setlu("s12", ott)
                        PT3 = PT.rearrange("p (k c) -> p k c", c=512)

                        def mk_tv(qt=qt, kb_lo=kb_lo, kb_hi=kb_hi, ptp=ptp, PT3=PT3):
                            nkb = kb_hi - kb_lo + 1
                            dst = PT3[:, kb_lo:kb_hi + 1, qt * 128:(qt + 1) * 128]
                            p3 = ptp[:].rearrange("p (k c) -> p k c", c=128)[:, 0:nkb, :]
                            return lambda v: ecopy(v, dst, p3)
                        oev = pr.op("scalar" if qt % 2 else "vector", mk_tv(),
                                    deps=[ott, ozh[h]] + bld.war("PTz2"))
                        bld.setlu(id(ptp), oev)
                        pt_ev.append(oev)
                    pend.append((1, emit_trans))

                def emit_av(h=h, w=w, fc=fc, r0=r0, pt_ev=pt_ev):
                    pav = pA[h % 2]
                    warA = bld.war(id(pav))
                    kbs = (4, 5, 6, 7) if w == 0 else (3, 4, 0, 1, 2, 5, 6, 7)

                    def mk_av(w=w, h=h, pav=pav, kbs=kbs):
                        def fn(t):
                            for i, kb in enumerate(kbs):
                                gk = 4 * (w - 1) + kb
                                if w == 0:
                                    c0, c1_ = (kb - 4) * 128, 512
                                else:
                                    c0, c1_ = max(0, kb - 4) * 128, min(3, kb) * 128 + 128
                                instr = t.matmul(pav[0:64, c0:c1_],
                                                 v_tm[:, gk * 512 + h * 64:gk * 512 + h * 64 + 64],
                                                 PT[:, kb * 512 + c0:kb * 512 + c1_],
                                                 start=(i == 0), stop=(i == len(kbs) - 1))
                            return instr
                        return fn
                    oav = pr.op("tensor", mk_av(), deps=list(pt_ev) + pv_done + warA)
                    oevav = pr.op("scalar", (lambda fc, r0, pav: lambda v: ecopy(v,
                        o_w[r0:r0 + 64, fc * 512:(fc + 1) * 512], pav[0:64, :]))(fc, r0, pav),
                        deps=[oav] + bld.war(("o_w", fc, r0)))
                    bld.setlu(id(pav), oevav)
                    bld.setlu("PTz", oav)
                    ow_ops.append(oevav)
                    if h + 1 < NH:
                        ozh[h + 1] = pr.op("gpsimd", lambda e: e.memset(PT, 0.0),
                                           deps=bld.war("PTz") + bld.war("arena_ep2"))
                pend.append((2, emit_av))
            flush_pend(force=True)
            for mm in range(4):
                m = 4 * w + mm
                for eb in range(2):
                    p = pA[(mm * 2 + eb) % 2]
                    war = bld.war(id(p))

                    def mk_t(mm=mm, eb=eb, p=p):
                        def fn(t):
                            for kc in range(4):
                                instr = t.matmul(p[:],
                                                 o_w[:, kc * 512 + mm * 128:kc * 512 + (mm + 1) * 128],
                                                 wo_sb[:, kc * D + eb * 512:kc * D + (eb + 1) * 512],
                                                 start=(kc == 0), stop=(kc == 3))
                            return instr
                        return fn
                    ot = pr.op("tensor", mk_t(), deps=ow_ops + [ld_wo] + war)
                    last_wo_mm = ot
                    oe = pr.op("vector", (lambda m, eb, p: lambda v: v.tensor_add(
                        Y[:, m * D + eb * 512:m * D + (eb + 1) * 512],
                        Y[:, m * D + eb * 512:m * D + (eb + 1) * 512], p[:]))(m, eb, p),
                        deps=[ot] + bld.war(("Yc", m)))
                    bld.setlu(id(p), oe)
                    bld.setlu(("Yc", m), oe)
                    wo_res.append(oe)
            for h in range(NH):
                fc, r0 = h // 2, (h % 2) * 64
                bld.setlu(("o_w", fc, r0), last_wo_mm)
        fence = wo_res[-1]
        bld.setlu(id(pB), bld.war(id(pB)) + bld.war(("pS", 0)) + bld.war(("pS", 1)))
        bld.setlu("pNk", bld.war("pNk") + bld.war(("pS", 1)))

        ld_g = pr.op("gpsimd", (lambda L: lambda e: e.dma_start(
            out=rows[:, 0:D], in_=lng_d[L][None, :]))(L), deps=bld.war("rows"), dma=True)
        bcs = emit_bcast_rows(pr, [(rows[:, 0:D], gamr, D)], ones_row, pA, [ld_g, o_ones])
        bc_g = bcs[0]
        gb_ops = []
        for f in range(DCH):
            og = pr.op("tensor", (lambda f: lambda t: t.matmul(
                pB[:, 6:7], rows[:, f * 128:(f + 1) * 128], onescol[0:1, :],
                start=True, stop=True))(f), deps=[ld_g, o_onec] + bld.war(id(pB)))
            oe1 = pr.op("vector", (lambda f: lambda v: v.tensor_copy(
                gbc[:, f:f + 1], pB[:, 6:7]))(f), deps=[og])
            bld.setlu(id(pB), oe1)
            gb_ops.append(oe1)
        ld_b = pr.op("gpsimd", (lambda L: lambda e: e.dma_start(
            out=rows[:, 0:D], in_=lnb_d[L][None, :]))(L),
            deps=bc_g + gb_ops, dma=True)
        bcs = emit_bcast_rows(pr, [(rows[:, 0:D], betr, D)], ones_row, pA, [ld_b, o_ones])
        bc_b = bcs[0]
        for f in range(DCH):
            ob_ = pr.op("tensor", (lambda f: lambda t: t.matmul(
                pB[:, 6:7], rows[:, f * 128:(f + 1) * 128], onescol[0:1, :],
                start=True, stop=True))(f), deps=[ld_b, o_onec] + bld.war(id(pB)))
            oe2 = pr.op("vector", (lambda f: lambda v: v.tensor_copy(
                gbc[:, 8 + f:9 + f], pB[:, 6:7]))(f), deps=[ob_])
            bld.setlu(id(pB), oe2)
            gb_ops.append(oe2)
        bld.setlu("rows", bc_b + gb_ops)
        hops = []
        for c in range(TC):
            Yc = Y[:, c * D:(c + 1) * D]
            dep = bld.war(("Yc", c))
            o1 = pr.op("vector", (lambda c, Yc: lambda v: v.tensor_reduce(
                hs["s1a"][:, c:c + 1], Yc[:, 0:512], AX.X, ALU.add))(c, Yc), deps=dep)
            o2 = pr.op("vector", (lambda c, Yc: lambda v: v.tensor_reduce(
                hs["s1b"][:, c:c + 1], Yc[:, 512:1024], AX.X, ALU.add))(c, Yc), deps=dep)
            o3 = pr.op("scalar", (lambda c, Yc: lambda s: s.activation(
                junk[:, 0:512], Yc[:, 0:512], AF.Square, accum_out=hs["s2a"][:, c:c + 1]))(c, Yc),
                deps=dep)
            o4 = pr.op("scalar", (lambda c, Yc: lambda s: s.activation(
                junk[:, 0:512], Yc[:, 512:1024], AF.Square, accum_out=hs["s2b"][:, c:c + 1]))(c, Yc),
                deps=dep)
            hops += [o1, o2, o3, o4]
        osz = pr.op("gpsimd", lambda e: e.memset(hs["sh1"][0:1, 0:1], 0.0), deps=bld.war("hsh"))
        osz2 = pr.op("gpsimd", lambda e: e.memset(hs["sh2"][0:1, 0:1], 0.0), deps=bld.war("hsh"))
        od1 = pr.op("gpsimd", lambda e: e.dma_start(out=hs["sh1"][1:128, :], in_=hs["s1b"][0:127, :]),
                    deps=hops + bld.war("hsh"), dma=True)
        od2 = pr.op("gpsimd", lambda e: e.dma_start(out=hs["sh1"][0:1, 1:TC], in_=hs["s1b"][127:128, 0:TC - 1]),
                    deps=hops, dma=True)
        od3 = pr.op("gpsimd", lambda e: e.dma_start(out=hs["sh2"][1:128, :], in_=hs["s2b"][0:127, :]),
                    deps=hops, dma=True)
        od4 = pr.op("gpsimd", lambda e: e.dma_start(out=hs["sh2"][0:1, 1:TC], in_=hs["s2b"][127:128, 0:TC - 1]),
                    deps=hops, dma=True)
        shd = [osz, osz2, od1, od2, od3, od4]
        n = TC
        s1 = pr.op("vector", lambda v: v.tensor_add(st2["s1"][:, 0:n], hs["s1a"][:, 0:n], hs["sh1"][:, 0:n]),
                   deps=shd + hops)
        s2 = pr.op("vector", lambda v: v.tensor_add(st2["s2"][:, 0:n], hs["s2a"][:, 0:n], hs["sh2"][:, 0:n]),
                   deps=shd + hops)
        o3b = pr.op("vector", lambda v: v.tensor_scalar_mul(st2["mu"][:, 0:n], st2["s1"][:, 0:n], 1.0 / D),
                    deps=[s1])
        o4b = pr.op("vector", lambda v: v.tensor_mul(st2["var"][:, 0:n], st2["mu"][:, 0:n], st2["mu"][:, 0:n]),
                    deps=[o3b])
        o5b = pr.op("vector", lambda v: v.scalar_tensor_tensor(st2["var"][:, 0:n], st2["s2"][:, 0:n],
                                                               1.0 / D, st2["var"][:, 0:n],
                                                               ALU.mult, ALU.subtract), deps=[s2, o4b])
        eps_ap, eps_op = bld.constcol(pr, 1e-5)
        o6b = pr.op("scalar", lambda s: s.activation(st2["sd"][:, 0:n], st2["var"][:, 0:n], AF.Sqrt,
                                                     bias=eps_ap), deps=[o5b, eps_op])
        o7b = pr.op("vector", lambda v: v.reciprocal(st2["rstd"][:, 0:n], st2["sd"][:, 0:n]), deps=[o6b])
        o8b = pr.op("vector", lambda v: v.scalar_tensor_tensor(st2["nmr"][:, 0:n], st2["mu"][:, 0:n],
                                                               -1.0, st2["rstd"][:, 0:n],
                                                               ALU.mult, ALU.mult), deps=[o7b])
        oj1 = pr.op("gpsimd", lambda e: e.dma_start(out=hs["rsJ"][0:127, :], in_=st2["rstd"][1:128, :]),
                    deps=[o7b, o8b] + bld.war("hsJ"), dma=True)
        oj2 = pr.op("gpsimd", lambda e: e.dma_start(out=hs["rsJ"][127:128, 0:TC - 1], in_=st2["rstd"][0:1, 1:TC]),
                    deps=[o7b, o8b], dma=True)
        oj3 = pr.op("gpsimd", lambda e: e.dma_start(out=hs["nmJ"][0:127, :], in_=st2["nmr"][1:128, :]),
                    deps=[o8b], dma=True)
        oj4 = pr.op("gpsimd", lambda e: e.dma_start(out=hs["nmJ"][127:128, 0:TC - 1], in_=st2["nmr"][0:1, 1:TC]),
                    deps=[o8b], dma=True)
        jd = [oj1, oj2, oj3, oj4, o8b]
        ocp0 = pr.op("vector", lambda v: v.tensor_copy(nm0b[:], st2["nmr"][0:1, 0:1]),
                     deps=[o8b] + bld.war("nm0b"))
        bld.setlu("nm0b", ocp0)
        ob0 = pr.op("tensor", lambda t: t.matmul(pB[:, 5:6], ones_row[:], nm0b[:],
                                                 start=True, stop=True),
                    deps=[ocp0, o_ones] + bld.war(id(pB)))
        ob0e = pr.op("vector", lambda v: v.tensor_copy(n0b[:], pB[:, 5:6]), deps=[ob0])
        bld.setlu(id(pB), ob0e)

        h2ops = []
        for tc in range(TC):
            hst = h2st
            depw = bld.war("h2st")
            oa = pr.op("scalar", (lambda tc, hst: lambda s: s.activation(
                hst[:, 0:512], Y[:, tc * D:tc * D + 512], AF.Identity,
                bias=st2["nmr"][:, tc:tc + 1], scale=st2["rstd"][:, tc:tc + 1]))(tc, hst),
                deps=[o8b] + depw + [fence])
            ob = pr.op("scalar", (lambda tc, hst: lambda s: s.activation(
                hst[:, 512:1024], Y[:, tc * D + 512:tc * D + 1024], AF.Identity,
                bias=hs["nmJ"][:, tc:tc + 1], scale=hs["rsJ"][:, tc:tc + 1]))(tc, hst),
                deps=jd + depw)
            og1 = pr.op("vector", (lambda hst: lambda v: v.tensor_mul(
                hst[:, 0:1024], hst[:, 0:1024], gamr[:, 0:1024]))(hst), deps=[oa, ob] + bc_g)
            og2 = pr.op("vector", (lambda hst: lambda v: v.tensor_add(
                hst[:, 0:1024], hst[:, 0:1024], betr[:, 0:1024]))(hst), deps=[og1] + bc_b)
            p = pT[tc % 2]
            war = bld.war(id(p))

            def mk_t(hst=hst, p=p):
                def fn(t):
                    for f in range(DCH):
                        instr = t.transpose(p[:, f * 128:(f + 1) * 128],
                                            hst[:, f * 128:(f + 1) * 128], idn[:])
                    return instr
                return fn
            ot = pr.op("tensor", mk_t(), deps=[og2] + war)
            bld.setlu("h2st", ot)

            def mk_v(tc=tc, p=p):
                def fn(v):
                    for f in range(DCH):
                        if f < 4:
                            instr = v.tensor_copy(
                                Y_fm[:, f * YW + 1 + tc * 128:f * YW + 1 + (tc + 1) * 128],
                                p[:, f * 128:(f + 1) * 128])
                        else:
                            wdt = 128 if tc < TC - 1 else 127
                            instr = v.tensor_copy(
                                Y_fm[:, f * YW + 2 + tc * 128:f * YW + 2 + tc * 128 + wdt],
                                p[:, f * 128:f * 128 + wdt])
                    return instr
                return fn
            oe = pr.op("vector", mk_v(), deps=[ot] + bld.war("Yfm_rd2") +
                       (bld.war("Yfm_rd") if tc == 0 else []))
            bld.setlu(id(p), oe)
            h2ops.append(oe)
        pad_ops = []
        for f in range(4, DCH):
            opd = pr.op("vector", (lambda f: lambda v: v.scalar_tensor_tensor(
                Y_fm[:, f * YW + 1:f * YW + 2], n0b[:], 1.0, junk[:, 0:1], ALU.mult, ALU.bypass))(f),
                deps=[ob0e] + h2ops) if False else None
            op1 = pr.op("vector", (lambda f: lambda v: v.tensor_single_scalar(
                Y_fm[:, f * YW + 1:f * YW + 2], n0b[:], gbc[:, f:f + 1], ALU.mult))(f),
                deps=[ob0e] + gb_ops + h2ops)
            op2 = pr.op("vector", (lambda f: lambda v: v.tensor_single_scalar(
                Y_fm[:, f * YW + 1:f * YW + 2], Y_fm[:, f * YW + 1:f * YW + 2],
                gbc[:, 8 + f:9 + f], ALU.add))(f), deps=[op1])
            pad_ops.append(op2)
        y2done = h2ops + pad_ops
        bld.setlu("hsh", [s1, s2])
        bld.setlu("hsJ", y2done)

        ld_b1 = pr.op("gpsimd", (lambda L: lambda e: e.dma_start(
            out=rows[:, 0:1408], in_=b1_d[L][None, 0:1408]))(L), deps=bld.war("rows"), dma=True)
        ld_b2r = pr.op("gpsimd", (lambda L: lambda e: e.dma_start(
            out=rows[:, 0:D] if False else junk[0:1, 0:D], in_=b2_d[L][None, :]))(L),
            deps=bld.war("junkrow"), dma=True)
        b1_ops = []
        widths = [128] * 21 + [42]
        segs = [(0, 0, 11), (1408, 11, 22), (2730, 22, 33), (2730 + 1408, 33, 44)]
        for base, c0, c1_ in segs:
            ldp = pr.op("gpsimd", (lambda L, base: lambda e: e.dma_start(
                out=rows[:, 0:min(1408, 5460 - base)],
                in_=b1_d[L][None, base:base + min(1408, 5460 - base)]))(L, base),
                deps=bld.war("rows") + list(b1_ops), dma=True)
            for ci in range(c0, c1_):
                e = ci % 22
                w_ = widths[e]
                off = (ci - c0) * 128
                oga = pr.op("tensor", (lambda off, w_: lambda t: t.matmul(
                    pB[0:w_, 7:8], rows[:, off:off + w_], onescol[0:1, :],
                    start=True, stop=True))(off, w_), deps=[ldp, o_onec] + bld.war(id(pB)))
                oea = pr.op("vector", (lambda ci, w_: lambda v: v.tensor_copy(
                    b1c[0:w_, ci:ci + 1], pB[0:w_, 7:8]))(ci, w_), deps=[oga])
                bld.setlu(id(pB), oea)
                b1_ops.append(oea)
            bld.setlu("rows", list(b1_ops))
        bld.setlu("rows", b1_ops)
        bcs2 = emit_bcast_rows(pr, [(junk[0:1, 0:D], b2r, D)], ones_row, pA, [ld_b2r, o_ones])
        bc_b2 = bcs2[0]
        bld.setlu("junkrow", bc_b2)

        ffn_last = []
        for tb in range(4):
            h4_ops = {}
            for e in range(22):
                me = widths[e]
                w1buf = w1buf_[e % 2]
                lda = pr.op("gpsimd", (lambda L, e, me, w1buf: lambda g: g.dma_start(
                    out=w1buf[:, 0:me * 8].rearrange("p (c q) -> p c q", q=me),
                    in_=w1_d[L, :, e * 128:e * 128 + me].rearrange("(c p) q -> p c q", p=128)))(L, e, me, w1buf),
                    deps=bld.war(("w1b", e % 2, 0)) + ([fence] if tb == 0 and e == 0 else []), dma=True)
                ldg = pr.op("gpsimd", (lambda L, e, me, w1buf: lambda g: g.dma_start(
                    out=w1buf[:, 1024:1024 + me * 8].rearrange("p (c q) -> p c q", q=me),
                    in_=w1_d[L, :, 2730 + e * 128:2730 + e * 128 + me].rearrange("(c p) q -> p c q", p=128)))(L, e, me, w1buf),
                    deps=bld.war(("w1b", e % 2, 1)), dma=True)
                pa_ = pA[e % 2]
                war = bld.war(id(pa_))

                def mk_ta(e=e, me=me, tb=tb, pa_=pa_, w1buf=w1buf):
                    def fn(t):
                        for kc in range(DCH):
                            instr = t.matmul(pa_[0:me, :],
                                             w1buf[:, kc * me:(kc + 1) * me],
                                             Y_fm[:, kc * YW + 1 + tb * 512:kc * YW + 1 + (tb + 1) * 512],
                                             start=(kc == 0), stop=(kc == DCH - 1))
                        return instr
                    return fn
                ota = pr.op("tensor", mk_ta(), deps=[lda] + y2done + war)
                oea = pr.op("scalar", (lambda e, me, pa_: lambda s: s.activation(
                    agst[:, 0:512][0:me, :], pa_[0:me, :], AF.Identity,
                    bias=b1c[0:me, e:e + 1]))(e, me, pa_),
                    deps=[ota] + b1_ops + bld.war(("ag", 0)))
                bld.setlu(id(pa_), oea)
                war = bld.war(id(pa_))

                def mk_tg(e=e, me=me, tb=tb, pa_=pa_, w1buf=w1buf):
                    def fn(t):
                        for kc in range(DCH):
                            instr = t.matmul(pa_[0:me, :],
                                             w1buf[:, 1024 + kc * me:1024 + (kc + 1) * me],
                                             Y_fm[:, kc * YW + 1 + tb * 512:kc * YW + 1 + (tb + 1) * 512],
                                             start=(kc == 0), stop=(kc == DCH - 1))
                        return instr
                    return fn
                otg = pr.op("tensor", mk_tg(), deps=[ldg] + y2done + war)
                oeg = pr.op("scalar", (lambda e, me, pa_: lambda s: s.activation(
                    agst[:, 512:1024][0:me, :], pa_[0:me, :], AF.Gelu,
                    bias=b1c[0:me, 22 + e:23 + e]))(e, me, pa_),
                    deps=[otg] + b1_ops + bld.war(("ag", 1)))
                bld.setlu(id(pa_), oeg)
                bld.setlu(("w1b", e % 2, 0), ota)
                bld.setlu(("w1b", e % 2, 1), otg)
                oh = pr.op("vector", (lambda e, me: lambda v: v.tensor_mul(
                    h4[:, e * 512:(e + 1) * 512][0:me, :], agst[0:me, 0:512],
                    agst[0:me, 512:1024]))(e, me), deps=[oea, oeg] + bld.war(("h4", e)))
                bld.setlu(("ag", 0), oh)
                bld.setlu(("ag", 1), oh)
                h4_ops[e] = oh
            for eb in range(2):
                ldw2a = pr.op("gpsimd", (lambda L, eb: lambda g: g.dma_start(
                    out=w2buf[:, 0:11 * 512].rearrange("p (c q) -> p c q", q=512),
                    in_=w2_d[L, 0:1408, eb * 512:(eb + 1) * 512].rearrange("(c p) q -> p c q", p=128)))(L, eb),
                    deps=bld.war("w2ba"), dma=True)
                ldw2b = pr.op("gpsimd", (lambda L, eb: lambda g: g.dma_start(
                    out=w2buf[:, 11 * 512:21 * 512].rearrange("p (c q) -> p c q", q=512),
                    in_=w2_d[L, 1408:2688, eb * 512:(eb + 1) * 512].rearrange("(c p) q -> p c q", p=128)))(L, eb),
                    deps=bld.war("w2bb"), dma=True)
                ldw2c = pr.op("gpsimd", (lambda L, eb: lambda g: g.dma_start(
                    out=w2buf[0:42, 21 * 512:22 * 512],
                    in_=w2_d[L, 2688:2730, eb * 512:(eb + 1) * 512]))(L, eb),
                    deps=bld.war("w2bb"), dma=True)
                w2mm = []
                w2mma = []
                for mm in range(4):
                    m = 4 * tb + mm
                    p = pA[mm % 2]
                    war = bld.war(id(p))

                    def mk_ta(mm=mm, p=p):
                        def fn(t):
                            for kc in range(11):
                                me = widths[kc]
                                instr = t.matmul(p[:],
                                                 h4[:, kc * 512 + mm * 128:kc * 512 + (mm + 1) * 128][0:me, :],
                                                 w2buf[:, kc * 512:(kc + 1) * 512][0:me, :],
                                                 start=(kc == 0), stop=False)
                            return instr
                        return fn
                    ota = pr.op("tensor", mk_ta(), deps=[ldw2a] + list(h4_ops.values()) + war)
                    w2mma.append(ota)

                    def mk_tb(mm=mm, p=p):
                        def fn(t):
                            for kc in range(11, 22):
                                me = widths[kc]
                                instr = t.matmul(p[:],
                                                 h4[:, kc * 512 + mm * 128:kc * 512 + (mm + 1) * 128][0:me, :],
                                                 w2buf[:, kc * 512:(kc + 1) * 512][0:me, :],
                                                 start=False, stop=(kc == 21))
                            return instr
                        return fn
                    ot = pr.op("tensor", mk_tb(), deps=[ota, ldw2b, ldw2c], )
                    w2mm.append(ot)
                    oe1 = pr.op("vector", (lambda m, eb, p: lambda v: v.tensor_add(
                        Y[:, m * D + eb * 512:m * D + (eb + 1) * 512],
                        Y[:, m * D + eb * 512:m * D + (eb + 1) * 512], p[:]))(m, eb, p),
                        deps=[ot] + bld.war(("Yc", m)))
                    oe2 = pr.op("vector", (lambda m, eb: lambda v: v.tensor_add(
                        Y[:, m * D + eb * 512:m * D + (eb + 1) * 512],
                        Y[:, m * D + eb * 512:m * D + (eb + 1) * 512],
                        b2r[:, eb * 512:(eb + 1) * 512]))(m, eb), deps=[oe1] + bc_b2)
                    bld.setlu(id(p), oe2)
                    bld.setlu(("Yc", m), oe2)
                    ffn_last.append(oe2)
                bld.setlu("w2ba", w2mma)
                bld.setlu("w2bb", w2mm)
            for e in range(22):
                bld.setlu(("h4", e), w2mm)
        bld.setlu("arena_ep", ffn_last)
        bld.setlu("arena_ep2", ffn_last)
        bld.setlu("Yfm_rd", ffn_last)
        bld.setlu("Yfm_rd2", None)
        bld.setlu("qksc", ffn_last)

    def get_F(m):
        return Y[:, m * D:(m + 1) * D], bld.war(("Yc", m))

    stores = emit_sandwich_tail(bld, pr, get_F, st3, xch, x_src, xo_dst, nsl=2)
    pr.op("sync", lambda e: e.nop(), deps=stores)


def block2_tables():
    HD = 64
    t = np.arange(T, dtype=np.float64)
    half = np.arange(0, HD, 2, dtype=np.float64)
    inv_freq = 1.0 / (10000.0 ** (half / HD))
    base = (half + 0.4 * HD) / (1.4 * HD)
    fm = np.concatenate([np.arange(32), np.arange(32)])
    freqs = t[None, :] * inv_freq[fm][:, None]
    cos = np.cos(freqs)
    sin = np.sin(freqs)

    tm = (t % 512) / 256.0
    xq = base[fm][:, None] ** tm[None, :]
    xk = base[fm][:, None] ** (-tm[None, :])
    c1q = (cos * xq)
    c2q = (sin * xq)
    c1k = (cos * xk)
    c2k = (sin * xk)
    b2c = (base[fm] ** 2.0)
    dup = lambda a: np.concatenate([a, a], 0).astype(np.float32)
    c1q, c2q, c1k, c2k = dup(c1q), dup(c2q), dup(c1k), dup(c2k)
    b2col = np.concatenate([b2c, b2c]).astype(np.float32)[:, None]
    PT_ = np.zeros((128, 128))
    for f in range(128):
        h0 = (f // 64) * 64
        fl = f % 64
        if fl < 32:
            PT_[h0 + fl + 32, f] = -1.0
        else:
            PT_[h0 + fl - 32, f] = 1.0
    ii, jj = np.mgrid[0:128, 0:128]
    mU = np.where(jj >= ii, 0.0, -1e9).astype(np.float32)
    mL = np.where(jj <= ii, 0.0, -1e9).astype(np.float32)
    return c1q, c2q, c1k, c2k, b2col, mU, mL, PT_.astype(np.float32)


def build_block2():
    bld = Bld()
    nc = bld.nc
    dram = (
        nc.dram_tensor("x", [T, D], F32, kind="ExternalInput").ap(),
        nc.dram_tensor("wqkv", [2, D, 1536], F32, kind="ExternalInput").ap(),
        nc.dram_tensor("q_scale", [2, 64], F32, kind="ExternalInput").ap(),
        nc.dram_tensor("k_scale", [2, 64], F32, kind="ExternalInput").ap(),
        nc.dram_tensor("wo", [2, 512, D], F32, kind="ExternalInput").ap(),
        nc.dram_tensor("ln_g", [2, D], F32, kind="ExternalInput").ap(),
        nc.dram_tensor("ln_b", [2, D], F32, kind="ExternalInput").ap(),
        nc.dram_tensor("w1", [2, D, 5460], F32, kind="ExternalInput").ap(),
        nc.dram_tensor("b1", [2, 5460], F32, kind="ExternalInput").ap(),
        nc.dram_tensor("w2", [2, 2730, D], F32, kind="ExternalInput").ap(),
        nc.dram_tensor("b2", [2, D], F32, kind="ExternalInput").ap(),
        nc.dram_tensor("iden", [128, 128], F32, kind="ExternalInput").ap(),
        nc.dram_tensor("c1q", [128, T], F32, kind="ExternalInput").ap(),
        nc.dram_tensor("c2q", [128, T], F32, kind="ExternalInput").ap(),
        nc.dram_tensor("c1k", [128, T], F32, kind="ExternalInput").ap(),
        nc.dram_tensor("c2k", [128, T], F32, kind="ExternalInput").ap(),
        nc.dram_tensor("b2c", [128, 1], F32, kind="ExternalInput").ap(),
        nc.dram_tensor("maskU", [128, 128], F32, kind="ExternalInput").ap(),
        nc.dram_tensor("maskL", [128, 128], F32, kind="ExternalInput").ap(),
        nc.dram_tensor("rotm", [128, 128], F32, kind="ExternalInput").ap(),
        nc.dram_tensor("xo", [T, D], F32, kind="ExternalOutput").ap(),
    )
    with bld.es:
        pr = Prog(bld, "b2")
        body_block2(bld, pr, dram)
        pr.run()
    return bld.nc


def build_fused():
    bld = Bld()
    nc = bld.nc

    def din(name, shape):
        return nc.dram_tensor(name, shape, F32, kind="ExternalInput").ap()

    x_in = din("x", [T, D])
    h1 = nc.dram_tensor("h1", [T, D], F32, kind="Internal").ap()
    h2 = nc.dram_tensor("h2", [T, D], F32, kind="Internal").ap()
    h3 = nc.dram_tensor("h3", [T, D], F32, kind="Internal").ap()
    xo = nc.dram_tensor("out", [T, D], F32, kind="ExternalOutput").ap()
    iden = din("iden", [128, 128])

    d1 = (x_in, din("Wq", [D, D]), din("Wk", [D, D]), din("Wv", [D, D]),
          din("Wo", [D, D]), din("bo", [D]),
          din("Aq", [16, D]), din("Bq", [D, 16]), din("Ak", [16, D]), din("Bk", [D, 16]),
          din("Av", [16, D]), din("Bv", [D, 16]), din("Ao", [16, D]), din("Bo", [D, 16]),
          iden, h1)
    d2 = (h1, din("wqkv", [2, D, 1536]), din("q_scale", [2, 64]), din("k_scale", [2, 64]),
          din("wo2", [2, 512, D]), din("ln_g", [2, D]), din("ln_b", [2, D]),
          din("w1", [2, D, 5460]), din("b1", [2, 5460]), din("w2", [2, 2730, D]),
          din("b2", [2, D]), iden, din("c1q", [128, T]), din("c2q", [128, T]),
          din("c1k", [128, T]), din("c2k", [128, T]), din("b2c", [128, 1]),
          din("maskU", [128, 128]), din("maskL", [128, 128]), din("rotm", [128, 128]), h2)
    d3 = (h2, din("w_qkv", [D, 1536]), din("b_qkv", [1536]), din("w_fc", [512, D]),
          din("b_fc", [D]), din("gamma", [D]), din("beta", [D]), iden, h3)
    d4 = (h3, din("gate_w", [16, D]), din("bias_e", [16]), din("up_w", [16, 512, D]),
          din("down_w", [16, D, 512]), din("shared_up", [2048, D]),
          din("shared_down", [D, 2048]), iden, xo)

    with bld.es:
        for body, dd in ((body_block1, d1), (body_block2, d2),
                         (body_block3, d3), (body_block4, d4)):
            bld.push_scope()
            pr = Prog(bld)
            body(bld, pr, dd)
            pr.run(barrier=True)
            bld.pop_scope()
    return bld.nc


FUSED = True
_NC_CACHE = {}


def _ncs():
    if "b" not in _NC_CACHE:
        _NC_CACHE["b"] = (build_block1(), build_block2(), build_block3(), build_block4())
    return _NC_CACHE["b"]


def kernel_fused(**inputs):
    g = {k: np.ascontiguousarray(np.asarray(v, dtype=np.float32)) for k, v in inputs.items()}
    iden = np.eye(128, dtype=np.float32)
    c1q, c2q, c1k, c2k, b2c, mU, mL, rotm = block2_tables()
    if "f" not in _NC_CACHE:
        _NC_CACHE["f"] = build_fused()
    nc = _NC_CACHE["f"]
    x = g["L0_attn_x"]
    ims = []
    for c in range(NCORE):
        ims.append({
            "x": np.ascontiguousarray(x[c]), "iden": iden,
            "Wq": g["L0_attn_Wq"], "Wk": g["L0_attn_Wk"], "Wv": g["L0_attn_Wv"],
            "Wo": g["L0_attn_Wo"], "bo": g["L0_attn_bo"],
            "Aq": g["L0_attn_Aq"], "Bq": g["L0_attn_Bq"], "Ak": g["L0_attn_Ak"],
            "Bk": g["L0_attn_Bk"], "Av": g["L0_attn_Av"], "Bv": g["L0_attn_Bv"],
            "Ao": g["L0_attn_Ao"], "Bo": g["L0_attn_Bo"],
            "wqkv": g["L0_ffn_wqkv"], "q_scale": g["L0_ffn_q_scale"],
            "k_scale": g["L0_ffn_k_scale"], "wo2": g["L0_ffn_wo"],
            "ln_g": g["L0_ffn_ln_g"], "ln_b": g["L0_ffn_ln_b"],
            "w1": g["L0_ffn_w1"], "b1": g["L0_ffn_b1"], "w2": g["L0_ffn_w2"],
            "b2": g["L0_ffn_b2"], "c1q": c1q, "c2q": c2q, "c1k": c1k, "c2k": c2k,
            "b2c": b2c, "maskU": mU, "maskL": mL, "rotm": rotm,
            "w_qkv": g["L1_attn_w_qkv"], "b_qkv": g["L1_attn_b_qkv"],
            "w_fc": g["L1_attn_w_fc"], "b_fc": g["L1_attn_b_fc"],
            "gamma": g["L1_attn_gamma"], "beta": g["L1_attn_beta"],
            "gate_w": g["L1_ffn_gate_w"], "bias_e": g["L1_ffn_bias_e"],
            "up_w": g["L1_ffn_up_w"], "down_w": g["L1_ffn_down_w"],
            "shared_up": g["L1_ffn_shared_up"], "shared_down": g["L1_ffn_shared_down"],
        })
    res = run_bass_kernel_spmd(nc, ims, core_ids=list(range(NCORE)))
    return np.stack([np.asarray(res.results[c]["out"], dtype=np.float32)
                     for c in range(NCORE)], axis=0)


def kernel(**inputs):
    if FUSED:
        return kernel_fused(**inputs)
    g = {k: np.ascontiguousarray(np.asarray(v, dtype=np.float32)) for k, v in inputs.items()}
    iden = np.eye(128, dtype=np.float32)
    c1q, c2q, c1k, c2k, b2c, mU, mL, rotm = block2_tables()
    nc1, nc2, nc3, nc4 = _ncs()

    x = g["L0_attn_x"]
    xs = [np.ascontiguousarray(x[c]) for c in range(NCORE)]

    im1 = [{"x": xs[c], "Wq": g["L0_attn_Wq"], "Wk": g["L0_attn_Wk"],
            "Wv": g["L0_attn_Wv"], "Wo": g["L0_attn_Wo"], "bo": g["L0_attn_bo"],
            "Aq": g["L0_attn_Aq"], "Bq": g["L0_attn_Bq"], "Ak": g["L0_attn_Ak"],
            "Bk": g["L0_attn_Bk"], "Av": g["L0_attn_Av"], "Bv": g["L0_attn_Bv"],
            "Ao": g["L0_attn_Ao"], "Bo": g["L0_attn_Bo"], "iden": iden}
           for c in range(NCORE)]
    r1 = run_bass_kernel_spmd(nc1, im1, core_ids=list(range(NCORE)))
    xs = [np.ascontiguousarray(np.asarray(r1.results[c]["xo"], dtype=np.float32))
          for c in range(NCORE)]

    im2 = [{"x": xs[c], "wqkv": g["L0_ffn_wqkv"], "q_scale": g["L0_ffn_q_scale"],
            "k_scale": g["L0_ffn_k_scale"], "wo": g["L0_ffn_wo"], "ln_g": g["L0_ffn_ln_g"],
            "ln_b": g["L0_ffn_ln_b"], "w1": g["L0_ffn_w1"], "b1": g["L0_ffn_b1"],
            "w2": g["L0_ffn_w2"], "b2": g["L0_ffn_b2"], "iden": iden,
            "c1q": c1q, "c2q": c2q, "c1k": c1k, "c2k": c2k, "b2c": b2c,
            "maskU": mU, "maskL": mL, "rotm": rotm} for c in range(NCORE)]
    r2 = run_bass_kernel_spmd(nc2, im2, core_ids=list(range(NCORE)))
    xs = [np.ascontiguousarray(np.asarray(r2.results[c]["xo"], dtype=np.float32))
          for c in range(NCORE)]

    im3 = [{"x": xs[c], "w_qkv": g["L1_attn_w_qkv"], "b_qkv": g["L1_attn_b_qkv"],
            "w_fc": g["L1_attn_w_fc"], "b_fc": g["L1_attn_b_fc"],
            "gamma": g["L1_attn_gamma"], "beta": g["L1_attn_beta"], "iden": iden}
           for c in range(NCORE)]
    r3 = run_bass_kernel_spmd(nc3, im3, core_ids=list(range(NCORE)))
    xs = [np.ascontiguousarray(np.asarray(r3.results[c]["xo"], dtype=np.float32))
          for c in range(NCORE)]

    im4 = [{"x": xs[c], "gate_w": g["L1_ffn_gate_w"], "bias_e": g["L1_ffn_bias_e"],
            "up_w": g["L1_ffn_up_w"], "down_w": g["L1_ffn_down_w"],
            "shared_up": g["L1_ffn_shared_up"], "shared_down": g["L1_ffn_shared_down"],
            "iden": iden} for c in range(NCORE)]
    r4 = run_bass_kernel_spmd(nc4, im4, core_ids=list(range(NCORE)))
    out = np.stack([np.asarray(r4.results[c]["xo"], dtype=np.float32)
                    for c in range(NCORE)], axis=0)
    return out
